# version 17
# baseline (speedup 1.0000x reference)
import numpy as np
import ml_dtypes
from contextlib import ExitStack

import concourse.bass as bass
import concourse.mybir as mybir
from concourse.bass_utils import run_bass_kernel_spmd

F32 = mybir.dt.float32
BF16 = mybir.dt.bfloat16
AF = mybir.ActivationFunctionType
ALU = mybir.AluOpType
AX = mybir.AxisListType

NCORE = 8
GRP = 4
B, S, D = 2, 8192, 1024
TLOC = 2048
NCH = D // 128
NH, HD = 16, 64
HPC = NH // GRP
EPS = 1e-5
BF = ml_dtypes.bfloat16

SUB_ATTA0, SUB_FFN0A, SUB_ATTB0, SUB_FFN0B, SUB_ATT5, SUB_FFN1A, SUB_ATT7, SUB_FFN1B = range(8)



def _bias_cols(v):
    return np.ascontiguousarray(v.reshape(NCH, 128).T)


def _rope_table():
    theta = (1.0 / 10000.0 ** (2.0 * np.arange(0, HD // 2, dtype=np.float32) / HD)).astype(np.float32)
    seq = np.arange(1, S + 1, dtype=np.float32)
    ang = np.einsum("n,d->nd", seq, theta).astype(np.float32)
    ang = np.concatenate([ang, ang], axis=-1)
    f = (np.cos(ang) + np.sin(ang)).astype(np.float32)
    return np.ascontiguousarray(np.tile(f.T, (2, 1)))


def _masks():
    p = np.arange(128)[:, None]
    j = np.arange(128)[None, :]
    t_upp = (j >= p).astype(BF)
    t_low = (j <= p).astype(BF)
    return t_upp, np.ascontiguousarray((j <= p).astype(BF)), t_low


def host_prepare(inp):
    x = np.asarray(inp["L0_attn_a_x"])

    WattnT = np.ascontiguousarray(np.asarray(inp["L0_attn_a_W_attn"]).T)
    b_attn = np.asarray(inp["L0_attn_a_b_attn"])
    WprojT = np.ascontiguousarray(np.asarray(inp["L0_attn_a_W_proj"]).T)
    b_proj = np.asarray(inp["L0_attn_a_b_proj"])
    WQT = np.ascontiguousarray(np.asarray(inp["L0_attn_b_W_Q"]).T)
    WKT = np.ascontiguousarray(np.asarray(inp["L0_attn_b_W_K"]).T)
    WVT = np.ascontiguousarray(np.asarray(inp["L0_attn_b_W_V"]).T)
    WOT = np.ascontiguousarray(np.asarray(inp["L0_attn_b_W_O"]).T)
    W0 = np.asarray(inp["L0_ffn_weight"])
    A1 = np.asarray(inp["L0_ffn_aux_weight1"])
    A2 = np.asarray(inp["L0_ffn_aux_weight2"])
    b0 = np.asarray(inp["L0_ffn_bias"])
    W5 = np.asarray(inp["L1_attn_a_W"])
    g7 = np.asarray(inp["L1_attn_b_ln_gamma"])
    be7 = np.asarray(inp["L1_attn_b_ln_beta"])
    Wqkv7 = np.asarray(inp["L1_attn_b_w_qkv"])
    Wout7 = np.asarray(inp["L1_attn_b_w_out"])
    W6T = np.ascontiguousarray(np.asarray(inp["L1_ffn_weights"]).T)
    sc6 = np.asarray(inp["L1_ffn_scale"])
    b6 = np.asarray(inp["L1_ffn_bias"])

    f_rope = _rope_table()
    t_upp, t_low, _ = _masks()
    ident = np.eye(128, dtype=BF)

    pj = (np.arange(128)[:, None] - np.arange(512)[None, :]).astype(np.float32)

    in_maps = []
    for c in range(NCORE):
        b = c // GRP
        r = c % GRP
        g = r
        qbase = r * TLOC
        xT = np.ascontiguousarray(x[b, qbase:qbase + TLOC, :].T)

        hsl = slice(256 * g, 256 * (g + 1))
        Wqkv0_my = np.ascontiguousarray(
            np.concatenate([WattnT[:, 0:1024][:, hsl], WattnT[:, 1024:2048][:, hsl], WattnT[:, 2048:3072][:, hsl]], axis=1))
        bq_my = np.ascontiguousarray(b_attn[0:1024][hsl].reshape(2, 128).T)
        bk_my = np.ascontiguousarray(b_attn[1024:2048][hsl].reshape(2, 128).T)
        bv_my = np.ascontiguousarray(b_attn[2048:3072][hsl].reshape(1, 256))
        Wqkv3_my = np.ascontiguousarray(
            np.concatenate([WQT[:, hsl], WKT[:, hsl], WVT[:, hsl]], axis=1))

        cbc5 = np.zeros((128, 256), dtype=np.float32)
        for st_ in range(4):
            for kt in range(64):
                cbc5[:, st_ * 64 + kt] = qbase + st_ * 512 - kt * 128
        mask7 = np.zeros((128, 128), dtype=BF) if r == 0 else t_low

        m = {
            "x": xT,
            "Wqkv0_my": Wqkv0_my, "bq_my": bq_my, "bk_my": bk_my, "bv_my": bv_my,
            "WprojT": WprojT, "b_proj": _bias_cols(b_proj),
            "Wqkv3_my": Wqkv3_my, "WOT": WOT,
            "W0": W0, "A1": A1, "A2": A2, "b0": _bias_cols(b0),
            "W5": W5,
            "g7": _bias_cols(g7), "be7": _bias_cols(be7),
            "Wqkv7": Wqkv7, "Wout7": Wout7,
            "W6T": W6T, "sc6": _bias_cols(sc6), "b6": _bias_cols(b6),
            "f_rope": f_rope, "t_upp": t_upp, "t_low": t_low,
            "ident": ident, "pj": pj, "cbc5": cbc5, "mask7": mask7,
        }
        in_maps.append(m)
    return in_maps


SUB_INPUTS = {
    SUB_ATTA0: ["x", "Wqkv0_my", "bq_my", "bk_my", "bv_my", "WprojT", "b_proj", "f_rope", "t_upp"],
    SUB_FFN0A: ["x", "W0", "A1", "A2", "b0"],
    SUB_ATTB0: ["x", "Wqkv3_my", "WOT", "t_upp"],
    SUB_FFN0B: ["x", "W0", "A1", "A2", "b0"],
    SUB_ATT5:  ["x", "W5", "pj", "cbc5", "ident"],
    SUB_FFN1A: ["x", "W6T", "sc6", "b6"],
    SUB_ATT7:  ["x", "g7", "be7", "Wqkv7", "Wout7", "t_upp", "t_low", "mask7", "ident"],
    SUB_FFN1B: ["x", "W6T", "sc6", "b6"],
}




def _ap(t):
    return t if isinstance(t, bass.AP) else t.ap()

class Ctr:
    def __init__(self, sem):
        self.sem = sem
        self.v = 0

    def inc(self, n=1):
        self.v += n
        return self.v


class Prog:
    def __init__(self, name="prog"):
        self.nc = bass.Bass("TRN2", target_bir_lowering=False)
        self.ctx = ExitStack()
        self.name = name
        self.prefix = ""
        self._scope = None
        self._inputs = {}

    def _stack(self):
        return self._scope if self._scope is not None else self.ctx

    def push_scope(self, prefix):
        assert self._scope is None
        self._scope = ExitStack()
        self.prefix = prefix

    def pop_scope(self):
        self._scope.close()
        self._scope = None
        self.prefix = ""

    def sbuf(self, name, shape, dtype):
        return self._stack().enter_context(self.nc.sbuf_tensor(self.prefix + name, shape, dtype))

    def psum(self, name, shape, dtype=F32):
        return self._stack().enter_context(self.nc.psum_tensor(self.prefix + name, shape, dtype))

    def dram(self, name, shape, dtype):
        return self.nc.dram_tensor(self.prefix + name, shape, dtype)

    def in_ap(self, name, shape, dtype):
        if name not in self._inputs:
            self._inputs[name] = self.nc.dram_tensor(name, shape, dtype, kind="ExternalInput").ap()
        return self._inputs[name]

    def sem(self, name):
        return Ctr(self.ctx.enter_context(self.nc.semaphore(self.prefix + name)))

    def dma_sem(self, name=None):
        self._dma_sem_i = getattr(self, "_dma_sem_i", 0) + 1
        return Ctr(self.ctx.enter_context(
            self.nc.semaphore(self.prefix + (name or "sD") + str(self._dma_sem_i))))

    def start(self):
        self.block = self.ctx.enter_context(self.nc.Block())

    def finish(self):
        self.ctx.close()
        return self.nc


def common_sems(p):
    p.sPE = p.sem("sPE")
    p.sDV = p.sem("sDV")
    p.sAC = p.sem("sAC")
    p.sIO = p.sem("sIO")
    p.sST = p.dma_sem("sST")
    p.sCN = p.sem("sCN")



def emit_load_x(p, x_d, xres, after=None):
    if x_d is None:
        return (p.sDV.sem, 0)
    w = p.dma_sem("sX")

    @p.block.sync
    def _(sy):
        if after is not None:
            sy.wait_ge(after[0], after[1])
        sy.dma_start(out=xres[:, :, :], in_=x_d.rearrange("(c p) t -> p c t", p=128)).then_inc(w.sem, 16)
    w.inc(16)
    return (w.sem, w.v)


def emit_store_x(p, xres, xo_d, wait_dv=None):
    if xo_d is None:
        return None
    st = p.sST

    @p.block.sync
    def _(sy):
        if wait_dv is not None:
            sy.wait_ge(p.sDV.sem, wait_dv)
        sy.dma_start(out=xo_d.rearrange("(c p) t -> p c t", p=128), in_=xres[:, :, :]).then_inc(st.sem, 16)
        st.inc(16)
        sy.wait_ge(st.sem, st.v)
    return (st.sem, st.v)


def emit_consts(p):
    cn = p.sCN

    @p.block.gpsimd
    def _(g):
        g.memset(p.ones128_bf[:, :], 1.0)
        g.memset(p.ones1x128_f[:, :], 1.0).then_inc(cn.sem, 1)
    cn.inc(1)
    return cn.v


def emit_ln(p, xres, xn, xsq_scratch, tmp_f32, x_ready, extra=None):
    pe, dv, ac = p.sPE, p.sDV, p.sAC
    psBIG = p.psBIG
    xall = _ap(xres).rearrange("p c t -> p (c t)")
    xnall = _ap(xn).rearrange("p c t -> p (c t)")
    sqall = _ap(xsq_scratch).rearrange("p c t -> p (c t)")

    @p.block.vector
    def _(v):
        v.wait_ge(x_ready[0], x_ready[1])
        v.tensor_copy(xnall, xall).then_inc(dv.sem, 1)
        cast_done_ = dv.inc()
        v.wait_ge(dv.sem, cast_done_)
        v.tensor_mul(sqall, xnall, xnall).then_inc(dv.sem, 1)
        dv.inc()
    cast_done = dv.v - 1
    sq_done = dv.v

    @p.block.tensor
    def _(t):
        t.wait_ge(dv.sem, cast_done)
        t.wait_ge(p.sCN.sem, p.sCN.v)
        for c in range(NCH):
            for q in range(4):
                mm = t.matmul(psBIG[0:1, q * 512:(q + 1) * 512],
                              p.ones128_bf[:, :], xn[:, c, q * 512:(q + 1) * 512],
                              start=(c == 0), stop=(c == NCH - 1))
        mm.then_inc(pe.sem, 1)
    sx_done = pe.inc()

    @p.block.vector
    def _(v):
        v.wait_ge(pe.sem, sx_done)
        v.tensor_scalar_mul(p.r_m[:, :], psBIG[0:1, 0:2048], 1.0 / D).then_inc(dv.sem, 1)
    m_done = dv.inc()

    @p.block.tensor
    def _(t):
        t.wait_ge(dv.sem, max(sq_done, m_done))
        for c in range(NCH):
            for q in range(4):
                mm = t.matmul(psBIG[0:1, q * 512:(q + 1) * 512],
                              p.ones128_bf[:, :], xsq_scratch[:, c, q * 512:(q + 1) * 512],
                              start=(c == 0), stop=(c == NCH - 1))
        mm.then_inc(pe.sem, 1)
    sxx_done = pe.inc()

    @p.block.vector
    def _(v):
        v.wait_ge(pe.sem, sxx_done)
        v.tensor_scalar_mul(p.r_a[:, :], psBIG[0:1, 0:2048], 1.0 / D).then_inc(dv.sem, 1)
        dv.inc()
        v.tensor_mul(p.r_b[:, :], p.r_m[:, :], p.r_m[:, :]).then_inc(dv.sem, 1)
        v.wait_ge(dv.sem, dv.inc())
        v.tensor_sub(p.r_a[:, :], p.r_a[:, :], p.r_b[:, :]).then_inc(dv.sem, 1)
        v.wait_ge(dv.sem, dv.inc())
        v.tensor_scalar_add(p.r_a[:, :], p.r_a[:, :], EPS).then_inc(dv.sem, 1)
    vpe_done = dv.inc()

    @p.block.scalar
    def _(s):
        s.wait_ge(dv.sem, vpe_done)
        s.activation(p.r_b[:, :], p.r_a[:, :], AF.Sqrt).then_inc(ac.sem, 1)
    sqrt_done = ac.inc()

    @p.block.vector
    def _(v):
        v.wait_ge(ac.sem, sqrt_done)
        v.reciprocal(p.r_a[:, :], p.r_b[:, :]).then_inc(dv.sem, 1)
    rows_done = dv.inc()

    psum_free = rows_done
    for h in range(2):
        tok = slice(h * 1024, (h + 1) * 1024)

        @p.block.tensor
        def _(t):
            t.wait_ge(dv.sem, psum_free)
            for q in range(2):
                sl = slice(h * 1024 + q * 512, h * 1024 + (q + 1) * 512)
                t.matmul(psBIG[:, q * 512:(q + 1) * 512],
                         p.ones1x128_f[:, :], p.r_m[0:1, sl], start=True, stop=True)
                mm = t.matmul(psBIG[:, 1024 + q * 512:1024 + (q + 1) * 512],
                              p.ones1x128_f[:, :], p.r_a[0:1, sl], start=True, stop=True)
            mm.then_inc(pe.sem, 1)
        bc_done = pe.inc()

        @p.block.vector
        def _(v):
            v.wait_ge(pe.sem, bc_done)
            for c in range(NCH):
                v.tensor_sub(tmp_f32[:, 0:1024], xres[:, c, tok], psBIG[:, 0:1024]).then_inc(dv.sem, 1)
                v.wait_ge(dv.sem, dv.inc())
                v.tensor_mul(xn[:, c, tok], tmp_f32[:, 0:1024], psBIG[:, 1024:2048]).then_inc(dv.sem, 1)
                v.wait_ge(dv.sem, dv.inc())
        psum_free = dv.v

    if extra is not None:
        extra["m_row"] = p.r_m
        extra["rstd_row"] = p.r_a[:, :]
    return dv.v


def alloc_ln_rows(p):
    p.r_m = p.sbuf("r_m_t", [1, TLOC], F32).ap()
    p.r_a = p.sbuf("r_a", [1, TLOC], F32).ap()
    p.r_b = p.sbuf("r_b", [1, TLOC], F32).ap()


def emit_w_load(p, dram_ap, sbuf_t, n_din_ch, sem_ctr=None):
    w = sem_ctr if sem_ctr is not None else p.dma_sem()

    @p.block.gpsimd
    def _(g):
        g.dma_start(out=sbuf_t[:, :, :],
                    in_=dram_ap.rearrange("(k p) j -> p k j", p=128)).then_inc(w.sem, 16)
    w.inc(16)
    return (w.sem, w.v)


def emit_mm_dense(p, w_sb, x_sb, n_k, out_cb, wait_w=None, wait_dv=None,
                  m_chunks=None, psum_rot=None):
    pe, dv = p.sPE, p.sDV
    if m_chunks is None:
        m_chunks = range(NCH)
    slots = psum_rot
    n_slots = len(slots)
    evac_vals = [None] * n_slots

    it = 0
    for m in m_chunks:
        for q in range(4):
            slot = it % n_slots
            qsl = slice(q * 512, (q + 1) * 512)

            @p.block.tensor
            def _(t):
                if it == 0:
                    if wait_dv is not None:
                        t.wait_ge(dv.sem, wait_dv)
                    if wait_w is not None:
                        t.wait_ge(wait_w[0], wait_w[1])
                if evac_vals[slot] is not None:
                    t.wait_ge(dv.sem, evac_vals[slot])
                for k in range(n_k):
                    last = t.matmul(slots[slot], w_sb[:, k, m * 128:(m + 1) * 128], x_sb[:, k, qsl],
                                    start=(k == 0), stop=(k == n_k - 1))
                last.then_inc(pe.sem, 1)
            mm_done = pe.inc()
            evac_vals[slot] = out_cb(m, q, slots[slot], mm_done)
            it += 1




def _standalone(sub, prefix):
    p = Prog(prefix)
    x_d = p.nc.dram_tensor("x", [D, TLOC], F32, kind="ExternalInput").ap()
    xo_d = p.nc.dram_tensor("xo", [D, TLOC], F32, kind="ExternalOutput").ap()
    common_sems(p)
    p.xres = p.sbuf("xres_root", [128, NCH, TLOC], F32)
    p.start()
    p.push_scope(prefix)
    sub(p, x_d, xo_d)
    p.pop_scope()
    return p.finish()


def sub_ffn0(p, x_d, xo_d, after=None):
    nc = p.nc
    W0_d = p.in_ap("W0", [D, D], F32)
    A1_d = p.in_ap("A1", [D, D], F32)
    A2_d = p.in_ap("A2", [D, D], F32)
    b0_d = p.in_ap("b0", [128, NCH], F32)

    xres = p.xres
    xn = p.sbuf("xn", [128, NCH, TLOC], BF16)
    hA = p.sbuf("hA", [128, NCH, TLOC], BF16)
    w0 = p.sbuf("w0", [128, NCH, D], BF16)
    w1 = p.sbuf("w1", [128, NCH, D], BF16)
    w2 = p.sbuf("w2", [128, NCH, D], BF16)
    b0_sb = p.sbuf("b0_sb", [128, NCH], F32)
    tres = p.sbuf("tres", [128, 2, 512], F32)
    tmp_f32 = tres.ap().rearrange("p a b -> p (a b)")
    p.ones128_bf = p.sbuf("ones128_bf", [128, 1], BF16)
    p.ones1x128_f = p.sbuf("ones1x128_f", [1, 128], F32)
    alloc_ln_rows(p)
    p.psBIG = p.psum("psBIG", [128, 2048], F32)

    emit_consts(p)
    x_ready = emit_load_x(p, x_d, xres, after)
    swt = p.dma_sem("sWts")
    emit_w_load(p, W0_d, w0, NCH, swt)
    emit_w_load(p, A1_d, w1, NCH, swt)
    emit_w_load(p, A2_d, w2, NCH, swt)

    @p.block.gpsimd
    def _(g):
        g.dma_start(out=b0_sb[:, :], in_=b0_d).then_inc(swt.sem, 16)
    swt.inc(16)
    w_ready = (swt.sem, swt.v)
    b_ready = w_ready

    xn_ready = emit_ln(p, xres, xn, hA, tmp_f32, x_ready)

    done = emit_ffn0_body(p, xn, hA, w0, w1, w2, b0_sb, xres, tres,
                          xn_ready, w_ready, b_ready)

    return emit_store_x(p, xres, xo_d, wait_dv=done)


def emit_ffn0_body(p, xn, hA, w0, w1, w2, b0_sb, xres, tres, xn_ready, w_ready, b_ready):
    pe, dv, ac = p.sPE, p.sDV, p.sAC
    psBIG = p.psBIG
    slots = [psBIG[:, i * 512:(i + 1) * 512] for i in range(4)]

    def cb1(m, q, ps, mmv):
        @p.block.vector
        def _(v):
            v.wait_ge(pe.sem, mmv)
            v.tensor_copy(hA[:, m, q * 512:(q + 1) * 512], ps).then_inc(dv.sem, 1)
        return dv.inc()

    emit_mm_dense(p, w0, xn, NCH, cb1, wait_w=w_ready, wait_dv=xn_ready, psum_rot=slots)
    l1_done = dv.v

    def cb2(m, q, ps, mmv):
        @p.block.vector
        def _(v):
            v.wait_ge(pe.sem, mmv)
            v.tensor_copy(xn[:, m, q * 512:(q + 1) * 512], ps).then_inc(dv.sem, 1)
        return dv.inc()

    emit_mm_dense(p, w1, hA, NCH, cb2, wait_dv=l1_done, psum_rot=slots)
    l2_done = dv.v

    pe0_3 = pe.v
    ac0_3 = ac.v
    dv0_3 = dv.v

    @p.block.scalar
    def _(s):
        psl = [psBIG[:, i * 512:(i + 1) * 512] for i in range(4)]
        for i in range(32):
            m = i // 4
            s.wait_ge(pe.sem, pe0_3 + i + 1)
            if i == 0:
                s.wait_ge(b_ready[0], b_ready[1])
            if i >= 2:
                s.wait_ge(dv.sem, dv0_3 + i - 1)
            s.activation(tres[:, i % 2, :], psl[i % 4], AF.Tanh,
                         bias=b0_sb[:, m:m + 1]).then_inc(ac.sem, 1)
    ac.inc(32)

    tcnt = [0]

    def cb3(m, q, ps, mmv):
        i = tcnt[0]
        slot3 = i % 2
        acv = ac0_3 + i + 1

        @p.block.vector
        def _(v):
            v.wait_ge(ac.sem, acv)
            v.tensor_add(xres[:, m, q * 512:(q + 1) * 512],
                         xres[:, m, q * 512:(q + 1) * 512], tres[:, slot3, :]).then_inc(dv.sem, 1)
        tcnt[0] += 1
        val = dv.inc()
        assert val == dv0_3 + i + 1
        return val

    emit_mm_dense(p, w2, xn, NCH, cb3, wait_dv=l2_done, psum_rot=slots)
    return dv.v



def sub_ffn1(p, x_d, xo_d, after=None):
    nc = p.nc
    W6T_d = p.in_ap("W6T", [D, D], F32)
    sc_d = p.in_ap("sc6", [128, NCH], F32)
    b_d = p.in_ap("b6", [128, NCH], F32)

    xres = p.xres
    xn = p.sbuf("xn", [128, NCH, TLOC], BF16)
    scr = p.sbuf("scr", [128, NCH, TLOC], BF16)
    wf = p.sbuf("wf", [128, NCH, 1024], F32)
    ws = p.sbuf("ws", [128, NCH, D], BF16)
    sc_sb = p.sbuf("sc_sb", [128, NCH], F32)
    b_sb = p.sbuf("b_sb", [128, NCH], F32)
    tres = p.sbuf("tres", [128, 2, 512], F32)
    tmp_f32 = tres.ap().rearrange("p a b -> p (a b)")
    p.ones128_bf = p.sbuf("ones128_bf", [128, 1], BF16)
    p.ones1x128_f = p.sbuf("ones1x128_f", [1, 128], F32)
    alloc_ln_rows(p)
    p.psBIG = p.psum("psBIG", [128, 2048], F32)

    emit_consts(p)
    x_ready = emit_load_x(p, x_d, xres, after)

    swt = p.dma_sem("sWts")

    @p.block.sync
    def _(sy):
        sy.dma_start(out=wf[:, :, :], in_=W6T_d.rearrange("(k p) j -> p k j", p=128)).then_inc(swt.sem, 16)
        sy.dma_start(out=sc_sb[:, :], in_=sc_d).then_inc(swt.sem, 16)
        sy.dma_start(out=b_sb[:, :], in_=b_d).then_inc(swt.sem, 16)
    w_ready = (swt.sem, swt.inc(48))

    @p.block.vector
    def _(v):
        v.wait_ge(w_ready[0], w_ready[1])
        wfa = wf.ap().rearrange("p k j -> p (k j)")
        wsa = ws.ap().rearrange("p k j -> p (k j)")
        v.tensor_scalar(wsa, wfa, 0.0, 2.0, ALU.is_ge, ALU.mult).then_inc(p.sDV.sem, 1)
        v.wait_ge(p.sDV.sem, p.sDV.inc())
        v.tensor_scalar(wsa, wsa, 1.0, None, ALU.subtract).then_inc(p.sDV.sem, 1)
    sign_done = p.sDV.inc()

    xn_ready = emit_ln(p, xres, xn, scr, tmp_f32, x_ready)

    pe, dv = p.sPE, p.sDV
    slots = [p.psBIG[:, i * 512:(i + 1) * 512] for i in range(4)]

    def cb(m, q, ps, mmv):
        @p.block.vector
        def _(v):
            v.wait_ge(pe.sem, mmv)
            v.wait_ge(dv.sem, dv.v)
            tslice = tres[:, 0, :]
            v.tensor_scalar(tslice, ps, sc_sb[:, m:m + 1], b_sb[:, m:m + 1], ALU.mult, ALU.add).then_inc(dv.sem, 1)
            v.wait_ge(dv.sem, dv.inc())
            v.tensor_scalar(tslice, tslice, 100.0, -100.0, ALU.min, ALU.max).then_inc(dv.sem, 1)
            v.wait_ge(dv.sem, dv.inc())
            v.tensor_add(xres[:, m, q * 512:(q + 1) * 512],
                         xres[:, m, q * 512:(q + 1) * 512], tslice).then_inc(dv.sem, 1)
        return dv.inc()

    emit_mm_dense(p, ws, xn, NCH, cb, wait_dv=max(xn_ready, sign_done), psum_rot=slots)

    return emit_store_x(p, xres, xo_d, wait_dv=dv.v)



class DmaSlot:
    def __init__(self, p, name):
        self.ctr = p.dma_sem(name)

    def issue(self, eng, out, in_):
        if self.ctr.v > 0:
            eng.wait_ge(self.ctr.sem, self.ctr.v)
        eng.dma_start(out=out, in_=in_).then_inc(self.ctr.sem, 16)
        self.ctr.inc(16)
        return (self.ctr.sem, self.ctr.v)


def sub_att5(p, x_d, xo_d, after=None):
    nc = p.nc
    W5_d = p.in_ap("W5", [D, 3 * D], F32)
    pj_d = p.in_ap("pj", [128, 512], F32)
    cbc5_d = p.in_ap("cbc5", [128, 256], F32)
    ident_d = p.in_ap("ident", [128, 128], BF16)

    NROW = TLOC + 1
    cin = p.dram("cin5", [NROW, 2056], BF16)
    coutA = p.dram("cout5A", [GRP * 1024, 2056], BF16)
    coutB = p.dram("cout5B", [GRP * 1024, 2056], BF16)
    coutS = p.dram("cout5S", [GRP, 2056], BF16)
    cA_v = coutA.ap().rearrange("(r n) c -> r n c", r=GRP)
    cB_v = coutB.ap().rearrange("(r n) c -> r n c", r=GRP)

    def cout_v(rank, r0, csl):
        if r0 < 1024:
            return cA_v[rank, r0:r0 + 128, csl]
        return cB_v[rank, r0 - 1024:r0 - 1024 + 128, csl]
    KB = p.dram("KB5", [S, 1024], BF16)
    KB_v = KB.ap().rearrange("(t p) c -> t p c", p=128)

    xres = p.xres
    qfm = p.sbuf("qfm", [128, NCH, TLOC], BF16)
    arena2 = p.sbuf("arena2", [128, 4 * TLOC], F32)
    xn = arena2.ap().bitcast(BF16).rearrange("p (c t) -> p c t", c=NCH)
    O_sb = arena2.ap()[:, 0:NCH * 512].rearrange("p (c t) -> p c t", c=NCH)
    arena1 = p.sbuf("arena1", [128, 24576], BF16)
    w5 = arena1.ap().rearrange("p (k j) -> p k j", k=NCH)
    Vg = arena1.ap()[:, 0:16384].rearrange("p (g k t) -> p g k t", g=2, k=8)
    Pg = arena1.ap()[:, 16384:20480].rearrange("p (k t) -> p k t", k=8)
    Kt = arena1.ap()[:, 20480:22528].rearrange("p (s t) -> p s t", s=2)
    Praw = arena1.ap()[:, 22528:23552].rearrange("p (s t) -> p s t", s=2)
    qsq = arena1.ap()[:, 16384:20480].rearrange("p (a t) -> p a t", a=2)
    arena3 = p.sbuf("arena3", [128, 6144], F32)
    a3b = arena3.ap().bitcast(BF16)
    CT = a3b[:, 0:4112].rearrange("p (s c) -> p s c", s=2)
    sq_scr = a3b[:, 4112:5136]
    normc = a3b[:, 5136:5152]
    a2b = arena2.ap().bitcast(BF16)
    KBs = a2b[:, 10240:12288].rearrange("p (s c) -> p s c", s=2)
    Kt2 = a2b[:, 8192:10240].rearrange("p (s c) -> p s c", s=2)
    nm16f = p.sbuf("nm16f", [16, 1], F32)
    nm16b = p.sbuf("nm16b", [16, 1], BF16)
    srow = p.sbuf("srow", [1, 8], F32)
    srow_b = p.sbuf("srow_b", [1, 8], BF16)
    kms = p.sbuf("kms", [1, 4], BF16)
    negMb = p.sbuf("negMb", [128, 1], F32)
    rl = p.sbuf("rl", [1, 512], F32)
    pj_sb = p.sbuf("pj_sb", [128, 512], F32)
    cbc_sb = p.sbuf("cbc_sb", [128, 256], F32)
    ident = p.sbuf("ident_sb", [128, 128], BF16)
    tmp_f32 = arena3.ap()[:, 4096:5120]
    p.ones128_bf = p.sbuf("ones128_bf", [128, 1], BF16)
    p.ones1x128_f = p.sbuf("ones1x128_f", [1, 128], F32)
    p.r_m = arena3.ap()[0:1, 0:2048]
    p.r_a = arena3.ap()[0:1, 2048:4096]
    p.r_b = arena3.ap()[0:1, 4096:6144]

    p.psBIG = p.psum("psBIG", [128, 2048], F32)
    psE = p.psum("psE", [128, 1024], F32)
    psT_a = p.psum("psT_a", [128, 128], BF16)
    psT_b = p.psum("psT_b", [128, 128], BF16)
    psBIG = p.psBIG
    S_bank = [psBIG[:, 0:512], psBIG[:, 512:1024]]
    l_bank = psBIG[0:1, 1024:1536]
    O_bank = [psBIG[:, 1536:2048], psE[:, 0:512]]
    rlb_bank = psE[:, 512:1024]

    pe, dv, ac, io = p.sPE, p.sDV, p.sAC, p.sIO

    emit_consts(p)
    x_ready = emit_load_x(p, x_d, xres, after)
    swt = p.dma_sem("sWts")

    @p.block.gpsimd
    def _(g):
        g.dma_start(out=w5[:, :, :], in_=W5_d.rearrange("(k p) j -> p k j", p=128)).then_inc(swt.sem, 16)
        g.dma_start(out=pj_sb[:, :], in_=pj_d).then_inc(swt.sem, 16)
        g.dma_start(out=cbc_sb[:, :], in_=cbc5_d).then_inc(swt.sem, 16)
        g.dma_start(out=ident[:, :], in_=ident_d).then_inc(swt.sem, 16)
    w_ready = (swt.sem, swt.inc(64))

    xn_ready = emit_ln(p, xres, xn, qfm.ap().rearrange("p c t -> p c t"), tmp_f32, x_ready)

    slots = [psBIG[:, i * 512:(i + 1) * 512] for i in range(4)]

    def cb_q(m, q, ps, mmv):
        @p.block.vector
        def _(v):
            v.wait_ge(pe.sem, mmv)
            v.tensor_copy(qfm[:, m, q * 512:(q + 1) * 512], ps).then_inc(dv.sem, 1)
        return dv.inc()

    emit_mm_dense(p, w5, xn, NCH, cb_q, wait_w=w_ready, wait_dv=xn_ready, psum_rot=slots)
    q_done = dv.v

    co32 = p.sbuf("co32", [128, 1], F32)

    @p.block.gpsimd
    def _(g):
        g.memset(co32[:, :], 1.0 / 32.0)
        g.memset(srow[:, :], 0.0)
        g.wait_ge(dv.sem, xn_ready)
        g.memset(CT[:, 0, 2049:2056], 0.0)
        g.memset(CT[:, 1, 2049:2056], 0.0).then_inc(p.sCN.sem, 1)
    ct_pad_done = p.sCN.inc()

    ct_slot = [DmaSlot(p, f"sCT{i}") for i in range(2)]
    ct_out_ev = [None, None]
    ps_evac = [q_done] * 4
    for tt in range(16):
        sl = tt % 2
        tsl = slice(tt * 128, (tt + 1) * 128)
        copy_vals = []
        for j in range(4):
            @p.block.tensor
            def _(t):
                t.wait_ge(dv.sem, ps_evac[j])
                for k in range(NCH):
                    mm = t.matmul(slots[j], xn[:, k, tsl], w5[:, k, 1024 + j * 512:1024 + (j + 1) * 512],
                                  start=(k == 0), stop=(k == NCH - 1))
                mm.then_inc(pe.sem, 1)
            mmv = pe.inc()

            @p.block.vector
            def _(v):
                v.wait_ge(pe.sem, mmv)
                if tt < 2 and j == 0:
                    v.wait_ge(p.sCN.sem, ct_pad_done)
                if j == 0 and ct_out_ev[sl] is not None:
                    v.wait_ge(ct_out_ev[sl][0], ct_out_ev[sl][1])
                v.tensor_copy(CT[:, sl, j * 512:(j + 1) * 512], slots[j]).then_inc(dv.sem, 1)
            ps_evac[j] = dv.inc()
            copy_vals.append(ps_evac[j])

        @p.block.vector
        def _(v):
            v.wait_ge(dv.sem, copy_vals[-1])
            v.tensor_mul(sq_scr[:, :], CT[:, sl, 0:1024], CT[:, sl, 0:1024]).then_inc(dv.sem, 1)
            v.wait_ge(dv.sem, dv.inc())
            with nc.allow_low_precision(reason="k-norm row for softmax bound only"):
                v.tensor_reduce(CT[:, sl, 2048:2049], sq_scr[:, :], AX.X, ALU.add).then_inc(dv.sem, 1)
            v.wait_ge(dv.sem, dv.inc())
            v.tensor_copy(normc[:, tt:tt + 1], CT[:, sl, 2048:2049]).then_inc(dv.sem, 1)
        ct_stage_done = dv.inc()

        @p.block.sync
        def _(sy):
            sy.wait_ge(dv.sem, ct_stage_done)
            ct_out_ev[sl] = ct_slot[sl].issue(sy, cin.ap()[tsl, :], CT[:, sl, :])
        if tt == 7:
            @p.block.gpsimd
            def _(g, w0=ct_slot[0].ctr.v, w1=ct_slot[1].ctr.v):
                g.wait_ge(ct_slot[0].ctr.sem, w0)
                g.wait_ge(ct_slot[1].ctr.sem, w1)
                g.collective_compute(
                    "AllGather", ALU.bypass,
                    replica_groups=[[0, 1, 2, 3], [4, 5, 6, 7]],
                    ins=[cin.ap()[0:1024, :].opt()], outs=[coutA.ap().opt()],
                ).then_inc(io.sem, 1)
            cc_h = [io.inc(), None]
    kv_proj_done = dv.v

    @p.block.tensor
    def _(t):
        t.wait_ge(dv.sem, kv_proj_done)
        t.transpose(psT_a[0:16, 0:128], normc[:, 0:16], ident[:, :]).then_inc(pe.sem, 1)
    trn1 = pe.inc()

    @p.block.vector
    def _(v):
        v.wait_ge(pe.sem, trn1)
        v.tensor_reduce(nm16f[:, :], psT_a[0:16, 0:128], AX.X, ALU.max).then_inc(dv.sem, 1)
        v.wait_ge(dv.sem, dv.inc())
        v.tensor_copy(nm16b[:, :], nm16f[:, :]).then_inc(dv.sem, 1)
    nmb_done = dv.inc()

    @p.block.tensor
    def _(t):
        t.wait_ge(dv.sem, nmb_done)
        t.transpose(psT_b[0:1, 0:16], nm16b[0:16, 0:1], ident[0:16, 0:16]).then_inc(pe.sem, 1)
    trn2 = pe.inc()

    @p.block.vector
    def _(v):
        v.wait_ge(pe.sem, trn2)
        v.wait_ge(p.sCN.sem, ct_pad_done)
        v.tensor_reduce(srow[:, 1:2], psT_b[0:1, 0:16], AX.X, ALU.max).then_inc(dv.sem, 1)
        v.wait_ge(dv.sem, dv.inc())
        v.tensor_copy(srow_b[:, 0:8], srow[:, 0:8]).then_inc(dv.sem, 1)
    maxk_done = dv.inc()

    sc_slot = DmaSlot(p, "sCsc")

    @p.block.vector
    def _(v):
        v.wait_ge(dv.sem, maxk_done)
        if ct_out_ev[0] is not None:
            v.wait_ge(ct_out_ev[0][0], ct_out_ev[0][1])
        v.memset(CT[0:1, 0, 0:2056], 0.0).then_inc(dv.sem, 1)
        v.wait_ge(dv.sem, dv.inc())
        v.tensor_copy(CT[0:1, 0, 0:8], srow_b[:, 0:8]).then_inc(dv.sem, 1)
    scrow_done = dv.inc()

    @p.block.sync
    def _(sy):
        sy.wait_ge(dv.sem, scrow_done)
        sc_slot.issue(sy, cin.ap()[2048:2049, 0:2056], CT[0:1, 0, 0:2056])

    @p.block.gpsimd
    def _(g):
        for s in ct_slot:
            g.wait_ge(s.ctr.sem, s.ctr.v)
        g.collective_compute(
            "AllGather", ALU.bypass,
            replica_groups=[[0, 1, 2, 3], [4, 5, 6, 7]],
            ins=[cin.ap()[1024:2048, :].opt()], outs=[coutB.ap().opt()],
        ).then_inc(io.sem, 1)
    cc_h[1] = io.inc()

    @p.block.gpsimd
    def _(g):
        g.wait_ge(sc_slot.ctr.sem, sc_slot.ctr.v)
        g.collective_compute(
            "AllGather", ALU.bypass,
            replica_groups=[[0, 1, 2, 3], [4, 5, 6, 7]],
            ins=[cin.ap()[2048:2049, :].opt()], outs=[coutS.ap().opt()],
        ).then_inc(io.sem, 1)
    cc_st = io.inc()
    cc_done = cc_st

    qn_prev = None
    for c in range(NCH):
        @p.block.vector
        def _(v):
            v.wait_ge(dv.sem, q_done)
            if qn_prev is not None:
                v.wait_ge(pe.sem, qn_prev)
            v.tensor_mul(qsq[:, 0, :], qfm[:, c, :], qfm[:, c, :]).then_inc(dv.sem, 1)
        sqv = dv.inc()

        @p.block.tensor
        def _(t):
            t.wait_ge(dv.sem, sqv)
            if c == 0:
                t.wait_ge(dv.sem, kv_proj_done)
            for q in range(4):
                mm = t.matmul(psBIG[0:1, q * 512:(q + 1) * 512],
                              p.ones128_bf[:, :], qsq[:, 0, q * 512:(q + 1) * 512],
                              start=(c == 0), stop=(c == NCH - 1))
            mm.then_inc(pe.sem, 1)
        qn_prev = pe.inc()

    @p.block.vector
    def _(v):
        v.wait_ge(pe.sem, qn_prev)
        v.tensor_reduce(srow[:, 0:1], psBIG[0:1, 0:2048], AX.X, ALU.max).then_inc(dv.sem, 1)
    qn_done = dv.inc()

    km_slot = DmaSlot(p, "sKms")

    @p.block.sync
    def _(sy):
        sy.wait_ge(io.sem, cc_done)
        with nc.allow_non_contiguous_dma(reason="4 scalar max-norm values"):
            km_slot.issue(sy, kms[:, :], coutS.ap()[:, 1:2].rearrange("r b -> b r"))

    @p.block.vector
    def _(v):
        v.wait_ge(km_slot.ctr.sem, km_slot.ctr.v)
        v.wait_ge(dv.sem, qn_done)
        v.tensor_reduce(srow[:, 1:2], kms[:, :], AX.X, ALU.max).then_inc(dv.sem, 1)
        v.wait_ge(dv.sem, dv.inc())
        v.tensor_mul(srow[:, 2:3], srow[:, 0:1], srow[:, 1:2]).then_inc(dv.sem, 1)
    m_mul_done = dv.inc()

    @p.block.scalar
    def _(s):
        s.wait_ge(dv.sem, m_mul_done)
        s.activation(srow[:, 3:4], srow[:, 2:3], AF.Sqrt).then_inc(ac.sem, 1)
    m_sqrt_done = ac.inc()

    @p.block.vector
    def _(v):
        v.wait_ge(ac.sem, m_sqrt_done)
        v.tensor_scalar_mul(srow[:, 4:5], srow[:, 3:4], -1.0 / 32.0).then_inc(dv.sem, 1)
    negm_done = dv.inc()

    @p.block.tensor
    def _(t):
        t.wait_ge(dv.sem, negm_done)
        t.matmul(psE[:, 0:1], p.ones1x128_f[:, :], srow[0:1, 4:5], start=True, stop=True).then_inc(pe.sem, 1)
    negmb_mm = pe.inc()

    @p.block.vector
    def _(v):
        v.wait_ge(pe.sem, negmb_mm)
        v.tensor_copy(negMb[:, :], psE[:, 0:1]).then_inc(dv.sem, 1)
    negmb_done = dv.inc()

    kin_slot = [DmaSlot(p, f"sKin{i}") for i in range(2)]
    kb_out_slot = [DmaSlot(p, f"sKBo{i}") for i in range(2)]
    kt_consumed = [None, None]
    ct2_free = [None, None]
    psT = [psT_a, psT_b]
    psT_free = [None, None]
    kin_ev = [None, None]

    kbo_ev = [None] * 64

    def emit_preproc_group(kt_lo, kt_hi):
        for kt in range(kt_lo, kt_hi):
            sl = kt % 2
            rank = kt // 16
            r0 = (kt % 16) * 128

            @p.block.sync
            def _(sy, kt=kt, sl=sl, rank=rank, r0=r0):
                sy.wait_ge(io.sem, cc_h[(kt % 16) // 8])
                if kt_consumed[sl] is not None:
                    sy.wait_ge(pe.sem, kt_consumed[sl])
                kin_ev[sl] = kin_slot[sl].issue(sy, Kt[:, sl, :], cout_v(rank, r0, slice(0, 1024)))

            tr_vals = []
            for c2 in range(NCH):
                tb = c2 % 2

                @p.block.tensor
                def _(t, sl=sl, c2=c2, tb=tb):
                    if c2 == 0:
                        t.wait_ge(kin_ev[sl][0], kin_ev[sl][1])
                    if psT_free[tb] is not None:
                        t.wait_ge(dv.sem, psT_free[tb])
                    t.transpose(psT[tb][:, :], Kt[:, sl, c2 * 128:(c2 + 1) * 128], ident[:, :]).then_inc(pe.sem, 1)
                trv = pe.inc()
                if c2 == NCH - 1:
                    kt_consumed[sl] = trv

                @p.block.vector
                def _(v, sl=sl, c2=c2, tb=tb, trv=trv):
                    v.wait_ge(pe.sem, trv)
                    if c2 == 0 and kb_out_slot[sl].ctr.v > 0:
                        v.wait_ge(kb_out_slot[sl].ctr.sem, kb_out_slot[sl].ctr.v)
                    v.tensor_copy(KBs[:, sl, c2 * 128:(c2 + 1) * 128], psT[tb][:, :]).then_inc(dv.sem, 1)
                psT_free[tb] = dv.inc()
                tr_vals.append(psT_free[tb])

            @p.block.sync
            def _(sy, kt=kt, sl=sl, last=tr_vals[-1]):
                sy.wait_ge(dv.sem, last)
                kbo_ev[kt] = kb_out_slot[sl].issue(sy, KB_v[kt, :, 0:1024], KBs[:, sl, :])

    emit_preproc_group(0, 8)

    S_free = [negmb_done, negmb_done]
    S_exp = [None, None]
    praw_free = [None, None]
    O_free = [None, None]
    l_free = negmb_done
    rlb_free = None
    kt2_consumed = [None, None]
    vg_slot = [DmaSlot(p, f"sVg{i}") for i in range(2)]
    kin2_slot = [DmaSlot(p, f"sKin2{i}") for i in range(2)]
    vg_consumed = [None, None]
    kin2_ev = [None, None]
    vg_ev = [None, None]
    B_last = None

    for st in range(4):
        qsl = slice(st * 512, (st + 1) * 512)
        ktmin = 4 * st
        strip_first_grp = True
        for g in range(8):
            gsl = g % 2
            keep = [i for i in range(8) if g * 8 + i >= ktmin]
            if not keep:
                continue
            nk = len(keep)
            if st == 0 and g < 7:
                emit_preproc_group(8 * (g + 1), 8 * (g + 2))

            @p.block.gpsimd
            def _(gq, keep=keep):
                if st == 0 and g < 2:
                    gq.wait_ge(io.sem, cc_done)
                vs = vg_slot[gsl].ctr
                if vg_consumed[gsl] is not None:
                    gq.wait_ge(pe.sem, vg_consumed[gsl])
                    gq.wait_ge(vs.sem, vs.v)
                for i in keep:
                    kt = g * 8 + i
                    rank = kt // 16
                    r0 = (kt % 16) * 128
                    gq.wait_ge(io.sem, cc_h[(kt % 16) // 8])
                    gq.dma_start(out=Vg[:, gsl, i, :],
                                 in_=cout_v(rank, r0, slice(1024, 2048))).then_inc(vs.sem, 16)
                    vs.inc(16)
                vg_ev[gsl] = (vs.sem, vs.v)

            stt_vals = {}
            pe0 = pe.v
            ac0 = ac.v
            dv0 = dv.v
            carry_praw = list(praw_free)
            first_grp = (st == 0 and g == 0)

            qk_val5 = [pe0 + (1 if pos == 0 else 2 * pos) for pos in range(nk)]

            @p.block.scalar
            def _(s, keep=keep, qk_val5=qk_val5, dv0=dv0, carry=carry_praw, first=first_grp, g=g):
                for pos, i in enumerate(keep):
                    sb = (g * 8 + i) % 2
                    s.wait_ge(pe.sem, qk_val5[pos])
                    if first and pos == 0:
                        s.wait_ge(dv.sem, negmb_done)
                    if pos >= 2:
                        s.wait_ge(dv.sem, dv0 + pos - 1)
                    elif carry[sb] is not None:
                        s.wait_ge(dv.sem, carry[sb])
                    s.activation(Praw[:, sb, :], S_bank[sb], AF.Exp, bias=negMb[:, 0:1],
                                 scale=co32[:, 0:1]).then_inc(ac.sem, 1)
            ac.inc(nk)

            def emit_lmm5(j, g=g, ktmin=ktmin):
                kt_j = g * 8 + j

                @p.block.tensor
                def _(t):
                    t.wait_ge(dv.sem, stt_vals[j])
                    if kt_j == ktmin:
                        t.wait_ge(dv.sem, l_free)
                    t.matmul(l_bank, p.ones128_bf[:, :], Pg[:, j, :],
                             start=(kt_j == ktmin), stop=(kt_j == 63), skip_group_check=True).then_inc(pe.sem, 1)
                pe.inc()

            for pos, i in enumerate(keep):
                kt = g * 8 + i
                sl = kt % 2
                rank = kt // 16
                r0 = (kt % 16) * 128

                @p.block.sync
                def _(sy):
                    if kt2_consumed[sl] is not None:
                        sy.wait_ge(pe.sem, kt2_consumed[sl])
                    if st == 0:
                        sy.wait_ge(kbo_ev[kt][0], kbo_ev[kt][1])
                    kin2_ev[sl] = kin2_slot[sl].issue(sy, Kt2[:, sl, :], KB_v[kt, :, 0:1024])

                sb = kt % 2

                @p.block.tensor
                def _(t):
                    t.wait_ge(kin2_ev[sl][0], kin2_ev[sl][1])
                    if st == 0 and g == 0 and i == 0:
                        t.wait_ge(dv.sem, negmb_done)
                    if S_exp[sb] is not None:
                        t.wait_ge(ac.sem, S_exp[sb])
                    for c in range(NCH):
                        mm = t.matmul(S_bank[sb], Kt2[:, sl, c * 128:(c + 1) * 128], qfm[:, c, qsl],
                                      start=(c == 0), stop=(c == NCH - 1))
                    mm.then_inc(pe.sem, 1)
                qk_v = pe.inc()
                kt2_consumed[sl] = qk_v
                assert qk_v == qk_val5[pos]

                exp_v = ac0 + pos + 1
                S_exp[sb] = exp_v

                @p.block.vector
                def _(v):
                    v.wait_ge(ac.sem, exp_v)
                    if B_last is not None and pos == 0:
                        v.wait_ge(pe.sem, B_last)
                    v.scalar_tensor_tensor(Pg[:, i, :], pj_sb[:, :], cbc_sb[:, st * 64 + kt:st * 64 + kt + 1],
                                           Praw[:, sb, :], ALU.is_ge, ALU.mult).then_inc(dv.sem, 1)
                praw_free[sb] = dv.inc()
                assert praw_free[sb] == dv0 + pos + 1
                stt_vals[i] = praw_free[sb]

                if pos >= 1:
                    emit_lmm5(keep[pos - 1])
            emit_lmm5(keep[-1])

            for c in range(NCH):
                ob = c % 2

                @p.block.tensor
                def _(t, keep=keep):
                    if c == 0:
                        t.wait_ge(vg_ev[gsl][0], vg_ev[gsl][1])
                    if O_free[ob] is not None:
                        t.wait_ge(dv.sem, O_free[ob])
                    for pos, i in enumerate(keep):
                        mm = t.matmul(O_bank[ob], Vg[:, gsl, i, c * 128:(c + 1) * 128], Pg[:, i, :],
                                      start=(pos == 0), stop=(pos == len(keep) - 1), skip_group_check=True)
                    mm.then_inc(pe.sem, 1)
                pv_v = pe.inc()
                if c == NCH - 1:
                    vg_consumed[gsl] = pv_v
                    B_last = pv_v

                @p.block.vector
                def _(v, strip_first_grp=strip_first_grp):
                    v.wait_ge(pe.sem, pv_v)
                    if strip_first_grp:
                        v.tensor_copy(O_sb[:, c, :], O_bank[ob]).then_inc(dv.sem, 1)
                    else:
                        v.tensor_add(O_sb[:, c, :], O_sb[:, c, :], O_bank[ob]).then_inc(dv.sem, 1)
                O_free[ob] = dv.inc()
            strip_first_grp = False

        @p.block.vector
        def _(v):
            v.wait_ge(pe.sem, B_last)
            v.reciprocal(rl[:, :], l_bank).then_inc(dv.sem, 1)
        l_free = dv.inc()

        @p.block.tensor
        def _(t):
            t.wait_ge(dv.sem, l_free)
            if rlb_free is not None:
                t.wait_ge(dv.sem, rlb_free)
            t.matmul(rlb_bank, p.ones1x128_f[:, :], rl[0:1, :], start=True, stop=True).then_inc(pe.sem, 1)
        rlb_mm = pe.inc()

        fin_vals = []
        for c in range(NCH):
            @p.block.vector
            def _(v):
                v.wait_ge(pe.sem, rlb_mm)
                v.wait_ge(dv.sem, dv.v)
                v.tensor_mul(tmp_f32[:, 0:512], O_sb[:, c, :], rlb_bank).then_inc(dv.sem, 1)
                v.wait_ge(dv.sem, dv.inc())
                v.tensor_add(xres[:, c, qsl], xres[:, c, qsl], tmp_f32[:, 0:512]).then_inc(dv.sem, 1)
            fin_vals.append(dv.inc())
        rlb_free = fin_vals[-1]

    return emit_store_x(p, xres, xo_d, wait_dv=dv.v)




def sub_att7(p, x_d, xo_d, after=None):
    nc = p.nc
    g7_d = p.in_ap("g7", [128, NCH], F32)
    be7_d = p.in_ap("be7", [128, NCH], F32)
    Wqkv_d = p.in_ap("Wqkv7", [D, 3 * D], F32)
    Wout_d = p.in_ap("Wout7", [D, D], F32)
    tupp_d = p.in_ap("t_upp", [128, 128], BF16)
    tlow_d = p.in_ap("t_low", [128, 128], BF16)
    mask7_d = p.in_ap("mask7", [128, 128], BF16)
    ident_d = p.in_ap("ident", [128, 128], BF16)

    cKb = p.dram("cKb7", [D, 128], BF16)
    gKb = p.dram("gKb7", [GRP * D, 128], BF16)
    gKb_v = gKb.ap().rearrange("(r k p) j -> r p k j", r=GRP, p=128)
    cVb = p.dram("cVb7", [128, 1040], BF16)
    gVb = p.dram("gVb7", [GRP * 128, 1040], BF16)
    gVb_v = gVb.ap().rearrange("(r p) j -> r p j", r=GRP)
    cSb = p.dram("cSb7", [2, 8], BF16)
    gSb = p.dram("gSb7", [GRP * 2, 8], BF16)
    stb = p.dram("stb7", [2, 24], F32)

    xch = p.sbuf("xch", [128, 2, TLOC], F32)
    xnscr = p.sbuf("xnscr", [128, 2, TLOC], BF16)
    xr_b = p.xres.ap().rearrange("p c t -> p (c t)").bitcast(BF16)
    qfm = xr_b[:, 0:16384].rearrange("p (c t) -> p c t", c=NCH)
    kfm = xr_b[:, 16384:32768].rearrange("p (c t) -> p c t", c=NCH)
    V7t = p.sbuf("V7t", [128, 17, 1040], BF16)
    V7 = V7t.ap().rearrange("p w (pr two c) -> p w pr two c", pr=NCH, two=2)
    arenaO = p.sbuf("arenaO", [128, NCH * TLOC // 1], BF16)
    xn = arenaO.ap().rearrange("p (c t) -> p c t", c=NCH)
    O7 = arenaO.ap().rearrange("p (c t) -> p c t", c=NCH)
    wbuf = p.sbuf("wbuf", [128, NCH, 2048], BF16)
    g7_sb = p.sbuf("g7_sb", [128, NCH], F32)
    be7_sb = p.sbuf("be7_sb", [128, NCH], F32)
    kb7 = p.sbuf("kb7", [128, NCH, 128], BF16)
    P7 = p.sbuf("P7", [128, 4, 128], BF16)
    Praw7 = p.sbuf("Praw7", [128, 4, 128], BF16)
    tupp = p.sbuf("tupp_sb", [128, 128], BF16)
    tlow = p.sbuf("tlow_sb", [128, 128], BF16)
    mask7 = p.sbuf("mask7_sb", [128, 128], BF16)
    ident = p.sbuf("ident_sb", [128, 128], BF16)
    stats = p.sbuf("stats", [2, 24], F32)
    statrow = p.sbuf("statrow", [1, 48], F32)
    stb_b = p.sbuf("stb_b", [2, 24], BF16)
    srow7 = p.sbuf("srow7", [1, 32], F32)
    negM7b = p.sbuf("negM7b", [128, 16], F32)
    co8 = p.sbuf("co8", [128, 1], F32)
    lrow65 = p.sbuf("lrow65", [65, TLOC], F32)
    lrow = lrow65.ap()[0:1, :]
    ones_b64 = p.sbuf("ones_b64", [65, 64], F32)
    co_n1 = p.sbuf("co_n1", [65, 1], F32)
    onespat = p.sbuf("onespat", [128, 2], BF16)
    Ofp = None
    sqs = p.sbuf("sqs", [128, TLOC], BF16)
    p.ones128_bf = p.sbuf("ones128_bf", [128, 1], BF16)
    p.ones1x128_f = p.sbuf("ones1x128_f", [1, 128], F32)
    tmp_f32 = sqs.ap().bitcast(F32)
    v7f = V7t.ap().rearrange("p w c -> p (w c)").bitcast(F32)
    p.r_m = v7f[0:1, 0:2048]
    p.r_a = v7f[0:1, 2048:4096]
    p.r_b = v7f[0:1, 4096:6144]
    Ofp = xch.ap()[0:64, 0, :]

    p.psBIG = p.psum("psBIG", [128, 2048], F32)
    psF = p.psum("psF", [128, 2048], F32)
    psBIG = p.psBIG

    pe, dv, ac, io = p.sPE, p.sDV, p.sAC, p.sIO

    @p.block.gpsimd
    def _(g):
        g.memset(p.ones128_bf[:, :], 1.0)
        g.memset(p.ones1x128_f[:, :], 1.0)
        g.memset(co8[:, :], 0.125)
        g.memset(ones_b64[:, :], 1.0)
        g.memset(co_n1[:, :], -1.0)
        g.memset(onespat[:, :], 0.0).then_inc(p.sCN.sem, 1)
        g.wait_ge(p.sCN.sem, p.sCN.inc())
        g.memset(onespat[0:64, 0:1], 1.0)
        g.memset(onespat[64:128, 1:2], 1.0).then_inc(p.sCN.sem, 1)
    cn_done = p.sCN.inc()

    swt = p.dma_sem("sWts")

    @p.block.gpsimd
    def _(g):
        g.dma_start(out=wbuf[:, :, :], in_=Wqkv_d[:, 0:2048].rearrange("(k p) j -> p k j", p=128)).then_inc(swt.sem, 16)
        g.dma_start(out=g7_sb[:, :], in_=g7_d).then_inc(swt.sem, 16)
        g.dma_start(out=be7_sb[:, :], in_=be7_d).then_inc(swt.sem, 16)
        g.dma_start(out=tupp[:, :], in_=tupp_d).then_inc(swt.sem, 16)
        g.dma_start(out=tlow[:, :], in_=tlow_d).then_inc(swt.sem, 16)
        g.dma_start(out=mask7[:, :], in_=mask7_d).then_inc(swt.sem, 16)
        g.dma_start(out=ident[:, :], in_=ident_d).then_inc(swt.sem, 16)
    wqk_ready = (swt.sem, swt.inc(112))

    xsl = [DmaSlot(p, f"sXc{i}") for i in range(2)]
    x_ev = [None, None]
    cast_done = [None, None]
    stat_prev = [None, None]
    stat_mm = None
    for c in range(NCH):
        sl = c % 2

        @p.block.sync
        def _(sy):
            if c == 0 and after is not None:
                sy.wait_ge(after[0], after[1])
            if cast_done[sl] is not None:
                sy.wait_ge(dv.sem, cast_done[sl])
            x_ev[sl] = xsl[sl].issue(sy, xch[:, sl, :], x_d[bass.ds(c * 128, 128), :])

        @p.block.vector
        def _(v):
            v.wait_ge(x_ev[sl][0], x_ev[sl][1])
            v.tensor_copy(xn[:, c, :], xch[:, sl, :]).then_inc(dv.sem, 1)
            cast_done[sl] = dv.inc()
            v.wait_ge(dv.sem, cast_done[sl])
            if stat_prev[sl] is not None:
                v.wait_ge(pe.sem, stat_prev[sl])
            v.tensor_mul(xnscr[:, sl, :], xn[:, c, :], xn[:, c, :]).then_inc(dv.sem, 1)
        sq_done = dv.inc()

        @p.block.tensor
        def _(t):
            t.wait_ge(dv.sem, sq_done)
            if c == 0:
                t.wait_ge(p.sCN.sem, cn_done)
            for q in range(4):
                t.matmul(psBIG[0:1, q * 512:(q + 1) * 512], p.ones128_bf[:, :],
                         xn[:, c, q * 512:(q + 1) * 512], start=(c == 0), stop=(c == NCH - 1),
                         skip_group_check=True)
            for q in range(4):
                mm = t.matmul(psF[0:1, q * 512:(q + 1) * 512], p.ones128_bf[:, :],
                              xnscr[:, sl, q * 512:(q + 1) * 512], start=(c == 0), stop=(c == NCH - 1),
                              skip_group_check=True)
            mm.then_inc(pe.sem, 1)
        stat_mm = pe.inc()
        stat_prev[sl] = stat_mm

    @p.block.vector
    def _(v):
        v.wait_ge(pe.sem, stat_mm)
        v.tensor_scalar_mul(p.r_m[:, :], psBIG[0:1, 0:2048], 1.0 / D).then_inc(dv.sem, 1)
        v.tensor_scalar_mul(p.r_a[:, :], psF[0:1, 0:2048], 1.0 / D).then_inc(dv.sem, 1)
        v.wait_ge(dv.sem, dv.inc(2))
        v.tensor_mul(p.r_b[:, :], p.r_m[:, :], p.r_m[:, :]).then_inc(dv.sem, 1)
        v.wait_ge(dv.sem, dv.inc())
        v.tensor_sub(p.r_a[:, :], p.r_a[:, :], p.r_b[:, :]).then_inc(dv.sem, 1)
        v.wait_ge(dv.sem, dv.inc())
        v.tensor_scalar_add(p.r_a[:, :], p.r_a[:, :], EPS).then_inc(dv.sem, 1)
    vpe_done = dv.inc()

    @p.block.scalar
    def _(s):
        s.wait_ge(dv.sem, vpe_done)
        s.activation(p.r_b[:, :], p.r_a[:, :], AF.Sqrt).then_inc(ac.sem, 1)
    sq1 = ac.inc()

    @p.block.vector
    def _(v):
        v.wait_ge(ac.sem, sq1)
        v.reciprocal(p.r_a[:, :], p.r_b[:, :]).then_inc(dv.sem, 1)
        v.wait_ge(dv.sem, dv.inc())
        v.tensor_mul(p.r_b[:, :], p.r_a[:, :], p.r_a[:, :]).then_inc(dv.sem, 1)
        v.wait_ge(dv.sem, dv.inc())
        v.tensor_scalar(p.r_b[:, :], p.r_b[:, :], -EPS, 1.0 + EPS, ALU.mult, ALU.add).then_inc(dv.sem, 1)
    vt_done = dv.inc()

    @p.block.scalar
    def _(s):
        s.wait_ge(dv.sem, vt_done)
        s.activation(p.r_b[:, :], p.r_b[:, :], AF.Sqrt).then_inc(ac.sem, 1)
    sq2 = ac.inc()

    @p.block.vector
    def _(v):
        v.wait_ge(ac.sem, sq2)
        v.reciprocal(lrow[:, :], p.r_b[:, :]).then_inc(dv.sem, 1)
        v.wait_ge(dv.sem, dv.inc())
        v.tensor_mul(p.r_a[:, :], p.r_a[:, :], lrow[:, :]).then_inc(dv.sem, 1)
    shat_done = dv.inc()

    psum_free = shat_done
    for h in range(2):
        tok = slice(h * 1024, (h + 1) * 1024)

        @p.block.tensor
        def _(t):
            t.wait_ge(dv.sem, psum_free)
            for q in range(2):
                sl2 = slice(h * 1024 + q * 512, h * 1024 + (q + 1) * 512)
                t.matmul(psBIG[:, q * 512:(q + 1) * 512], p.ones1x128_f[:, :], p.r_m[0:1, sl2],
                         start=True, stop=True)
                mm = t.matmul(psBIG[:, 1024 + q * 512:1024 + (q + 1) * 512],
                              p.ones1x128_f[:, :], p.r_a[0:1, sl2], start=True, stop=True)
            mm.then_inc(pe.sem, 1)
        bc_done = pe.inc()

        @p.block.vector
        def _(v):
            v.wait_ge(pe.sem, bc_done)
            if h == 0:
                v.wait_ge(wqk_ready[0], wqk_ready[1])
            for c in range(NCH):
                v.tensor_sub(tmp_f32[:, 0:1024], xn[:, c, tok], psBIG[:, 0:1024]).then_inc(dv.sem, 1)
                v.wait_ge(dv.sem, dv.inc())
                v.tensor_mul(tmp_f32[:, 0:1024], tmp_f32[:, 0:1024], psBIG[:, 1024:2048]).then_inc(dv.sem, 1)
                v.wait_ge(dv.sem, dv.inc())
                v.tensor_scalar(xn[:, c, tok], tmp_f32[:, 0:1024], g7_sb[:, c:c + 1], be7_sb[:, c:c + 1],
                                ALU.mult, ALU.add).then_inc(dv.sem, 1)
                v.wait_ge(dv.sem, dv.inc())
        psum_free = dv.v
    u_ready = dv.v

    @p.block.gpsimd
    def _(g):
        g.wait_ge(dv.sem, u_ready)
        g.memset(V7t[:, :, :], 0.0).then_inc(p.sCN.sem, 1)
        g.wait_ge(p.sCN.sem, p.sCN.inc())
        g.memset(V7[:, :, :, :, 64:65], 1.0).then_inc(p.sCN.sem, 1)
    cn2_done = p.sCN.inc()

    slots = [psBIG[:, i * 512:(i + 1) * 512] for i in range(4)]

    def cb_q(m, q, ps, mmv):
        @p.block.vector
        def _(v):
            v.wait_ge(pe.sem, mmv)
            v.tensor_copy(qfm[:, m, q * 512:(q + 1) * 512], ps).then_inc(dv.sem, 1)
        return dv.inc()

    emit_mm_dense(p, wbuf.ap()[:, :, 0:1024].rearrange("p k j -> p k j"), xn, NCH, cb_q,
                  wait_w=wqk_ready, wait_dv=u_ready, psum_rot=slots)

    def cb_k(m, q, ps, mmv):
        @p.block.vector
        def _(v):
            v.wait_ge(pe.sem, mmv)
            v.tensor_copy(kfm[:, m, q * 512:(q + 1) * 512], ps).then_inc(dv.sem, 1)
        return dv.inc()

    emit_mm_dense(p, wbuf.ap()[:, :, 1024:2048].rearrange("p k j -> p k j"), xn, NCH, cb_k,
                  wait_dv=dv.v, psum_rot=slots)
    qk_done = dv.v

    @p.block.gpsimd
    def _(g):
        g.wait_ge(pe.sem, pe.v)
        g.dma_start(out=wbuf[:, :, 0:1024],
                    in_=Wqkv_d[:, 2048:3072].rearrange("(k p) j -> p k j", p=128)).then_inc(swt.sem, 16)
    wv_ready = (swt.sem, swt.inc(16))

    vslot_ev = [None] * 4
    for w in range(16):
        for pp in range(NCH):
            j = (w * NCH + pp) % 4

            @p.block.tensor
            def _(t):
                if w == 0 and pp == 0:
                    t.wait_ge(wv_ready[0], wv_ready[1])
                    t.wait_ge(dv.sem, qk_done)
                if vslot_ev[j] is not None:
                    t.wait_ge(dv.sem, vslot_ev[j])
                for k in range(NCH):
                    mm = t.matmul(slots[j][:, 0:128], xn[:, k, w * 128:(w + 1) * 128],
                                  wbuf[:, k, pp * 128:(pp + 1) * 128],
                                  start=(k == 0), stop=(k == NCH - 1))
                mm.then_inc(pe.sem, 1)
            mmv = pe.inc()

            @p.block.vector
            def _(v):
                v.wait_ge(pe.sem, mmv)
                if w == 0 and pp == 0:
                    v.wait_ge(p.sCN.sem, cn2_done)
                v.tensor_copy(V7[:, w + 1, pp, :, 0:64],
                              slots[j][:, 0:128].rearrange("p (two c) -> p two c", two=2)).then_inc(dv.sem, 1)
            vslot_ev[j] = dv.inc()
    vprime_done = dv.v

    hm_prev = [None]

    def emit_headmax(src_fm, col0, tokens_sl, extra_cols):
        for c in range(NCH):
            @p.block.vector
            def _(v):
                v.wait_ge(dv.sem, dv.v)
                if c == 0:
                    v.wait_ge(dv.sem, vprime_done)
                if hm_prev[0] is not None:
                    v.wait_ge(pe.sem, hm_prev[0])
                v.tensor_mul(sqs[:, 0:2048], src_fm[:, c, :], src_fm[:, c, :]).then_inc(dv.sem, 1)
            sqv = dv.inc()

            @p.block.tensor
            def _(t):
                t.wait_ge(dv.sem, sqv)
                for q in range(4):
                    mm = t.matmul(psF[0:2, q * 512:(q + 1) * 512], onespat[:, :],
                                  sqs[:, q * 512:(q + 1) * 512], start=True, stop=True)
                mm.then_inc(pe.sem, 1)
            nmv = pe.inc()
            hm_prev[0] = nmv

            @p.block.vector
            def _(v):
                v.wait_ge(pe.sem, nmv)
                v.tensor_reduce(stats[0:2, col0 + c:col0 + c + 1], psF[0:2, tokens_sl], AX.X, ALU.max).then_inc(dv.sem, 1)
                if extra_cols is not None:
                    v.wait_ge(dv.sem, dv.inc())
                    v.tensor_reduce(stats[0:2, extra_cols + c:extra_cols + c + 1],
                                    psF[0:2, 1920:2048], AX.X, ALU.max).then_inc(dv.sem, 1)
                    dv.inc()
                else:
                    dv.inc()

    emit_headmax(kfm, 0, slice(0, 2048), 16)
    emit_headmax(qfm, 8, slice(0, 2048), None)
    stats_done = dv.v

    @p.block.vector
    def _(v):
        v.wait_ge(dv.sem, stats_done)
        v.tensor_copy(stb_b[:, :], stats[:, :]).then_inc(dv.sem, 1)
    stb_done = dv.inc()

    bnd = DmaSlot(p, "sBnd")

    @p.block.sync
    def _(sy):
        sy.wait_ge(dv.sem, stb_done)
        for c in range(NCH):
            sy.dma_start(out=cKb.ap().rearrange("(k p) j -> p k j", p=128)[:, c, :],
                         in_=kfm[:, c, 1920:2048]).then_inc(bnd.ctr.sem, 16)
            bnd.ctr.inc(16)
        sy.dma_start(out=cVb.ap(), in_=V7t[:, 16, :]).then_inc(bnd.ctr.sem, 16)
        sy.dma_start(out=cSb.ap(), in_=stb_b[:, 0:8]).then_inc(bnd.ctr.sem, 16)
        bnd.ctr.inc(32)

    @p.block.gpsimd
    def _(g):
        g.wait_ge(bnd.ctr.sem, bnd.ctr.v)
        g.collective_compute("AllGather", ALU.bypass, replica_groups=[[0, 1, 2, 3], [4, 5, 6, 7]],
                             ins=[cKb.ap().opt()], outs=[gKb.ap().opt()]).then_inc(io.sem, 1)
        g.collective_compute("AllGather", ALU.bypass, replica_groups=[[0, 1, 2, 3], [4, 5, 6, 7]],
                             ins=[cVb.ap().opt()], outs=[gVb.ap().opt()]).then_inc(io.sem, 1)
        g.collective_compute("AllGather", ALU.bypass, replica_groups=[[0, 1, 2, 3], [4, 5, 6, 7]],
                             ins=[cSb.ap().opt()], outs=[gSb.ap().opt()]).then_inc(io.sem, 1)
    cc_done = io.inc(3)

    bin_ = DmaSlot(p, "sBin")

    @p.block.sync
    def _(sy):
        sy.wait_ge(io.sem, cc_done)
        pid = nc.partition_id()
        prev = (pid + 3) % 4
        prev4 = (pid % 4 + 3) % 4
        sy.dma_start(out=kb7[:, :, :], in_=gKb_v[prev4]).then_inc(bin_.ctr.sem, 16)
        sy.dma_start(out=V7t[:, 0, :], in_=gVb_v[prev4]).then_inc(bin_.ctr.sem, 16)
        bin_.ctr.inc(32)

    stb_slot = DmaSlot(p, "sStb")

    @p.block.sync
    def _(sy):
        sy.wait_ge(dv.sem, stb_done)
        sy.dma_start(out=stb.ap(), in_=stats[:, :]).then_inc(stb_slot.ctr.sem, 16)
        stb_slot.ctr.inc(16)
        sy.wait_ge(stb_slot.ctr.sem, stb_slot.ctr.v)
        sy.dma_start(out=statrow[0:1, 0:48], in_=stb.ap().rearrange("a b -> (a b)")).then_inc(stb_slot.ctr.sem, 16)
        stb_slot.ctr.inc(16)

    prevst = DmaSlot(p, "sPrevSt")

    @p.block.gpsimd
    def _(gq):
        gq.wait_ge(io.sem, cc_done)
        pid = nc.partition_id()
        prev4 = (pid % 4 + 3) % 4
        gq.dma_start(out=lrow[0:1, 0:16],
                     in_=gSb.ap().rearrange("(r a) b -> r (a b)", r=GRP)[bass.ds(prev4, 1)]).then_inc(prevst.ctr.sem, 16)
        prevst.ctr.inc(16)

    @p.block.vector
    def _(v):
        v.wait_ge(stb_slot.ctr.sem, stb_slot.ctr.v)
        v.wait_ge(prevst.ctr.sem, prevst.ctr.v)
        stv = statrow.ap()
        lv = lrow
        v.tensor_max(srow7[0:1, 0:8], stv[0:1, 0:8], lv[0:1, 0:8])
        v.tensor_max(srow7[0:1, 8:16], stv[0:1, 24:32], lv[0:1, 8:16]).then_inc(dv.sem, 1)
        v.wait_ge(dv.sem, dv.inc())
        v.tensor_mul(srow7[0:1, 0:8], srow7[0:1, 0:8], stv[0:1, 8:16])
        v.tensor_mul(srow7[0:1, 8:16], srow7[0:1, 8:16], stv[0:1, 32:40]).then_inc(dv.sem, 1)
    prod_done = dv.inc()

    @p.block.scalar
    def _(s):
        s.wait_ge(dv.sem, prod_done)
        s.activation(srow7[0:1, 16:32], srow7[0:1, 0:16], AF.Sqrt).then_inc(ac.sem, 1)
    msq = ac.inc()

    @p.block.vector
    def _(v):
        v.wait_ge(ac.sem, msq)
        v.tensor_scalar_mul(srow7[0:1, 0:16], srow7[0:1, 16:32], -0.125).then_inc(dv.sem, 1)
    negm_done = dv.inc()

    @p.block.tensor
    def _(t):
        t.wait_ge(dv.sem, negm_done)
        t.matmul(psF[:, 0:16], p.ones1x128_f[:, :], srow7[0:1, 0:16], start=True, stop=True).then_inc(pe.sem, 1)
    nbm = pe.inc()

    @p.block.vector
    def _(v):
        v.wait_ge(pe.sem, nbm)
        v.tensor_copy(negM7b[:, :], psF[:, 0:16]).then_inc(dv.sem, 1)
    negmb_done = dv.inc()

    Sslot = [psF[:, 0:128], psF[:, 128:256], psF[:, 512:640], psF[:, 640:768]]
    Spair7 = [psF[:, 0:256], psF[:, 512:768]]
    s_exp = [None] * 4
    p_used = [None, None]
    o_evac = negmb_done
    prev_head = [None]
    for pp in range(NCH):
        for hh in range(2):
            head_col = hh * 8 + pp
            hsl = slice(hh * 64, hh * 64 + 64)
            pending = prev_head[0]

            pe0 = pe.v
            ac0 = ac.v
            dv0 = dv.v
            carry_pused = list(p_used)
            qk_val = [None] * 32
            pv_val = [None] * 32
            cnt = pe0
            for i in range(32):
                cnt += 1
                qk_val[i] = cnt
                if i == 0 and pending is not None:
                    cnt += 1
                    rbv_val = cnt
                if i >= 2:
                    cnt += 1
                    pv_val[i - 2] = cnt
            pv_val[30] = cnt + 1
            pv_val[31] = cnt + 2
            pv_flush = pv_val[31]
            if pending is not None:
                mul_val = dv0 + 1
                msk_val = [dv0 + i + 2 for i in range(32)]
                o_evac = mul_val
            else:
                msk_val = [dv0 + i + 1 for i in range(32)]
            acr = 2 if pending is not None else 0
            exp_val = [ac0 + acr + i // 2 + 1 for i in range(32)]
            recip_val = ac0 + 2 if pending is not None else None

            @p.block.scalar
            def _(s, pending=pending, qk_val=qk_val, pv_val=pv_val,
                  carry=carry_pused, head_col=head_col):
                if pending is not None:
                    s.wait_ge(pe.sem, pending["pv_flush"])
                    s.activation(lrow65[64:65, 0:TLOC], psBIG[64:65, 0:TLOC],
                                 AF.Ln).then_inc(ac.sem, 1)
                    s.wait_ge(ac.sem, ac.v + 1)
                    s.activation(lrow65[64:65, 0:TLOC], lrow65[64:65, 0:TLOC],
                                 AF.Exp, scale=co_n1[64:65, 0:1]).then_inc(ac.sem, 1)
                for k in range(16):
                    sp = k % 2
                    s.wait_ge(pe.sem, qk_val[2 * k + 1])
                    if k >= 2:
                        s.wait_ge(pe.sem, pv_val[2 * k - 3])
                    elif carry[sp] is not None:
                        s.wait_ge(pe.sem, carry[sp])
                    s.activation(Praw7.ap().rearrange("p a t -> p (a t)")[:, sp * 256:sp * 256 + 256],
                                 Spair7[sp], AF.Exp,
                                 bias=negM7b[:, head_col:head_col + 1],
                                 scale=co8[:, 0:1]).then_inc(ac.sem, 1)
            ac.inc(16 + acr)

            def emit_pv7(j, pp=pp, hh=hh, msk_val=msk_val, pv_val=pv_val,
                         oe0=o_evac):
                w_, pv_ = j // 2, j % 2
                si = j % 4

                @p.block.tensor
                def _(t):
                    t.wait_ge(dv.sem, msk_val[j])
                    if j == 0:
                        t.wait_ge(dv.sem, oe0)
                    t.matmul(psBIG[0:65, w_ * 128:(w_ + 1) * 128],
                             V7[:, w_ + pv_, pp, hh, :], P7[:, si, :],
                             start=(pv_ == 0), stop=(pv_ == 1), skip_group_check=True).then_inc(pe.sem, 1)
                p_used[(j // 2) % 2] = pe.inc()
                assert p_used[(j // 2) % 2] == pv_val[j]

            for w in range(16):
                for pv in range(2):
                    i = w * 2 + pv
                    si = i % 4

                    @p.block.tensor
                    def _(t):
                        if pp == 0 and hh == 0 and w == 0 and pv == 0:
                            t.wait_ge(bin_.ctr.sem, bin_.ctr.v)
                            t.wait_ge(dv.sem, negmb_done)
                        if s_exp[si] is not None:
                            t.wait_ge(ac.sem, s_exp[si])
                        if pv == 0 and w == 0:
                            lhs = kb7[hsl, pp, :]
                        elif pv == 0:
                            lhs = kfm[hsl, pp, (w - 1) * 128:w * 128]
                        else:
                            lhs = kfm[hsl, pp, w * 128:(w + 1) * 128]
                        t.matmul(Sslot[si], lhs, qfm[hsl, pp, w * 128:(w + 1) * 128],
                                 start=True, stop=True).then_inc(pe.sem, 1)
                    qkv_ = pe.inc()
                    assert qkv_ == qk_val[i]

                    if i == 0 and pending is not None:
                        @p.block.tensor
                        def _(t, pending=pending, recip_val=recip_val):
                            t.wait_ge(ac.sem, recip_val)
                            t.wait_ge(dv.sem, pending["copy_v"])
                            for q in range(4):
                                mm = t.matmul(psBIG[0:64, q * 512:(q + 1) * 512],
                                              ones_b64[64:65, 0:64],
                                              lrow65[64:65, q * 512:(q + 1) * 512],
                                              start=True, stop=True)
                            mm.then_inc(pe.sem, 1)
                        rv = pe.inc()
                        assert rv == rbv_val

                        @p.block.vector
                        def _(v, pending=pending, rv=rv):
                            v.wait_ge(pe.sem, rv)
                            v.tensor_mul(O7[pending["hsl"], pending["pp"], :], Ofp[:, :],
                                         psBIG[0:64, 0:2048]).then_inc(dv.sem, 1)
                        mv = dv.inc()
                        assert mv == mul_val

                    expv = exp_val[i]
                    s_exp[si] = expv

                    @p.block.vector
                    def _(v):
                        v.wait_ge(ac.sem, expv)
                        msk = tupp if pv == 1 else (mask7 if w == 0 else tlow)
                        v.tensor_mul(P7[:, si, :], Praw7[:, si, :], msk[:, :]).then_inc(dv.sem, 1)
                    mskv = dv.inc()
                    assert mskv == msk_val[i]

                    if i >= 2:
                        emit_pv7(i - 2)
            emit_pv7(30)
            emit_pv7(31)

            @p.block.vector
            def _(v, pv_flush=pv_flush):
                v.wait_ge(pe.sem, pv_flush)
                v.tensor_copy(Ofp[:, :], psBIG[0:64, 0:2048]).then_inc(dv.sem, 1)
            copy_v = dv.inc()
            prev_head[0] = {"hsl": hsl, "pp": pp, "pv_flush": pv_flush,
                            "copy_v": copy_v}

    pending = prev_head[0]

    @p.block.scalar
    def _(s, pending=pending):
        s.wait_ge(pe.sem, pending["pv_flush"])
        s.activation(lrow65[64:65, 0:TLOC], psBIG[64:65, 0:TLOC],
                     AF.Ln).then_inc(ac.sem, 1)
        s.wait_ge(ac.sem, ac.v + 1)
        s.activation(lrow65[64:65, 0:TLOC], lrow65[64:65, 0:TLOC],
                     AF.Exp, scale=co_n1[64:65, 0:1]).then_inc(ac.sem, 1)
    ac.inc()
    recip_val = ac.inc()

    @p.block.tensor
    def _(t):
        t.wait_ge(ac.sem, recip_val)
        t.wait_ge(dv.sem, pending["copy_v"])
        for q in range(4):
            mm = t.matmul(psBIG[0:64, q * 512:(q + 1) * 512], ones_b64[64:65, 0:64],
                          lrow65[64:65, q * 512:(q + 1) * 512], start=True, stop=True)
        mm.then_inc(pe.sem, 1)
    rv = pe.inc()

    @p.block.vector
    def _(v):
        v.wait_ge(pe.sem, rv)
        v.tensor_mul(O7[pending["hsl"], pending["pp"], :], Ofp[:, :],
                     psBIG[0:64, 0:2048]).then_inc(dv.sem, 1)
    dv.inc()
    attn_done = dv.v


    @p.block.gpsimd
    def _(g):
        g.wait_ge(pe.sem, pe.v)
        g.dma_start(out=wbuf[:, :, 1024:2048],
                    in_=Wout_d.rearrange("(k p) j -> p k j", p=128)).then_inc(swt.sem, 16)
    wo_ready = (swt.sem, swt.inc(16))

    xr_ev = [None, None]
    xuse = [None, None]
    it2 = [0]

    def cb_out(m, q, ps, mmv):
        sl = it2[0] % 2

        @p.block.sync
        def _(sy):
            sy.wait_ge(dv.sem, xuse[sl] if xuse[sl] is not None else attn_done)
            xr_ev[sl] = xsl[sl].issue(sy, xch[:, sl, 0:512], x_d[bass.ds(m * 128, 128), q * 512:(q + 1) * 512])

        @p.block.vector
        def _(v):
            v.wait_ge(pe.sem, mmv)
            v.wait_ge(xr_ev[sl][0], xr_ev[sl][1])
            v.scalar_tensor_tensor(xch[:, sl, 512:1024][:, 0:512], ps, 0.0, xch[:, sl, 0:512],
                                   ALU.add, ALU.add).then_inc(dv.sem, 1)
        outv = dv.inc()
        xuse[sl] = outv

        @p.block.sync
        def _(sy):
            sy.wait_ge(dv.sem, outv)
            xsl[sl].issue(sy, xo_d[bass.ds(m * 128, 128), q * 512:(q + 1) * 512], xch[:, sl, 512:1024][:, 0:512])
        it2[0] += 1
        return outv

    emit_mm_dense(p, wbuf.ap()[:, :, 1024:2048].rearrange("p k j -> p k j"), O7, NCH, cb_out,
                  wait_w=wo_ready, wait_dv=attn_done, psum_rot=slots)

    @p.block.sync
    def _(sy):
        for s in xsl:
            sy.wait_ge(s.ctr.sem, s.ctr.v)
    ev = (xsl[0].ctr.sem, xsl[0].ctr.v)
    return [(s.ctr.sem, s.ctr.v) for s in xsl]



def sub_att02(p, x_d, xo_d, rope_bias, after=None):
    nc = p.nc
    wname = "Wqkv0_my" if rope_bias else "Wqkv3_my"
    Wqkv_d = p.in_ap(wname, [D, 768], F32)
    if rope_bias:
        bq_d = p.in_ap("bq_my", [128, 2], F32)
        bk_d = p.in_ap("bk_my", [128, 2], F32)
        bv_d = p.in_ap("bv_my", [1, 256], F32)
        Wp_d = p.in_ap("WprojT", [D, D], F32)
        bp_d = p.in_ap("b_proj", [128, NCH], F32)
        f_d = p.in_ap("f_rope", [128, S], F32)
    else:
        Wp_d = p.in_ap("WOT", [D, D], F32)
    tupp_d = p.in_ap("t_upp", [128, 128], BF16)

    cXn = p.dram("cXn", [2, D, TLOC // 2], BF16)
    gXn = p.dram("gXn", [2, GRP * D, TLOC // 2], BF16)
    gXn_v = gXn.ap().rearrange("h (r c p) t -> h r p c t", r=GRP, p=128)
    cO = p.dram("cO", [4, 4 * HD, TLOC], BF16)
    gO = p.dram("gO", [4, GRP * 4 * HD, TLOC], BF16)
    gO_v = gO.ap().rearrange("ch (k p) t -> ch p k t", p=128)
    shb = None

    xres = p.xres
    Q2t = p.sbuf("Q2t", [128, 2, S], BF16)
    xn = Q2t.ap().rearrange("p a s -> p (a s)").rearrange("p (c t) -> p c t", c=NCH)
    Q2 = Q2t.ap()
    K2t = p.sbuf("K2t", [128, 2, S], BF16)
    K2 = K2t.ap()
    V2t = p.sbuf("V2t", [128, 2, 64, 130], BF16)
    V2 = V2t.ap()
    v2f = V2t.ap().rearrange("p a k c -> p (a k c)").bitcast(F32)
    p.r_m = v2f[0:1, 0:2048]
    p.r_a = v2f[0:1, 2048:4096]
    p.r_b = v2f[0:1, 4096:6144]
    XG = p.sbuf("XG", [128, NCH, 1024], BF16)
    wq = p.sbuf("wq", [128, NCH, 768], BF16)
    P4 = p.sbuf("P4", [128, 2, 512], BF16)
    tmpf = P4.ap().rearrange("p a t -> p (a t)").bitcast(F32)[:, 0:512]
    if rope_bias:
        f_bf = p.sbuf("f_bf", [128, S], BF16)
        WP = f_bf.ap().rearrange("p (k j) -> p k j", k=NCH)
        bq_sb = p.sbuf("bq_sb", [128, 2], F32)
        bk_sb = p.sbuf("bk_sb", [128, 2], F32)
        bv_sb = P4.ap().rearrange("p a t -> p (a t)")[0:1, 512:768]
        bp_sb = p.sbuf("bp_sb", [128, NCH], F32)

    else:
        WP = p.sbuf("WP", [128, NCH, 1024], BF16).ap()
    mx = p.sbuf("mx", [2, 8], F32)
    shb = p.dram("shb02", [2, S], F32)
    scr512 = P4.ap().rearrange("p a t -> p (a t)")[:, 0:512]
    shst = None
    onesb = p.sbuf("onesb", [128, 128], BF16)
    co8 = p.sbuf("co8_02", [128, 1], F32)
    rl = None
    tupp = wq.ap().rearrange("p k j -> p (k j)")[:, 0:128]
    p.ones128_bf = p.sbuf("ones128_bf", [128, 1], BF16)
    p.ones1x128_f = p.sbuf("ones1x128_f", [1, 128], F32)
    onespat = p.sbuf("onespat02", [128, 2], BF16)

    tmp_f32 = K2t.ap().rearrange("p a s -> p (a s)").bitcast(F32)[:, 0:1024]
    xg_flat = XG.ap().rearrange("p c t -> p (c t)")
    wq_flat_b = wq.ap().rearrange("p k j -> p (k j)")
    wq_flat_f = wq_flat_b.bitcast(F32)
    rl = wq_flat_f[0:1, 1792:2304]
    Ost = wq_flat_f[0:64, 256:768]
    Ostb = wq_flat_b[0:64, 4608:5120]
    shst = wq_flat_f[0:2, 2560:3072]
    OQ = Q2t.ap().rearrange("p a s -> p (a s)").rearrange("p (c t) -> p c t", c=NCH)

    p.psBIG = p.psum("psBIG", [128, 2048], F32)
    psS = p.psum("psS", [128, 1024], F32)
    psO = p.psum("psO", [128, 512], F32)
    psV = p.psum("psV", [128, 512], F32)
    psBIG = p.psBIG

    pe, dv, ac, io = p.sPE, p.sDV, p.sAC, p.sIO

    @p.block.gpsimd
    def _(g):
        g.memset(p.ones128_bf[:, :], 1.0)
        g.memset(p.ones1x128_f[:, :], 1.0)
        g.memset(co8[:, :], 0.125)
        g.memset(onesb[:, :], 1.0)
        g.memset(onespat[:, :], 0.0).then_inc(p.sCN.sem, 1)
        g.wait_ge(p.sCN.sem, p.sCN.inc())
        g.memset(onespat[0:64, 0:1], 1.0)
        g.memset(onespat[64:128, 1:2], 1.0).then_inc(p.sCN.sem, 1)
    cn_done = p.sCN.inc()

    x_ready = emit_load_x(p, x_d, xres, after)
    swt = p.dma_sem("sWts")

    @p.block.gpsimd
    def _(g):
        g.dma_start(out=wq[:, :, :], in_=Wqkv_d.rearrange("(k p) j -> p k j", p=128)).then_inc(swt.sem, 16)
        n = 16
        if rope_bias:
            g.dma_start(out=bq_sb[:, :], in_=bq_d).then_inc(swt.sem, 16)
            g.dma_start(out=bk_sb[:, :], in_=bk_d).then_inc(swt.sem, 16)
            g.dma_start(out=bv_sb[:, :], in_=bv_d).then_inc(swt.sem, 16)
            g.dma_start(out=bp_sb[:, :], in_=bp_d).then_inc(swt.sem, 16)
            g.dma_start(out=f_bf[:, :], in_=f_d).then_inc(swt.sem, 16)
            n += 80
        swt.inc(n)
    w_ready = (swt.sem, swt.v)

    xn_ready = emit_ln(p, xres, xn, K2t.ap().rearrange("p a s -> p (a s)").rearrange("p (c t) -> p c t", c=NCH), tmp_f32, x_ready)

    cxn_sl = [DmaSlot(p, f"sCxn{h}") for h in range(2)]
    ccx_half = [None, None]
    for h in range(2):
        @p.block.sync
        def _(sy, h=h):
            sy.wait_ge(dv.sem, xn_ready)
            cxn_sl[h].issue(sy, cXn.ap()[h].rearrange("(c p) t -> p c t", p=128),
                            xn[:, :, h * 1024:(h + 1) * 1024])

        @p.block.gpsimd
        def _(g, h=h):
            g.wait_ge(cxn_sl[h].ctr.sem, cxn_sl[h].ctr.v)
            g.collective_compute("AllGather", ALU.bypass, replica_groups=[[0, 1, 2, 3], [4, 5, 6, 7]],
                                 ins=[cXn.ap()[h].opt()], outs=[gXn.ap()[h].opt()]).then_inc(io.sem, 1)
        ccx_half[h] = io.inc()
    ccx_done = ccx_half[1]

    @p.block.gpsimd
    def _(g):
        g.wait_ge(dv.sem, xn_ready)
        g.memset(V2t[:, :, :, :], 0.0).then_inc(p.sCN.sem, 1)
        g.wait_ge(p.sCN.sem, p.sCN.inc())
        g.memset(V2[:, :, :, 64:65], 1.0)
        g.memset(V2[:, :, :, 129:130], 1.0).then_inc(p.sCN.sem, 1)
    vones_done = p.sCN.inc()

    xg_s = p.dma_sem("sXg")
    slots6 = [psBIG[:, i * 512:(i + 1) * 512] for i in range(4)] + [psS[:, 0:512], psS[:, 512:1024]]
    ps_free = [xn_ready] * 6
    vt_free = [None]
    xg_used = None
    for RH in range(2 * GRP):
        half, R = RH // GRP, RH % GRP
        @p.block.sync
        def _(sy):
            sy.wait_ge(io.sem, ccx_half[half])
            if RH > 0:
                sy.wait_ge(pe.sem, xg_used)
            if xg_s.v > 0:
                sy.wait_ge(xg_s.sem, xg_s.v)
            sy.dma_start(out=XG[:, :, :], in_=gXn_v[half, R]).then_inc(xg_s.sem, 16)
        xg_s.inc(16)
        xg_ev = (xg_s.sem, xg_s.v)

        for s4 in range(2):
            tsl = slice(s4 * 512, (s4 + 1) * 512)
            gsl = slice(R * 2048 + half * 1024 + s4 * 512, R * 2048 + half * 1024 + (s4 + 1) * 512)
            for j in range(4):
                @p.block.tensor
                def _(t):
                    if RH == 0 and s4 == 0 and j == 0:
                        t.wait_ge(w_ready[0], w_ready[1])
                    t.wait_ge(xg_ev[0], xg_ev[1])
                    t.wait_ge(dv.sem, ps_free[j])
                    for k in range(NCH):
                        mm = t.matmul(slots6[j], wq[:, k, j * 128:(j + 1) * 128], XG[:, k, tsl],
                                      start=(k == 0), stop=(k == NCH - 1))
                    mm.then_inc(pe.sem, 1)
                mmv = pe.inc()

                @p.block.vector
                def _(v):
                    v.wait_ge(pe.sem, mmv)
                    pj = j % 2
                    if j < 2:
                        dst = Q2[:, pj, gsl]
                        bias = bq_sb[:, pj:pj + 1] if rope_bias else None
                    else:
                        dst = K2[:, pj, gsl]
                        bias = bk_sb[:, pj:pj + 1] if rope_bias else None
                    if rope_bias:
                        v.scalar_tensor_tensor(dst, slots6[j], bias, f_bf[:, gsl],
                                               ALU.add, ALU.mult).then_inc(dv.sem, 1)
                    else:
                        v.tensor_copy(dst, slots6[j]).then_inc(dv.sem, 1)
                ps_free[j] = dv.inc()

        for tt in range(8):
            kt = R * 16 + half * 8 + tt
            ttsl = slice(tt * 128, (tt + 1) * 128)
            for pj in range(2):
                @p.block.tensor
                def _(t):
                    if vt_free[0] is not None:
                        t.wait_ge(dv.sem, vt_free[0])
                    if RH == 0 and tt == 0 and pj == 0:
                        t.wait_ge(p.sCN.sem, vones_done)
                    for k in range(NCH):
                        last_k = (k == NCH - 1) and not rope_bias
                        mm = t.matmul(psV[:, 0:128], XG[:, k, ttsl],
                                      wq[:, k, 512 + pj * 128:512 + (pj + 1) * 128],
                                      start=(k == 0), stop=last_k)
                    if rope_bias:
                        mm = t.matmul(psV[:, 0:128], onesb[0:1, 0:128],
                                      bv_sb[0:1, pj * 128:(pj + 1) * 128],
                                      start=False, stop=True)
                    mm.then_inc(pe.sem, 1)
                mmv2 = pe.inc()

                @p.block.vector
                def _(v):
                    v.wait_ge(pe.sem, mmv2)
                    v.tensor_copy(V2[:, pj, kt, 0:64], psV[:, 0:64])
                    v.tensor_copy(V2[:, pj, kt, 65:129], psV[:, 64:128]).then_inc(dv.sem, 1)
                vt_free[0] = dv.inc()
        xg_used = pe.v
    proj_done = dv.v

    nrm_prev = [None]

    def headmax02(src2, col0):
        for pj in range(2):
            for seg in range(4):
                ssl = slice(seg * 2048, (seg + 1) * 2048)

                @p.block.vector
                def _(v):
                    v.wait_ge(dv.sem, dv.v)
                    if nrm_prev[0] is not None:
                        v.wait_ge(pe.sem, nrm_prev[0])
                    if pj == 0 and seg == 0 and col0 == 0:
                        v.wait_ge(dv.sem, proj_done)
                    v.tensor_mul(xg_flat[:, 0:2048],
                                 src2[:, pj, ssl], src2[:, pj, ssl]).then_inc(dv.sem, 1)
                sqv = dv.inc()

                for q in range(4):
                    first = (seg == 0 and q == 0)

                    @p.block.tensor
                    def _(t):
                        t.wait_ge(dv.sem, sqv if q == 0 else dv.v)
                        t.matmul(psO[0:2, 0:512], onespat[:, :],
                                 xg_flat[:, q * 512:(q + 1) * 512],
                                 start=True, stop=True).then_inc(pe.sem, 1)
                    nmv = pe.inc()
                    nrm_prev[0] = nmv

                    @p.block.vector
                    def _(v):
                        v.wait_ge(pe.sem, nmv)
                        if first:
                            v.tensor_reduce(mx[0:2, col0 + pj:col0 + pj + 1], psO[0:2, 0:512],
                                            AX.X, ALU.max).then_inc(dv.sem, 1)
                            dv.inc()
                        else:
                            v.tensor_reduce(mx[0:2, 4 + col0 + pj:4 + col0 + pj + 1], psO[0:2, 0:512],
                                            AX.X, ALU.max).then_inc(dv.sem, 1)
                            v.wait_ge(dv.sem, dv.inc())
                            v.tensor_max(mx[0:2, col0 + pj:col0 + pj + 1], mx[0:2, col0 + pj:col0 + pj + 1],
                                         mx[0:2, 4 + col0 + pj:4 + col0 + pj + 1]).then_inc(dv.sem, 1)
                            dv.inc()

    headmax02(K2, 0)
    mx_done = dv.v

    tup_s = p.dma_sem("sTup")

    @p.block.gpsimd
    def _(g):
        g.wait_ge(dv.sem, proj_done)
        g.dma_start(out=tupp, in_=tupp_d).then_inc(tup_s.sem, 16)
    tup_s.inc(16)

    @p.block.gpsimd
    def _(g):
        if rope_bias:
            g.wait_ge(dv.sem, proj_done)
        g.dma_start(out=WP, in_=Wp_d.rearrange("(k p) j -> p k j", p=128)).then_inc(swt.sem, 16)
    wp_ready = (swt.sem, swt.inc(16))

    xg_flat2 = XG.ap().rearrange("p c t -> p (c t)")
    Qp = xg_flat2[:, 0:2048].rearrange("p (s h t) -> p s h t", s=2, h=2)
    Kp = xg_flat2[:, 2048:3072].rearrange("p (s k) -> p s k", s=8)
    P4a = [P4.ap()[:, 0, :], P4.ap()[:, 1, :], xg_flat2[:, 3072:3584]]
    qS = [[DmaSlot(p, f"sQp{sl}{j}") for j in range(4)] for sl in range(2)]
    kS = [DmaSlot(p, f"sKp{i}") for i in range(8)]
    kb_ev = [None] * 8
    kp_lastuse = [None] * 8
    qp_lastuse = [None, None]
    qp_ready = [None, None]
    Sslot = [psBIG[:, 0:512], psBIG[:, 512:1024], psBIG[:, 1536:2048]]
    psBB = psBIG[:, 1024:1536]

    def emit_og_chunk(c):
        @p.block.gpsimd
        def _(g, c=c):
            g.wait_ge(cow.ctr.sem, cow.ctr.v)
            g.collective_compute("AllGather", ALU.bypass,
                                 replica_groups=[[0, 1, 2, 3], [4, 5, 6, 7]],
                                 ins=[cO.ap()[c].opt()], outs=[gO.ap()[c].opt()]).then_inc(io.sem, 1)
        io.inc()
    s_exp = [None] * 3
    p_cons = [None] * 3
    Oslot_par = [[psS[:, 0:512], psS[:, 512:1024]], [psO[:, 0:512], psV[:, 0:512]]]
    o_free_banks = [[mx_done, mx_done], [mx_done, mx_done]]
    pending_epi = [None]
    cow = DmaSlot(p, "sCO")

    def emit_qp_build(pj, st):
        sq = st % 2
        qsl2 = slice(st * 512, (st + 1) * 512)

        @p.block.gpsimd
        def _(g):
            if st < 2:
                g.wait_ge(shs.ctr.sem, shs.ctr.v)
                g.wait_ge(dv.sem, mx_done)
            if pj == 0 and st == 0:
                g.memset(Kp[64:96, :, :], 0.0)
                g.memset(Qp[64:96, :, :, :], 0.0).then_inc(p.sCN.sem, 1)
                g.wait_ge(p.sCN.sem, p.sCN.inc())
                g.memset(Kp[64:65, :, :], 1.0)
            if qp_lastuse[sq] is not None:
                g.wait_ge(pe.sem, qp_lastuse[sq])
            evs = [
                qS[sq][0].issue(g, Qp[0:64, sq, 0, :], Q2[0:64, pj, qsl2]),
                qS[sq][1].issue(g, Qp[0:64, sq, 1, :], Q2[64:128, pj, qsl2]),
                qS[sq][2].issue(g, Qp[64:65, sq, 0, :], shb.ap()[0:1, qsl2]),
                qS[sq][3].issue(g, Qp[64:65, sq, 1, :], shb.ap()[1:2, qsl2]),
            ]
        qp_ready[sq] = [(s.ctr.sem, s.ctr.v) for s in qS[sq]]
    shs = DmaSlot(p, "sShs")
    sh_prev = [None]

    for pj in range(2):
        dv0_sh = dv.v
        ac0_sh = ac.v
        for ch in range(16):
            csl = slice(ch * 512, (ch + 1) * 512)
            sl2 = ch % 2

            @p.block.vector
            def _(v):
                v.wait_ge(dv.sem, dv.v)
                if sh_prev[0] is not None:
                    v.wait_ge(pe.sem, sh_prev[0])
                if pj == 0 and ch == 0:
                    v.wait_ge(dv.sem, mx_done)
                v.tensor_mul(scr512[:, :], Q2[:, pj, csl], K2[:, pj, csl]).then_inc(dv.sem, 1)
            d1 = dv.inc()

            @p.block.tensor
            def _(t):
                t.wait_ge(dv.sem, d1)
                t.matmul(psO[0:2, 0:512], onespat[:, :], scr512[:, :],
                         start=True, stop=True).then_inc(pe.sem, 1)
            m1 = pe.inc()

            @p.block.vector
            def _(v):
                v.wait_ge(pe.sem, m1)
                v.tensor_mul(scr512[:, :], Q2[:, pj, csl], Q2[:, pj, csl]).then_inc(dv.sem, 1)
            d2 = dv.inc()

            @p.block.tensor
            def _(t):
                t.wait_ge(dv.sem, d2)
                t.matmul(psV[0:2, 0:512], onespat[:, :], scr512[:, :],
                         start=True, stop=True).then_inc(pe.sem, 1)
            m2 = pe.inc()
            sh_prev[0] = m2

            @p.block.vector
            def _(v):
                v.wait_ge(pe.sem, m2)
                if shs.ctr.v > 0:
                    v.wait_ge(shs.ctr.sem, shs.ctr.v)
                v.tensor_scalar(shst[0:2, :], psV[0:2, 0:512],
                                mx[0:2, pj:pj + 1], None, ALU.mult).then_inc(dv.sem, 1)
            d3 = dv.inc()

            assert d3 == dv0_sh + 5 * ch + 3

            @p.block.scalar
            def _(s):
                s.wait_ge(dv.sem, d3)
                s.activation(shst[0:2, :], shst[0:2, :], AF.Sqrt).then_inc(ac.sem, 1)
            a1 = ac.inc()
            assert a1 == ac0_sh + ch + 1

            @p.block.vector
            def _(v):
                v.wait_ge(ac.sem, a1)
                v.tensor_add(shst[0:2, :], shst[0:2, :], psO[0:2, 0:512]).then_inc(dv.sem, 1)
                v.wait_ge(dv.sem, dv.inc())
                v.tensor_scalar_mul(shst[0:2, :], shst[0:2, :], -0.5).then_inc(dv.sem, 1)
            d4 = dv.inc()

            @p.block.sync
            def _(sy):
                sy.wait_ge(dv.sem, d4)
                if pj == 1 and ch == 0:
                    for qsl_ in qS:
                        for s_ in qsl_:
                            if s_.ctr.v > 0:
                                sy.wait_ge(s_.ctr.sem, s_.ctr.v)
                shs.issue(sy, shb.ap()[:, csl], shst[0:2, :])

        o_free_banks[1] = [dv.v, dv.v]

        emit_qp_build(pj, 0)
        emit_qp_build(pj, 1)

        for st in range(16):
            if pj == 1 and st >= 5 and (st - 5) % 4 == 0:
                emit_og_chunk((st - 5) // 4)
            qsl = slice(st * 512, (st + 1) * 512)
            ktmax = 4 * (st + 1)
            iters = []
            for kt in range(ktmax):
                diag_sub = kt - 4 * st
                q0 = diag_sub * 128 if diag_sub >= 0 else 0
                for hh in range(2):
                    iters.append((kt, hh, len(iters) % 3, q0, 512 - q0, diag_sub))

            L = len(iters)
            par = st % 2
            Osl = Oslot_par[par]

            pe0 = pe.v
            ac0 = ac.v
            carry_pcons = list(p_cons)
            i_epi = min(8, L - 1) if pending_epi[0] is not None else None
            qk_val = [None] * L
            pv_val = [None] * L
            cnt = pe0
            for i in range(L):
                cnt += 1
                qk_val[i] = cnt
                if i >= 2:
                    cnt += 1
                    pv_val[i - 2] = cnt
                if i == i_epi:
                    cnt += 2
            pv_val[L - 2] = cnt + 1
            pv_val[L - 1] = cnt + 2
            qp_ev = qp_ready[st % 2]
            qp_lastuse[st % 2] = qk_val[-1]
            if st + 2 < 16:
                emit_qp_build(pj, st + 2)

            @p.block.scalar
            def _(s, iters=iters, qk_val=qk_val, pv_val=pv_val, carry=carry_pcons):
                for i, (kt, hh, si, q0, Nq, ds) in enumerate(iters):
                    s.wait_ge(pe.sem, qk_val[i])
                    if i >= 3:
                        s.wait_ge(pe.sem, pv_val[i - 3])
                    elif carry[si] is not None:
                        s.wait_ge(pe.sem, carry[si])
                    s.activation(P4a[si][:, 0:Nq], Sslot[si][:, 0:Nq], AF.Exp,
                                 scale=co8[:, 0:1]).then_inc(ac.sem, 1)
            ac.inc(L)

            mskv_arr = [None] * L

            def emit_pv(j, iters=iters, par=par, Osl=Osl, ac0=ac0,
                        pv_val=pv_val, ktmax=ktmax, pj=pj):
                kt, hh, si, q0, Nq, diag_sub = iters[j]
                expv = ac0 + j + 1

                @p.block.tensor
                def _(t):
                    if mskv_arr[j] is not None:
                        t.wait_ge(dv.sem, mskv_arr[j])
                    else:
                        t.wait_ge(ac.sem, expv)
                    if kt == 0:
                        t.wait_ge(dv.sem, o_free_banks[par][hh])
                    t.matmul(Osl[hh][0:65, q0:512], V2[:, pj, kt, hh * 65:(hh + 1) * 65],
                             P4a[si][:, 0:Nq], start=(kt == 0), stop=(kt == ktmax - 1),
                             skip_group_check=True).then_inc(pe.sem, 1)
                p_cons[si] = pe.inc()
                assert p_cons[si] == pv_val[j]

            for i, (kt, hh, si, q0, Nq, diag_sub) in enumerate(iters):
                    hsl = slice(hh * 64, hh * 64 + 64)
                    ksl = i % 8

                    @p.block.sync
                    def _(sy):
                        if kp_lastuse[ksl] is not None:
                            sy.wait_ge(pe.sem, kp_lastuse[ksl])
                        if pj == 0 and st == 0 and i < 4:
                            sy.wait_ge(shs.ctr.sem, shs.ctr.v)
                        kb_ev[ksl] = kS[ksl].issue(sy, Kp[0:64, ksl, :],
                                                   K2[hsl, pj, kt * 128:(kt + 1) * 128])

                    @p.block.tensor
                    def _(t):
                        t.wait_ge(kb_ev[ksl][0], kb_ev[ksl][1])
                        if i < 2:
                            for ev in qp_ev:
                                t.wait_ge(ev[0], ev[1])
                        if s_exp[si] is not None:
                            t.wait_ge(ac.sem, s_exp[si])
                        t.matmul(Sslot[si][:, 0:Nq], Kp[0:96, ksl, :],
                                 Qp[0:96, st % 2, hh, q0:512],
                                 start=True, stop=True).then_inc(pe.sem, 1)
                    qkv_ = pe.inc()
                    assert qkv_ == qk_val[i]
                    kp_lastuse[ksl] = qkv_

                    expv = ac0 + i + 1
                    s_exp[si] = expv

                    if diag_sub >= 0:
                        @p.block.vector
                        def _(v):
                            v.wait_ge(ac.sem, expv)
                            v.wait_ge(tup_s.sem, tup_s.v)
                            v.tensor_mul(P4a[si][:, 0:128], P4a[si][:, 0:128], tupp).then_inc(dv.sem, 1)
                        mskv_arr[i] = dv.inc()

                    if i >= 2:
                        emit_pv(i - 2)
                    if i == i_epi:
                        pending_epi[0]()
                        pending_epi[0] = None

            emit_pv(L - 2)
            emit_pv(L - 1)
            if pending_epi[0] is not None:
                pending_epi[0]()
                pending_epi[0] = None

            def make_epilogue(pj=pj, st=st, par=par, qsl=qsl, pv_done=pe.v):
                def emit():
                    rbv_prev = None
                    for hh in range(2):
                        @p.block.vector
                        def _(v, hh=hh, rbv_prev=rbv_prev):
                            v.wait_ge(pe.sem, pv_done)
                            if rbv_prev is not None:
                                v.wait_ge(pe.sem, rbv_prev)
                            v.reciprocal(rl[:, :], Oslot_par[par][hh][64:65, 0:512]).then_inc(dv.sem, 1)
                            v.wait_ge(dv.sem, dv.inc())
                            v.tensor_copy(Ost[:, :], Oslot_par[par][hh][0:64, 0:512]).then_inc(dv.sem, 1)
                        oe = dv.inc()

                        @p.block.tensor
                        def _(t):
                            t.wait_ge(dv.sem, oe)
                            t.matmul(psBB[0:64, 0:512], p.ones1x128_f[0:1, 0:64], rl[0:1, :],
                                     start=True, stop=True).then_inc(pe.sem, 1)
                        rbv = pe.inc()
                        rbv_prev = rbv

                        @p.block.vector
                        def _(v):
                            v.wait_ge(pe.sem, rbv)
                            if cow.ctr.v > 0:
                                v.wait_ge(cow.ctr.sem, cow.ctr.v)
                            v.tensor_mul(Ostb[:, :], Ost[:, :], psBB[0:64, 0:512]).then_inc(dv.sem, 1)
                        o_free_banks[par][hh] = dv.inc()

                        @p.block.sync
                        def _(sy, mfree=o_free_banks[par][hh], hh=hh):
                            sy.wait_ge(dv.sem, mfree)
                            r0_ = (2 * pj + hh) * 64
                            cow.issue(sy, cO.ap()[st // 4, r0_:r0_ + 64,
                                                  (st % 4) * 512:(st % 4) * 512 + 512],
                                      Ostb[:, :])
                return emit

            pending_epi[0] = make_epilogue()

        if pending_epi[0] is not None:
            pending_epi[0]()
            pending_epi[0] = None
    attn_done = dv.v

    emit_og_chunk(3)
    cco_done = io.v

    oq_s = p.dma_sem("sOq")

    @p.block.sync
    def _(sy):
        sy.wait_ge(io.sem, cco_done)
        rq = nc.partition_id() % 4
        sy.dma_start(out=OQ, in_=gO_v[rq]).then_inc(oq_s.sem, 16)
    oq_s.inc(16)
    oq_ev = (oq_s.sem, oq_s.v)

    slots = [psBIG[:, i * 512:(i + 1) * 512] for i in range(4)]

    def cb_out(m, q, ps, mmv):
        @p.block.vector
        def _(v):
            v.wait_ge(pe.sem, mmv)
            v.wait_ge(dv.sem, dv.v)
            bias = bp_sb[:, m:m + 1] if rope_bias else 0.0
            v.scalar_tensor_tensor(tmpf, ps, bias, xres[:, m, q * 512:(q + 1) * 512],
                                   ALU.add, ALU.add).then_inc(dv.sem, 1)
            v.wait_ge(dv.sem, dv.inc())
            v.tensor_copy(xres[:, m, q * 512:(q + 1) * 512], tmpf).then_inc(dv.sem, 1)
        return dv.inc()

    @p.block.tensor
    def _(t):
        t.wait_ge(oq_ev[0], oq_ev[1])
        t.wait_ge(wp_ready[0], wp_ready[1])
    emit_mm_dense(p, WP, OQ, NCH, cb_out, wait_dv=attn_done, psum_rot=slots)

    return emit_store_x(p, xres, xo_d, wait_dv=dv.v)


def build_ffn0():
    return _standalone(sub_ffn0, "f0_")


def build_ffn1():
    return _standalone(sub_ffn1, "f1_")


def build_att5():
    return _standalone(sub_att5, "a5_")


def build_att7():
    return _standalone(sub_att7, "a7_")


def build_att0():
    return _standalone(lambda p, x, xo, after=None: sub_att02(p, x, xo, True, after), "a0_")


def build_att2():
    return _standalone(lambda p, x, xo, after=None: sub_att02(p, x, xo, False, after), "a2_")





def emit_barrier(p, extra=()):
    vals = [(p.sPE.sem, p.sPE.v), (p.sDV.sem, p.sDV.v), (p.sAC.sem, p.sAC.v),
            (p.sIO.sem, p.sIO.v), (p.sCN.sem, p.sCN.v), (p.sST.sem, p.sST.v)]
    vals += list(extra)
    for secf in [p.block.tensor, p.block.vector, p.block.scalar, p.block.gpsimd, p.block.sync]:
        @secf
        def _(e):
            for s, v in vals:
                if v > 0:
                    e.wait_ge(s, v)


def build_fused():
    p = Prog("fx")
    x_d = p.nc.dram_tensor("x", [D, TLOC], F32, kind="ExternalInput").ap()
    xo_d = p.nc.dram_tensor("xo", [D, TLOC], F32, kind="ExternalOutput").ap()
    xs = [x_d] + [p.nc.dram_tensor(f"xmid{i}", [D, TLOC], F32).ap() for i in range(7)] + [xo_d]
    common_sems(p)
    p.xres = p.sbuf("xres_root", [128, NCH, TLOC], F32)
    p.start()
    subs = [
        (lambda pp, a, b, after=None: sub_att02(pp, a, b, True, after), "s0_"),
        (sub_ffn0, "s1_"),
        (lambda pp, a, b, after=None: sub_att02(pp, a, b, False, after), "s2_"),
        (sub_ffn0, "s3_"),
        (sub_att5, "s4_"),
        (sub_ffn1, "s5_"),
        (sub_att7, "s6_"),
        (sub_ffn1, "s7_"),
    ]
    io_map = [(xs[0], None), (None, None), (None, None), (None, None),
              (None, None), (None, xs[6]), (xs[6], xs[7]), (xs[7], xs[8])]
    for i, (fn, pref) in enumerate(subs):
        p.push_scope(pref)
        r = fn(p, io_map[i][0], io_map[i][1])
        p.pop_scope()
        evs = r if isinstance(r, list) else [r]
        emit_barrier(p, extra=[e for e in evs if e is not None])
    return p.finish()


_fused_cache = {}


def _get_fused():
    if "nc" not in _fused_cache:
        _fused_cache["nc"] = build_fused()
    return _fused_cache["nc"]


FUSED = True


_BUILDERS = {
    SUB_FFN0A: build_ffn0,
    SUB_FFN0B: build_ffn0,
    SUB_FFN1A: build_ffn1,
    SUB_FFN1B: build_ffn1,
    SUB_ATT5: build_att5,
    SUB_ATT7: build_att7,
    SUB_ATTA0: build_att0,
    SUB_ATTB0: build_att2,
}

_prog_cache = {}


def _get_prog(idx):
    key = _BUILDERS[idx].__name__
    if key not in _prog_cache:
        _prog_cache[key] = _BUILDERS[idx]()
    return _prog_cache[key]


def run_sublayer(idx, in_maps, x_cur):
    nc = _get_prog(idx)
    names = SUB_INPUTS[idx]
    maps = []
    for c in range(NCORE):
        m = {k: in_maps[c][k] for k in names if k != "x"}
        m["x"] = x_cur[c]
        maps.append(m)
    res = run_bass_kernel_spmd(nc, maps, core_ids=list(range(NCORE)))
    return [np.asarray(res.results[c]["xo"]) for c in range(NCORE)]


def kernel(**inputs):
    in_maps = host_prepare(inputs)
    out = np.empty((B, S, D), dtype=np.float32)
    if FUSED:
        nc = _get_fused()
        res = run_bass_kernel_spmd(nc, in_maps, core_ids=list(range(NCORE)))
        for c in range(NCORE):
            b, r = c // GRP, c % GRP
            out[b, r * TLOC:(r + 1) * TLOC, :] = np.asarray(res.results[c]["xo"]).T
        return out
    x_cur = [im["x"] for im in in_maps]
    for idx in range(8):
        x_cur = run_sublayer(idx, in_maps, x_cur)
    for c in range(NCORE):
        b, r = c // GRP, c % GRP
        out[b, r * TLOC:(r + 1) * TLOC, :] = x_cur[c].T
    return out


# version 18
# speedup vs baseline: 1.0069x; 1.0069x over previous
import numpy as np
import ml_dtypes
from contextlib import ExitStack

import concourse.bass as bass
import concourse.mybir as mybir
from concourse.bass_utils import run_bass_kernel_spmd

F32 = mybir.dt.float32
BF16 = mybir.dt.bfloat16
AF = mybir.ActivationFunctionType
ALU = mybir.AluOpType
AX = mybir.AxisListType

NCORE = 8
GRP = 4
B, S, D = 2, 8192, 1024
TLOC = 2048
NCH = D // 128
NH, HD = 16, 64
HPC = NH // GRP
EPS = 1e-5
BF = ml_dtypes.bfloat16

SUB_ATTA0, SUB_FFN0A, SUB_ATTB0, SUB_FFN0B, SUB_ATT5, SUB_FFN1A, SUB_ATT7, SUB_FFN1B = range(8)



def _bias_cols(v):
    return np.ascontiguousarray(v.reshape(NCH, 128).T)


def _rope_table():
    theta = (1.0 / 10000.0 ** (2.0 * np.arange(0, HD // 2, dtype=np.float32) / HD)).astype(np.float32)
    seq = np.arange(1, S + 1, dtype=np.float32)
    ang = np.einsum("n,d->nd", seq, theta).astype(np.float32)
    ang = np.concatenate([ang, ang], axis=-1)
    f = (np.cos(ang) + np.sin(ang)).astype(np.float32)
    return np.ascontiguousarray(np.tile(f.T, (2, 1)))


def _masks():
    p = np.arange(128)[:, None]
    j = np.arange(128)[None, :]
    t_upp = (j >= p).astype(BF)
    t_low = (j <= p).astype(BF)
    return t_upp, np.ascontiguousarray((j <= p).astype(BF)), t_low


def host_prepare(inp):
    x = np.asarray(inp["L0_attn_a_x"])

    WattnT = np.ascontiguousarray(np.asarray(inp["L0_attn_a_W_attn"]).T)
    b_attn = np.asarray(inp["L0_attn_a_b_attn"])
    WprojT = np.ascontiguousarray(np.asarray(inp["L0_attn_a_W_proj"]).T)
    b_proj = np.asarray(inp["L0_attn_a_b_proj"])
    WQT = np.ascontiguousarray(np.asarray(inp["L0_attn_b_W_Q"]).T)
    WKT = np.ascontiguousarray(np.asarray(inp["L0_attn_b_W_K"]).T)
    WVT = np.ascontiguousarray(np.asarray(inp["L0_attn_b_W_V"]).T)
    WOT = np.ascontiguousarray(np.asarray(inp["L0_attn_b_W_O"]).T)
    W0 = np.asarray(inp["L0_ffn_weight"])
    A1 = np.asarray(inp["L0_ffn_aux_weight1"])
    A2 = np.asarray(inp["L0_ffn_aux_weight2"])
    b0 = np.asarray(inp["L0_ffn_bias"])
    W5 = np.asarray(inp["L1_attn_a_W"])
    g7 = np.asarray(inp["L1_attn_b_ln_gamma"])
    be7 = np.asarray(inp["L1_attn_b_ln_beta"])
    Wqkv7 = np.asarray(inp["L1_attn_b_w_qkv"])
    Wout7 = np.asarray(inp["L1_attn_b_w_out"])
    W6T = np.ascontiguousarray(np.asarray(inp["L1_ffn_weights"]).T)
    sc6 = np.asarray(inp["L1_ffn_scale"])
    b6 = np.asarray(inp["L1_ffn_bias"])

    f_rope = _rope_table()
    t_upp, t_low, _ = _masks()
    ident = np.eye(128, dtype=BF)

    pj = (np.arange(128)[:, None] - np.arange(512)[None, :]).astype(np.float32)

    in_maps = []
    for c in range(NCORE):
        b = c // GRP
        r = c % GRP
        g = r
        qbase = r * TLOC
        xT = np.ascontiguousarray(x[b, qbase:qbase + TLOC, :].T)

        hsl = slice(256 * g, 256 * (g + 1))
        Wqkv0_my = np.ascontiguousarray(
            np.concatenate([WattnT[:, 0:1024][:, hsl], WattnT[:, 1024:2048][:, hsl], WattnT[:, 2048:3072][:, hsl]], axis=1))
        bq_my = np.ascontiguousarray(b_attn[0:1024][hsl].reshape(2, 128).T)
        bk_my = np.ascontiguousarray(b_attn[1024:2048][hsl].reshape(2, 128).T)
        bv_my = np.ascontiguousarray(b_attn[2048:3072][hsl].reshape(1, 256))
        Wqkv3_my = np.ascontiguousarray(
            np.concatenate([WQT[:, hsl], WKT[:, hsl], WVT[:, hsl]], axis=1))

        cbc5 = np.zeros((128, 256), dtype=np.float32)
        for st_ in range(4):
            for kt in range(64):
                cbc5[:, st_ * 64 + kt] = qbase + st_ * 512 - kt * 128
        mask7 = np.zeros((128, 128), dtype=BF) if r == 0 else t_low

        m = {
            "x": xT,
            "Wqkv0_my": Wqkv0_my, "bq_my": bq_my, "bk_my": bk_my, "bv_my": bv_my,
            "WprojT": WprojT, "b_proj": _bias_cols(b_proj),
            "Wqkv3_my": Wqkv3_my, "WOT": WOT,
            "W0": W0, "A1": A1, "A2": A2, "b0": _bias_cols(b0),
            "W5": W5,
            "g7": _bias_cols(g7), "be7": _bias_cols(be7),
            "Wqkv7": Wqkv7, "Wout7": Wout7,
            "W6T": W6T, "sc6": _bias_cols(sc6), "b6": _bias_cols(b6),
            "f_rope": f_rope, "t_upp": t_upp, "t_low": t_low,
            "ident": ident, "pj": pj, "cbc5": cbc5, "mask7": mask7,
        }
        in_maps.append(m)
    return in_maps


SUB_INPUTS = {
    SUB_ATTA0: ["x", "Wqkv0_my", "bq_my", "bk_my", "bv_my", "WprojT", "b_proj", "f_rope", "t_upp"],
    SUB_FFN0A: ["x", "W0", "A1", "A2", "b0"],
    SUB_ATTB0: ["x", "Wqkv3_my", "WOT", "t_upp"],
    SUB_FFN0B: ["x", "W0", "A1", "A2", "b0"],
    SUB_ATT5:  ["x", "W5", "pj", "cbc5", "ident"],
    SUB_FFN1A: ["x", "W6T", "sc6", "b6"],
    SUB_ATT7:  ["x", "g7", "be7", "Wqkv7", "Wout7", "t_upp", "t_low", "mask7", "ident"],
    SUB_FFN1B: ["x", "W6T", "sc6", "b6"],
}




def _ap(t):
    return t if isinstance(t, bass.AP) else t.ap()

class Ctr:
    def __init__(self, sem):
        self.sem = sem
        self.v = 0

    def inc(self, n=1):
        self.v += n
        return self.v


class Prog:
    def __init__(self, name="prog"):
        self.nc = bass.Bass("TRN2", target_bir_lowering=False)
        self.ctx = ExitStack()
        self.name = name
        self.prefix = ""
        self._scope = None
        self._inputs = {}

    def _stack(self):
        return self._scope if self._scope is not None else self.ctx

    def push_scope(self, prefix):
        assert self._scope is None
        self._scope = ExitStack()
        self.prefix = prefix

    def pop_scope(self):
        self._scope.close()
        self._scope = None
        self.prefix = ""

    def sbuf(self, name, shape, dtype):
        return self._stack().enter_context(self.nc.sbuf_tensor(self.prefix + name, shape, dtype))

    def psum(self, name, shape, dtype=F32):
        return self._stack().enter_context(self.nc.psum_tensor(self.prefix + name, shape, dtype))

    def dram(self, name, shape, dtype):
        return self.nc.dram_tensor(self.prefix + name, shape, dtype)

    def in_ap(self, name, shape, dtype):
        if name not in self._inputs:
            self._inputs[name] = self.nc.dram_tensor(name, shape, dtype, kind="ExternalInput").ap()
        return self._inputs[name]

    def sem(self, name):
        return Ctr(self.ctx.enter_context(self.nc.semaphore(self.prefix + name)))

    def dma_sem(self, name=None):
        self._dma_sem_i = getattr(self, "_dma_sem_i", 0) + 1
        return Ctr(self.ctx.enter_context(
            self.nc.semaphore(self.prefix + (name or "sD") + str(self._dma_sem_i))))

    def start(self):
        self.block = self.ctx.enter_context(self.nc.Block())

    def finish(self):
        self.ctx.close()
        return self.nc


def common_sems(p):
    p.sPE = p.sem("sPE")
    p.sDV = p.sem("sDV")
    p.sAC = p.sem("sAC")
    p.sIO = p.sem("sIO")
    p.sST = p.dma_sem("sST")
    p.sCN = p.sem("sCN")



def emit_load_x(p, x_d, xres, after=None):
    if x_d is None:
        return (p.sDV.sem, 0)
    w = p.dma_sem("sX")
    xv = x_d.rearrange("(c p) t -> p c t", p=128)

    @p.block.sync
    def _(sy):
        if after is not None:
            sy.wait_ge(after[0], after[1])
        sy.dma_start(out=xres[:, 0:4, :], in_=xv[:, 0:4, :]).then_inc(w.sem, 16)

    @p.block.scalar
    def _(s):
        if after is not None:
            s.wait_ge(after[0], after[1])
        s.dma_start(out=xres[:, 4:8, :], in_=xv[:, 4:8, :]).then_inc(w.sem, 16)
    w.inc(32)
    return (w.sem, w.v)


def emit_store_x(p, xres, xo_d, wait_dv=None):
    if xo_d is None:
        return None
    st = p.sST
    st2 = p.dma_sem("sXo2")
    xov = xo_d.rearrange("(c p) t -> p c t", p=128)

    @p.block.scalar
    def _(s):
        if wait_dv is not None:
            s.wait_ge(p.sDV.sem, wait_dv)
        s.dma_start(out=xov[:, 4:8, :], in_=xres[:, 4:8, :]).then_inc(st2.sem, 16)
        st2.inc(16)

    @p.block.sync
    def _(sy):
        if wait_dv is not None:
            sy.wait_ge(p.sDV.sem, wait_dv)
        sy.dma_start(out=xov[:, 0:4, :], in_=xres[:, 0:4, :]).then_inc(st.sem, 16)
        st.inc(16)
        sy.wait_ge(st.sem, st.v)
        sy.wait_ge(st2.sem, st2.v)
    return [(st.sem, st.v), (st2.sem, st2.v)]


def emit_consts(p):
    cn = p.sCN

    @p.block.gpsimd
    def _(g):
        g.memset(p.ones128_bf[:, :], 1.0)
        g.memset(p.ones1x128_f[:, :], 1.0).then_inc(cn.sem, 1)
    cn.inc(1)
    return cn.v


def emit_ln(p, xres, xn, xsq_scratch, tmp_f32, x_ready, extra=None):
    pe, dv, ac = p.sPE, p.sDV, p.sAC
    psBIG = p.psBIG
    xall = _ap(xres).rearrange("p c t -> p (c t)")
    xnall = _ap(xn).rearrange("p c t -> p (c t)")
    sqall = _ap(xsq_scratch).rearrange("p c t -> p (c t)")

    @p.block.vector
    def _(v):
        v.wait_ge(x_ready[0], x_ready[1])
        v.tensor_copy(xnall, xall).then_inc(dv.sem, 1)
        cast_done_ = dv.inc()
        v.wait_ge(dv.sem, cast_done_)
        v.tensor_mul(sqall, xnall, xnall).then_inc(dv.sem, 1)
        dv.inc()
    cast_done = dv.v - 1
    sq_done = dv.v

    @p.block.tensor
    def _(t):
        t.wait_ge(dv.sem, cast_done)
        t.wait_ge(p.sCN.sem, p.sCN.v)
        for c in range(NCH):
            for q in range(4):
                mm = t.matmul(psBIG[0:1, q * 512:(q + 1) * 512],
                              p.ones128_bf[:, :], xn[:, c, q * 512:(q + 1) * 512],
                              start=(c == 0), stop=(c == NCH - 1))
        mm.then_inc(pe.sem, 1)
    sx_done = pe.inc()

    @p.block.vector
    def _(v):
        v.wait_ge(pe.sem, sx_done)
        v.tensor_scalar_mul(p.r_m[:, :], psBIG[0:1, 0:2048], 1.0 / D).then_inc(dv.sem, 1)
    m_done = dv.inc()

    @p.block.tensor
    def _(t):
        t.wait_ge(dv.sem, max(sq_done, m_done))
        for c in range(NCH):
            for q in range(4):
                mm = t.matmul(psBIG[0:1, q * 512:(q + 1) * 512],
                              p.ones128_bf[:, :], xsq_scratch[:, c, q * 512:(q + 1) * 512],
                              start=(c == 0), stop=(c == NCH - 1))
        mm.then_inc(pe.sem, 1)
    sxx_done = pe.inc()

    @p.block.vector
    def _(v):
        v.wait_ge(pe.sem, sxx_done)
        v.tensor_scalar_mul(p.r_a[:, :], psBIG[0:1, 0:2048], 1.0 / D).then_inc(dv.sem, 1)
        dv.inc()
        v.tensor_mul(p.r_b[:, :], p.r_m[:, :], p.r_m[:, :]).then_inc(dv.sem, 1)
        v.wait_ge(dv.sem, dv.inc())
        v.tensor_sub(p.r_a[:, :], p.r_a[:, :], p.r_b[:, :]).then_inc(dv.sem, 1)
        v.wait_ge(dv.sem, dv.inc())
        v.tensor_scalar_add(p.r_a[:, :], p.r_a[:, :], EPS).then_inc(dv.sem, 1)
    vpe_done = dv.inc()

    @p.block.scalar
    def _(s):
        s.wait_ge(dv.sem, vpe_done)
        s.activation(p.r_b[:, :], p.r_a[:, :], AF.Sqrt).then_inc(ac.sem, 1)
    sqrt_done = ac.inc()

    @p.block.vector
    def _(v):
        v.wait_ge(ac.sem, sqrt_done)
        v.reciprocal(p.r_a[:, :], p.r_b[:, :]).then_inc(dv.sem, 1)
    rows_done = dv.inc()

    psum_free = rows_done
    for h in range(2):
        tok = slice(h * 1024, (h + 1) * 1024)

        @p.block.tensor
        def _(t):
            t.wait_ge(dv.sem, psum_free)
            for q in range(2):
                sl = slice(h * 1024 + q * 512, h * 1024 + (q + 1) * 512)
                t.matmul(psBIG[:, q * 512:(q + 1) * 512],
                         p.ones1x128_f[:, :], p.r_m[0:1, sl], start=True, stop=True)
                mm = t.matmul(psBIG[:, 1024 + q * 512:1024 + (q + 1) * 512],
                              p.ones1x128_f[:, :], p.r_a[0:1, sl], start=True, stop=True)
            mm.then_inc(pe.sem, 1)
        bc_done = pe.inc()

        @p.block.vector
        def _(v):
            v.wait_ge(pe.sem, bc_done)
            for c in range(NCH):
                v.tensor_sub(tmp_f32[:, 0:1024], xres[:, c, tok], psBIG[:, 0:1024]).then_inc(dv.sem, 1)
                v.wait_ge(dv.sem, dv.inc())
                v.tensor_mul(xn[:, c, tok], tmp_f32[:, 0:1024], psBIG[:, 1024:2048]).then_inc(dv.sem, 1)
                v.wait_ge(dv.sem, dv.inc())
        psum_free = dv.v

    if extra is not None:
        extra["m_row"] = p.r_m
        extra["rstd_row"] = p.r_a[:, :]
    return dv.v


def alloc_ln_rows(p):
    p.r_m = p.sbuf("r_m_t", [1, TLOC], F32).ap()
    p.r_a = p.sbuf("r_a", [1, TLOC], F32).ap()
    p.r_b = p.sbuf("r_b", [1, TLOC], F32).ap()


def emit_w_load(p, dram_ap, sbuf_t, n_din_ch, sem_ctr=None):
    w = sem_ctr if sem_ctr is not None else p.dma_sem()

    @p.block.gpsimd
    def _(g):
        g.dma_start(out=sbuf_t[:, :, :],
                    in_=dram_ap.rearrange("(k p) j -> p k j", p=128)).then_inc(w.sem, 16)
    w.inc(16)
    return (w.sem, w.v)


def emit_mm_dense(p, w_sb, x_sb, n_k, out_cb, wait_w=None, wait_dv=None,
                  m_chunks=None, psum_rot=None):
    pe, dv = p.sPE, p.sDV
    if m_chunks is None:
        m_chunks = range(NCH)
    slots = psum_rot
    n_slots = len(slots)
    evac_vals = [None] * n_slots

    it = 0
    for m in m_chunks:
        for q in range(4):
            slot = it % n_slots
            qsl = slice(q * 512, (q + 1) * 512)

            @p.block.tensor
            def _(t):
                if it == 0:
                    if wait_dv is not None:
                        t.wait_ge(dv.sem, wait_dv)
                    if wait_w is not None:
                        t.wait_ge(wait_w[0], wait_w[1])
                if evac_vals[slot] is not None:
                    t.wait_ge(dv.sem, evac_vals[slot])
                for k in range(n_k):
                    last = t.matmul(slots[slot], w_sb[:, k, m * 128:(m + 1) * 128], x_sb[:, k, qsl],
                                    start=(k == 0), stop=(k == n_k - 1))
                last.then_inc(pe.sem, 1)
            mm_done = pe.inc()
            evac_vals[slot] = out_cb(m, q, slots[slot], mm_done)
            it += 1




def _standalone(sub, prefix):
    p = Prog(prefix)
    x_d = p.nc.dram_tensor("x", [D, TLOC], F32, kind="ExternalInput").ap()
    xo_d = p.nc.dram_tensor("xo", [D, TLOC], F32, kind="ExternalOutput").ap()
    common_sems(p)
    p.xres = p.sbuf("xres_root", [128, NCH, TLOC], F32)
    p.start()
    p.push_scope(prefix)
    sub(p, x_d, xo_d)
    p.pop_scope()
    return p.finish()


def sub_ffn0(p, x_d, xo_d, after=None):
    nc = p.nc
    W0_d = p.in_ap("W0", [D, D], F32)
    A1_d = p.in_ap("A1", [D, D], F32)
    A2_d = p.in_ap("A2", [D, D], F32)
    b0_d = p.in_ap("b0", [128, NCH], F32)

    xres = p.xres
    xn = p.sbuf("xn", [128, NCH, TLOC], BF16)
    hA = p.sbuf("hA", [128, NCH, TLOC], BF16)
    w0 = p.sbuf("w0", [128, NCH, D], BF16)
    w1 = p.sbuf("w1", [128, NCH, D], BF16)
    w2 = p.sbuf("w2", [128, NCH, D], BF16)
    b0_sb = p.sbuf("b0_sb", [128, NCH], F32)
    tres = p.sbuf("tres", [128, 2, 512], F32)
    tmp_f32 = tres.ap().rearrange("p a b -> p (a b)")
    p.ones128_bf = p.sbuf("ones128_bf", [128, 1], BF16)
    p.ones1x128_f = p.sbuf("ones1x128_f", [1, 128], F32)
    alloc_ln_rows(p)
    p.psBIG = p.psum("psBIG", [128, 2048], F32)

    emit_consts(p)
    x_ready = emit_load_x(p, x_d, xres, after)
    swt = p.dma_sem("sWts")
    emit_w_load(p, W0_d, w0, NCH, swt)
    emit_w_load(p, A1_d, w1, NCH, swt)
    emit_w_load(p, A2_d, w2, NCH, swt)

    @p.block.gpsimd
    def _(g):
        g.dma_start(out=b0_sb[:, :], in_=b0_d).then_inc(swt.sem, 16)
    swt.inc(16)
    w_ready = (swt.sem, swt.v)
    b_ready = w_ready

    xn_ready = emit_ln(p, xres, xn, hA, tmp_f32, x_ready)

    done = emit_ffn0_body(p, xn, hA, w0, w1, w2, b0_sb, xres, tres,
                          xn_ready, w_ready, b_ready)

    return emit_store_x(p, xres, xo_d, wait_dv=done)


def emit_ffn0_body(p, xn, hA, w0, w1, w2, b0_sb, xres, tres, xn_ready, w_ready, b_ready):
    pe, dv, ac = p.sPE, p.sDV, p.sAC
    psBIG = p.psBIG
    slots = [psBIG[:, i * 512:(i + 1) * 512] for i in range(4)]

    def cb1(m, q, ps, mmv):
        @p.block.vector
        def _(v):
            v.wait_ge(pe.sem, mmv)
            v.tensor_copy(hA[:, m, q * 512:(q + 1) * 512], ps).then_inc(dv.sem, 1)
        return dv.inc()

    emit_mm_dense(p, w0, xn, NCH, cb1, wait_w=w_ready, wait_dv=xn_ready, psum_rot=slots)
    l1_done = dv.v

    def cb2(m, q, ps, mmv):
        @p.block.vector
        def _(v):
            v.wait_ge(pe.sem, mmv)
            v.tensor_copy(xn[:, m, q * 512:(q + 1) * 512], ps).then_inc(dv.sem, 1)
        return dv.inc()

    emit_mm_dense(p, w1, hA, NCH, cb2, wait_dv=l1_done, psum_rot=slots)
    l2_done = dv.v

    pe0_3 = pe.v
    ac0_3 = ac.v
    dv0_3 = dv.v

    @p.block.scalar
    def _(s):
        psl = [psBIG[:, i * 512:(i + 1) * 512] for i in range(4)]
        for i in range(32):
            m = i // 4
            s.wait_ge(pe.sem, pe0_3 + i + 1)
            if i == 0:
                s.wait_ge(b_ready[0], b_ready[1])
            if i >= 2:
                s.wait_ge(dv.sem, dv0_3 + i - 1)
            s.activation(tres[:, i % 2, :], psl[i % 4], AF.Tanh,
                         bias=b0_sb[:, m:m + 1]).then_inc(ac.sem, 1)
    ac.inc(32)

    tcnt = [0]

    def cb3(m, q, ps, mmv):
        i = tcnt[0]
        slot3 = i % 2
        acv = ac0_3 + i + 1

        @p.block.vector
        def _(v):
            v.wait_ge(ac.sem, acv)
            v.tensor_add(xres[:, m, q * 512:(q + 1) * 512],
                         xres[:, m, q * 512:(q + 1) * 512], tres[:, slot3, :]).then_inc(dv.sem, 1)
        tcnt[0] += 1
        val = dv.inc()
        assert val == dv0_3 + i + 1
        return val

    emit_mm_dense(p, w2, xn, NCH, cb3, wait_dv=l2_done, psum_rot=slots)
    return dv.v



def sub_ffn1(p, x_d, xo_d, after=None):
    nc = p.nc
    W6T_d = p.in_ap("W6T", [D, D], F32)
    sc_d = p.in_ap("sc6", [128, NCH], F32)
    b_d = p.in_ap("b6", [128, NCH], F32)

    xres = p.xres
    xn = p.sbuf("xn", [128, NCH, TLOC], BF16)
    scr = p.sbuf("scr", [128, NCH, TLOC], BF16)
    wf = p.sbuf("wf", [128, NCH, 1024], F32)
    ws = p.sbuf("ws", [128, NCH, D], BF16)
    sc_sb = p.sbuf("sc_sb", [128, NCH], F32)
    b_sb = p.sbuf("b_sb", [128, NCH], F32)
    tres = p.sbuf("tres", [128, 2, 512], F32)
    tmp_f32 = tres.ap().rearrange("p a b -> p (a b)")
    p.ones128_bf = p.sbuf("ones128_bf", [128, 1], BF16)
    p.ones1x128_f = p.sbuf("ones1x128_f", [1, 128], F32)
    alloc_ln_rows(p)
    p.psBIG = p.psum("psBIG", [128, 2048], F32)

    emit_consts(p)
    x_ready = emit_load_x(p, x_d, xres, after)

    swt = p.dma_sem("sWts")

    @p.block.sync
    def _(sy):
        sy.dma_start(out=wf[:, :, :], in_=W6T_d.rearrange("(k p) j -> p k j", p=128)).then_inc(swt.sem, 16)
        sy.dma_start(out=sc_sb[:, :], in_=sc_d).then_inc(swt.sem, 16)
        sy.dma_start(out=b_sb[:, :], in_=b_d).then_inc(swt.sem, 16)
    w_ready = (swt.sem, swt.inc(48))

    @p.block.vector
    def _(v):
        v.wait_ge(w_ready[0], w_ready[1])
        wfa = wf.ap().rearrange("p k j -> p (k j)")
        wsa = ws.ap().rearrange("p k j -> p (k j)")
        v.tensor_scalar(wsa, wfa, 0.0, 2.0, ALU.is_ge, ALU.mult).then_inc(p.sDV.sem, 1)
        v.wait_ge(p.sDV.sem, p.sDV.inc())
        v.tensor_scalar(wsa, wsa, 1.0, None, ALU.subtract).then_inc(p.sDV.sem, 1)
    sign_done = p.sDV.inc()

    xn_ready = emit_ln(p, xres, xn, scr, tmp_f32, x_ready)

    pe, dv = p.sPE, p.sDV
    slots = [p.psBIG[:, i * 512:(i + 1) * 512] for i in range(4)]

    def cb(m, q, ps, mmv):
        @p.block.vector
        def _(v):
            v.wait_ge(pe.sem, mmv)
            v.wait_ge(dv.sem, dv.v)
            tslice = tres[:, 0, :]
            v.tensor_scalar(tslice, ps, sc_sb[:, m:m + 1], b_sb[:, m:m + 1], ALU.mult, ALU.add).then_inc(dv.sem, 1)
            v.wait_ge(dv.sem, dv.inc())
            v.tensor_scalar(tslice, tslice, 100.0, -100.0, ALU.min, ALU.max).then_inc(dv.sem, 1)
            v.wait_ge(dv.sem, dv.inc())
            v.tensor_add(xres[:, m, q * 512:(q + 1) * 512],
                         xres[:, m, q * 512:(q + 1) * 512], tslice).then_inc(dv.sem, 1)
        return dv.inc()

    emit_mm_dense(p, ws, xn, NCH, cb, wait_dv=max(xn_ready, sign_done), psum_rot=slots)

    return emit_store_x(p, xres, xo_d, wait_dv=dv.v)



class DmaSlot:
    def __init__(self, p, name):
        self.ctr = p.dma_sem(name)

    def issue(self, eng, out, in_):
        if self.ctr.v > 0:
            eng.wait_ge(self.ctr.sem, self.ctr.v)
        eng.dma_start(out=out, in_=in_).then_inc(self.ctr.sem, 16)
        self.ctr.inc(16)
        return (self.ctr.sem, self.ctr.v)


def sub_att5(p, x_d, xo_d, after=None):
    nc = p.nc
    W5_d = p.in_ap("W5", [D, 3 * D], F32)
    pj_d = p.in_ap("pj", [128, 512], F32)
    cbc5_d = p.in_ap("cbc5", [128, 256], F32)
    ident_d = p.in_ap("ident", [128, 128], BF16)

    NROW = TLOC + 1
    cin = p.dram("cin5", [NROW, 2056], BF16)
    coutA = p.dram("cout5A", [GRP * 1024, 2056], BF16)
    coutB = p.dram("cout5B", [GRP * 1024, 2056], BF16)
    coutS = p.dram("cout5S", [GRP, 2056], BF16)
    cA_v = coutA.ap().rearrange("(r n) c -> r n c", r=GRP)
    cB_v = coutB.ap().rearrange("(r n) c -> r n c", r=GRP)

    def cout_v(rank, r0, csl):
        if r0 < 1024:
            return cA_v[rank, r0:r0 + 128, csl]
        return cB_v[rank, r0 - 1024:r0 - 1024 + 128, csl]
    KB = p.dram("KB5", [S, 1024], BF16)
    KB_v = KB.ap().rearrange("(t p) c -> t p c", p=128)

    xres = p.xres
    qfm = p.sbuf("qfm", [128, NCH, TLOC], BF16)
    arena2 = p.sbuf("arena2", [128, 4 * TLOC], F32)
    xn = arena2.ap().bitcast(BF16).rearrange("p (c t) -> p c t", c=NCH)
    O_sb = arena2.ap()[:, 0:NCH * 512].rearrange("p (c t) -> p c t", c=NCH)
    arena1 = p.sbuf("arena1", [128, 24576], BF16)
    w5 = arena1.ap().rearrange("p (k j) -> p k j", k=NCH)
    Vg = arena1.ap()[:, 0:16384].rearrange("p (g k t) -> p g k t", g=2, k=8)
    Pg = arena1.ap()[:, 16384:20480].rearrange("p (k t) -> p k t", k=8)
    Kt = arena1.ap()[:, 20480:22528].rearrange("p (s t) -> p s t", s=2)
    Praw = arena1.ap()[:, 22528:23552].rearrange("p (s t) -> p s t", s=2)
    qsq = arena1.ap()[:, 16384:20480].rearrange("p (a t) -> p a t", a=2)
    arena3 = p.sbuf("arena3", [128, 6144], F32)
    a3b = arena3.ap().bitcast(BF16)
    CT = a3b[:, 0:4112].rearrange("p (s c) -> p s c", s=2)
    sq_scr = a3b[:, 4112:5136]
    normc = a3b[:, 5136:5152]
    a2b = arena2.ap().bitcast(BF16)
    KBs = a2b[:, 10240:12288].rearrange("p (s c) -> p s c", s=2)
    Kt2 = a2b[:, 8192:10240].rearrange("p (s c) -> p s c", s=2)
    nm16f = p.sbuf("nm16f", [16, 1], F32)
    nm16b = p.sbuf("nm16b", [16, 1], BF16)
    srow = p.sbuf("srow", [1, 8], F32)
    srow_b = p.sbuf("srow_b", [1, 8], BF16)
    kms = p.sbuf("kms", [1, 4], BF16)
    negMb = p.sbuf("negMb", [128, 1], F32)
    rl = p.sbuf("rl", [1, 512], F32)
    pj_sb = p.sbuf("pj_sb", [128, 512], F32)
    cbc_sb = p.sbuf("cbc_sb", [128, 256], F32)
    ident = p.sbuf("ident_sb", [128, 128], BF16)
    tmp_f32 = arena3.ap()[:, 4096:5120]
    p.ones128_bf = p.sbuf("ones128_bf", [128, 1], BF16)
    p.ones1x128_f = p.sbuf("ones1x128_f", [1, 128], F32)
    p.r_m = arena3.ap()[0:1, 0:2048]
    p.r_a = arena3.ap()[0:1, 2048:4096]
    p.r_b = arena3.ap()[0:1, 4096:6144]

    p.psBIG = p.psum("psBIG", [128, 2048], F32)
    psE = p.psum("psE", [128, 1024], F32)
    psT_a = p.psum("psT_a", [128, 128], BF16)
    psT_b = p.psum("psT_b", [128, 128], BF16)
    psBIG = p.psBIG
    S_bank = [psBIG[:, 0:512], psBIG[:, 512:1024]]
    l_bank = psBIG[0:1, 1024:1536]
    O_bank = [psBIG[:, 1536:2048], psE[:, 0:512]]
    rlb_bank = psE[:, 512:1024]

    pe, dv, ac, io = p.sPE, p.sDV, p.sAC, p.sIO

    emit_consts(p)
    x_ready = emit_load_x(p, x_d, xres, after)
    swt = p.dma_sem("sWts")

    @p.block.gpsimd
    def _(g):
        g.dma_start(out=w5[:, :, :], in_=W5_d.rearrange("(k p) j -> p k j", p=128)).then_inc(swt.sem, 16)
        g.dma_start(out=pj_sb[:, :], in_=pj_d).then_inc(swt.sem, 16)
        g.dma_start(out=cbc_sb[:, :], in_=cbc5_d).then_inc(swt.sem, 16)
        g.dma_start(out=ident[:, :], in_=ident_d).then_inc(swt.sem, 16)
    w_ready = (swt.sem, swt.inc(64))

    xn_ready = emit_ln(p, xres, xn, qfm.ap().rearrange("p c t -> p c t"), tmp_f32, x_ready)

    slots = [psBIG[:, i * 512:(i + 1) * 512] for i in range(4)]

    def cb_q(m, q, ps, mmv):
        @p.block.vector
        def _(v):
            v.wait_ge(pe.sem, mmv)
            v.tensor_copy(qfm[:, m, q * 512:(q + 1) * 512], ps).then_inc(dv.sem, 1)
        return dv.inc()

    emit_mm_dense(p, w5, xn, NCH, cb_q, wait_w=w_ready, wait_dv=xn_ready, psum_rot=slots)
    q_done = dv.v

    co32 = p.sbuf("co32", [128, 1], F32)

    @p.block.gpsimd
    def _(g):
        g.memset(co32[:, :], 1.0 / 32.0)
        g.memset(srow[:, :], 0.0)
        g.wait_ge(dv.sem, xn_ready)
        g.memset(CT[:, 0, 2049:2056], 0.0)
        g.memset(CT[:, 1, 2049:2056], 0.0).then_inc(p.sCN.sem, 1)
    ct_pad_done = p.sCN.inc()

    ct_slot = [DmaSlot(p, f"sCT{i}") for i in range(2)]
    ct_out_ev = [None, None]
    ps_evac = [q_done] * 4
    for tt in range(16):
        sl = tt % 2
        tsl = slice(tt * 128, (tt + 1) * 128)
        copy_vals = []
        for j in range(4):
            @p.block.tensor
            def _(t):
                t.wait_ge(dv.sem, ps_evac[j])
                for k in range(NCH):
                    mm = t.matmul(slots[j], xn[:, k, tsl], w5[:, k, 1024 + j * 512:1024 + (j + 1) * 512],
                                  start=(k == 0), stop=(k == NCH - 1))
                mm.then_inc(pe.sem, 1)
            mmv = pe.inc()

            @p.block.vector
            def _(v):
                v.wait_ge(pe.sem, mmv)
                if tt < 2 and j == 0:
                    v.wait_ge(p.sCN.sem, ct_pad_done)
                if j == 0 and ct_out_ev[sl] is not None:
                    v.wait_ge(ct_out_ev[sl][0], ct_out_ev[sl][1])
                v.tensor_copy(CT[:, sl, j * 512:(j + 1) * 512], slots[j]).then_inc(dv.sem, 1)
            ps_evac[j] = dv.inc()
            copy_vals.append(ps_evac[j])

        @p.block.vector
        def _(v):
            v.wait_ge(dv.sem, copy_vals[-1])
            v.tensor_mul(sq_scr[:, :], CT[:, sl, 0:1024], CT[:, sl, 0:1024]).then_inc(dv.sem, 1)
            v.wait_ge(dv.sem, dv.inc())
            with nc.allow_low_precision(reason="k-norm row for softmax bound only"):
                v.tensor_reduce(CT[:, sl, 2048:2049], sq_scr[:, :], AX.X, ALU.add).then_inc(dv.sem, 1)
            v.wait_ge(dv.sem, dv.inc())
            v.tensor_copy(normc[:, tt:tt + 1], CT[:, sl, 2048:2049]).then_inc(dv.sem, 1)
        ct_stage_done = dv.inc()

        @p.block.sync
        def _(sy):
            sy.wait_ge(dv.sem, ct_stage_done)
            ct_out_ev[sl] = ct_slot[sl].issue(sy, cin.ap()[tsl, :], CT[:, sl, :])
        if tt == 7:
            @p.block.gpsimd
            def _(g, w0=ct_slot[0].ctr.v, w1=ct_slot[1].ctr.v):
                g.wait_ge(ct_slot[0].ctr.sem, w0)
                g.wait_ge(ct_slot[1].ctr.sem, w1)
                g.collective_compute(
                    "AllGather", ALU.bypass,
                    replica_groups=[[0, 1, 2, 3], [4, 5, 6, 7]],
                    ins=[cin.ap()[0:1024, :].opt()], outs=[coutA.ap().opt()],
                ).then_inc(io.sem, 1)
            cc_h = [io.inc(), None]
    kv_proj_done = dv.v

    @p.block.tensor
    def _(t):
        t.wait_ge(dv.sem, kv_proj_done)
        t.transpose(psT_a[0:16, 0:128], normc[:, 0:16], ident[:, :]).then_inc(pe.sem, 1)
    trn1 = pe.inc()

    @p.block.vector
    def _(v):
        v.wait_ge(pe.sem, trn1)
        v.tensor_reduce(nm16f[:, :], psT_a[0:16, 0:128], AX.X, ALU.max).then_inc(dv.sem, 1)
        v.wait_ge(dv.sem, dv.inc())
        v.tensor_copy(nm16b[:, :], nm16f[:, :]).then_inc(dv.sem, 1)
    nmb_done = dv.inc()

    @p.block.tensor
    def _(t):
        t.wait_ge(dv.sem, nmb_done)
        t.transpose(psT_b[0:1, 0:16], nm16b[0:16, 0:1], ident[0:16, 0:16]).then_inc(pe.sem, 1)
    trn2 = pe.inc()

    @p.block.vector
    def _(v):
        v.wait_ge(pe.sem, trn2)
        v.wait_ge(p.sCN.sem, ct_pad_done)
        v.tensor_reduce(srow[:, 1:2], psT_b[0:1, 0:16], AX.X, ALU.max).then_inc(dv.sem, 1)
        v.wait_ge(dv.sem, dv.inc())
        v.tensor_copy(srow_b[:, 0:8], srow[:, 0:8]).then_inc(dv.sem, 1)
    maxk_done = dv.inc()

    sc_slot = DmaSlot(p, "sCsc")

    @p.block.vector
    def _(v):
        v.wait_ge(dv.sem, maxk_done)
        if ct_out_ev[0] is not None:
            v.wait_ge(ct_out_ev[0][0], ct_out_ev[0][1])
        v.memset(CT[0:1, 0, 0:2056], 0.0).then_inc(dv.sem, 1)
        v.wait_ge(dv.sem, dv.inc())
        v.tensor_copy(CT[0:1, 0, 0:8], srow_b[:, 0:8]).then_inc(dv.sem, 1)
    scrow_done = dv.inc()

    @p.block.sync
    def _(sy):
        sy.wait_ge(dv.sem, scrow_done)
        sc_slot.issue(sy, cin.ap()[2048:2049, 0:2056], CT[0:1, 0, 0:2056])

    @p.block.gpsimd
    def _(g):
        for s in ct_slot:
            g.wait_ge(s.ctr.sem, s.ctr.v)
        g.collective_compute(
            "AllGather", ALU.bypass,
            replica_groups=[[0, 1, 2, 3], [4, 5, 6, 7]],
            ins=[cin.ap()[1024:2048, :].opt()], outs=[coutB.ap().opt()],
        ).then_inc(io.sem, 1)
    cc_h[1] = io.inc()

    @p.block.gpsimd
    def _(g):
        g.wait_ge(sc_slot.ctr.sem, sc_slot.ctr.v)
        g.collective_compute(
            "AllGather", ALU.bypass,
            replica_groups=[[0, 1, 2, 3], [4, 5, 6, 7]],
            ins=[cin.ap()[2048:2049, :].opt()], outs=[coutS.ap().opt()],
        ).then_inc(io.sem, 1)
    cc_st = io.inc()
    cc_done = cc_st

    qn_prev = None
    for c in range(NCH):
        @p.block.vector
        def _(v):
            v.wait_ge(dv.sem, q_done)
            if qn_prev is not None:
                v.wait_ge(pe.sem, qn_prev)
            v.tensor_mul(qsq[:, 0, :], qfm[:, c, :], qfm[:, c, :]).then_inc(dv.sem, 1)
        sqv = dv.inc()

        @p.block.tensor
        def _(t):
            t.wait_ge(dv.sem, sqv)
            if c == 0:
                t.wait_ge(dv.sem, kv_proj_done)
            for q in range(4):
                mm = t.matmul(psBIG[0:1, q * 512:(q + 1) * 512],
                              p.ones128_bf[:, :], qsq[:, 0, q * 512:(q + 1) * 512],
                              start=(c == 0), stop=(c == NCH - 1))
            mm.then_inc(pe.sem, 1)
        qn_prev = pe.inc()

    @p.block.vector
    def _(v):
        v.wait_ge(pe.sem, qn_prev)
        v.tensor_reduce(srow[:, 0:1], psBIG[0:1, 0:2048], AX.X, ALU.max).then_inc(dv.sem, 1)
    qn_done = dv.inc()

    km_slot = DmaSlot(p, "sKms")

    @p.block.sync
    def _(sy):
        sy.wait_ge(io.sem, cc_done)
        with nc.allow_non_contiguous_dma(reason="4 scalar max-norm values"):
            km_slot.issue(sy, kms[:, :], coutS.ap()[:, 1:2].rearrange("r b -> b r"))

    @p.block.vector
    def _(v):
        v.wait_ge(km_slot.ctr.sem, km_slot.ctr.v)
        v.wait_ge(dv.sem, qn_done)
        v.tensor_reduce(srow[:, 1:2], kms[:, :], AX.X, ALU.max).then_inc(dv.sem, 1)
        v.wait_ge(dv.sem, dv.inc())
        v.tensor_mul(srow[:, 2:3], srow[:, 0:1], srow[:, 1:2]).then_inc(dv.sem, 1)
    m_mul_done = dv.inc()

    @p.block.scalar
    def _(s):
        s.wait_ge(dv.sem, m_mul_done)
        s.activation(srow[:, 3:4], srow[:, 2:3], AF.Sqrt).then_inc(ac.sem, 1)
    m_sqrt_done = ac.inc()

    @p.block.vector
    def _(v):
        v.wait_ge(ac.sem, m_sqrt_done)
        v.tensor_scalar_mul(srow[:, 4:5], srow[:, 3:4], -1.0 / 32.0).then_inc(dv.sem, 1)
    negm_done = dv.inc()

    @p.block.tensor
    def _(t):
        t.wait_ge(dv.sem, negm_done)
        t.matmul(psE[:, 0:1], p.ones1x128_f[:, :], srow[0:1, 4:5], start=True, stop=True).then_inc(pe.sem, 1)
    negmb_mm = pe.inc()

    @p.block.vector
    def _(v):
        v.wait_ge(pe.sem, negmb_mm)
        v.tensor_copy(negMb[:, :], psE[:, 0:1]).then_inc(dv.sem, 1)
    negmb_done = dv.inc()

    kin_slot = [DmaSlot(p, f"sKin{i}") for i in range(2)]
    kb_out_slot = [DmaSlot(p, f"sKBo{i}") for i in range(2)]
    kt_consumed = [None, None]
    ct2_free = [None, None]
    psT = [psT_a, psT_b]
    psT_free = [None, None]
    kin_ev = [None, None]

    kbo_ev = [None] * 64

    def emit_preproc_group(kt_lo, kt_hi):
        for kt in range(kt_lo, kt_hi):
            sl = kt % 2
            rank = kt // 16
            r0 = (kt % 16) * 128

            @p.block.sync
            def _(sy, kt=kt, sl=sl, rank=rank, r0=r0):
                sy.wait_ge(io.sem, cc_h[(kt % 16) // 8])
                if kt_consumed[sl] is not None:
                    sy.wait_ge(pe.sem, kt_consumed[sl])
                kin_ev[sl] = kin_slot[sl].issue(sy, Kt[:, sl, :], cout_v(rank, r0, slice(0, 1024)))

            tr_vals = []
            for c2 in range(NCH):
                tb = c2 % 2

                @p.block.tensor
                def _(t, sl=sl, c2=c2, tb=tb):
                    if c2 == 0:
                        t.wait_ge(kin_ev[sl][0], kin_ev[sl][1])
                    if psT_free[tb] is not None:
                        t.wait_ge(dv.sem, psT_free[tb])
                    t.transpose(psT[tb][:, :], Kt[:, sl, c2 * 128:(c2 + 1) * 128], ident[:, :]).then_inc(pe.sem, 1)
                trv = pe.inc()
                if c2 == NCH - 1:
                    kt_consumed[sl] = trv

                @p.block.vector
                def _(v, sl=sl, c2=c2, tb=tb, trv=trv):
                    v.wait_ge(pe.sem, trv)
                    if c2 == 0 and kb_out_slot[sl].ctr.v > 0:
                        v.wait_ge(kb_out_slot[sl].ctr.sem, kb_out_slot[sl].ctr.v)
                    v.tensor_copy(KBs[:, sl, c2 * 128:(c2 + 1) * 128], psT[tb][:, :]).then_inc(dv.sem, 1)
                psT_free[tb] = dv.inc()
                tr_vals.append(psT_free[tb])

            @p.block.sync
            def _(sy, kt=kt, sl=sl, last=tr_vals[-1]):
                sy.wait_ge(dv.sem, last)
                kbo_ev[kt] = kb_out_slot[sl].issue(sy, KB_v[kt, :, 0:1024], KBs[:, sl, :])

    emit_preproc_group(0, 8)

    S_free = [negmb_done, negmb_done]
    S_exp = [None, None]
    praw_free = [None, None]
    O_free = [None, None]
    l_free = negmb_done
    rlb_free = None
    kt2_consumed = [None, None]
    vg_slot = [DmaSlot(p, f"sVg{i}") for i in range(2)]
    kin2_slot = [DmaSlot(p, f"sKin2{i}") for i in range(2)]
    vg_consumed = [None, None]
    kin2_ev = [None, None]
    vg_ev = [None, None]
    B_last = None

    for st in range(4):
        qsl = slice(st * 512, (st + 1) * 512)
        ktmin = 4 * st
        strip_first_grp = True
        for g in range(8):
            gsl = g % 2
            keep = [i for i in range(8) if g * 8 + i >= ktmin]
            if not keep:
                continue
            nk = len(keep)
            if st == 0 and g < 7:
                emit_preproc_group(8 * (g + 1), 8 * (g + 2))

            @p.block.gpsimd
            def _(gq, keep=keep):
                if st == 0 and g < 2:
                    gq.wait_ge(io.sem, cc_done)
                vs = vg_slot[gsl].ctr
                if vg_consumed[gsl] is not None:
                    gq.wait_ge(pe.sem, vg_consumed[gsl])
                    gq.wait_ge(vs.sem, vs.v)
                for i in keep:
                    kt = g * 8 + i
                    rank = kt // 16
                    r0 = (kt % 16) * 128
                    gq.wait_ge(io.sem, cc_h[(kt % 16) // 8])
                    gq.dma_start(out=Vg[:, gsl, i, :],
                                 in_=cout_v(rank, r0, slice(1024, 2048))).then_inc(vs.sem, 16)
                    vs.inc(16)
                vg_ev[gsl] = (vs.sem, vs.v)

            stt_vals = {}
            pe0 = pe.v
            ac0 = ac.v
            dv0 = dv.v
            carry_praw = list(praw_free)
            first_grp = (st == 0 and g == 0)

            qk_val5 = [pe0 + (1 if pos == 0 else 2 * pos) for pos in range(nk)]

            @p.block.scalar
            def _(s, keep=keep, qk_val5=qk_val5, dv0=dv0, carry=carry_praw, first=first_grp, g=g):
                for pos, i in enumerate(keep):
                    sb = (g * 8 + i) % 2
                    s.wait_ge(pe.sem, qk_val5[pos])
                    if first and pos == 0:
                        s.wait_ge(dv.sem, negmb_done)
                    if pos >= 2:
                        s.wait_ge(dv.sem, dv0 + pos - 1)
                    elif carry[sb] is not None:
                        s.wait_ge(dv.sem, carry[sb])
                    s.activation(Praw[:, sb, :], S_bank[sb], AF.Exp, bias=negMb[:, 0:1],
                                 scale=co32[:, 0:1]).then_inc(ac.sem, 1)
            ac.inc(nk)

            def emit_lmm5(j, g=g, ktmin=ktmin):
                kt_j = g * 8 + j

                @p.block.tensor
                def _(t):
                    t.wait_ge(dv.sem, stt_vals[j])
                    if kt_j == ktmin:
                        t.wait_ge(dv.sem, l_free)
                    t.matmul(l_bank, p.ones128_bf[:, :], Pg[:, j, :],
                             start=(kt_j == ktmin), stop=(kt_j == 63), skip_group_check=True).then_inc(pe.sem, 1)
                pe.inc()

            for pos, i in enumerate(keep):
                kt = g * 8 + i
                sl = kt % 2
                rank = kt // 16
                r0 = (kt % 16) * 128

                @p.block.sync
                def _(sy):
                    if kt2_consumed[sl] is not None:
                        sy.wait_ge(pe.sem, kt2_consumed[sl])
                    if st == 0:
                        sy.wait_ge(kbo_ev[kt][0], kbo_ev[kt][1])
                    kin2_ev[sl] = kin2_slot[sl].issue(sy, Kt2[:, sl, :], KB_v[kt, :, 0:1024])

                sb = kt % 2

                @p.block.tensor
                def _(t):
                    t.wait_ge(kin2_ev[sl][0], kin2_ev[sl][1])
                    if st == 0 and g == 0 and i == 0:
                        t.wait_ge(dv.sem, negmb_done)
                    if S_exp[sb] is not None:
                        t.wait_ge(ac.sem, S_exp[sb])
                    for c in range(NCH):
                        mm = t.matmul(S_bank[sb], Kt2[:, sl, c * 128:(c + 1) * 128], qfm[:, c, qsl],
                                      start=(c == 0), stop=(c == NCH - 1))
                    mm.then_inc(pe.sem, 1)
                qk_v = pe.inc()
                kt2_consumed[sl] = qk_v
                assert qk_v == qk_val5[pos]

                exp_v = ac0 + pos + 1
                S_exp[sb] = exp_v

                @p.block.vector
                def _(v):
                    v.wait_ge(ac.sem, exp_v)
                    if B_last is not None and pos == 0:
                        v.wait_ge(pe.sem, B_last)
                    v.scalar_tensor_tensor(Pg[:, i, :], pj_sb[:, :], cbc_sb[:, st * 64 + kt:st * 64 + kt + 1],
                                           Praw[:, sb, :], ALU.is_ge, ALU.mult).then_inc(dv.sem, 1)
                praw_free[sb] = dv.inc()
                assert praw_free[sb] == dv0 + pos + 1
                stt_vals[i] = praw_free[sb]

                if pos >= 1:
                    emit_lmm5(keep[pos - 1])
            emit_lmm5(keep[-1])

            for c in range(NCH):
                ob = c % 2

                @p.block.tensor
                def _(t, keep=keep):
                    if c == 0:
                        t.wait_ge(vg_ev[gsl][0], vg_ev[gsl][1])
                    if O_free[ob] is not None:
                        t.wait_ge(dv.sem, O_free[ob])
                    for pos, i in enumerate(keep):
                        mm = t.matmul(O_bank[ob], Vg[:, gsl, i, c * 128:(c + 1) * 128], Pg[:, i, :],
                                      start=(pos == 0), stop=(pos == len(keep) - 1), skip_group_check=True)
                    mm.then_inc(pe.sem, 1)
                pv_v = pe.inc()
                if c == NCH - 1:
                    vg_consumed[gsl] = pv_v
                    B_last = pv_v

                @p.block.vector
                def _(v, strip_first_grp=strip_first_grp):
                    v.wait_ge(pe.sem, pv_v)
                    if strip_first_grp:
                        v.tensor_copy(O_sb[:, c, :], O_bank[ob]).then_inc(dv.sem, 1)
                    else:
                        v.tensor_add(O_sb[:, c, :], O_sb[:, c, :], O_bank[ob]).then_inc(dv.sem, 1)
                O_free[ob] = dv.inc()
            strip_first_grp = False

        @p.block.vector
        def _(v):
            v.wait_ge(pe.sem, B_last)
            v.reciprocal(rl[:, :], l_bank).then_inc(dv.sem, 1)
        l_free = dv.inc()

        @p.block.tensor
        def _(t):
            t.wait_ge(dv.sem, l_free)
            if rlb_free is not None:
                t.wait_ge(dv.sem, rlb_free)
            t.matmul(rlb_bank, p.ones1x128_f[:, :], rl[0:1, :], start=True, stop=True).then_inc(pe.sem, 1)
        rlb_mm = pe.inc()

        fin_vals = []
        for c in range(NCH):
            @p.block.vector
            def _(v):
                v.wait_ge(pe.sem, rlb_mm)
                v.wait_ge(dv.sem, dv.v)
                v.tensor_mul(tmp_f32[:, 0:512], O_sb[:, c, :], rlb_bank).then_inc(dv.sem, 1)
                v.wait_ge(dv.sem, dv.inc())
                v.tensor_add(xres[:, c, qsl], xres[:, c, qsl], tmp_f32[:, 0:512]).then_inc(dv.sem, 1)
            fin_vals.append(dv.inc())
        rlb_free = fin_vals[-1]

    return emit_store_x(p, xres, xo_d, wait_dv=dv.v)




def sub_att7(p, x_d, xo_d, after=None):
    nc = p.nc
    g7_d = p.in_ap("g7", [128, NCH], F32)
    be7_d = p.in_ap("be7", [128, NCH], F32)
    Wqkv_d = p.in_ap("Wqkv7", [D, 3 * D], F32)
    Wout_d = p.in_ap("Wout7", [D, D], F32)
    tupp_d = p.in_ap("t_upp", [128, 128], BF16)
    tlow_d = p.in_ap("t_low", [128, 128], BF16)
    mask7_d = p.in_ap("mask7", [128, 128], BF16)
    ident_d = p.in_ap("ident", [128, 128], BF16)

    cKb = p.dram("cKb7", [D, 128], BF16)
    gKb = p.dram("gKb7", [GRP * D, 128], BF16)
    gKb_v = gKb.ap().rearrange("(r k p) j -> r p k j", r=GRP, p=128)
    cVb = p.dram("cVb7", [128, 1040], BF16)
    gVb = p.dram("gVb7", [GRP * 128, 1040], BF16)
    gVb_v = gVb.ap().rearrange("(r p) j -> r p j", r=GRP)
    cSb = p.dram("cSb7", [2, 8], BF16)
    gSb = p.dram("gSb7", [GRP * 2, 8], BF16)
    stb = p.dram("stb7", [2, 24], F32)

    xch = p.sbuf("xch", [128, 2, TLOC], F32)
    xnscr = p.sbuf("xnscr", [128, 2, TLOC], BF16)
    xr_b = p.xres.ap().rearrange("p c t -> p (c t)").bitcast(BF16)
    qfm = xr_b[:, 0:16384].rearrange("p (c t) -> p c t", c=NCH)
    kfm = xr_b[:, 16384:32768].rearrange("p (c t) -> p c t", c=NCH)
    V7t = p.sbuf("V7t", [128, 17, 1040], BF16)
    V7 = V7t.ap().rearrange("p w (pr two c) -> p w pr two c", pr=NCH, two=2)
    arenaO = p.sbuf("arenaO", [128, NCH * TLOC // 1], BF16)
    xn = arenaO.ap().rearrange("p (c t) -> p c t", c=NCH)
    O7 = arenaO.ap().rearrange("p (c t) -> p c t", c=NCH)
    wbuf = p.sbuf("wbuf", [128, NCH, 2048], BF16)
    g7_sb = p.sbuf("g7_sb", [128, NCH], F32)
    be7_sb = p.sbuf("be7_sb", [128, NCH], F32)
    kb7 = p.sbuf("kb7", [128, NCH, 128], BF16)
    P7 = p.sbuf("P7", [128, 4, 128], BF16)
    Praw7 = p.sbuf("Praw7", [128, 4, 128], BF16)
    tupp = p.sbuf("tupp_sb", [128, 128], BF16)
    tlow = p.sbuf("tlow_sb", [128, 128], BF16)
    mask7 = p.sbuf("mask7_sb", [128, 128], BF16)
    ident = p.sbuf("ident_sb", [128, 128], BF16)
    stats = p.sbuf("stats", [2, 24], F32)
    statrow = p.sbuf("statrow", [1, 48], F32)
    stb_b = p.sbuf("stb_b", [2, 24], BF16)
    srow7 = p.sbuf("srow7", [1, 32], F32)
    negM7b = p.sbuf("negM7b", [128, 16], F32)
    co8 = p.sbuf("co8", [128, 1], F32)
    lrow65 = p.sbuf("lrow65", [65, TLOC], F32)
    lrow = lrow65.ap()[0:1, :]
    ones_b64 = p.sbuf("ones_b64", [65, 64], F32)
    co_n1 = p.sbuf("co_n1", [65, 1], F32)
    onespat = p.sbuf("onespat", [128, 2], BF16)
    Ofp = None
    sqs = p.sbuf("sqs", [128, TLOC], BF16)
    p.ones128_bf = p.sbuf("ones128_bf", [128, 1], BF16)
    p.ones1x128_f = p.sbuf("ones1x128_f", [1, 128], F32)
    tmp_f32 = sqs.ap().bitcast(F32)
    v7f = V7t.ap().rearrange("p w c -> p (w c)").bitcast(F32)
    p.r_m = v7f[0:1, 0:2048]
    p.r_a = v7f[0:1, 2048:4096]
    p.r_b = v7f[0:1, 4096:6144]
    Ofp = xch.ap()[0:64, 0, :]

    p.psBIG = p.psum("psBIG", [128, 2048], F32)
    psF = p.psum("psF", [128, 2048], F32)
    psBIG = p.psBIG

    pe, dv, ac, io = p.sPE, p.sDV, p.sAC, p.sIO

    @p.block.gpsimd
    def _(g):
        g.memset(p.ones128_bf[:, :], 1.0)
        g.memset(p.ones1x128_f[:, :], 1.0)
        g.memset(co8[:, :], 0.125)
        g.memset(ones_b64[:, :], 1.0)
        g.memset(co_n1[:, :], -1.0)
        g.memset(onespat[:, :], 0.0).then_inc(p.sCN.sem, 1)
        g.wait_ge(p.sCN.sem, p.sCN.inc())
        g.memset(onespat[0:64, 0:1], 1.0)
        g.memset(onespat[64:128, 1:2], 1.0).then_inc(p.sCN.sem, 1)
    cn_done = p.sCN.inc()

    swt = p.dma_sem("sWts")

    @p.block.gpsimd
    def _(g):
        g.dma_start(out=wbuf[:, :, :], in_=Wqkv_d[:, 0:2048].rearrange("(k p) j -> p k j", p=128)).then_inc(swt.sem, 16)
        g.dma_start(out=g7_sb[:, :], in_=g7_d).then_inc(swt.sem, 16)
        g.dma_start(out=be7_sb[:, :], in_=be7_d).then_inc(swt.sem, 16)
        g.dma_start(out=tupp[:, :], in_=tupp_d).then_inc(swt.sem, 16)
        g.dma_start(out=tlow[:, :], in_=tlow_d).then_inc(swt.sem, 16)
        g.dma_start(out=mask7[:, :], in_=mask7_d).then_inc(swt.sem, 16)
        g.dma_start(out=ident[:, :], in_=ident_d).then_inc(swt.sem, 16)
    wqk_ready = (swt.sem, swt.inc(112))

    xsl = [DmaSlot(p, f"sXc{i}") for i in range(2)]
    x_ev = [None, None]
    cast_done = [None, None]
    stat_prev = [None, None]
    stat_mm = None
    for c in range(NCH):
        sl = c % 2

        @p.block.sync
        def _(sy):
            if c == 0 and after is not None:
                sy.wait_ge(after[0], after[1])
            if cast_done[sl] is not None:
                sy.wait_ge(dv.sem, cast_done[sl])
            x_ev[sl] = xsl[sl].issue(sy, xch[:, sl, :], x_d[bass.ds(c * 128, 128), :])

        @p.block.vector
        def _(v):
            v.wait_ge(x_ev[sl][0], x_ev[sl][1])
            v.tensor_copy(xn[:, c, :], xch[:, sl, :]).then_inc(dv.sem, 1)
            cast_done[sl] = dv.inc()
            v.wait_ge(dv.sem, cast_done[sl])
            if stat_prev[sl] is not None:
                v.wait_ge(pe.sem, stat_prev[sl])
            v.tensor_mul(xnscr[:, sl, :], xn[:, c, :], xn[:, c, :]).then_inc(dv.sem, 1)
        sq_done = dv.inc()

        @p.block.tensor
        def _(t):
            t.wait_ge(dv.sem, sq_done)
            if c == 0:
                t.wait_ge(p.sCN.sem, cn_done)
            for q in range(4):
                t.matmul(psBIG[0:1, q * 512:(q + 1) * 512], p.ones128_bf[:, :],
                         xn[:, c, q * 512:(q + 1) * 512], start=(c == 0), stop=(c == NCH - 1),
                         skip_group_check=True)
            for q in range(4):
                mm = t.matmul(psF[0:1, q * 512:(q + 1) * 512], p.ones128_bf[:, :],
                              xnscr[:, sl, q * 512:(q + 1) * 512], start=(c == 0), stop=(c == NCH - 1),
                              skip_group_check=True)
            mm.then_inc(pe.sem, 1)
        stat_mm = pe.inc()
        stat_prev[sl] = stat_mm

    @p.block.vector
    def _(v):
        v.wait_ge(pe.sem, stat_mm)
        v.tensor_scalar_mul(p.r_m[:, :], psBIG[0:1, 0:2048], 1.0 / D).then_inc(dv.sem, 1)
        v.tensor_scalar_mul(p.r_a[:, :], psF[0:1, 0:2048], 1.0 / D).then_inc(dv.sem, 1)
        v.wait_ge(dv.sem, dv.inc(2))
        v.tensor_mul(p.r_b[:, :], p.r_m[:, :], p.r_m[:, :]).then_inc(dv.sem, 1)
        v.wait_ge(dv.sem, dv.inc())
        v.tensor_sub(p.r_a[:, :], p.r_a[:, :], p.r_b[:, :]).then_inc(dv.sem, 1)
        v.wait_ge(dv.sem, dv.inc())
        v.tensor_scalar_add(p.r_a[:, :], p.r_a[:, :], EPS).then_inc(dv.sem, 1)
    vpe_done = dv.inc()

    @p.block.scalar
    def _(s):
        s.wait_ge(dv.sem, vpe_done)
        s.activation(p.r_b[:, :], p.r_a[:, :], AF.Sqrt).then_inc(ac.sem, 1)
    sq1 = ac.inc()

    @p.block.vector
    def _(v):
        v.wait_ge(ac.sem, sq1)
        v.reciprocal(p.r_a[:, :], p.r_b[:, :]).then_inc(dv.sem, 1)
        v.wait_ge(dv.sem, dv.inc())
        v.tensor_mul(p.r_b[:, :], p.r_a[:, :], p.r_a[:, :]).then_inc(dv.sem, 1)
        v.wait_ge(dv.sem, dv.inc())
        v.tensor_scalar(p.r_b[:, :], p.r_b[:, :], -EPS, 1.0 + EPS, ALU.mult, ALU.add).then_inc(dv.sem, 1)
    vt_done = dv.inc()

    @p.block.scalar
    def _(s):
        s.wait_ge(dv.sem, vt_done)
        s.activation(p.r_b[:, :], p.r_b[:, :], AF.Sqrt).then_inc(ac.sem, 1)
    sq2 = ac.inc()

    @p.block.vector
    def _(v):
        v.wait_ge(ac.sem, sq2)
        v.reciprocal(lrow[:, :], p.r_b[:, :]).then_inc(dv.sem, 1)
        v.wait_ge(dv.sem, dv.inc())
        v.tensor_mul(p.r_a[:, :], p.r_a[:, :], lrow[:, :]).then_inc(dv.sem, 1)
    shat_done = dv.inc()

    psum_free = shat_done
    for h in range(2):
        tok = slice(h * 1024, (h + 1) * 1024)

        @p.block.tensor
        def _(t):
            t.wait_ge(dv.sem, psum_free)
            for q in range(2):
                sl2 = slice(h * 1024 + q * 512, h * 1024 + (q + 1) * 512)
                t.matmul(psBIG[:, q * 512:(q + 1) * 512], p.ones1x128_f[:, :], p.r_m[0:1, sl2],
                         start=True, stop=True)
                mm = t.matmul(psBIG[:, 1024 + q * 512:1024 + (q + 1) * 512],
                              p.ones1x128_f[:, :], p.r_a[0:1, sl2], start=True, stop=True)
            mm.then_inc(pe.sem, 1)
        bc_done = pe.inc()

        @p.block.vector
        def _(v):
            v.wait_ge(pe.sem, bc_done)
            if h == 0:
                v.wait_ge(wqk_ready[0], wqk_ready[1])
            for c in range(NCH):
                v.tensor_sub(tmp_f32[:, 0:1024], xn[:, c, tok], psBIG[:, 0:1024]).then_inc(dv.sem, 1)
                v.wait_ge(dv.sem, dv.inc())
                v.tensor_mul(tmp_f32[:, 0:1024], tmp_f32[:, 0:1024], psBIG[:, 1024:2048]).then_inc(dv.sem, 1)
                v.wait_ge(dv.sem, dv.inc())
                v.tensor_scalar(xn[:, c, tok], tmp_f32[:, 0:1024], g7_sb[:, c:c + 1], be7_sb[:, c:c + 1],
                                ALU.mult, ALU.add).then_inc(dv.sem, 1)
                v.wait_ge(dv.sem, dv.inc())
        psum_free = dv.v
    u_ready = dv.v

    @p.block.gpsimd
    def _(g):
        g.wait_ge(dv.sem, u_ready)
        g.memset(V7t[:, :, :], 0.0).then_inc(p.sCN.sem, 1)
        g.wait_ge(p.sCN.sem, p.sCN.inc())
        g.memset(V7[:, :, :, :, 64:65], 1.0).then_inc(p.sCN.sem, 1)
    cn2_done = p.sCN.inc()

    slots = [psBIG[:, i * 512:(i + 1) * 512] for i in range(4)]

    def cb_q(m, q, ps, mmv):
        @p.block.vector
        def _(v):
            v.wait_ge(pe.sem, mmv)
            v.tensor_copy(qfm[:, m, q * 512:(q + 1) * 512], ps).then_inc(dv.sem, 1)
        return dv.inc()

    emit_mm_dense(p, wbuf.ap()[:, :, 0:1024].rearrange("p k j -> p k j"), xn, NCH, cb_q,
                  wait_w=wqk_ready, wait_dv=u_ready, psum_rot=slots)

    def cb_k(m, q, ps, mmv):
        @p.block.vector
        def _(v):
            v.wait_ge(pe.sem, mmv)
            v.tensor_copy(kfm[:, m, q * 512:(q + 1) * 512], ps).then_inc(dv.sem, 1)
        return dv.inc()

    emit_mm_dense(p, wbuf.ap()[:, :, 1024:2048].rearrange("p k j -> p k j"), xn, NCH, cb_k,
                  wait_dv=dv.v, psum_rot=slots)
    qk_done = dv.v

    @p.block.gpsimd
    def _(g):
        g.wait_ge(pe.sem, pe.v)
        g.dma_start(out=wbuf[:, :, 0:1024],
                    in_=Wqkv_d[:, 2048:3072].rearrange("(k p) j -> p k j", p=128)).then_inc(swt.sem, 16)
    wv_ready = (swt.sem, swt.inc(16))

    vslot_ev = [None] * 4
    for w in range(16):
        for pp in range(NCH):
            j = (w * NCH + pp) % 4

            @p.block.tensor
            def _(t):
                if w == 0 and pp == 0:
                    t.wait_ge(wv_ready[0], wv_ready[1])
                    t.wait_ge(dv.sem, qk_done)
                if vslot_ev[j] is not None:
                    t.wait_ge(dv.sem, vslot_ev[j])
                for k in range(NCH):
                    mm = t.matmul(slots[j][:, 0:128], xn[:, k, w * 128:(w + 1) * 128],
                                  wbuf[:, k, pp * 128:(pp + 1) * 128],
                                  start=(k == 0), stop=(k == NCH - 1))
                mm.then_inc(pe.sem, 1)
            mmv = pe.inc()

            @p.block.vector
            def _(v):
                v.wait_ge(pe.sem, mmv)
                if w == 0 and pp == 0:
                    v.wait_ge(p.sCN.sem, cn2_done)
                v.tensor_copy(V7[:, w + 1, pp, :, 0:64],
                              slots[j][:, 0:128].rearrange("p (two c) -> p two c", two=2)).then_inc(dv.sem, 1)
            vslot_ev[j] = dv.inc()
    vprime_done = dv.v

    hm_prev = [None]

    def emit_headmax(src_fm, col0, tokens_sl, extra_cols):
        for c in range(NCH):
            @p.block.vector
            def _(v):
                v.wait_ge(dv.sem, dv.v)
                if c == 0:
                    v.wait_ge(dv.sem, vprime_done)
                if hm_prev[0] is not None:
                    v.wait_ge(pe.sem, hm_prev[0])
                v.tensor_mul(sqs[:, 0:2048], src_fm[:, c, :], src_fm[:, c, :]).then_inc(dv.sem, 1)
            sqv = dv.inc()

            @p.block.tensor
            def _(t):
                t.wait_ge(dv.sem, sqv)
                for q in range(4):
                    mm = t.matmul(psF[0:2, q * 512:(q + 1) * 512], onespat[:, :],
                                  sqs[:, q * 512:(q + 1) * 512], start=True, stop=True)
                mm.then_inc(pe.sem, 1)
            nmv = pe.inc()
            hm_prev[0] = nmv

            @p.block.vector
            def _(v):
                v.wait_ge(pe.sem, nmv)
                v.tensor_reduce(stats[0:2, col0 + c:col0 + c + 1], psF[0:2, tokens_sl], AX.X, ALU.max).then_inc(dv.sem, 1)
                if extra_cols is not None:
                    v.wait_ge(dv.sem, dv.inc())
                    v.tensor_reduce(stats[0:2, extra_cols + c:extra_cols + c + 1],
                                    psF[0:2, 1920:2048], AX.X, ALU.max).then_inc(dv.sem, 1)
                    dv.inc()
                else:
                    dv.inc()

    emit_headmax(kfm, 0, slice(0, 2048), 16)
    emit_headmax(qfm, 8, slice(0, 2048), None)
    stats_done = dv.v

    @p.block.vector
    def _(v):
        v.wait_ge(dv.sem, stats_done)
        v.tensor_copy(stb_b[:, :], stats[:, :]).then_inc(dv.sem, 1)
    stb_done = dv.inc()

    bnd = DmaSlot(p, "sBnd")

    @p.block.sync
    def _(sy):
        sy.wait_ge(dv.sem, stb_done)
        for c in range(NCH):
            sy.dma_start(out=cKb.ap().rearrange("(k p) j -> p k j", p=128)[:, c, :],
                         in_=kfm[:, c, 1920:2048]).then_inc(bnd.ctr.sem, 16)
            bnd.ctr.inc(16)
        sy.dma_start(out=cVb.ap(), in_=V7t[:, 16, :]).then_inc(bnd.ctr.sem, 16)
        sy.dma_start(out=cSb.ap(), in_=stb_b[:, 0:8]).then_inc(bnd.ctr.sem, 16)
        bnd.ctr.inc(32)

    @p.block.gpsimd
    def _(g):
        g.wait_ge(bnd.ctr.sem, bnd.ctr.v)
        g.collective_compute("AllGather", ALU.bypass, replica_groups=[[0, 1, 2, 3], [4, 5, 6, 7]],
                             ins=[cKb.ap().opt()], outs=[gKb.ap().opt()]).then_inc(io.sem, 1)
        g.collective_compute("AllGather", ALU.bypass, replica_groups=[[0, 1, 2, 3], [4, 5, 6, 7]],
                             ins=[cVb.ap().opt()], outs=[gVb.ap().opt()]).then_inc(io.sem, 1)
        g.collective_compute("AllGather", ALU.bypass, replica_groups=[[0, 1, 2, 3], [4, 5, 6, 7]],
                             ins=[cSb.ap().opt()], outs=[gSb.ap().opt()]).then_inc(io.sem, 1)
    cc_done = io.inc(3)

    bin_ = DmaSlot(p, "sBin")

    @p.block.sync
    def _(sy):
        sy.wait_ge(io.sem, cc_done)
        pid = nc.partition_id()
        prev = (pid + 3) % 4
        prev4 = (pid % 4 + 3) % 4
        sy.dma_start(out=kb7[:, :, :], in_=gKb_v[prev4]).then_inc(bin_.ctr.sem, 16)
        sy.dma_start(out=V7t[:, 0, :], in_=gVb_v[prev4]).then_inc(bin_.ctr.sem, 16)
        bin_.ctr.inc(32)

    stb_slot = DmaSlot(p, "sStb")

    @p.block.sync
    def _(sy):
        sy.wait_ge(dv.sem, stb_done)
        sy.dma_start(out=stb.ap(), in_=stats[:, :]).then_inc(stb_slot.ctr.sem, 16)
        stb_slot.ctr.inc(16)
        sy.wait_ge(stb_slot.ctr.sem, stb_slot.ctr.v)
        sy.dma_start(out=statrow[0:1, 0:48], in_=stb.ap().rearrange("a b -> (a b)")).then_inc(stb_slot.ctr.sem, 16)
        stb_slot.ctr.inc(16)

    prevst = DmaSlot(p, "sPrevSt")

    @p.block.gpsimd
    def _(gq):
        gq.wait_ge(io.sem, cc_done)
        pid = nc.partition_id()
        prev4 = (pid % 4 + 3) % 4
        gq.dma_start(out=lrow[0:1, 0:16],
                     in_=gSb.ap().rearrange("(r a) b -> r (a b)", r=GRP)[bass.ds(prev4, 1)]).then_inc(prevst.ctr.sem, 16)
        prevst.ctr.inc(16)

    @p.block.vector
    def _(v):
        v.wait_ge(stb_slot.ctr.sem, stb_slot.ctr.v)
        v.wait_ge(prevst.ctr.sem, prevst.ctr.v)
        stv = statrow.ap()
        lv = lrow
        v.tensor_max(srow7[0:1, 0:8], stv[0:1, 0:8], lv[0:1, 0:8])
        v.tensor_max(srow7[0:1, 8:16], stv[0:1, 24:32], lv[0:1, 8:16]).then_inc(dv.sem, 1)
        v.wait_ge(dv.sem, dv.inc())
        v.tensor_mul(srow7[0:1, 0:8], srow7[0:1, 0:8], stv[0:1, 8:16])
        v.tensor_mul(srow7[0:1, 8:16], srow7[0:1, 8:16], stv[0:1, 32:40]).then_inc(dv.sem, 1)
    prod_done = dv.inc()

    @p.block.scalar
    def _(s):
        s.wait_ge(dv.sem, prod_done)
        s.activation(srow7[0:1, 16:32], srow7[0:1, 0:16], AF.Sqrt).then_inc(ac.sem, 1)
    msq = ac.inc()

    @p.block.vector
    def _(v):
        v.wait_ge(ac.sem, msq)
        v.tensor_scalar_mul(srow7[0:1, 0:16], srow7[0:1, 16:32], -0.125).then_inc(dv.sem, 1)
    negm_done = dv.inc()

    @p.block.tensor
    def _(t):
        t.wait_ge(dv.sem, negm_done)
        t.matmul(psF[:, 0:16], p.ones1x128_f[:, :], srow7[0:1, 0:16], start=True, stop=True).then_inc(pe.sem, 1)
    nbm = pe.inc()

    @p.block.vector
    def _(v):
        v.wait_ge(pe.sem, nbm)
        v.tensor_copy(negM7b[:, :], psF[:, 0:16]).then_inc(dv.sem, 1)
    negmb_done = dv.inc()

    Sslot = [psF[:, 0:128], psF[:, 512:640], psF[:, 1024:1152], psF[:, 1536:1664]]
    s_exp = [None] * 4
    p_used = [None] * 4
    o_evac = negmb_done
    prev_head = [None]
    for pp in range(NCH):
        for hh in range(2):
            head_col = hh * 8 + pp
            hsl = slice(hh * 64, hh * 64 + 64)
            pending = prev_head[0]

            pe0 = pe.v
            ac0 = ac.v
            dv0 = dv.v
            carry_pused = list(p_used)
            qk_val = [None] * 32
            pv_val = [None] * 32
            cnt = pe0
            for i in range(32):
                cnt += 1
                qk_val[i] = cnt
                if i == 0 and pending is not None:
                    cnt += 1
                    rbv_val = cnt
                if i >= 2:
                    cnt += 1
                    pv_val[i - 2] = cnt
            pv_val[30] = cnt + 1
            pv_val[31] = cnt + 2
            pv_flush = pv_val[31]
            if pending is not None:
                mul_val = dv0 + 1
                msk_val = [dv0 + i + 2 for i in range(32)]
                o_evac = mul_val
            else:
                msk_val = [dv0 + i + 1 for i in range(32)]
            acr = 2 if pending is not None else 0
            exp_val = [ac0 + acr + i + 1 for i in range(32)]
            recip_val = ac0 + 2 if pending is not None else None

            @p.block.scalar
            def _(s, pending=pending, qk_val=qk_val, pv_val=pv_val,
                  carry=carry_pused, head_col=head_col):
                if pending is not None:
                    s.wait_ge(pe.sem, pending["pv_flush"])
                    s.activation(lrow65[64:65, 0:TLOC], psBIG[64:65, 0:TLOC],
                                 AF.Ln).then_inc(ac.sem, 1)
                    s.wait_ge(ac.sem, ac.v + 1)
                    s.activation(lrow65[64:65, 0:TLOC], lrow65[64:65, 0:TLOC],
                                 AF.Exp, scale=co_n1[64:65, 0:1]).then_inc(ac.sem, 1)
                for i in range(32):
                    si = i % 4
                    s.wait_ge(pe.sem, qk_val[i])
                    if i >= 4:
                        s.wait_ge(pe.sem, pv_val[i - 4])
                    elif carry[si] is not None:
                        s.wait_ge(pe.sem, carry[si])
                    s.activation(Praw7[:, si, :], Sslot[si], AF.Exp,
                                 bias=negM7b[:, head_col:head_col + 1],
                                 scale=co8[:, 0:1]).then_inc(ac.sem, 1)
            ac.inc(32 + acr)

            def emit_pv7(j, pp=pp, hh=hh, msk_val=msk_val, pv_val=pv_val,
                         oe0=o_evac):
                w_, pv_ = j // 2, j % 2
                si = j % 4

                @p.block.tensor
                def _(t):
                    t.wait_ge(dv.sem, msk_val[j])
                    if j == 0:
                        t.wait_ge(dv.sem, oe0)
                    t.matmul(psBIG[0:65, w_ * 128:(w_ + 1) * 128],
                             V7[:, w_ + pv_, pp, hh, :], P7[:, si, :],
                             start=(pv_ == 0), stop=(pv_ == 1), skip_group_check=True).then_inc(pe.sem, 1)
                p_used[si] = pe.inc()
                assert p_used[si] == pv_val[j]

            for w in range(16):
                for pv in range(2):
                    i = w * 2 + pv
                    si = i % 4

                    @p.block.tensor
                    def _(t):
                        if pp == 0 and hh == 0 and w == 0 and pv == 0:
                            t.wait_ge(bin_.ctr.sem, bin_.ctr.v)
                            t.wait_ge(dv.sem, negmb_done)
                        if s_exp[si] is not None:
                            t.wait_ge(ac.sem, s_exp[si])
                        if pv == 0 and w == 0:
                            lhs = kb7[hsl, pp, :]
                        elif pv == 0:
                            lhs = kfm[hsl, pp, (w - 1) * 128:w * 128]
                        else:
                            lhs = kfm[hsl, pp, w * 128:(w + 1) * 128]
                        t.matmul(Sslot[si], lhs, qfm[hsl, pp, w * 128:(w + 1) * 128],
                                 start=True, stop=True).then_inc(pe.sem, 1)
                    qkv_ = pe.inc()
                    assert qkv_ == qk_val[i]

                    if i == 0 and pending is not None:
                        @p.block.tensor
                        def _(t, pending=pending, recip_val=recip_val):
                            t.wait_ge(ac.sem, recip_val)
                            t.wait_ge(dv.sem, pending["copy_v"])
                            for q in range(4):
                                mm = t.matmul(psBIG[0:64, q * 512:(q + 1) * 512],
                                              ones_b64[64:65, 0:64],
                                              lrow65[64:65, q * 512:(q + 1) * 512],
                                              start=True, stop=True)
                            mm.then_inc(pe.sem, 1)
                        rv = pe.inc()
                        assert rv == rbv_val

                        @p.block.vector
                        def _(v, pending=pending, rv=rv):
                            v.wait_ge(pe.sem, rv)
                            v.tensor_mul(O7[pending["hsl"], pending["pp"], :], Ofp[:, :],
                                         psBIG[0:64, 0:2048]).then_inc(dv.sem, 1)
                        mv = dv.inc()
                        assert mv == mul_val

                    expv = exp_val[i]
                    s_exp[si] = expv

                    @p.block.vector
                    def _(v):
                        v.wait_ge(ac.sem, expv)
                        msk = tupp if pv == 1 else (mask7 if w == 0 else tlow)
                        v.tensor_mul(P7[:, si, :], Praw7[:, si, :], msk[:, :]).then_inc(dv.sem, 1)
                    mskv = dv.inc()
                    assert mskv == msk_val[i]

                    if i >= 2:
                        emit_pv7(i - 2)
            emit_pv7(30)
            emit_pv7(31)

            @p.block.vector
            def _(v, pv_flush=pv_flush):
                v.wait_ge(pe.sem, pv_flush)
                v.tensor_copy(Ofp[:, :], psBIG[0:64, 0:2048]).then_inc(dv.sem, 1)
            copy_v = dv.inc()
            prev_head[0] = {"hsl": hsl, "pp": pp, "pv_flush": pv_flush,
                            "copy_v": copy_v}

    pending = prev_head[0]

    @p.block.scalar
    def _(s, pending=pending):
        s.wait_ge(pe.sem, pending["pv_flush"])
        s.activation(lrow65[64:65, 0:TLOC], psBIG[64:65, 0:TLOC],
                     AF.Ln).then_inc(ac.sem, 1)
        s.wait_ge(ac.sem, ac.v + 1)
        s.activation(lrow65[64:65, 0:TLOC], lrow65[64:65, 0:TLOC],
                     AF.Exp, scale=co_n1[64:65, 0:1]).then_inc(ac.sem, 1)
    ac.inc()
    recip_val = ac.inc()

    @p.block.tensor
    def _(t):
        t.wait_ge(ac.sem, recip_val)
        t.wait_ge(dv.sem, pending["copy_v"])
        for q in range(4):
            mm = t.matmul(psBIG[0:64, q * 512:(q + 1) * 512], ones_b64[64:65, 0:64],
                          lrow65[64:65, q * 512:(q + 1) * 512], start=True, stop=True)
        mm.then_inc(pe.sem, 1)
    rv = pe.inc()

    @p.block.vector
    def _(v):
        v.wait_ge(pe.sem, rv)
        v.tensor_mul(O7[pending["hsl"], pending["pp"], :], Ofp[:, :],
                     psBIG[0:64, 0:2048]).then_inc(dv.sem, 1)
    dv.inc()
    attn_done = dv.v


    @p.block.gpsimd
    def _(g):
        g.wait_ge(pe.sem, pe.v)
        g.dma_start(out=wbuf[:, :, 1024:2048],
                    in_=Wout_d.rearrange("(k p) j -> p k j", p=128)).then_inc(swt.sem, 16)
    wo_ready = (swt.sem, swt.inc(16))

    xr_ev = [None, None]
    xuse = [None, None]
    it2 = [0]

    def cb_out(m, q, ps, mmv):
        sl = it2[0] % 2

        @p.block.sync
        def _(sy):
            sy.wait_ge(dv.sem, xuse[sl] if xuse[sl] is not None else attn_done)
            xr_ev[sl] = xsl[sl].issue(sy, xch[:, sl, 0:512], x_d[bass.ds(m * 128, 128), q * 512:(q + 1) * 512])

        @p.block.vector
        def _(v):
            v.wait_ge(pe.sem, mmv)
            v.wait_ge(xr_ev[sl][0], xr_ev[sl][1])
            v.scalar_tensor_tensor(xch[:, sl, 512:1024][:, 0:512], ps, 0.0, xch[:, sl, 0:512],
                                   ALU.add, ALU.add).then_inc(dv.sem, 1)
        outv = dv.inc()
        xuse[sl] = outv

        @p.block.sync
        def _(sy):
            sy.wait_ge(dv.sem, outv)
            xsl[sl].issue(sy, xo_d[bass.ds(m * 128, 128), q * 512:(q + 1) * 512], xch[:, sl, 512:1024][:, 0:512])
        it2[0] += 1
        return outv

    emit_mm_dense(p, wbuf.ap()[:, :, 1024:2048].rearrange("p k j -> p k j"), O7, NCH, cb_out,
                  wait_w=wo_ready, wait_dv=attn_done, psum_rot=slots)

    @p.block.sync
    def _(sy):
        for s in xsl:
            sy.wait_ge(s.ctr.sem, s.ctr.v)
    ev = (xsl[0].ctr.sem, xsl[0].ctr.v)
    return [(s.ctr.sem, s.ctr.v) for s in xsl]



def sub_att02(p, x_d, xo_d, rope_bias, after=None):
    nc = p.nc
    wname = "Wqkv0_my" if rope_bias else "Wqkv3_my"
    Wqkv_d = p.in_ap(wname, [D, 768], F32)
    if rope_bias:
        bq_d = p.in_ap("bq_my", [128, 2], F32)
        bk_d = p.in_ap("bk_my", [128, 2], F32)
        bv_d = p.in_ap("bv_my", [1, 256], F32)
        Wp_d = p.in_ap("WprojT", [D, D], F32)
        bp_d = p.in_ap("b_proj", [128, NCH], F32)
        f_d = p.in_ap("f_rope", [128, S], F32)
    else:
        Wp_d = p.in_ap("WOT", [D, D], F32)
    tupp_d = p.in_ap("t_upp", [128, 128], BF16)

    cXn = p.dram("cXn", [2, D, TLOC // 2], BF16)
    gXn = p.dram("gXn", [2, GRP * D, TLOC // 2], BF16)
    gXn_v = gXn.ap().rearrange("h (r c p) t -> h r p c t", r=GRP, p=128)
    cO = p.dram("cO", [4, 4 * HD, TLOC], BF16)
    gO = p.dram("gO", [4, GRP * 4 * HD, TLOC], BF16)
    gO_v = gO.ap().rearrange("ch (k p) t -> ch p k t", p=128)
    shb = None

    xres = p.xres
    Q2t = p.sbuf("Q2t", [128, 2, S], BF16)
    xn = Q2t.ap().rearrange("p a s -> p (a s)").rearrange("p (c t) -> p c t", c=NCH)
    Q2 = Q2t.ap()
    K2t = p.sbuf("K2t", [128, 2, S], BF16)
    K2 = K2t.ap()
    V2t = p.sbuf("V2t", [128, 2, 64, 130], BF16)
    V2 = V2t.ap()
    v2f = V2t.ap().rearrange("p a k c -> p (a k c)").bitcast(F32)
    p.r_m = v2f[0:1, 0:2048]
    p.r_a = v2f[0:1, 2048:4096]
    p.r_b = v2f[0:1, 4096:6144]
    XG = p.sbuf("XG", [128, NCH, 1024], BF16)
    wq = p.sbuf("wq", [128, NCH, 768], BF16)
    P4 = p.sbuf("P4", [128, 2, 512], BF16)
    tmpf = P4.ap().rearrange("p a t -> p (a t)").bitcast(F32)[:, 0:512]
    if rope_bias:
        f_bf = p.sbuf("f_bf", [128, S], BF16)
        WP = f_bf.ap().rearrange("p (k j) -> p k j", k=NCH)
        bq_sb = p.sbuf("bq_sb", [128, 2], F32)
        bk_sb = p.sbuf("bk_sb", [128, 2], F32)
        bv_sb = P4.ap().rearrange("p a t -> p (a t)")[0:1, 512:768]
        bp_sb = p.sbuf("bp_sb", [128, NCH], F32)

    else:
        WP = p.sbuf("WP", [128, NCH, 1024], BF16).ap()
    mx = p.sbuf("mx", [2, 8], F32)
    shb = p.dram("shb02", [2, S], F32)
    scr512 = P4.ap().rearrange("p a t -> p (a t)")[:, 0:512]
    shst = None
    onesb = p.sbuf("onesb", [128, 128], BF16)
    co8 = p.sbuf("co8_02", [128, 1], F32)
    rl = None
    tupp = wq.ap().rearrange("p k j -> p (k j)")[:, 0:128]
    p.ones128_bf = p.sbuf("ones128_bf", [128, 1], BF16)
    p.ones1x128_f = p.sbuf("ones1x128_f", [1, 128], F32)
    onespat = p.sbuf("onespat02", [128, 2], BF16)

    tmp_f32 = K2t.ap().rearrange("p a s -> p (a s)").bitcast(F32)[:, 0:1024]
    xg_flat = XG.ap().rearrange("p c t -> p (c t)")
    wq_flat_b = wq.ap().rearrange("p k j -> p (k j)")
    wq_flat_f = wq_flat_b.bitcast(F32)
    rl = wq_flat_f[0:1, 1792:2304]
    Ost = wq_flat_f[0:64, 256:768]
    Ostb = wq_flat_b[0:64, 4608:5120]
    shst = wq_flat_f[0:2, 2560:3072]
    OQ = Q2t.ap().rearrange("p a s -> p (a s)").rearrange("p (c t) -> p c t", c=NCH)

    p.psBIG = p.psum("psBIG", [128, 2048], F32)
    psS = p.psum("psS", [128, 1024], F32)
    psO = p.psum("psO", [128, 512], F32)
    psV = p.psum("psV", [128, 512], F32)
    psBIG = p.psBIG

    pe, dv, ac, io = p.sPE, p.sDV, p.sAC, p.sIO

    @p.block.gpsimd
    def _(g):
        g.memset(p.ones128_bf[:, :], 1.0)
        g.memset(p.ones1x128_f[:, :], 1.0)
        g.memset(co8[:, :], 0.125)
        g.memset(onesb[:, :], 1.0)
        g.memset(onespat[:, :], 0.0).then_inc(p.sCN.sem, 1)
        g.wait_ge(p.sCN.sem, p.sCN.inc())
        g.memset(onespat[0:64, 0:1], 1.0)
        g.memset(onespat[64:128, 1:2], 1.0).then_inc(p.sCN.sem, 1)
    cn_done = p.sCN.inc()

    x_ready = emit_load_x(p, x_d, xres, after)
    swt = p.dma_sem("sWts")

    @p.block.gpsimd
    def _(g):
        g.dma_start(out=wq[:, :, :], in_=Wqkv_d.rearrange("(k p) j -> p k j", p=128)).then_inc(swt.sem, 16)
        n = 16
        if rope_bias:
            g.dma_start(out=bq_sb[:, :], in_=bq_d).then_inc(swt.sem, 16)
            g.dma_start(out=bk_sb[:, :], in_=bk_d).then_inc(swt.sem, 16)
            g.dma_start(out=bv_sb[:, :], in_=bv_d).then_inc(swt.sem, 16)
            g.dma_start(out=bp_sb[:, :], in_=bp_d).then_inc(swt.sem, 16)
            g.dma_start(out=f_bf[:, :], in_=f_d).then_inc(swt.sem, 16)
            n += 80
        swt.inc(n)
    w_ready = (swt.sem, swt.v)

    xn_ready = emit_ln(p, xres, xn, K2t.ap().rearrange("p a s -> p (a s)").rearrange("p (c t) -> p c t", c=NCH), tmp_f32, x_ready)

    cxn_sl = [DmaSlot(p, f"sCxn{h}") for h in range(2)]
    ccx_half = [None, None]
    for h in range(2):
        @p.block.sync
        def _(sy, h=h):
            sy.wait_ge(dv.sem, xn_ready)
            cxn_sl[h].issue(sy, cXn.ap()[h].rearrange("(c p) t -> p c t", p=128),
                            xn[:, :, h * 1024:(h + 1) * 1024])

        @p.block.gpsimd
        def _(g, h=h):
            g.wait_ge(cxn_sl[h].ctr.sem, cxn_sl[h].ctr.v)
            g.collective_compute("AllGather", ALU.bypass, replica_groups=[[0, 1, 2, 3], [4, 5, 6, 7]],
                                 ins=[cXn.ap()[h].opt()], outs=[gXn.ap()[h].opt()]).then_inc(io.sem, 1)
        ccx_half[h] = io.inc()
    ccx_done = ccx_half[1]

    @p.block.gpsimd
    def _(g):
        g.wait_ge(dv.sem, xn_ready)
        g.memset(V2t[:, :, :, :], 0.0).then_inc(p.sCN.sem, 1)
        g.wait_ge(p.sCN.sem, p.sCN.inc())
        g.memset(V2[:, :, :, 64:65], 1.0)
        g.memset(V2[:, :, :, 129:130], 1.0).then_inc(p.sCN.sem, 1)
    vones_done = p.sCN.inc()

    xg_s = p.dma_sem("sXg")
    slots6 = [psBIG[:, i * 512:(i + 1) * 512] for i in range(4)] + [psS[:, 0:512], psS[:, 512:1024]]
    ps_free = [xn_ready] * 6
    vt_free = [None]
    xg_used = None
    for RH in range(2 * GRP):
        half, R = RH // GRP, RH % GRP
        @p.block.sync
        def _(sy):
            sy.wait_ge(io.sem, ccx_half[half])
            if RH > 0:
                sy.wait_ge(pe.sem, xg_used)
            if xg_s.v > 0:
                sy.wait_ge(xg_s.sem, xg_s.v)
            sy.dma_start(out=XG[:, :, :], in_=gXn_v[half, R]).then_inc(xg_s.sem, 16)
        xg_s.inc(16)
        xg_ev = (xg_s.sem, xg_s.v)

        for s4 in range(2):
            tsl = slice(s4 * 512, (s4 + 1) * 512)
            gsl = slice(R * 2048 + half * 1024 + s4 * 512, R * 2048 + half * 1024 + (s4 + 1) * 512)
            for j in range(4):
                @p.block.tensor
                def _(t):
                    if RH == 0 and s4 == 0 and j == 0:
                        t.wait_ge(w_ready[0], w_ready[1])
                    t.wait_ge(xg_ev[0], xg_ev[1])
                    t.wait_ge(dv.sem, ps_free[j])
                    for k in range(NCH):
                        mm = t.matmul(slots6[j], wq[:, k, j * 128:(j + 1) * 128], XG[:, k, tsl],
                                      start=(k == 0), stop=(k == NCH - 1))
                    mm.then_inc(pe.sem, 1)
                mmv = pe.inc()

                @p.block.vector
                def _(v):
                    v.wait_ge(pe.sem, mmv)
                    pj = j % 2
                    if j < 2:
                        dst = Q2[:, pj, gsl]
                        bias = bq_sb[:, pj:pj + 1] if rope_bias else None
                    else:
                        dst = K2[:, pj, gsl]
                        bias = bk_sb[:, pj:pj + 1] if rope_bias else None
                    if rope_bias:
                        v.scalar_tensor_tensor(dst, slots6[j], bias, f_bf[:, gsl],
                                               ALU.add, ALU.mult).then_inc(dv.sem, 1)
                    else:
                        v.tensor_copy(dst, slots6[j]).then_inc(dv.sem, 1)
                ps_free[j] = dv.inc()

        for tt in range(8):
            kt = R * 16 + half * 8 + tt
            ttsl = slice(tt * 128, (tt + 1) * 128)
            for pj in range(2):
                @p.block.tensor
                def _(t):
                    if vt_free[0] is not None:
                        t.wait_ge(dv.sem, vt_free[0])
                    if RH == 0 and tt == 0 and pj == 0:
                        t.wait_ge(p.sCN.sem, vones_done)
                    for k in range(NCH):
                        last_k = (k == NCH - 1) and not rope_bias
                        mm = t.matmul(psV[:, 0:128], XG[:, k, ttsl],
                                      wq[:, k, 512 + pj * 128:512 + (pj + 1) * 128],
                                      start=(k == 0), stop=last_k)
                    if rope_bias:
                        mm = t.matmul(psV[:, 0:128], onesb[0:1, 0:128],
                                      bv_sb[0:1, pj * 128:(pj + 1) * 128],
                                      start=False, stop=True)
                    mm.then_inc(pe.sem, 1)
                mmv2 = pe.inc()

                @p.block.vector
                def _(v):
                    v.wait_ge(pe.sem, mmv2)
                    v.tensor_copy(V2[:, pj, kt, 0:64], psV[:, 0:64])
                    v.tensor_copy(V2[:, pj, kt, 65:129], psV[:, 64:128]).then_inc(dv.sem, 1)
                vt_free[0] = dv.inc()
        xg_used = pe.v
    proj_done = dv.v

    nrm_prev = [None]

    def headmax02(src2, col0):
        for pj in range(2):
            for seg in range(4):
                ssl = slice(seg * 2048, (seg + 1) * 2048)

                @p.block.vector
                def _(v):
                    v.wait_ge(dv.sem, dv.v)
                    if nrm_prev[0] is not None:
                        v.wait_ge(pe.sem, nrm_prev[0])
                    if pj == 0 and seg == 0 and col0 == 0:
                        v.wait_ge(dv.sem, proj_done)
                    v.tensor_mul(xg_flat[:, 0:2048],
                                 src2[:, pj, ssl], src2[:, pj, ssl]).then_inc(dv.sem, 1)
                sqv = dv.inc()

                for q in range(4):
                    first = (seg == 0 and q == 0)

                    @p.block.tensor
                    def _(t):
                        t.wait_ge(dv.sem, sqv if q == 0 else dv.v)
                        t.matmul(psO[0:2, 0:512], onespat[:, :],
                                 xg_flat[:, q * 512:(q + 1) * 512],
                                 start=True, stop=True).then_inc(pe.sem, 1)
                    nmv = pe.inc()
                    nrm_prev[0] = nmv

                    @p.block.vector
                    def _(v):
                        v.wait_ge(pe.sem, nmv)
                        if first:
                            v.tensor_reduce(mx[0:2, col0 + pj:col0 + pj + 1], psO[0:2, 0:512],
                                            AX.X, ALU.max).then_inc(dv.sem, 1)
                            dv.inc()
                        else:
                            v.tensor_reduce(mx[0:2, 4 + col0 + pj:4 + col0 + pj + 1], psO[0:2, 0:512],
                                            AX.X, ALU.max).then_inc(dv.sem, 1)
                            v.wait_ge(dv.sem, dv.inc())
                            v.tensor_max(mx[0:2, col0 + pj:col0 + pj + 1], mx[0:2, col0 + pj:col0 + pj + 1],
                                         mx[0:2, 4 + col0 + pj:4 + col0 + pj + 1]).then_inc(dv.sem, 1)
                            dv.inc()

    headmax02(K2, 0)
    mx_done = dv.v

    tup_s = p.dma_sem("sTup")

    @p.block.gpsimd
    def _(g):
        g.wait_ge(dv.sem, proj_done)
        g.dma_start(out=tupp, in_=tupp_d).then_inc(tup_s.sem, 16)
    tup_s.inc(16)

    @p.block.gpsimd
    def _(g):
        if rope_bias:
            g.wait_ge(dv.sem, proj_done)
        g.dma_start(out=WP, in_=Wp_d.rearrange("(k p) j -> p k j", p=128)).then_inc(swt.sem, 16)
    wp_ready = (swt.sem, swt.inc(16))

    xg_flat2 = XG.ap().rearrange("p c t -> p (c t)")
    Qp = xg_flat2[:, 0:2048].rearrange("p (s h t) -> p s h t", s=2, h=2)
    Kp = xg_flat2[:, 2048:3072].rearrange("p (s k) -> p s k", s=8)
    P4a = [P4.ap()[:, 0, :], P4.ap()[:, 1, :], xg_flat2[:, 3072:3584]]
    qS = [[DmaSlot(p, f"sQp{sl}{j}") for j in range(4)] for sl in range(2)]
    kS = [DmaSlot(p, f"sKp{i}") for i in range(8)]
    kb_ev = [None] * 8
    kp_lastuse = [None] * 8
    qp_lastuse = [None, None]
    qp_ready = [None, None]
    Sslot = [psBIG[:, 0:512], psBIG[:, 512:1024], psBIG[:, 1536:2048]]
    psBB = psBIG[:, 1024:1536]

    def emit_og_chunk(c):
        @p.block.gpsimd
        def _(g, c=c):
            g.wait_ge(cow.ctr.sem, cow.ctr.v)
            g.collective_compute("AllGather", ALU.bypass,
                                 replica_groups=[[0, 1, 2, 3], [4, 5, 6, 7]],
                                 ins=[cO.ap()[c].opt()], outs=[gO.ap()[c].opt()]).then_inc(io.sem, 1)
        io.inc()
    s_exp = [None] * 3
    p_cons = [None] * 3
    Oslot_par = [[psS[:, 0:512], psS[:, 512:1024]], [psO[:, 0:512], psV[:, 0:512]]]
    o_free_banks = [[mx_done, mx_done], [mx_done, mx_done]]
    pending_epi = [None]
    cow = DmaSlot(p, "sCO")

    def emit_qp_build(pj, st):
        sq = st % 2
        qsl2 = slice(st * 512, (st + 1) * 512)

        @p.block.gpsimd
        def _(g):
            if st < 2:
                g.wait_ge(shs.ctr.sem, shs.ctr.v)
                g.wait_ge(dv.sem, mx_done)
            if pj == 0 and st == 0:
                g.memset(Kp[64:96, :, :], 0.0)
                g.memset(Qp[64:96, :, :, :], 0.0).then_inc(p.sCN.sem, 1)
                g.wait_ge(p.sCN.sem, p.sCN.inc())
                g.memset(Kp[64:65, :, :], 1.0)
            if qp_lastuse[sq] is not None:
                g.wait_ge(pe.sem, qp_lastuse[sq])
            evs = [
                qS[sq][0].issue(g, Qp[0:64, sq, 0, :], Q2[0:64, pj, qsl2]),
                qS[sq][1].issue(g, Qp[0:64, sq, 1, :], Q2[64:128, pj, qsl2]),
                qS[sq][2].issue(g, Qp[64:65, sq, 0, :], shb.ap()[0:1, qsl2]),
                qS[sq][3].issue(g, Qp[64:65, sq, 1, :], shb.ap()[1:2, qsl2]),
            ]
        qp_ready[sq] = [(s.ctr.sem, s.ctr.v) for s in qS[sq]]
    shs = DmaSlot(p, "sShs")
    sh_prev = [None]

    for pj in range(2):
        dv0_sh = dv.v
        ac0_sh = ac.v
        for ch in range(16):
            csl = slice(ch * 512, (ch + 1) * 512)
            sl2 = ch % 2

            @p.block.vector
            def _(v):
                v.wait_ge(dv.sem, dv.v)
                if sh_prev[0] is not None:
                    v.wait_ge(pe.sem, sh_prev[0])
                if pj == 0 and ch == 0:
                    v.wait_ge(dv.sem, mx_done)
                v.tensor_mul(scr512[:, :], Q2[:, pj, csl], K2[:, pj, csl]).then_inc(dv.sem, 1)
            d1 = dv.inc()

            @p.block.tensor
            def _(t):
                t.wait_ge(dv.sem, d1)
                t.matmul(psO[0:2, 0:512], onespat[:, :], scr512[:, :],
                         start=True, stop=True).then_inc(pe.sem, 1)
            m1 = pe.inc()

            @p.block.vector
            def _(v):
                v.wait_ge(pe.sem, m1)
                v.tensor_mul(scr512[:, :], Q2[:, pj, csl], Q2[:, pj, csl]).then_inc(dv.sem, 1)
            d2 = dv.inc()

            @p.block.tensor
            def _(t):
                t.wait_ge(dv.sem, d2)
                t.matmul(psV[0:2, 0:512], onespat[:, :], scr512[:, :],
                         start=True, stop=True).then_inc(pe.sem, 1)
            m2 = pe.inc()
            sh_prev[0] = m2

            @p.block.vector
            def _(v):
                v.wait_ge(pe.sem, m2)
                if shs.ctr.v > 0:
                    v.wait_ge(shs.ctr.sem, shs.ctr.v)
                v.tensor_scalar(shst[0:2, :], psV[0:2, 0:512],
                                mx[0:2, pj:pj + 1], None, ALU.mult).then_inc(dv.sem, 1)
            d3 = dv.inc()

            assert d3 == dv0_sh + 5 * ch + 3

            @p.block.scalar
            def _(s):
                s.wait_ge(dv.sem, d3)
                s.activation(shst[0:2, :], shst[0:2, :], AF.Sqrt).then_inc(ac.sem, 1)
            a1 = ac.inc()
            assert a1 == ac0_sh + ch + 1

            @p.block.vector
            def _(v):
                v.wait_ge(ac.sem, a1)
                v.tensor_add(shst[0:2, :], shst[0:2, :], psO[0:2, 0:512]).then_inc(dv.sem, 1)
                v.wait_ge(dv.sem, dv.inc())
                v.tensor_scalar_mul(shst[0:2, :], shst[0:2, :], -0.5).then_inc(dv.sem, 1)
            d4 = dv.inc()

            @p.block.sync
            def _(sy):
                sy.wait_ge(dv.sem, d4)
                if pj == 1 and ch == 0:
                    for qsl_ in qS:
                        for s_ in qsl_:
                            if s_.ctr.v > 0:
                                sy.wait_ge(s_.ctr.sem, s_.ctr.v)
                shs.issue(sy, shb.ap()[:, csl], shst[0:2, :])

        o_free_banks[1] = [dv.v, dv.v]

        emit_qp_build(pj, 0)
        emit_qp_build(pj, 1)

        for st in range(16):
            if pj == 1 and st >= 5 and (st - 5) % 4 == 0:
                emit_og_chunk((st - 5) // 4)
            qsl = slice(st * 512, (st + 1) * 512)
            ktmax = 4 * (st + 1)
            iters = []
            for kt in range(ktmax):
                diag_sub = kt - 4 * st
                q0 = diag_sub * 128 if diag_sub >= 0 else 0
                for hh in range(2):
                    iters.append((kt, hh, len(iters) % 3, q0, 512 - q0, diag_sub))

            L = len(iters)
            par = st % 2
            Osl = Oslot_par[par]

            pe0 = pe.v
            ac0 = ac.v
            carry_pcons = list(p_cons)
            i_epi = min(8, L - 1) if pending_epi[0] is not None else None
            qk_val = [None] * L
            pv_val = [None] * L
            cnt = pe0
            for i in range(L):
                cnt += 1
                qk_val[i] = cnt
                if i >= 2:
                    cnt += 1
                    pv_val[i - 2] = cnt
                if i == i_epi:
                    cnt += 2
            pv_val[L - 2] = cnt + 1
            pv_val[L - 1] = cnt + 2
            qp_ev = qp_ready[st % 2]
            qp_lastuse[st % 2] = qk_val[-1]
            if st + 2 < 16:
                emit_qp_build(pj, st + 2)

            @p.block.scalar
            def _(s, iters=iters, qk_val=qk_val, pv_val=pv_val, carry=carry_pcons):
                for i, (kt, hh, si, q0, Nq, ds) in enumerate(iters):
                    s.wait_ge(pe.sem, qk_val[i])
                    if i >= 3:
                        s.wait_ge(pe.sem, pv_val[i - 3])
                    elif carry[si] is not None:
                        s.wait_ge(pe.sem, carry[si])
                    s.activation(P4a[si][:, 0:Nq], Sslot[si][:, 0:Nq], AF.Exp,
                                 scale=co8[:, 0:1]).then_inc(ac.sem, 1)
            ac.inc(L)

            mskv_arr = [None] * L

            def emit_pv(j, iters=iters, par=par, Osl=Osl, ac0=ac0,
                        pv_val=pv_val, ktmax=ktmax, pj=pj):
                kt, hh, si, q0, Nq, diag_sub = iters[j]
                expv = ac0 + j + 1

                @p.block.tensor
                def _(t):
                    if mskv_arr[j] is not None:
                        t.wait_ge(dv.sem, mskv_arr[j])
                    else:
                        t.wait_ge(ac.sem, expv)
                    if kt == 0:
                        t.wait_ge(dv.sem, o_free_banks[par][hh])
                    t.matmul(Osl[hh][0:65, q0:512], V2[:, pj, kt, hh * 65:(hh + 1) * 65],
                             P4a[si][:, 0:Nq], start=(kt == 0), stop=(kt == ktmax - 1),
                             skip_group_check=True).then_inc(pe.sem, 1)
                p_cons[si] = pe.inc()
                assert p_cons[si] == pv_val[j]

            for i, (kt, hh, si, q0, Nq, diag_sub) in enumerate(iters):
                    hsl = slice(hh * 64, hh * 64 + 64)
                    ksl = i % 8

                    @p.block.sync
                    def _(sy):
                        if kp_lastuse[ksl] is not None:
                            sy.wait_ge(pe.sem, kp_lastuse[ksl])
                        if pj == 0 and st == 0 and i < 4:
                            sy.wait_ge(shs.ctr.sem, shs.ctr.v)
                        kb_ev[ksl] = kS[ksl].issue(sy, Kp[0:64, ksl, :],
                                                   K2[hsl, pj, kt * 128:(kt + 1) * 128])

                    @p.block.tensor
                    def _(t):
                        t.wait_ge(kb_ev[ksl][0], kb_ev[ksl][1])
                        if i < 2:
                            for ev in qp_ev:
                                t.wait_ge(ev[0], ev[1])
                        if s_exp[si] is not None:
                            t.wait_ge(ac.sem, s_exp[si])
                        t.matmul(Sslot[si][:, 0:Nq], Kp[0:96, ksl, :],
                                 Qp[0:96, st % 2, hh, q0:512],
                                 start=True, stop=True).then_inc(pe.sem, 1)
                    qkv_ = pe.inc()
                    assert qkv_ == qk_val[i]
                    kp_lastuse[ksl] = qkv_

                    expv = ac0 + i + 1
                    s_exp[si] = expv

                    if diag_sub >= 0:
                        @p.block.vector
                        def _(v):
                            v.wait_ge(ac.sem, expv)
                            v.wait_ge(tup_s.sem, tup_s.v)
                            v.tensor_mul(P4a[si][:, 0:128], P4a[si][:, 0:128], tupp).then_inc(dv.sem, 1)
                        mskv_arr[i] = dv.inc()

                    if i >= 2:
                        emit_pv(i - 2)
                    if i == i_epi:
                        pending_epi[0]()
                        pending_epi[0] = None

            emit_pv(L - 2)
            emit_pv(L - 1)
            if pending_epi[0] is not None:
                pending_epi[0]()
                pending_epi[0] = None

            def make_epilogue(pj=pj, st=st, par=par, qsl=qsl, pv_done=pe.v):
                def emit():
                    rbv_prev = None
                    for hh in range(2):
                        @p.block.vector
                        def _(v, hh=hh, rbv_prev=rbv_prev):
                            v.wait_ge(pe.sem, pv_done)
                            if rbv_prev is not None:
                                v.wait_ge(pe.sem, rbv_prev)
                            v.reciprocal(rl[:, :], Oslot_par[par][hh][64:65, 0:512]).then_inc(dv.sem, 1)
                            v.wait_ge(dv.sem, dv.inc())
                            v.tensor_copy(Ost[:, :], Oslot_par[par][hh][0:64, 0:512]).then_inc(dv.sem, 1)
                        oe = dv.inc()

                        @p.block.tensor
                        def _(t):
                            t.wait_ge(dv.sem, oe)
                            t.matmul(psBB[0:64, 0:512], p.ones1x128_f[0:1, 0:64], rl[0:1, :],
                                     start=True, stop=True).then_inc(pe.sem, 1)
                        rbv = pe.inc()
                        rbv_prev = rbv

                        @p.block.vector
                        def _(v):
                            v.wait_ge(pe.sem, rbv)
                            if cow.ctr.v > 0:
                                v.wait_ge(cow.ctr.sem, cow.ctr.v)
                            v.tensor_mul(Ostb[:, :], Ost[:, :], psBB[0:64, 0:512]).then_inc(dv.sem, 1)
                        o_free_banks[par][hh] = dv.inc()

                        @p.block.sync
                        def _(sy, mfree=o_free_banks[par][hh], hh=hh):
                            sy.wait_ge(dv.sem, mfree)
                            r0_ = (2 * pj + hh) * 64
                            cow.issue(sy, cO.ap()[st // 4, r0_:r0_ + 64,
                                                  (st % 4) * 512:(st % 4) * 512 + 512],
                                      Ostb[:, :])
                return emit

            pending_epi[0] = make_epilogue()

        if pending_epi[0] is not None:
            pending_epi[0]()
            pending_epi[0] = None
    attn_done = dv.v

    emit_og_chunk(3)
    cco_done = io.v

    oq_s = p.dma_sem("sOq")

    @p.block.sync
    def _(sy):
        sy.wait_ge(io.sem, cco_done)
        rq = nc.partition_id() % 4
        sy.dma_start(out=OQ, in_=gO_v[rq]).then_inc(oq_s.sem, 16)
    oq_s.inc(16)
    oq_ev = (oq_s.sem, oq_s.v)

    slots = [psBIG[:, i * 512:(i + 1) * 512] for i in range(4)]

    def cb_out(m, q, ps, mmv):
        @p.block.vector
        def _(v):
            v.wait_ge(pe.sem, mmv)
            v.wait_ge(dv.sem, dv.v)
            bias = bp_sb[:, m:m + 1] if rope_bias else 0.0
            v.scalar_tensor_tensor(tmpf, ps, bias, xres[:, m, q * 512:(q + 1) * 512],
                                   ALU.add, ALU.add).then_inc(dv.sem, 1)
            v.wait_ge(dv.sem, dv.inc())
            v.tensor_copy(xres[:, m, q * 512:(q + 1) * 512], tmpf).then_inc(dv.sem, 1)
        return dv.inc()

    @p.block.tensor
    def _(t):
        t.wait_ge(oq_ev[0], oq_ev[1])
        t.wait_ge(wp_ready[0], wp_ready[1])
    emit_mm_dense(p, WP, OQ, NCH, cb_out, wait_dv=attn_done, psum_rot=slots)

    return emit_store_x(p, xres, xo_d, wait_dv=dv.v)


def build_ffn0():
    return _standalone(sub_ffn0, "f0_")


def build_ffn1():
    return _standalone(sub_ffn1, "f1_")


def build_att5():
    return _standalone(sub_att5, "a5_")


def build_att7():
    return _standalone(sub_att7, "a7_")


def build_att0():
    return _standalone(lambda p, x, xo, after=None: sub_att02(p, x, xo, True, after), "a0_")


def build_att2():
    return _standalone(lambda p, x, xo, after=None: sub_att02(p, x, xo, False, after), "a2_")





def emit_barrier(p, extra=()):
    vals = [(p.sPE.sem, p.sPE.v), (p.sDV.sem, p.sDV.v), (p.sAC.sem, p.sAC.v),
            (p.sIO.sem, p.sIO.v), (p.sCN.sem, p.sCN.v), (p.sST.sem, p.sST.v)]
    vals += list(extra)
    for secf in [p.block.tensor, p.block.vector, p.block.scalar, p.block.gpsimd, p.block.sync]:
        @secf
        def _(e):
            for s, v in vals:
                if v > 0:
                    e.wait_ge(s, v)


def build_fused():
    p = Prog("fx")
    x_d = p.nc.dram_tensor("x", [D, TLOC], F32, kind="ExternalInput").ap()
    xo_d = p.nc.dram_tensor("xo", [D, TLOC], F32, kind="ExternalOutput").ap()
    xs = [x_d] + [p.nc.dram_tensor(f"xmid{i}", [D, TLOC], F32).ap() for i in range(7)] + [xo_d]
    common_sems(p)
    p.xres = p.sbuf("xres_root", [128, NCH, TLOC], F32)
    p.start()
    subs = [
        (lambda pp, a, b, after=None: sub_att02(pp, a, b, True, after), "s0_"),
        (sub_ffn0, "s1_"),
        (lambda pp, a, b, after=None: sub_att02(pp, a, b, False, after), "s2_"),
        (sub_ffn0, "s3_"),
        (sub_att5, "s4_"),
        (sub_ffn1, "s5_"),
        (sub_att7, "s6_"),
        (sub_ffn1, "s7_"),
    ]
    io_map = [(xs[0], None), (None, None), (None, None), (None, None),
              (None, None), (None, xs[6]), (xs[6], xs[7]), (xs[7], xs[8])]
    for i, (fn, pref) in enumerate(subs):
        p.push_scope(pref)
        r = fn(p, io_map[i][0], io_map[i][1])
        p.pop_scope()
        evs = r if isinstance(r, list) else [r]
        emit_barrier(p, extra=[e for e in evs if e is not None])
    return p.finish()


_fused_cache = {}


def _get_fused():
    if "nc" not in _fused_cache:
        _fused_cache["nc"] = build_fused()
    return _fused_cache["nc"]


FUSED = True


_BUILDERS = {
    SUB_FFN0A: build_ffn0,
    SUB_FFN0B: build_ffn0,
    SUB_FFN1A: build_ffn1,
    SUB_FFN1B: build_ffn1,
    SUB_ATT5: build_att5,
    SUB_ATT7: build_att7,
    SUB_ATTA0: build_att0,
    SUB_ATTB0: build_att2,
}

_prog_cache = {}


def _get_prog(idx):
    key = _BUILDERS[idx].__name__
    if key not in _prog_cache:
        _prog_cache[key] = _BUILDERS[idx]()
    return _prog_cache[key]


def run_sublayer(idx, in_maps, x_cur):
    nc = _get_prog(idx)
    names = SUB_INPUTS[idx]
    maps = []
    for c in range(NCORE):
        m = {k: in_maps[c][k] for k in names if k != "x"}
        m["x"] = x_cur[c]
        maps.append(m)
    res = run_bass_kernel_spmd(nc, maps, core_ids=list(range(NCORE)))
    return [np.asarray(res.results[c]["xo"]) for c in range(NCORE)]


def kernel(**inputs):
    in_maps = host_prepare(inputs)
    out = np.empty((B, S, D), dtype=np.float32)
    if FUSED:
        nc = _get_fused()
        res = run_bass_kernel_spmd(nc, in_maps, core_ids=list(range(NCORE)))
        for c in range(NCORE):
            b, r = c // GRP, c % GRP
            out[b, r * TLOC:(r + 1) * TLOC, :] = np.asarray(res.results[c]["xo"]).T
        return out
    x_cur = [im["x"] for im in in_maps]
    for idx in range(8):
        x_cur = run_sublayer(idx, in_maps, x_cur)
    for c in range(NCORE):
        b, r = c // GRP, c % GRP
        out[b, r * TLOC:(r + 1) * TLOC, :] = x_cur[c].T
    return out


# version 19
# speedup vs baseline: 1.0116x; 1.0047x over previous
import numpy as np
import ml_dtypes
from contextlib import ExitStack

import concourse.bass as bass
import concourse.mybir as mybir
from concourse.bass_utils import run_bass_kernel_spmd

F32 = mybir.dt.float32
BF16 = mybir.dt.bfloat16
AF = mybir.ActivationFunctionType
ALU = mybir.AluOpType
AX = mybir.AxisListType

NCORE = 8
GRP = 4
B, S, D = 2, 8192, 1024
TLOC = 2048
NCH = D // 128
NH, HD = 16, 64
HPC = NH // GRP
EPS = 1e-5
BF = ml_dtypes.bfloat16

SUB_ATTA0, SUB_FFN0A, SUB_ATTB0, SUB_FFN0B, SUB_ATT5, SUB_FFN1A, SUB_ATT7, SUB_FFN1B = range(8)



def _bias_cols(v):
    return np.ascontiguousarray(v.reshape(NCH, 128).T)


def _rope_table():
    theta = (1.0 / 10000.0 ** (2.0 * np.arange(0, HD // 2, dtype=np.float32) / HD)).astype(np.float32)
    seq = np.arange(1, S + 1, dtype=np.float32)
    ang = np.einsum("n,d->nd", seq, theta).astype(np.float32)
    ang = np.concatenate([ang, ang], axis=-1)
    f = (np.cos(ang) + np.sin(ang)).astype(np.float32)
    return np.ascontiguousarray(np.tile(f.T, (2, 1)))


def _masks():
    p = np.arange(128)[:, None]
    j = np.arange(128)[None, :]
    t_upp = (j >= p).astype(BF)
    t_low = (j <= p).astype(BF)
    return t_upp, np.ascontiguousarray((j <= p).astype(BF)), t_low


def host_prepare(inp):
    x = np.asarray(inp["L0_attn_a_x"])

    WattnT = np.ascontiguousarray(np.asarray(inp["L0_attn_a_W_attn"]).T)
    b_attn = np.asarray(inp["L0_attn_a_b_attn"])
    WprojT = np.ascontiguousarray(np.asarray(inp["L0_attn_a_W_proj"]).T)
    b_proj = np.asarray(inp["L0_attn_a_b_proj"])
    WQT = np.ascontiguousarray(np.asarray(inp["L0_attn_b_W_Q"]).T)
    WKT = np.ascontiguousarray(np.asarray(inp["L0_attn_b_W_K"]).T)
    WVT = np.ascontiguousarray(np.asarray(inp["L0_attn_b_W_V"]).T)
    WOT = np.ascontiguousarray(np.asarray(inp["L0_attn_b_W_O"]).T)
    W0 = np.asarray(inp["L0_ffn_weight"])
    A1 = np.asarray(inp["L0_ffn_aux_weight1"])
    A2 = np.asarray(inp["L0_ffn_aux_weight2"])
    b0 = np.asarray(inp["L0_ffn_bias"])
    W5 = np.asarray(inp["L1_attn_a_W"])
    g7 = np.asarray(inp["L1_attn_b_ln_gamma"])
    be7 = np.asarray(inp["L1_attn_b_ln_beta"])
    Wqkv7 = np.asarray(inp["L1_attn_b_w_qkv"])
    Wout7 = np.asarray(inp["L1_attn_b_w_out"])
    W6T = np.ascontiguousarray(np.asarray(inp["L1_ffn_weights"]).T)
    sc6 = np.asarray(inp["L1_ffn_scale"])
    b6 = np.asarray(inp["L1_ffn_bias"])

    f_rope = _rope_table()
    t_upp, t_low, _ = _masks()
    ident = np.eye(128, dtype=BF)

    pj = (np.arange(128)[:, None] - np.arange(512)[None, :]).astype(np.float32)

    in_maps = []
    for c in range(NCORE):
        b = c // GRP
        r = c % GRP
        g = r
        qbase = r * TLOC
        xT = np.ascontiguousarray(x[b, qbase:qbase + TLOC, :].T)

        hsl = slice(256 * g, 256 * (g + 1))
        Wqkv0_my = np.ascontiguousarray(
            np.concatenate([WattnT[:, 0:1024][:, hsl], WattnT[:, 1024:2048][:, hsl], WattnT[:, 2048:3072][:, hsl]], axis=1))
        bq_my = np.ascontiguousarray(b_attn[0:1024][hsl].reshape(2, 128).T)
        bk_my = np.ascontiguousarray(b_attn[1024:2048][hsl].reshape(2, 128).T)
        bv_my = np.ascontiguousarray(b_attn[2048:3072][hsl].reshape(1, 256))
        Wqkv3_my = np.ascontiguousarray(
            np.concatenate([WQT[:, hsl], WKT[:, hsl], WVT[:, hsl]], axis=1))

        cbc5 = np.zeros((128, 256), dtype=np.float32)
        for st_ in range(4):
            for kt in range(64):
                cbc5[:, st_ * 64 + kt] = qbase + st_ * 512 - kt * 128
        mask7 = np.zeros((128, 128), dtype=BF) if r == 0 else t_low

        m = {
            "x": xT,
            "Wqkv0_my": Wqkv0_my, "bq_my": bq_my, "bk_my": bk_my, "bv_my": bv_my,
            "WprojT": WprojT, "b_proj": _bias_cols(b_proj),
            "Wqkv3_my": Wqkv3_my, "WOT": WOT,
            "W0": W0, "A1": A1, "A2": A2, "b0": _bias_cols(b0),
            "W5": W5,
            "g7": _bias_cols(g7), "be7": _bias_cols(be7),
            "Wqkv7": Wqkv7, "Wout7": Wout7,
            "W6T": W6T, "sc6": _bias_cols(sc6), "b6": _bias_cols(b6),
            "f_rope": f_rope, "t_upp": t_upp, "t_low": t_low,
            "ident": ident, "pj": pj, "cbc5": cbc5, "mask7": mask7,
        }
        in_maps.append(m)
    return in_maps


SUB_INPUTS = {
    SUB_ATTA0: ["x", "Wqkv0_my", "bq_my", "bk_my", "bv_my", "WprojT", "b_proj", "f_rope", "t_upp"],
    SUB_FFN0A: ["x", "W0", "A1", "A2", "b0"],
    SUB_ATTB0: ["x", "Wqkv3_my", "WOT", "t_upp"],
    SUB_FFN0B: ["x", "W0", "A1", "A2", "b0"],
    SUB_ATT5:  ["x", "W5", "pj", "cbc5", "ident"],
    SUB_FFN1A: ["x", "W6T", "sc6", "b6"],
    SUB_ATT7:  ["x", "g7", "be7", "Wqkv7", "Wout7", "t_upp", "t_low", "mask7", "ident"],
    SUB_FFN1B: ["x", "W6T", "sc6", "b6"],
}




def _ap(t):
    return t if isinstance(t, bass.AP) else t.ap()

class Ctr:
    def __init__(self, sem):
        self.sem = sem
        self.v = 0

    def inc(self, n=1):
        self.v += n
        return self.v


class Prog:
    def __init__(self, name="prog"):
        self.nc = bass.Bass("TRN2", target_bir_lowering=False)
        self.ctx = ExitStack()
        self.name = name
        self.prefix = ""
        self._scope = None
        self._inputs = {}

    def _stack(self):
        return self._scope if self._scope is not None else self.ctx

    def push_scope(self, prefix):
        assert self._scope is None
        self._scope = ExitStack()
        self.prefix = prefix

    def pop_scope(self):
        self._scope.close()
        self._scope = None
        self.prefix = ""

    def sbuf(self, name, shape, dtype):
        return self._stack().enter_context(self.nc.sbuf_tensor(self.prefix + name, shape, dtype))

    def psum(self, name, shape, dtype=F32):
        return self._stack().enter_context(self.nc.psum_tensor(self.prefix + name, shape, dtype))

    def dram(self, name, shape, dtype):
        return self.nc.dram_tensor(self.prefix + name, shape, dtype)

    def in_ap(self, name, shape, dtype):
        if name not in self._inputs:
            self._inputs[name] = self.nc.dram_tensor(name, shape, dtype, kind="ExternalInput").ap()
        return self._inputs[name]

    def sem(self, name):
        return Ctr(self.ctx.enter_context(self.nc.semaphore(self.prefix + name)))

    def dma_sem(self, name=None):
        self._dma_sem_i = getattr(self, "_dma_sem_i", 0) + 1
        return Ctr(self.ctx.enter_context(
            self.nc.semaphore(self.prefix + (name or "sD") + str(self._dma_sem_i))))

    def start(self):
        self.block = self.ctx.enter_context(self.nc.Block())

    def finish(self):
        self.ctx.close()
        return self.nc


def common_sems(p):
    p.sPE = p.sem("sPE")
    p.sDV = p.sem("sDV")
    p.sAC = p.sem("sAC")
    p.sIO = p.sem("sIO")
    p.sST = p.dma_sem("sST")
    p.sCN = p.sem("sCN")



def emit_load_x(p, x_d, xres, after=None):
    if x_d is None:
        return (p.sDV.sem, 0)
    w = p.dma_sem("sX")

    @p.block.sync
    def _(sy):
        if after is not None:
            sy.wait_ge(after[0], after[1])
        sy.dma_start(out=xres[:, :, :], in_=x_d.rearrange("(c p) t -> p c t", p=128)).then_inc(w.sem, 16)
    w.inc(16)
    return (w.sem, w.v)


def emit_store_x(p, xres, xo_d, wait_dv=None):
    if xo_d is None:
        return None
    st = p.sST

    @p.block.sync
    def _(sy):
        if wait_dv is not None:
            sy.wait_ge(p.sDV.sem, wait_dv)
        sy.dma_start(out=xo_d.rearrange("(c p) t -> p c t", p=128), in_=xres[:, :, :]).then_inc(st.sem, 16)
        st.inc(16)
        sy.wait_ge(st.sem, st.v)
    return (st.sem, st.v)


def emit_consts(p):
    cn = p.sCN

    @p.block.gpsimd
    def _(g):
        g.memset(p.ones128_bf[:, :], 1.0)
        g.memset(p.ones1x128_f[:, :], 1.0).then_inc(cn.sem, 1)
    cn.inc(1)
    return cn.v


def emit_ln(p, xres, xn, xsq_scratch, tmp_f32, x_ready, extra=None):
    pe, dv, ac = p.sPE, p.sDV, p.sAC
    psBIG = p.psBIG
    xall = _ap(xres).rearrange("p c t -> p (c t)")
    xnall = _ap(xn).rearrange("p c t -> p (c t)")
    sqall = _ap(xsq_scratch).rearrange("p c t -> p (c t)")

    @p.block.vector
    def _(v):
        v.wait_ge(x_ready[0], x_ready[1])
        v.tensor_copy(xnall, xall).then_inc(dv.sem, 1)
        cast_done_ = dv.inc()
        v.wait_ge(dv.sem, cast_done_)
        v.tensor_mul(sqall, xnall, xnall).then_inc(dv.sem, 1)
        dv.inc()
    cast_done = dv.v - 1
    sq_done = dv.v

    @p.block.tensor
    def _(t):
        t.wait_ge(dv.sem, cast_done)
        t.wait_ge(p.sCN.sem, p.sCN.v)
        for c in range(NCH):
            for q in range(4):
                mm = t.matmul(psBIG[0:1, q * 512:(q + 1) * 512],
                              p.ones128_bf[:, :], xn[:, c, q * 512:(q + 1) * 512],
                              start=(c == 0), stop=(c == NCH - 1))
        mm.then_inc(pe.sem, 1)
    sx_done = pe.inc()

    @p.block.vector
    def _(v):
        v.wait_ge(pe.sem, sx_done)
        v.tensor_scalar_mul(p.r_m[:, :], psBIG[0:1, 0:2048], 1.0 / D).then_inc(dv.sem, 1)
    m_done = dv.inc()

    @p.block.tensor
    def _(t):
        t.wait_ge(dv.sem, max(sq_done, m_done))
        for c in range(NCH):
            for q in range(4):
                mm = t.matmul(psBIG[0:1, q * 512:(q + 1) * 512],
                              p.ones128_bf[:, :], xsq_scratch[:, c, q * 512:(q + 1) * 512],
                              start=(c == 0), stop=(c == NCH - 1))
        mm.then_inc(pe.sem, 1)
    sxx_done = pe.inc()

    @p.block.vector
    def _(v):
        v.wait_ge(pe.sem, sxx_done)
        v.tensor_scalar_mul(p.r_a[:, :], psBIG[0:1, 0:2048], 1.0 / D).then_inc(dv.sem, 1)
        dv.inc()
        v.tensor_mul(p.r_b[:, :], p.r_m[:, :], p.r_m[:, :]).then_inc(dv.sem, 1)
        v.wait_ge(dv.sem, dv.inc())
        v.tensor_sub(p.r_a[:, :], p.r_a[:, :], p.r_b[:, :]).then_inc(dv.sem, 1)
        v.wait_ge(dv.sem, dv.inc())
        v.tensor_scalar_add(p.r_a[:, :], p.r_a[:, :], EPS).then_inc(dv.sem, 1)
    vpe_done = dv.inc()

    @p.block.scalar
    def _(s):
        s.wait_ge(dv.sem, vpe_done)
        s.activation(p.r_b[:, :], p.r_a[:, :], AF.Sqrt).then_inc(ac.sem, 1)
    sqrt_done = ac.inc()

    @p.block.vector
    def _(v):
        v.wait_ge(ac.sem, sqrt_done)
        v.reciprocal(p.r_a[:, :], p.r_b[:, :]).then_inc(dv.sem, 1)
    rows_done = dv.inc()

    psum_free = rows_done
    for h in range(2):
        tok = slice(h * 1024, (h + 1) * 1024)

        @p.block.tensor
        def _(t):
            t.wait_ge(dv.sem, psum_free)
            for q in range(2):
                sl = slice(h * 1024 + q * 512, h * 1024 + (q + 1) * 512)
                t.matmul(psBIG[:, q * 512:(q + 1) * 512],
                         p.ones1x128_f[:, :], p.r_m[0:1, sl], start=True, stop=True)
                mm = t.matmul(psBIG[:, 1024 + q * 512:1024 + (q + 1) * 512],
                              p.ones1x128_f[:, :], p.r_a[0:1, sl], start=True, stop=True)
            mm.then_inc(pe.sem, 1)
        bc_done = pe.inc()

        @p.block.vector
        def _(v):
            v.wait_ge(pe.sem, bc_done)
            for c in range(NCH):
                v.tensor_sub(tmp_f32[:, 0:1024], xres[:, c, tok], psBIG[:, 0:1024]).then_inc(dv.sem, 1)
                v.wait_ge(dv.sem, dv.inc())
                v.tensor_mul(xn[:, c, tok], tmp_f32[:, 0:1024], psBIG[:, 1024:2048]).then_inc(dv.sem, 1)
                v.wait_ge(dv.sem, dv.inc())
        psum_free = dv.v

    if extra is not None:
        extra["m_row"] = p.r_m
        extra["rstd_row"] = p.r_a[:, :]
    return dv.v


def alloc_ln_rows(p):
    p.r_m = p.sbuf("r_m_t", [1, TLOC], F32).ap()
    p.r_a = p.sbuf("r_a", [1, TLOC], F32).ap()
    p.r_b = p.sbuf("r_b", [1, TLOC], F32).ap()


def emit_w_load(p, dram_ap, sbuf_t, n_din_ch, sem_ctr=None):
    w = sem_ctr if sem_ctr is not None else p.dma_sem()

    @p.block.gpsimd
    def _(g):
        g.dma_start(out=sbuf_t[:, :, :],
                    in_=dram_ap.rearrange("(k p) j -> p k j", p=128)).then_inc(w.sem, 16)
    w.inc(16)
    return (w.sem, w.v)


def emit_mm_dense(p, w_sb, x_sb, n_k, out_cb, wait_w=None, wait_dv=None,
                  m_chunks=None, psum_rot=None):
    pe, dv = p.sPE, p.sDV
    if m_chunks is None:
        m_chunks = range(NCH)
    slots = psum_rot
    n_slots = len(slots)
    evac_vals = [None] * n_slots

    it = 0
    for m in m_chunks:
        for q in range(4):
            slot = it % n_slots
            qsl = slice(q * 512, (q + 1) * 512)

            @p.block.tensor
            def _(t):
                if it == 0:
                    if wait_dv is not None:
                        t.wait_ge(dv.sem, wait_dv)
                    if wait_w is not None:
                        t.wait_ge(wait_w[0], wait_w[1])
                if evac_vals[slot] is not None:
                    t.wait_ge(dv.sem, evac_vals[slot])
                for k in range(n_k):
                    last = t.matmul(slots[slot], w_sb[:, k, m * 128:(m + 1) * 128], x_sb[:, k, qsl],
                                    start=(k == 0), stop=(k == n_k - 1))
                last.then_inc(pe.sem, 1)
            mm_done = pe.inc()
            evac_vals[slot] = out_cb(m, q, slots[slot], mm_done)
            it += 1




def _standalone(sub, prefix):
    p = Prog(prefix)
    x_d = p.nc.dram_tensor("x", [D, TLOC], F32, kind="ExternalInput").ap()
    xo_d = p.nc.dram_tensor("xo", [D, TLOC], F32, kind="ExternalOutput").ap()
    common_sems(p)
    p.xres = p.sbuf("xres_root", [128, NCH, TLOC], F32)
    p.start()
    p.push_scope(prefix)
    sub(p, x_d, xo_d)
    p.pop_scope()
    return p.finish()


def sub_ffn0(p, x_d, xo_d, after=None):
    nc = p.nc
    W0_d = p.in_ap("W0", [D, D], F32)
    A1_d = p.in_ap("A1", [D, D], F32)
    A2_d = p.in_ap("A2", [D, D], F32)
    b0_d = p.in_ap("b0", [128, NCH], F32)

    xres = p.xres
    xn = p.sbuf("xn", [128, NCH, TLOC], BF16)
    hA = p.sbuf("hA", [128, NCH, TLOC], BF16)
    w0 = p.sbuf("w0", [128, NCH, D], BF16)
    w1 = p.sbuf("w1", [128, NCH, D], BF16)
    w2 = p.sbuf("w2", [128, NCH, D], BF16)
    b0_sb = p.sbuf("b0_sb", [128, NCH], F32)
    tres = p.sbuf("tres", [128, 2, 512], F32)
    tmp_f32 = tres.ap().rearrange("p a b -> p (a b)")
    p.ones128_bf = p.sbuf("ones128_bf", [128, 1], BF16)
    p.ones1x128_f = p.sbuf("ones1x128_f", [1, 128], F32)
    alloc_ln_rows(p)
    p.psBIG = p.psum("psBIG", [128, 2048], F32)

    emit_consts(p)
    x_ready = emit_load_x(p, x_d, xres, after)
    swt = p.dma_sem("sWts")
    emit_w_load(p, W0_d, w0, NCH, swt)
    emit_w_load(p, A1_d, w1, NCH, swt)
    emit_w_load(p, A2_d, w2, NCH, swt)

    @p.block.gpsimd
    def _(g):
        g.dma_start(out=b0_sb[:, :], in_=b0_d).then_inc(swt.sem, 16)
    swt.inc(16)
    w_ready = (swt.sem, swt.v)
    b_ready = w_ready

    xn_ready = emit_ln(p, xres, xn, hA, tmp_f32, x_ready)

    done = emit_ffn0_body(p, xn, hA, w0, w1, w2, b0_sb, xres, tres,
                          xn_ready, w_ready, b_ready)

    return emit_store_x(p, xres, xo_d, wait_dv=done)


def emit_ffn0_body(p, xn, hA, w0, w1, w2, b0_sb, xres, tres, xn_ready, w_ready, b_ready):
    pe, dv, ac = p.sPE, p.sDV, p.sAC
    psBIG = p.psBIG
    slots = [psBIG[:, i * 512:(i + 1) * 512] for i in range(4)]

    def cb1(m, q, ps, mmv):
        @p.block.vector
        def _(v):
            v.wait_ge(pe.sem, mmv)
            v.tensor_copy(hA[:, m, q * 512:(q + 1) * 512], ps).then_inc(dv.sem, 1)
        return dv.inc()

    emit_mm_dense(p, w0, xn, NCH, cb1, wait_w=w_ready, wait_dv=xn_ready, psum_rot=slots)
    l1_done = dv.v

    def cb2(m, q, ps, mmv):
        @p.block.vector
        def _(v):
            v.wait_ge(pe.sem, mmv)
            v.tensor_copy(xn[:, m, q * 512:(q + 1) * 512], ps).then_inc(dv.sem, 1)
        return dv.inc()

    emit_mm_dense(p, w1, hA, NCH, cb2, wait_dv=l1_done, psum_rot=slots)
    l2_done = dv.v

    pe0_3 = pe.v
    ac0_3 = ac.v
    dv0_3 = dv.v

    @p.block.scalar
    def _(s):
        psl = [psBIG[:, i * 512:(i + 1) * 512] for i in range(4)]
        for i in range(32):
            m = i // 4
            s.wait_ge(pe.sem, pe0_3 + i + 1)
            if i == 0:
                s.wait_ge(b_ready[0], b_ready[1])
            if i >= 2:
                s.wait_ge(dv.sem, dv0_3 + i - 1)
            s.activation(tres[:, i % 2, :], psl[i % 4], AF.Tanh,
                         bias=b0_sb[:, m:m + 1]).then_inc(ac.sem, 1)
    ac.inc(32)

    tcnt = [0]

    def cb3(m, q, ps, mmv):
        i = tcnt[0]
        slot3 = i % 2
        acv = ac0_3 + i + 1

        @p.block.vector
        def _(v):
            v.wait_ge(ac.sem, acv)
            v.tensor_add(xres[:, m, q * 512:(q + 1) * 512],
                         xres[:, m, q * 512:(q + 1) * 512], tres[:, slot3, :]).then_inc(dv.sem, 1)
        tcnt[0] += 1
        val = dv.inc()
        assert val == dv0_3 + i + 1
        return val

    emit_mm_dense(p, w2, xn, NCH, cb3, wait_dv=l2_done, psum_rot=slots)
    return dv.v



def sub_ffn1(p, x_d, xo_d, after=None):
    nc = p.nc
    W6T_d = p.in_ap("W6T", [D, D], F32)
    sc_d = p.in_ap("sc6", [128, NCH], F32)
    b_d = p.in_ap("b6", [128, NCH], F32)

    xres = p.xres
    xn = p.sbuf("xn", [128, NCH, TLOC], BF16)
    scr = p.sbuf("scr", [128, NCH, TLOC], BF16)
    wf = p.sbuf("wf", [128, NCH, 1024], F32)
    ws = p.sbuf("ws", [128, NCH, D], BF16)
    sc_sb = p.sbuf("sc_sb", [128, NCH], F32)
    b_sb = p.sbuf("b_sb", [128, NCH], F32)
    tres = p.sbuf("tres", [128, 2, 512], F32)
    tmp_f32 = tres.ap().rearrange("p a b -> p (a b)")
    p.ones128_bf = p.sbuf("ones128_bf", [128, 1], BF16)
    p.ones1x128_f = p.sbuf("ones1x128_f", [1, 128], F32)
    alloc_ln_rows(p)
    p.psBIG = p.psum("psBIG", [128, 2048], F32)

    emit_consts(p)
    x_ready = emit_load_x(p, x_d, xres, after)

    swt = p.dma_sem("sWts")

    @p.block.sync
    def _(sy):
        sy.dma_start(out=wf[:, :, :], in_=W6T_d.rearrange("(k p) j -> p k j", p=128)).then_inc(swt.sem, 16)
        sy.dma_start(out=sc_sb[:, :], in_=sc_d).then_inc(swt.sem, 16)
        sy.dma_start(out=b_sb[:, :], in_=b_d).then_inc(swt.sem, 16)
    w_ready = (swt.sem, swt.inc(48))

    @p.block.vector
    def _(v):
        v.wait_ge(w_ready[0], w_ready[1])
        wfa = wf.ap().rearrange("p k j -> p (k j)")
        wsa = ws.ap().rearrange("p k j -> p (k j)")
        v.tensor_scalar(wsa, wfa, 0.0, 2.0, ALU.is_ge, ALU.mult).then_inc(p.sDV.sem, 1)
        v.wait_ge(p.sDV.sem, p.sDV.inc())
        v.tensor_scalar(wsa, wsa, 1.0, None, ALU.subtract).then_inc(p.sDV.sem, 1)
    sign_done = p.sDV.inc()

    xn_ready = emit_ln(p, xres, xn, scr, tmp_f32, x_ready)

    pe, dv = p.sPE, p.sDV
    slots = [p.psBIG[:, i * 512:(i + 1) * 512] for i in range(4)]

    def cb(m, q, ps, mmv):
        @p.block.vector
        def _(v):
            v.wait_ge(pe.sem, mmv)
            v.wait_ge(dv.sem, dv.v)
            tslice = tres[:, 0, :]
            v.tensor_scalar(tslice, ps, sc_sb[:, m:m + 1], b_sb[:, m:m + 1], ALU.mult, ALU.add).then_inc(dv.sem, 1)
            v.wait_ge(dv.sem, dv.inc())
            v.tensor_scalar(tslice, tslice, 100.0, -100.0, ALU.min, ALU.max).then_inc(dv.sem, 1)
            v.wait_ge(dv.sem, dv.inc())
            v.tensor_add(xres[:, m, q * 512:(q + 1) * 512],
                         xres[:, m, q * 512:(q + 1) * 512], tslice).then_inc(dv.sem, 1)
        return dv.inc()

    emit_mm_dense(p, ws, xn, NCH, cb, wait_dv=max(xn_ready, sign_done), psum_rot=slots)

    return emit_store_x(p, xres, xo_d, wait_dv=dv.v)



class DmaSlot:
    def __init__(self, p, name):
        self.ctr = p.dma_sem(name)

    def issue(self, eng, out, in_):
        if self.ctr.v > 0:
            eng.wait_ge(self.ctr.sem, self.ctr.v)
        eng.dma_start(out=out, in_=in_).then_inc(self.ctr.sem, 16)
        self.ctr.inc(16)
        return (self.ctr.sem, self.ctr.v)


def sub_att5(p, x_d, xo_d, after=None):
    nc = p.nc
    W5_d = p.in_ap("W5", [D, 3 * D], F32)
    pj_d = p.in_ap("pj", [128, 512], F32)
    cbc5_d = p.in_ap("cbc5", [128, 256], F32)
    ident_d = p.in_ap("ident", [128, 128], BF16)

    NROW = TLOC + 1
    cin = p.dram("cin5", [NROW, 2056], BF16)
    coutA = p.dram("cout5A", [GRP * 1024, 2056], BF16)
    coutB = p.dram("cout5B", [GRP * 1024, 2056], BF16)
    coutS = p.dram("cout5S", [GRP, 2056], BF16)
    cA_v = coutA.ap().rearrange("(r n) c -> r n c", r=GRP)
    cB_v = coutB.ap().rearrange("(r n) c -> r n c", r=GRP)

    def cout_v(rank, r0, csl):
        if r0 < 1024:
            return cA_v[rank, r0:r0 + 128, csl]
        return cB_v[rank, r0 - 1024:r0 - 1024 + 128, csl]
    KB = p.dram("KB5", [S, 1024], BF16)
    KB_v = KB.ap().rearrange("(t p) c -> t p c", p=128)

    xres = p.xres
    qfm = p.sbuf("qfm", [128, NCH, TLOC], BF16)
    arena2 = p.sbuf("arena2", [128, 4 * TLOC], F32)
    xn = arena2.ap().bitcast(BF16).rearrange("p (c t) -> p c t", c=NCH)
    O_sb = arena2.ap()[:, 0:NCH * 512].rearrange("p (c t) -> p c t", c=NCH)
    arena1 = p.sbuf("arena1", [128, 24576], BF16)
    w5 = arena1.ap().rearrange("p (k j) -> p k j", k=NCH)
    Vg = arena1.ap()[:, 0:16384].rearrange("p (g k t) -> p g k t", g=2, k=8)
    Pg = arena1.ap()[:, 16384:20480].rearrange("p (k t) -> p k t", k=8)
    Kt = arena1.ap()[:, 20480:22528].rearrange("p (s t) -> p s t", s=2)
    Praw = arena1.ap()[:, 22528:23552].rearrange("p (s t) -> p s t", s=2)
    qsq = arena1.ap()[:, 16384:20480].rearrange("p (a t) -> p a t", a=2)
    arena3 = p.sbuf("arena3", [128, 6144], F32)
    a3b = arena3.ap().bitcast(BF16)
    CT = a3b[:, 0:4112].rearrange("p (s c) -> p s c", s=2)
    sq_scr = a3b[:, 4112:5136]
    normc = a3b[:, 5136:5152]
    a2b = arena2.ap().bitcast(BF16)
    KBs = a2b[:, 10240:12288].rearrange("p (s c) -> p s c", s=2)
    Kt2 = a2b[:, 8192:10240].rearrange("p (s c) -> p s c", s=2)
    nm16f = p.sbuf("nm16f", [16, 1], F32)
    nm16b = p.sbuf("nm16b", [16, 1], BF16)
    srow = p.sbuf("srow", [1, 8], F32)
    srow_b = p.sbuf("srow_b", [1, 8], BF16)
    kms = p.sbuf("kms", [1, 4], BF16)
    negMb = p.sbuf("negMb", [128, 1], F32)
    rl = p.sbuf("rl", [1, 512], F32)
    pj_sb = p.sbuf("pj_sb", [128, 512], F32)
    cbc_sb = p.sbuf("cbc_sb", [128, 256], F32)
    ident = p.sbuf("ident_sb", [128, 128], BF16)
    tmp_f32 = arena3.ap()[:, 4096:5120]
    p.ones128_bf = p.sbuf("ones128_bf", [128, 1], BF16)
    p.ones1x128_f = p.sbuf("ones1x128_f", [1, 128], F32)
    p.r_m = arena3.ap()[0:1, 0:2048]
    p.r_a = arena3.ap()[0:1, 2048:4096]
    p.r_b = arena3.ap()[0:1, 4096:6144]

    p.psBIG = p.psum("psBIG", [128, 2048], F32)
    psE = p.psum("psE", [128, 1024], F32)
    psT_a = p.psum("psT_a", [128, 128], BF16)
    psT_b = p.psum("psT_b", [128, 128], BF16)
    psBIG = p.psBIG
    S_bank = [psBIG[:, 0:512], psBIG[:, 512:1024]]
    l_bank = psBIG[0:1, 1024:1536]
    O_bank = [psBIG[:, 1536:2048], psE[:, 0:512]]
    rlb_bank = psE[:, 512:1024]

    pe, dv, ac, io = p.sPE, p.sDV, p.sAC, p.sIO

    emit_consts(p)
    x_ready = emit_load_x(p, x_d, xres, after)
    swt = p.dma_sem("sWts")

    @p.block.gpsimd
    def _(g):
        g.dma_start(out=w5[:, :, :], in_=W5_d.rearrange("(k p) j -> p k j", p=128)).then_inc(swt.sem, 16)
        g.dma_start(out=pj_sb[:, :], in_=pj_d).then_inc(swt.sem, 16)
        g.dma_start(out=cbc_sb[:, :], in_=cbc5_d).then_inc(swt.sem, 16)
        g.dma_start(out=ident[:, :], in_=ident_d).then_inc(swt.sem, 16)
    w_ready = (swt.sem, swt.inc(64))

    xn_ready = emit_ln(p, xres, xn, qfm.ap().rearrange("p c t -> p c t"), tmp_f32, x_ready)

    slots = [psBIG[:, i * 512:(i + 1) * 512] for i in range(4)]

    def cb_q(m, q, ps, mmv):
        @p.block.vector
        def _(v):
            v.wait_ge(pe.sem, mmv)
            v.tensor_copy(qfm[:, m, q * 512:(q + 1) * 512], ps).then_inc(dv.sem, 1)
        return dv.inc()

    emit_mm_dense(p, w5, xn, NCH, cb_q, wait_w=w_ready, wait_dv=xn_ready, psum_rot=slots)
    q_done = dv.v

    co32 = p.sbuf("co32", [128, 1], F32)

    @p.block.gpsimd
    def _(g):
        g.memset(co32[:, :], 1.0 / 32.0)
        g.memset(srow[:, :], 0.0)
        g.wait_ge(dv.sem, xn_ready)
        g.memset(CT[:, 0, 2049:2056], 0.0)
        g.memset(CT[:, 1, 2049:2056], 0.0).then_inc(p.sCN.sem, 1)
    ct_pad_done = p.sCN.inc()

    ct_slot = [DmaSlot(p, f"sCT{i}") for i in range(2)]
    ct_out_ev = [None, None]
    ps_evac = [q_done] * 4
    for tt in range(16):
        sl = tt % 2
        tsl = slice(tt * 128, (tt + 1) * 128)
        copy_vals = []
        for j in range(4):
            @p.block.tensor
            def _(t):
                t.wait_ge(dv.sem, ps_evac[j])
                for k in range(NCH):
                    mm = t.matmul(slots[j], xn[:, k, tsl], w5[:, k, 1024 + j * 512:1024 + (j + 1) * 512],
                                  start=(k == 0), stop=(k == NCH - 1))
                mm.then_inc(pe.sem, 1)
            mmv = pe.inc()

            @p.block.vector
            def _(v):
                v.wait_ge(pe.sem, mmv)
                if tt < 2 and j == 0:
                    v.wait_ge(p.sCN.sem, ct_pad_done)
                if j == 0 and ct_out_ev[sl] is not None:
                    v.wait_ge(ct_out_ev[sl][0], ct_out_ev[sl][1])
                v.tensor_copy(CT[:, sl, j * 512:(j + 1) * 512], slots[j]).then_inc(dv.sem, 1)
            ps_evac[j] = dv.inc()
            copy_vals.append(ps_evac[j])

        @p.block.vector
        def _(v):
            v.wait_ge(dv.sem, copy_vals[-1])
            v.tensor_mul(sq_scr[:, :], CT[:, sl, 0:1024], CT[:, sl, 0:1024]).then_inc(dv.sem, 1)
            v.wait_ge(dv.sem, dv.inc())
            with nc.allow_low_precision(reason="k-norm row for softmax bound only"):
                v.tensor_reduce(CT[:, sl, 2048:2049], sq_scr[:, :], AX.X, ALU.add).then_inc(dv.sem, 1)
            v.wait_ge(dv.sem, dv.inc())
            v.tensor_copy(normc[:, tt:tt + 1], CT[:, sl, 2048:2049]).then_inc(dv.sem, 1)
        ct_stage_done = dv.inc()

        @p.block.sync
        def _(sy):
            sy.wait_ge(dv.sem, ct_stage_done)
            ct_out_ev[sl] = ct_slot[sl].issue(sy, cin.ap()[tsl, :], CT[:, sl, :])
        if tt == 7:
            @p.block.gpsimd
            def _(g, w0=ct_slot[0].ctr.v, w1=ct_slot[1].ctr.v):
                g.wait_ge(ct_slot[0].ctr.sem, w0)
                g.wait_ge(ct_slot[1].ctr.sem, w1)
                g.collective_compute(
                    "AllGather", ALU.bypass,
                    replica_groups=[[0, 1, 2, 3], [4, 5, 6, 7]],
                    ins=[cin.ap()[0:1024, :].opt()], outs=[coutA.ap().opt()],
                ).then_inc(io.sem, 1)
            cc_h = [io.inc(), None]
    kv_proj_done = dv.v

    @p.block.tensor
    def _(t):
        t.wait_ge(dv.sem, kv_proj_done)
        t.transpose(psT_a[0:16, 0:128], normc[:, 0:16], ident[:, :]).then_inc(pe.sem, 1)
    trn1 = pe.inc()

    @p.block.vector
    def _(v):
        v.wait_ge(pe.sem, trn1)
        v.tensor_reduce(nm16f[:, :], psT_a[0:16, 0:128], AX.X, ALU.max).then_inc(dv.sem, 1)
        v.wait_ge(dv.sem, dv.inc())
        v.tensor_copy(nm16b[:, :], nm16f[:, :]).then_inc(dv.sem, 1)
    nmb_done = dv.inc()

    @p.block.tensor
    def _(t):
        t.wait_ge(dv.sem, nmb_done)
        t.transpose(psT_b[0:1, 0:16], nm16b[0:16, 0:1], ident[0:16, 0:16]).then_inc(pe.sem, 1)
    trn2 = pe.inc()

    @p.block.vector
    def _(v):
        v.wait_ge(pe.sem, trn2)
        v.wait_ge(p.sCN.sem, ct_pad_done)
        v.tensor_reduce(srow[:, 1:2], psT_b[0:1, 0:16], AX.X, ALU.max).then_inc(dv.sem, 1)
        v.wait_ge(dv.sem, dv.inc())
        v.tensor_copy(srow_b[:, 0:8], srow[:, 0:8]).then_inc(dv.sem, 1)
    maxk_done = dv.inc()

    sc_slot = DmaSlot(p, "sCsc")

    @p.block.vector
    def _(v):
        v.wait_ge(dv.sem, maxk_done)
        if ct_out_ev[0] is not None:
            v.wait_ge(ct_out_ev[0][0], ct_out_ev[0][1])
        v.memset(CT[0:1, 0, 0:2056], 0.0).then_inc(dv.sem, 1)
        v.wait_ge(dv.sem, dv.inc())
        v.tensor_copy(CT[0:1, 0, 0:8], srow_b[:, 0:8]).then_inc(dv.sem, 1)
    scrow_done = dv.inc()

    @p.block.sync
    def _(sy):
        sy.wait_ge(dv.sem, scrow_done)
        sc_slot.issue(sy, cin.ap()[2048:2049, 0:2056], CT[0:1, 0, 0:2056])

    @p.block.gpsimd
    def _(g):
        for s in ct_slot:
            g.wait_ge(s.ctr.sem, s.ctr.v)
        g.collective_compute(
            "AllGather", ALU.bypass,
            replica_groups=[[0, 1, 2, 3], [4, 5, 6, 7]],
            ins=[cin.ap()[1024:2048, :].opt()], outs=[coutB.ap().opt()],
        ).then_inc(io.sem, 1)
    cc_h[1] = io.inc()

    @p.block.gpsimd
    def _(g):
        g.wait_ge(sc_slot.ctr.sem, sc_slot.ctr.v)
        g.collective_compute(
            "AllGather", ALU.bypass,
            replica_groups=[[0, 1, 2, 3], [4, 5, 6, 7]],
            ins=[cin.ap()[2048:2049, :].opt()], outs=[coutS.ap().opt()],
        ).then_inc(io.sem, 1)
    cc_st = io.inc()
    cc_done = cc_st

    qn_prev = None
    for c in range(NCH):
        @p.block.vector
        def _(v):
            v.wait_ge(dv.sem, q_done)
            if qn_prev is not None:
                v.wait_ge(pe.sem, qn_prev)
            v.tensor_mul(qsq[:, 0, :], qfm[:, c, :], qfm[:, c, :]).then_inc(dv.sem, 1)
        sqv = dv.inc()

        @p.block.tensor
        def _(t):
            t.wait_ge(dv.sem, sqv)
            if c == 0:
                t.wait_ge(dv.sem, kv_proj_done)
            for q in range(4):
                mm = t.matmul(psBIG[0:1, q * 512:(q + 1) * 512],
                              p.ones128_bf[:, :], qsq[:, 0, q * 512:(q + 1) * 512],
                              start=(c == 0), stop=(c == NCH - 1))
            mm.then_inc(pe.sem, 1)
        qn_prev = pe.inc()

    @p.block.vector
    def _(v):
        v.wait_ge(pe.sem, qn_prev)
        v.tensor_reduce(srow[:, 0:1], psBIG[0:1, 0:2048], AX.X, ALU.max).then_inc(dv.sem, 1)
    qn_done = dv.inc()

    km_slot = DmaSlot(p, "sKms")

    @p.block.sync
    def _(sy):
        sy.wait_ge(io.sem, cc_done)
        with nc.allow_non_contiguous_dma(reason="4 scalar max-norm values"):
            km_slot.issue(sy, kms[:, :], coutS.ap()[:, 1:2].rearrange("r b -> b r"))

    @p.block.vector
    def _(v):
        v.wait_ge(km_slot.ctr.sem, km_slot.ctr.v)
        v.wait_ge(dv.sem, qn_done)
        v.tensor_reduce(srow[:, 1:2], kms[:, :], AX.X, ALU.max).then_inc(dv.sem, 1)
        v.wait_ge(dv.sem, dv.inc())
        v.tensor_mul(srow[:, 2:3], srow[:, 0:1], srow[:, 1:2]).then_inc(dv.sem, 1)
    m_mul_done = dv.inc()

    @p.block.scalar
    def _(s):
        s.wait_ge(dv.sem, m_mul_done)
        s.activation(srow[:, 3:4], srow[:, 2:3], AF.Sqrt).then_inc(ac.sem, 1)
    m_sqrt_done = ac.inc()

    @p.block.vector
    def _(v):
        v.wait_ge(ac.sem, m_sqrt_done)
        v.tensor_scalar_mul(srow[:, 4:5], srow[:, 3:4], -1.0 / 32.0).then_inc(dv.sem, 1)
    negm_done = dv.inc()

    @p.block.tensor
    def _(t):
        t.wait_ge(dv.sem, negm_done)
        t.matmul(psE[:, 0:1], p.ones1x128_f[:, :], srow[0:1, 4:5], start=True, stop=True).then_inc(pe.sem, 1)
    negmb_mm = pe.inc()

    @p.block.vector
    def _(v):
        v.wait_ge(pe.sem, negmb_mm)
        v.tensor_copy(negMb[:, :], psE[:, 0:1]).then_inc(dv.sem, 1)
    negmb_done = dv.inc()

    kin_slot = [DmaSlot(p, f"sKin{i}") for i in range(2)]
    kb_out_slot = [DmaSlot(p, f"sKBo{i}") for i in range(2)]
    kt_consumed = [None, None]
    ct2_free = [None, None]
    psT = [psT_a, psT_b]
    psT_free = [None, None]
    kin_ev = [None, None]

    kbo_ev = [None] * 64

    def emit_preproc_group(kt_lo, kt_hi):
        for kt in range(kt_lo, kt_hi):
            sl = kt % 2
            rank = kt // 16
            r0 = (kt % 16) * 128

            @p.block.sync
            def _(sy, kt=kt, sl=sl, rank=rank, r0=r0):
                sy.wait_ge(io.sem, cc_h[(kt % 16) // 8])
                if kt_consumed[sl] is not None:
                    sy.wait_ge(pe.sem, kt_consumed[sl])
                kin_ev[sl] = kin_slot[sl].issue(sy, Kt[:, sl, :], cout_v(rank, r0, slice(0, 1024)))

            tr_vals = []
            for c2 in range(NCH):
                tb = c2 % 2

                @p.block.tensor
                def _(t, sl=sl, c2=c2, tb=tb):
                    if c2 == 0:
                        t.wait_ge(kin_ev[sl][0], kin_ev[sl][1])
                    if psT_free[tb] is not None:
                        t.wait_ge(dv.sem, psT_free[tb])
                    t.transpose(psT[tb][:, :], Kt[:, sl, c2 * 128:(c2 + 1) * 128], ident[:, :]).then_inc(pe.sem, 1)
                trv = pe.inc()
                if c2 == NCH - 1:
                    kt_consumed[sl] = trv

                @p.block.vector
                def _(v, sl=sl, c2=c2, tb=tb, trv=trv):
                    v.wait_ge(pe.sem, trv)
                    if c2 == 0 and kb_out_slot[sl].ctr.v > 0:
                        v.wait_ge(kb_out_slot[sl].ctr.sem, kb_out_slot[sl].ctr.v)
                    v.tensor_copy(KBs[:, sl, c2 * 128:(c2 + 1) * 128], psT[tb][:, :]).then_inc(dv.sem, 1)
                psT_free[tb] = dv.inc()
                tr_vals.append(psT_free[tb])

            @p.block.sync
            def _(sy, kt=kt, sl=sl, last=tr_vals[-1]):
                sy.wait_ge(dv.sem, last)
                kbo_ev[kt] = kb_out_slot[sl].issue(sy, KB_v[kt, :, 0:1024], KBs[:, sl, :])

    emit_preproc_group(0, 8)

    S_free = [negmb_done, negmb_done]
    S_exp = [None, None]
    praw_free = [None, None]
    O_free = [None, None]
    l_free = negmb_done
    rlb_free = None
    kt2_consumed = [None, None]
    vg_slot = [DmaSlot(p, f"sVg{i}") for i in range(2)]
    kin2_slot = [DmaSlot(p, f"sKin2{i}") for i in range(2)]
    vg_consumed = [None, None]
    kin2_ev = [None, None]
    vg_ev = [None, None]
    B_last = None

    for st in range(4):
        qsl = slice(st * 512, (st + 1) * 512)
        ktmin = 4 * st
        strip_first_grp = True
        for g in range(8):
            gsl = g % 2
            keep = [i for i in range(8) if g * 8 + i >= ktmin]
            if not keep:
                continue
            nk = len(keep)
            if st == 0 and g < 7:
                emit_preproc_group(8 * (g + 1), 8 * (g + 2))

            @p.block.gpsimd
            def _(gq, keep=keep):
                if st == 0 and g < 2:
                    gq.wait_ge(io.sem, cc_done)
                vs = vg_slot[gsl].ctr
                if vg_consumed[gsl] is not None:
                    gq.wait_ge(pe.sem, vg_consumed[gsl])
                    gq.wait_ge(vs.sem, vs.v)
                for i in keep:
                    kt = g * 8 + i
                    rank = kt // 16
                    r0 = (kt % 16) * 128
                    gq.wait_ge(io.sem, cc_h[(kt % 16) // 8])
                    gq.dma_start(out=Vg[:, gsl, i, :],
                                 in_=cout_v(rank, r0, slice(1024, 2048))).then_inc(vs.sem, 16)
                    vs.inc(16)
                vg_ev[gsl] = (vs.sem, vs.v)

            stt_vals = {}
            pe0 = pe.v
            ac0 = ac.v
            dv0 = dv.v
            carry_praw = list(praw_free)
            first_grp = (st == 0 and g == 0)

            qk_val5 = [pe0 + (1 if pos == 0 else 2 * pos) for pos in range(nk)]

            @p.block.scalar
            def _(s, keep=keep, qk_val5=qk_val5, dv0=dv0, carry=carry_praw, first=first_grp, g=g):
                for pos, i in enumerate(keep):
                    sb = (g * 8 + i) % 2
                    s.wait_ge(pe.sem, qk_val5[pos])
                    if first and pos == 0:
                        s.wait_ge(dv.sem, negmb_done)
                    if pos >= 2:
                        s.wait_ge(dv.sem, dv0 + pos - 1)
                    elif carry[sb] is not None:
                        s.wait_ge(dv.sem, carry[sb])
                    s.activation(Praw[:, sb, :], S_bank[sb], AF.Exp, bias=negMb[:, 0:1],
                                 scale=co32[:, 0:1]).then_inc(ac.sem, 1)
            ac.inc(nk)

            def emit_lmm5(j, g=g, ktmin=ktmin):
                kt_j = g * 8 + j

                @p.block.tensor
                def _(t):
                    t.wait_ge(dv.sem, stt_vals[j])
                    if kt_j == ktmin:
                        t.wait_ge(dv.sem, l_free)
                    t.matmul(l_bank, p.ones128_bf[:, :], Pg[:, j, :],
                             start=(kt_j == ktmin), stop=(kt_j == 63), skip_group_check=True).then_inc(pe.sem, 1)
                pe.inc()

            for pos, i in enumerate(keep):
                kt = g * 8 + i
                sl = kt % 2
                rank = kt // 16
                r0 = (kt % 16) * 128

                @p.block.sync
                def _(sy):
                    if kt2_consumed[sl] is not None:
                        sy.wait_ge(pe.sem, kt2_consumed[sl])
                    if st == 0:
                        sy.wait_ge(kbo_ev[kt][0], kbo_ev[kt][1])
                    kin2_ev[sl] = kin2_slot[sl].issue(sy, Kt2[:, sl, :], KB_v[kt, :, 0:1024])

                sb = kt % 2

                @p.block.tensor
                def _(t):
                    t.wait_ge(kin2_ev[sl][0], kin2_ev[sl][1])
                    if st == 0 and g == 0 and i == 0:
                        t.wait_ge(dv.sem, negmb_done)
                    if S_exp[sb] is not None:
                        t.wait_ge(ac.sem, S_exp[sb])
                    for c in range(NCH):
                        mm = t.matmul(S_bank[sb], Kt2[:, sl, c * 128:(c + 1) * 128], qfm[:, c, qsl],
                                      start=(c == 0), stop=(c == NCH - 1))
                    mm.then_inc(pe.sem, 1)
                qk_v = pe.inc()
                kt2_consumed[sl] = qk_v
                assert qk_v == qk_val5[pos]

                exp_v = ac0 + pos + 1
                S_exp[sb] = exp_v

                @p.block.vector
                def _(v):
                    v.wait_ge(ac.sem, exp_v)
                    if B_last is not None and pos == 0:
                        v.wait_ge(pe.sem, B_last)
                    v.scalar_tensor_tensor(Pg[:, i, :], pj_sb[:, :], cbc_sb[:, st * 64 + kt:st * 64 + kt + 1],
                                           Praw[:, sb, :], ALU.is_ge, ALU.mult).then_inc(dv.sem, 1)
                praw_free[sb] = dv.inc()
                assert praw_free[sb] == dv0 + pos + 1
                stt_vals[i] = praw_free[sb]

                if pos >= 1:
                    emit_lmm5(keep[pos - 1])
            emit_lmm5(keep[-1])

            for c in range(NCH):
                ob = c % 2

                @p.block.tensor
                def _(t, keep=keep):
                    if c == 0:
                        t.wait_ge(vg_ev[gsl][0], vg_ev[gsl][1])
                    if O_free[ob] is not None:
                        t.wait_ge(dv.sem, O_free[ob])
                    for pos, i in enumerate(keep):
                        mm = t.matmul(O_bank[ob], Vg[:, gsl, i, c * 128:(c + 1) * 128], Pg[:, i, :],
                                      start=(pos == 0), stop=(pos == len(keep) - 1), skip_group_check=True)
                    mm.then_inc(pe.sem, 1)
                pv_v = pe.inc()
                if c == NCH - 1:
                    vg_consumed[gsl] = pv_v
                    B_last = pv_v

                @p.block.vector
                def _(v, strip_first_grp=strip_first_grp):
                    v.wait_ge(pe.sem, pv_v)
                    if strip_first_grp:
                        v.tensor_copy(O_sb[:, c, :], O_bank[ob]).then_inc(dv.sem, 1)
                    else:
                        v.tensor_add(O_sb[:, c, :], O_sb[:, c, :], O_bank[ob]).then_inc(dv.sem, 1)
                O_free[ob] = dv.inc()
            strip_first_grp = False

        @p.block.vector
        def _(v):
            v.wait_ge(pe.sem, B_last)
            v.reciprocal(rl[:, :], l_bank).then_inc(dv.sem, 1)
        l_free = dv.inc()

        @p.block.tensor
        def _(t):
            t.wait_ge(dv.sem, l_free)
            if rlb_free is not None:
                t.wait_ge(dv.sem, rlb_free)
            t.matmul(rlb_bank, p.ones1x128_f[:, :], rl[0:1, :], start=True, stop=True).then_inc(pe.sem, 1)
        rlb_mm = pe.inc()

        fin_vals = []
        for c in range(NCH):
            @p.block.vector
            def _(v):
                v.wait_ge(pe.sem, rlb_mm)
                v.wait_ge(dv.sem, dv.v)
                v.tensor_mul(tmp_f32[:, 0:512], O_sb[:, c, :], rlb_bank).then_inc(dv.sem, 1)
                v.wait_ge(dv.sem, dv.inc())
                v.tensor_add(xres[:, c, qsl], xres[:, c, qsl], tmp_f32[:, 0:512]).then_inc(dv.sem, 1)
            fin_vals.append(dv.inc())
        rlb_free = fin_vals[-1]

    return emit_store_x(p, xres, xo_d, wait_dv=dv.v)




def sub_att7(p, x_d, xo_d, after=None):
    nc = p.nc
    g7_d = p.in_ap("g7", [128, NCH], F32)
    be7_d = p.in_ap("be7", [128, NCH], F32)
    Wqkv_d = p.in_ap("Wqkv7", [D, 3 * D], F32)
    Wout_d = p.in_ap("Wout7", [D, D], F32)
    tupp_d = p.in_ap("t_upp", [128, 128], BF16)
    tlow_d = p.in_ap("t_low", [128, 128], BF16)
    mask7_d = p.in_ap("mask7", [128, 128], BF16)
    ident_d = p.in_ap("ident", [128, 128], BF16)

    cKb = p.dram("cKb7", [D, 128], BF16)
    gKb = p.dram("gKb7", [GRP * D, 128], BF16)
    gKb_v = gKb.ap().rearrange("(r k p) j -> r p k j", r=GRP, p=128)
    cVb = p.dram("cVb7", [128, 1040], BF16)
    gVb = p.dram("gVb7", [GRP * 128, 1040], BF16)
    gVb_v = gVb.ap().rearrange("(r p) j -> r p j", r=GRP)
    cSb = p.dram("cSb7", [2, 8], BF16)
    gSb = p.dram("gSb7", [GRP * 2, 8], BF16)
    stb = p.dram("stb7", [2, 24], F32)

    xch = p.sbuf("xch", [128, 2, TLOC], F32)
    xnscr = p.sbuf("xnscr", [128, 2, TLOC], BF16)
    xr_b = p.xres.ap().rearrange("p c t -> p (c t)").bitcast(BF16)
    qfm = xr_b[:, 0:16384].rearrange("p (c t) -> p c t", c=NCH)
    kfm = xr_b[:, 16384:32768].rearrange("p (c t) -> p c t", c=NCH)
    V7t = p.sbuf("V7t", [128, 17, 1040], BF16)
    V7 = V7t.ap().rearrange("p w (pr two c) -> p w pr two c", pr=NCH, two=2)
    arenaO = p.sbuf("arenaO", [128, NCH * TLOC // 1], BF16)
    xn = arenaO.ap().rearrange("p (c t) -> p c t", c=NCH)
    O7 = arenaO.ap().rearrange("p (c t) -> p c t", c=NCH)
    wbuf = p.sbuf("wbuf", [128, NCH, 2048], BF16)
    g7_sb = p.sbuf("g7_sb", [128, NCH], F32)
    be7_sb = p.sbuf("be7_sb", [128, NCH], F32)
    kb7 = p.sbuf("kb7", [128, NCH, 128], BF16)
    P7 = p.sbuf("P7", [128, 4, 128], BF16)
    Praw7 = p.sbuf("Praw7", [128, 4, 128], BF16)
    tupp = p.sbuf("tupp_sb", [128, 128], BF16)
    tlow = p.sbuf("tlow_sb", [128, 128], BF16)
    mask7 = p.sbuf("mask7_sb", [128, 128], BF16)
    ident = p.sbuf("ident_sb", [128, 128], BF16)
    stats = p.sbuf("stats", [2, 24], F32)
    statrow = p.sbuf("statrow", [1, 48], F32)
    stb_b = p.sbuf("stb_b", [2, 24], BF16)
    srow7 = p.sbuf("srow7", [1, 32], F32)
    negM7b = p.sbuf("negM7b", [128, 16], F32)
    co8 = p.sbuf("co8", [128, 1], F32)
    lrow65 = p.sbuf("lrow65", [65, TLOC], F32)
    lrow = lrow65.ap()[0:1, :]
    ones_b64 = p.sbuf("ones_b64", [65, 64], F32)
    co_n1 = p.sbuf("co_n1", [65, 1], F32)
    onespat = p.sbuf("onespat", [128, 2], BF16)
    Ofp = None
    sqs = p.sbuf("sqs", [128, TLOC], BF16)
    p.ones128_bf = p.sbuf("ones128_bf", [128, 1], BF16)
    p.ones1x128_f = p.sbuf("ones1x128_f", [1, 128], F32)
    tmp_f32 = sqs.ap().bitcast(F32)
    v7f = V7t.ap().rearrange("p w c -> p (w c)").bitcast(F32)
    p.r_m = v7f[0:1, 0:2048]
    p.r_a = v7f[0:1, 2048:4096]
    p.r_b = v7f[0:1, 4096:6144]
    Ofp = xch.ap()[0:64, 0, :]

    p.psBIG = p.psum("psBIG", [128, 2048], F32)
    psF = p.psum("psF", [128, 2048], F32)
    psBIG = p.psBIG

    pe, dv, ac, io = p.sPE, p.sDV, p.sAC, p.sIO

    @p.block.gpsimd
    def _(g):
        g.memset(p.ones128_bf[:, :], 1.0)
        g.memset(p.ones1x128_f[:, :], 1.0)
        g.memset(co8[:, :], 0.125)
        g.memset(ones_b64[:, :], 1.0)
        g.memset(co_n1[:, :], -1.0)
        g.memset(onespat[:, :], 0.0).then_inc(p.sCN.sem, 1)
        g.wait_ge(p.sCN.sem, p.sCN.inc())
        g.memset(onespat[0:64, 0:1], 1.0)
        g.memset(onespat[64:128, 1:2], 1.0).then_inc(p.sCN.sem, 1)
    cn_done = p.sCN.inc()

    swt = p.dma_sem("sWts")

    @p.block.gpsimd
    def _(g):
        g.dma_start(out=wbuf[:, :, :], in_=Wqkv_d[:, 0:2048].rearrange("(k p) j -> p k j", p=128)).then_inc(swt.sem, 16)
        g.dma_start(out=g7_sb[:, :], in_=g7_d).then_inc(swt.sem, 16)
        g.dma_start(out=be7_sb[:, :], in_=be7_d).then_inc(swt.sem, 16)
        g.dma_start(out=tupp[:, :], in_=tupp_d).then_inc(swt.sem, 16)
        g.dma_start(out=tlow[:, :], in_=tlow_d).then_inc(swt.sem, 16)
        g.dma_start(out=mask7[:, :], in_=mask7_d).then_inc(swt.sem, 16)
        g.dma_start(out=ident[:, :], in_=ident_d).then_inc(swt.sem, 16)
    wqk_ready = (swt.sem, swt.inc(112))

    xsl = [DmaSlot(p, f"sXc{i}") for i in range(2)]
    x_ev = [None, None]
    cast_done = [None, None]
    stat_prev = [None, None]
    stat_mm = None
    for c in range(NCH):
        sl = c % 2

        @p.block.sync
        def _(sy):
            if c == 0 and after is not None:
                sy.wait_ge(after[0], after[1])
            if cast_done[sl] is not None:
                sy.wait_ge(dv.sem, cast_done[sl])
            x_ev[sl] = xsl[sl].issue(sy, xch[:, sl, :], x_d[bass.ds(c * 128, 128), :])

        @p.block.vector
        def _(v):
            v.wait_ge(x_ev[sl][0], x_ev[sl][1])
            v.tensor_copy(xn[:, c, :], xch[:, sl, :]).then_inc(dv.sem, 1)
            cast_done[sl] = dv.inc()
            v.wait_ge(dv.sem, cast_done[sl])
            if stat_prev[sl] is not None:
                v.wait_ge(pe.sem, stat_prev[sl])
            v.tensor_mul(xnscr[:, sl, :], xn[:, c, :], xn[:, c, :]).then_inc(dv.sem, 1)
        sq_done = dv.inc()

        @p.block.tensor
        def _(t):
            t.wait_ge(dv.sem, sq_done)
            if c == 0:
                t.wait_ge(p.sCN.sem, cn_done)
            for q in range(4):
                t.matmul(psBIG[0:1, q * 512:(q + 1) * 512], p.ones128_bf[:, :],
                         xn[:, c, q * 512:(q + 1) * 512], start=(c == 0), stop=(c == NCH - 1),
                         skip_group_check=True)
            for q in range(4):
                mm = t.matmul(psF[0:1, q * 512:(q + 1) * 512], p.ones128_bf[:, :],
                              xnscr[:, sl, q * 512:(q + 1) * 512], start=(c == 0), stop=(c == NCH - 1),
                              skip_group_check=True)
            mm.then_inc(pe.sem, 1)
        stat_mm = pe.inc()
        stat_prev[sl] = stat_mm

    @p.block.vector
    def _(v):
        v.wait_ge(pe.sem, stat_mm)
        v.tensor_scalar_mul(p.r_m[:, :], psBIG[0:1, 0:2048], 1.0 / D).then_inc(dv.sem, 1)
        v.tensor_scalar_mul(p.r_a[:, :], psF[0:1, 0:2048], 1.0 / D).then_inc(dv.sem, 1)
        v.wait_ge(dv.sem, dv.inc(2))
        v.tensor_mul(p.r_b[:, :], p.r_m[:, :], p.r_m[:, :]).then_inc(dv.sem, 1)
        v.wait_ge(dv.sem, dv.inc())
        v.tensor_sub(p.r_a[:, :], p.r_a[:, :], p.r_b[:, :]).then_inc(dv.sem, 1)
        v.wait_ge(dv.sem, dv.inc())
        v.tensor_scalar_add(p.r_a[:, :], p.r_a[:, :], EPS).then_inc(dv.sem, 1)
    vpe_done = dv.inc()

    @p.block.scalar
    def _(s):
        s.wait_ge(dv.sem, vpe_done)
        s.activation(p.r_b[:, :], p.r_a[:, :], AF.Sqrt).then_inc(ac.sem, 1)
    sq1 = ac.inc()

    @p.block.vector
    def _(v):
        v.wait_ge(ac.sem, sq1)
        v.reciprocal(p.r_a[:, :], p.r_b[:, :]).then_inc(dv.sem, 1)
        v.wait_ge(dv.sem, dv.inc())
        v.tensor_mul(p.r_b[:, :], p.r_a[:, :], p.r_a[:, :]).then_inc(dv.sem, 1)
        v.wait_ge(dv.sem, dv.inc())
        v.tensor_scalar(p.r_b[:, :], p.r_b[:, :], -EPS, 1.0 + EPS, ALU.mult, ALU.add).then_inc(dv.sem, 1)
    vt_done = dv.inc()

    @p.block.scalar
    def _(s):
        s.wait_ge(dv.sem, vt_done)
        s.activation(p.r_b[:, :], p.r_b[:, :], AF.Sqrt).then_inc(ac.sem, 1)
    sq2 = ac.inc()

    @p.block.vector
    def _(v):
        v.wait_ge(ac.sem, sq2)
        v.reciprocal(lrow[:, :], p.r_b[:, :]).then_inc(dv.sem, 1)
        v.wait_ge(dv.sem, dv.inc())
        v.tensor_mul(p.r_a[:, :], p.r_a[:, :], lrow[:, :]).then_inc(dv.sem, 1)
    shat_done = dv.inc()

    psum_free = shat_done
    for h in range(2):
        tok = slice(h * 1024, (h + 1) * 1024)

        @p.block.tensor
        def _(t):
            t.wait_ge(dv.sem, psum_free)
            for q in range(2):
                sl2 = slice(h * 1024 + q * 512, h * 1024 + (q + 1) * 512)
                t.matmul(psBIG[:, q * 512:(q + 1) * 512], p.ones1x128_f[:, :], p.r_m[0:1, sl2],
                         start=True, stop=True)
                mm = t.matmul(psBIG[:, 1024 + q * 512:1024 + (q + 1) * 512],
                              p.ones1x128_f[:, :], p.r_a[0:1, sl2], start=True, stop=True)
            mm.then_inc(pe.sem, 1)
        bc_done = pe.inc()

        @p.block.vector
        def _(v):
            v.wait_ge(pe.sem, bc_done)
            if h == 0:
                v.wait_ge(wqk_ready[0], wqk_ready[1])
            for c in range(NCH):
                v.tensor_sub(tmp_f32[:, 0:1024], xn[:, c, tok], psBIG[:, 0:1024]).then_inc(dv.sem, 1)
                v.wait_ge(dv.sem, dv.inc())
                v.tensor_mul(tmp_f32[:, 0:1024], tmp_f32[:, 0:1024], psBIG[:, 1024:2048]).then_inc(dv.sem, 1)
                v.wait_ge(dv.sem, dv.inc())
                v.tensor_scalar(xn[:, c, tok], tmp_f32[:, 0:1024], g7_sb[:, c:c + 1], be7_sb[:, c:c + 1],
                                ALU.mult, ALU.add).then_inc(dv.sem, 1)
                v.wait_ge(dv.sem, dv.inc())
        psum_free = dv.v
    u_ready = dv.v

    @p.block.gpsimd
    def _(g):
        g.wait_ge(dv.sem, u_ready)
        g.memset(V7t[:, :, :], 0.0).then_inc(p.sCN.sem, 1)
        g.wait_ge(p.sCN.sem, p.sCN.inc())
        g.memset(V7[:, :, :, :, 64:65], 1.0).then_inc(p.sCN.sem, 1)
    cn2_done = p.sCN.inc()

    slots = [psBIG[:, i * 512:(i + 1) * 512] for i in range(4)]

    def cb_q(m, q, ps, mmv):
        @p.block.vector
        def _(v):
            v.wait_ge(pe.sem, mmv)
            v.tensor_copy(qfm[:, m, q * 512:(q + 1) * 512], ps).then_inc(dv.sem, 1)
        return dv.inc()

    emit_mm_dense(p, wbuf.ap()[:, :, 0:1024].rearrange("p k j -> p k j"), xn, NCH, cb_q,
                  wait_w=wqk_ready, wait_dv=u_ready, psum_rot=slots)

    def cb_k(m, q, ps, mmv):
        @p.block.vector
        def _(v):
            v.wait_ge(pe.sem, mmv)
            v.tensor_copy(kfm[:, m, q * 512:(q + 1) * 512], ps).then_inc(dv.sem, 1)
        return dv.inc()

    emit_mm_dense(p, wbuf.ap()[:, :, 1024:2048].rearrange("p k j -> p k j"), xn, NCH, cb_k,
                  wait_dv=dv.v, psum_rot=slots)
    qk_done = dv.v

    @p.block.gpsimd
    def _(g):
        g.wait_ge(pe.sem, pe.v)
        g.dma_start(out=wbuf[:, :, 0:1024],
                    in_=Wqkv_d[:, 2048:3072].rearrange("(k p) j -> p k j", p=128)).then_inc(swt.sem, 16)
    wv_ready = (swt.sem, swt.inc(16))

    vslot_ev = [None] * 4
    for w in range(16):
        for pp in range(NCH):
            j = (w * NCH + pp) % 4

            @p.block.tensor
            def _(t):
                if w == 0 and pp == 0:
                    t.wait_ge(wv_ready[0], wv_ready[1])
                    t.wait_ge(dv.sem, qk_done)
                if vslot_ev[j] is not None:
                    t.wait_ge(dv.sem, vslot_ev[j])
                for k in range(NCH):
                    mm = t.matmul(slots[j][:, 0:128], xn[:, k, w * 128:(w + 1) * 128],
                                  wbuf[:, k, pp * 128:(pp + 1) * 128],
                                  start=(k == 0), stop=(k == NCH - 1))
                mm.then_inc(pe.sem, 1)
            mmv = pe.inc()

            @p.block.vector
            def _(v):
                v.wait_ge(pe.sem, mmv)
                if w == 0 and pp == 0:
                    v.wait_ge(p.sCN.sem, cn2_done)
                v.tensor_copy(V7[:, w + 1, pp, :, 0:64],
                              slots[j][:, 0:128].rearrange("p (two c) -> p two c", two=2)).then_inc(dv.sem, 1)
            vslot_ev[j] = dv.inc()
    vprime_done = dv.v

    hm_prev = [None]

    def emit_headmax(src_fm, col0, tokens_sl, extra_cols):
        for c in range(NCH):
            @p.block.vector
            def _(v):
                v.wait_ge(dv.sem, dv.v)
                if c == 0:
                    v.wait_ge(dv.sem, vprime_done)
                if hm_prev[0] is not None:
                    v.wait_ge(pe.sem, hm_prev[0])
                v.tensor_mul(sqs[:, 0:2048], src_fm[:, c, :], src_fm[:, c, :]).then_inc(dv.sem, 1)
            sqv = dv.inc()

            @p.block.tensor
            def _(t):
                t.wait_ge(dv.sem, sqv)
                for q in range(4):
                    mm = t.matmul(psF[0:2, q * 512:(q + 1) * 512], onespat[:, :],
                                  sqs[:, q * 512:(q + 1) * 512], start=True, stop=True)
                mm.then_inc(pe.sem, 1)
            nmv = pe.inc()
            hm_prev[0] = nmv

            @p.block.vector
            def _(v):
                v.wait_ge(pe.sem, nmv)
                v.tensor_reduce(stats[0:2, col0 + c:col0 + c + 1], psF[0:2, tokens_sl], AX.X, ALU.max).then_inc(dv.sem, 1)
                if extra_cols is not None:
                    v.wait_ge(dv.sem, dv.inc())
                    v.tensor_reduce(stats[0:2, extra_cols + c:extra_cols + c + 1],
                                    psF[0:2, 1920:2048], AX.X, ALU.max).then_inc(dv.sem, 1)
                    dv.inc()
                else:
                    dv.inc()

    emit_headmax(kfm, 0, slice(0, 2048), 16)
    emit_headmax(qfm, 8, slice(0, 2048), None)
    stats_done = dv.v

    @p.block.vector
    def _(v):
        v.wait_ge(dv.sem, stats_done)
        v.tensor_copy(stb_b[:, :], stats[:, :]).then_inc(dv.sem, 1)
    stb_done = dv.inc()

    bnd = DmaSlot(p, "sBnd")

    @p.block.sync
    def _(sy):
        sy.wait_ge(dv.sem, stb_done)
        for c in range(NCH):
            sy.dma_start(out=cKb.ap().rearrange("(k p) j -> p k j", p=128)[:, c, :],
                         in_=kfm[:, c, 1920:2048]).then_inc(bnd.ctr.sem, 16)
            bnd.ctr.inc(16)
        sy.dma_start(out=cVb.ap(), in_=V7t[:, 16, :]).then_inc(bnd.ctr.sem, 16)
        sy.dma_start(out=cSb.ap(), in_=stb_b[:, 0:8]).then_inc(bnd.ctr.sem, 16)
        bnd.ctr.inc(32)

    @p.block.gpsimd
    def _(g):
        g.wait_ge(bnd.ctr.sem, bnd.ctr.v)
        g.collective_compute("AllGather", ALU.bypass, replica_groups=[[0, 1, 2, 3], [4, 5, 6, 7]],
                             ins=[cKb.ap().opt()], outs=[gKb.ap().opt()]).then_inc(io.sem, 1)
        g.collective_compute("AllGather", ALU.bypass, replica_groups=[[0, 1, 2, 3], [4, 5, 6, 7]],
                             ins=[cVb.ap().opt()], outs=[gVb.ap().opt()]).then_inc(io.sem, 1)
        g.collective_compute("AllGather", ALU.bypass, replica_groups=[[0, 1, 2, 3], [4, 5, 6, 7]],
                             ins=[cSb.ap().opt()], outs=[gSb.ap().opt()]).then_inc(io.sem, 1)
    cc_done = io.inc(3)

    bin_ = DmaSlot(p, "sBin")

    @p.block.sync
    def _(sy):
        sy.wait_ge(io.sem, cc_done)
        pid = nc.partition_id()
        prev = (pid + 3) % 4
        prev4 = (pid % 4 + 3) % 4
        sy.dma_start(out=kb7[:, :, :], in_=gKb_v[prev4]).then_inc(bin_.ctr.sem, 16)
        sy.dma_start(out=V7t[:, 0, :], in_=gVb_v[prev4]).then_inc(bin_.ctr.sem, 16)
        bin_.ctr.inc(32)

    stb_slot = DmaSlot(p, "sStb")

    @p.block.sync
    def _(sy):
        sy.wait_ge(dv.sem, stb_done)
        sy.dma_start(out=stb.ap(), in_=stats[:, :]).then_inc(stb_slot.ctr.sem, 16)
        stb_slot.ctr.inc(16)
        sy.wait_ge(stb_slot.ctr.sem, stb_slot.ctr.v)
        sy.dma_start(out=statrow[0:1, 0:48], in_=stb.ap().rearrange("a b -> (a b)")).then_inc(stb_slot.ctr.sem, 16)
        stb_slot.ctr.inc(16)

    prevst = DmaSlot(p, "sPrevSt")

    @p.block.gpsimd
    def _(gq):
        gq.wait_ge(io.sem, cc_done)
        pid = nc.partition_id()
        prev4 = (pid % 4 + 3) % 4
        gq.dma_start(out=lrow[0:1, 0:16],
                     in_=gSb.ap().rearrange("(r a) b -> r (a b)", r=GRP)[bass.ds(prev4, 1)]).then_inc(prevst.ctr.sem, 16)
        prevst.ctr.inc(16)

    @p.block.vector
    def _(v):
        v.wait_ge(stb_slot.ctr.sem, stb_slot.ctr.v)
        v.wait_ge(prevst.ctr.sem, prevst.ctr.v)
        stv = statrow.ap()
        lv = lrow
        v.tensor_max(srow7[0:1, 0:8], stv[0:1, 0:8], lv[0:1, 0:8])
        v.tensor_max(srow7[0:1, 8:16], stv[0:1, 24:32], lv[0:1, 8:16]).then_inc(dv.sem, 1)
        v.wait_ge(dv.sem, dv.inc())
        v.tensor_mul(srow7[0:1, 0:8], srow7[0:1, 0:8], stv[0:1, 8:16])
        v.tensor_mul(srow7[0:1, 8:16], srow7[0:1, 8:16], stv[0:1, 32:40]).then_inc(dv.sem, 1)
    prod_done = dv.inc()

    @p.block.scalar
    def _(s):
        s.wait_ge(dv.sem, prod_done)
        s.activation(srow7[0:1, 16:32], srow7[0:1, 0:16], AF.Sqrt).then_inc(ac.sem, 1)
    msq = ac.inc()

    @p.block.vector
    def _(v):
        v.wait_ge(ac.sem, msq)
        v.tensor_scalar_mul(srow7[0:1, 0:16], srow7[0:1, 16:32], -0.125).then_inc(dv.sem, 1)
    negm_done = dv.inc()

    @p.block.tensor
    def _(t):
        t.wait_ge(dv.sem, negm_done)
        t.matmul(psF[:, 0:16], p.ones1x128_f[:, :], srow7[0:1, 0:16], start=True, stop=True).then_inc(pe.sem, 1)
    nbm = pe.inc()

    @p.block.vector
    def _(v):
        v.wait_ge(pe.sem, nbm)
        v.tensor_copy(negM7b[:, :], psF[:, 0:16]).then_inc(dv.sem, 1)
    negmb_done = dv.inc()

    Sslot = [psF[:, 0:128], psF[:, 512:640], psF[:, 1024:1152], psF[:, 1536:1664]]
    s_exp = [None] * 4
    p_used = [None] * 4
    o_evac = negmb_done
    prev_head = [None]
    for pp in range(NCH):
        for hh in range(2):
            head_col = hh * 8 + pp
            hsl = slice(hh * 64, hh * 64 + 64)
            pending = prev_head[0]

            pe0 = pe.v
            ac0 = ac.v
            dv0 = dv.v
            carry_pused = list(p_used)
            qk_val = [None] * 32
            pv_val = [None] * 32
            cnt = pe0
            for i in range(32):
                cnt += 1
                qk_val[i] = cnt
                if i == 0 and pending is not None:
                    cnt += 1
                    rbv_val = cnt
                if i >= 2:
                    cnt += 1
                    pv_val[i - 2] = cnt
            pv_val[30] = cnt + 1
            pv_val[31] = cnt + 2
            pv_flush = pv_val[31]
            if pending is not None:
                mul_val = dv0 + 1
                msk_val = [dv0 + i + 2 for i in range(32)]
                o_evac = mul_val
            else:
                msk_val = [dv0 + i + 1 for i in range(32)]
            acr = 2 if pending is not None else 0
            exp_val = [ac0 + acr + i + 1 for i in range(32)]
            recip_val = ac0 + 2 if pending is not None else None

            @p.block.scalar
            def _(s, pending=pending, qk_val=qk_val, pv_val=pv_val,
                  carry=carry_pused, head_col=head_col):
                if pending is not None:
                    s.wait_ge(pe.sem, pending["pv_flush"])
                    s.activation(lrow65[64:65, 0:TLOC], psBIG[64:65, 0:TLOC],
                                 AF.Ln).then_inc(ac.sem, 1)
                    s.wait_ge(ac.sem, ac.v + 1)
                    s.activation(lrow65[64:65, 0:TLOC], lrow65[64:65, 0:TLOC],
                                 AF.Exp, scale=co_n1[64:65, 0:1]).then_inc(ac.sem, 1)
                for i in range(32):
                    si = i % 4
                    s.wait_ge(pe.sem, qk_val[i])
                    if i >= 4:
                        s.wait_ge(pe.sem, pv_val[i - 4])
                    elif carry[si] is not None:
                        s.wait_ge(pe.sem, carry[si])
                    s.activation(Praw7[:, si, :], Sslot[si], AF.Exp,
                                 bias=negM7b[:, head_col:head_col + 1],
                                 scale=co8[:, 0:1]).then_inc(ac.sem, 1)
            ac.inc(32 + acr)

            def emit_pv7(j, pp=pp, hh=hh, msk_val=msk_val, pv_val=pv_val,
                         oe0=o_evac):
                w_, pv_ = j // 2, j % 2
                si = j % 4

                @p.block.tensor
                def _(t):
                    t.wait_ge(dv.sem, msk_val[j])
                    if j == 0:
                        t.wait_ge(dv.sem, oe0)
                    t.matmul(psBIG[0:65, w_ * 128:(w_ + 1) * 128],
                             V7[:, w_ + pv_, pp, hh, :], P7[:, si, :],
                             start=(pv_ == 0), stop=(pv_ == 1), skip_group_check=True).then_inc(pe.sem, 1)
                p_used[si] = pe.inc()
                assert p_used[si] == pv_val[j]

            for w in range(16):
                for pv in range(2):
                    i = w * 2 + pv
                    si = i % 4

                    @p.block.tensor
                    def _(t):
                        if pp == 0 and hh == 0 and w == 0 and pv == 0:
                            t.wait_ge(bin_.ctr.sem, bin_.ctr.v)
                            t.wait_ge(dv.sem, negmb_done)
                        if s_exp[si] is not None:
                            t.wait_ge(ac.sem, s_exp[si])
                        if pv == 0 and w == 0:
                            lhs = kb7[hsl, pp, :]
                        elif pv == 0:
                            lhs = kfm[hsl, pp, (w - 1) * 128:w * 128]
                        else:
                            lhs = kfm[hsl, pp, w * 128:(w + 1) * 128]
                        t.matmul(Sslot[si], lhs, qfm[hsl, pp, w * 128:(w + 1) * 128],
                                 start=True, stop=True).then_inc(pe.sem, 1)
                    qkv_ = pe.inc()
                    assert qkv_ == qk_val[i]

                    if i == 0 and pending is not None:
                        @p.block.tensor
                        def _(t, pending=pending, recip_val=recip_val):
                            t.wait_ge(ac.sem, recip_val)
                            t.wait_ge(dv.sem, pending["copy_v"])
                            for q in range(4):
                                mm = t.matmul(psBIG[0:64, q * 512:(q + 1) * 512],
                                              ones_b64[64:65, 0:64],
                                              lrow65[64:65, q * 512:(q + 1) * 512],
                                              start=True, stop=True)
                            mm.then_inc(pe.sem, 1)
                        rv = pe.inc()
                        assert rv == rbv_val

                        @p.block.vector
                        def _(v, pending=pending, rv=rv):
                            v.wait_ge(pe.sem, rv)
                            v.tensor_mul(O7[pending["hsl"], pending["pp"], :], Ofp[:, :],
                                         psBIG[0:64, 0:2048]).then_inc(dv.sem, 1)
                        mv = dv.inc()
                        assert mv == mul_val

                    expv = exp_val[i]
                    s_exp[si] = expv

                    @p.block.vector
                    def _(v):
                        v.wait_ge(ac.sem, expv)
                        msk = tupp if pv == 1 else (mask7 if w == 0 else tlow)
                        v.tensor_mul(P7[:, si, :], Praw7[:, si, :], msk[:, :]).then_inc(dv.sem, 1)
                    mskv = dv.inc()
                    assert mskv == msk_val[i]

                    if i >= 2:
                        emit_pv7(i - 2)
            emit_pv7(30)
            emit_pv7(31)

            @p.block.vector
            def _(v, pv_flush=pv_flush):
                v.wait_ge(pe.sem, pv_flush)
                v.tensor_copy(Ofp[:, :], psBIG[0:64, 0:2048]).then_inc(dv.sem, 1)
            copy_v = dv.inc()
            prev_head[0] = {"hsl": hsl, "pp": pp, "pv_flush": pv_flush,
                            "copy_v": copy_v}

    pending = prev_head[0]

    @p.block.scalar
    def _(s, pending=pending):
        s.wait_ge(pe.sem, pending["pv_flush"])
        s.activation(lrow65[64:65, 0:TLOC], psBIG[64:65, 0:TLOC],
                     AF.Ln).then_inc(ac.sem, 1)
        s.wait_ge(ac.sem, ac.v + 1)
        s.activation(lrow65[64:65, 0:TLOC], lrow65[64:65, 0:TLOC],
                     AF.Exp, scale=co_n1[64:65, 0:1]).then_inc(ac.sem, 1)
    ac.inc()
    recip_val = ac.inc()

    @p.block.tensor
    def _(t):
        t.wait_ge(ac.sem, recip_val)
        t.wait_ge(dv.sem, pending["copy_v"])
        for q in range(4):
            mm = t.matmul(psBIG[0:64, q * 512:(q + 1) * 512], ones_b64[64:65, 0:64],
                          lrow65[64:65, q * 512:(q + 1) * 512], start=True, stop=True)
        mm.then_inc(pe.sem, 1)
    rv = pe.inc()

    @p.block.vector
    def _(v):
        v.wait_ge(pe.sem, rv)
        v.tensor_mul(O7[pending["hsl"], pending["pp"], :], Ofp[:, :],
                     psBIG[0:64, 0:2048]).then_inc(dv.sem, 1)
    dv.inc()
    attn_done = dv.v


    @p.block.gpsimd
    def _(g):
        g.wait_ge(pe.sem, pe.v)
        g.dma_start(out=wbuf[:, :, 1024:2048],
                    in_=Wout_d.rearrange("(k p) j -> p k j", p=128)).then_inc(swt.sem, 16)
    wo_ready = (swt.sem, swt.inc(16))

    xr_ev = [None, None]
    xuse = [None, None]
    it2 = [0]

    def cb_out(m, q, ps, mmv):
        sl = it2[0] % 2

        @p.block.sync
        def _(sy):
            sy.wait_ge(dv.sem, xuse[sl] if xuse[sl] is not None else attn_done)
            xr_ev[sl] = xsl[sl].issue(sy, xch[:, sl, 0:512], x_d[bass.ds(m * 128, 128), q * 512:(q + 1) * 512])

        @p.block.vector
        def _(v):
            v.wait_ge(pe.sem, mmv)
            v.wait_ge(xr_ev[sl][0], xr_ev[sl][1])
            v.scalar_tensor_tensor(xch[:, sl, 512:1024][:, 0:512], ps, 0.0, xch[:, sl, 0:512],
                                   ALU.add, ALU.add).then_inc(dv.sem, 1)
        outv = dv.inc()
        xuse[sl] = outv

        @p.block.sync
        def _(sy):
            sy.wait_ge(dv.sem, outv)
            xsl[sl].issue(sy, xo_d[bass.ds(m * 128, 128), q * 512:(q + 1) * 512], xch[:, sl, 512:1024][:, 0:512])
        it2[0] += 1
        return outv

    emit_mm_dense(p, wbuf.ap()[:, :, 1024:2048].rearrange("p k j -> p k j"), O7, NCH, cb_out,
                  wait_w=wo_ready, wait_dv=attn_done, psum_rot=slots)

    @p.block.sync
    def _(sy):
        for s in xsl:
            sy.wait_ge(s.ctr.sem, s.ctr.v)
    ev = (xsl[0].ctr.sem, xsl[0].ctr.v)
    return [(s.ctr.sem, s.ctr.v) for s in xsl]



def sub_att02(p, x_d, xo_d, rope_bias, after=None):
    nc = p.nc
    wname = "Wqkv0_my" if rope_bias else "Wqkv3_my"
    Wqkv_d = p.in_ap(wname, [D, 768], F32)
    if rope_bias:
        bq_d = p.in_ap("bq_my", [128, 2], F32)
        bk_d = p.in_ap("bk_my", [128, 2], F32)
        bv_d = p.in_ap("bv_my", [1, 256], F32)
        Wp_d = p.in_ap("WprojT", [D, D], F32)
        bp_d = p.in_ap("b_proj", [128, NCH], F32)
        f_d = p.in_ap("f_rope", [128, S], F32)
    else:
        Wp_d = p.in_ap("WOT", [D, D], F32)
    tupp_d = p.in_ap("t_upp", [128, 128], BF16)

    cXn = p.dram("cXn", [2, D, TLOC // 2], BF16)
    gXn = p.dram("gXn", [2, GRP * D, TLOC // 2], BF16)
    gXn_v = gXn.ap().rearrange("h (r c p) t -> h r p c t", r=GRP, p=128)
    cO = p.dram("cO", [4, 4 * HD, TLOC], BF16)
    gO = p.dram("gO", [4, GRP * 4 * HD, TLOC], BF16)
    gO_v = gO.ap().rearrange("ch (k p) t -> ch p k t", p=128)
    shb = None

    xres = p.xres
    Q2t = p.sbuf("Q2t", [128, 2, S], BF16)
    xn = Q2t.ap().rearrange("p a s -> p (a s)").rearrange("p (c t) -> p c t", c=NCH)
    Q2 = Q2t.ap()
    K2t = p.sbuf("K2t", [128, 2, S], BF16)
    K2 = K2t.ap()
    V2t = p.sbuf("V2t", [128, 2, 64, 130], BF16)
    V2 = V2t.ap()
    v2f = V2t.ap().rearrange("p a k c -> p (a k c)").bitcast(F32)
    p.r_m = v2f[0:1, 0:2048]
    p.r_a = v2f[0:1, 2048:4096]
    p.r_b = v2f[0:1, 4096:6144]
    XG = p.sbuf("XG", [128, NCH, 1024], BF16)
    wq = p.sbuf("wq", [128, NCH, 768], BF16)
    P4 = p.sbuf("P4", [128, 2, 512], BF16)
    tmpf = P4.ap().rearrange("p a t -> p (a t)").bitcast(F32)[:, 0:512]
    if rope_bias:
        f_bf = p.sbuf("f_bf", [128, S], BF16)
        WP = f_bf.ap().rearrange("p (k j) -> p k j", k=NCH)
        bq_sb = p.sbuf("bq_sb", [128, 2], F32)
        bk_sb = p.sbuf("bk_sb", [128, 2], F32)
        bv_sb = P4.ap().rearrange("p a t -> p (a t)")[0:1, 512:768]
        bp_sb = p.sbuf("bp_sb", [128, NCH], F32)

    else:
        WP = p.sbuf("WP", [128, NCH, 1024], BF16).ap()
    mx = p.sbuf("mx", [2, 8], F32)
    shb = p.dram("shb02", [2, S], F32)
    scr512 = P4.ap().rearrange("p a t -> p (a t)")[:, 0:512]
    shst = None
    onesb = p.sbuf("onesb", [128, 128], BF16)
    co8 = p.sbuf("co8_02", [128, 1], F32)
    rl = None
    tupp = wq.ap().rearrange("p k j -> p (k j)")[:, 0:128]
    p.ones128_bf = p.sbuf("ones128_bf", [128, 1], BF16)
    p.ones1x128_f = p.sbuf("ones1x128_f", [1, 128], F32)
    onespat = p.sbuf("onespat02", [128, 2], BF16)

    tmp_f32 = K2t.ap().rearrange("p a s -> p (a s)").bitcast(F32)[:, 0:1024]
    xg_flat = XG.ap().rearrange("p c t -> p (c t)")
    wq_flat_b = wq.ap().rearrange("p k j -> p (k j)")
    wq_flat_f = wq_flat_b.bitcast(F32)
    rl = wq_flat_f[0:1, 1792:2304]
    Ost = wq_flat_f[0:64, 256:768]
    Ostb = wq_flat_b[0:64, 4608:5120]
    shst = wq_flat_f[0:2, 2560:3072]
    OQ = Q2t.ap().rearrange("p a s -> p (a s)").rearrange("p (c t) -> p c t", c=NCH)

    p.psBIG = p.psum("psBIG", [128, 2048], F32)
    psS = p.psum("psS", [128, 1024], F32)
    psO = p.psum("psO", [128, 512], F32)
    psV = p.psum("psV", [128, 512], F32)
    psBIG = p.psBIG

    pe, dv, ac, io = p.sPE, p.sDV, p.sAC, p.sIO

    @p.block.gpsimd
    def _(g):
        g.memset(p.ones128_bf[:, :], 1.0)
        g.memset(p.ones1x128_f[:, :], 1.0)
        g.memset(co8[:, :], 0.125)
        g.memset(onesb[:, :], 1.0)
        g.memset(onespat[:, :], 0.0).then_inc(p.sCN.sem, 1)
        g.wait_ge(p.sCN.sem, p.sCN.inc())
        g.memset(onespat[0:64, 0:1], 1.0)
        g.memset(onespat[64:128, 1:2], 1.0).then_inc(p.sCN.sem, 1)
    cn_done = p.sCN.inc()

    x_ready = emit_load_x(p, x_d, xres, after)
    swt = p.dma_sem("sWts")

    @p.block.gpsimd
    def _(g):
        g.dma_start(out=wq[:, :, :], in_=Wqkv_d.rearrange("(k p) j -> p k j", p=128)).then_inc(swt.sem, 16)
        n = 16
        if rope_bias:
            g.dma_start(out=bq_sb[:, :], in_=bq_d).then_inc(swt.sem, 16)
            g.dma_start(out=bk_sb[:, :], in_=bk_d).then_inc(swt.sem, 16)
            g.dma_start(out=bv_sb[:, :], in_=bv_d).then_inc(swt.sem, 16)
            g.dma_start(out=bp_sb[:, :], in_=bp_d).then_inc(swt.sem, 16)
            g.dma_start(out=f_bf[:, :], in_=f_d).then_inc(swt.sem, 16)
            n += 80
        swt.inc(n)
    w_ready = (swt.sem, swt.v)

    xn_ready = emit_ln(p, xres, xn, K2t.ap().rearrange("p a s -> p (a s)").rearrange("p (c t) -> p c t", c=NCH), tmp_f32, x_ready)

    cxn_sl = [DmaSlot(p, f"sCxn{h}") for h in range(2)]
    ccx_half = [None, None]
    for h in range(2):
        @p.block.sync
        def _(sy, h=h):
            sy.wait_ge(dv.sem, xn_ready)
            cxn_sl[h].issue(sy, cXn.ap()[h].rearrange("(c p) t -> p c t", p=128),
                            xn[:, :, h * 1024:(h + 1) * 1024])

        @p.block.gpsimd
        def _(g, h=h):
            g.wait_ge(cxn_sl[h].ctr.sem, cxn_sl[h].ctr.v)
            g.collective_compute("AllGather", ALU.bypass, replica_groups=[[0, 1, 2, 3], [4, 5, 6, 7]],
                                 ins=[cXn.ap()[h].opt()], outs=[gXn.ap()[h].opt()]).then_inc(io.sem, 1)
        ccx_half[h] = io.inc()
    ccx_done = ccx_half[1]

    @p.block.gpsimd
    def _(g):
        g.wait_ge(dv.sem, xn_ready)
        g.memset(V2t[:, :, :, :], 0.0).then_inc(p.sCN.sem, 1)
        g.wait_ge(p.sCN.sem, p.sCN.inc())
        g.memset(V2[:, :, :, 64:65], 1.0)
        g.memset(V2[:, :, :, 129:130], 1.0).then_inc(p.sCN.sem, 1)
    vones_done = p.sCN.inc()

    xg_s = p.dma_sem("sXg")
    slots6 = [psBIG[:, i * 512:(i + 1) * 512] for i in range(4)] + [psS[:, 0:512], psS[:, 512:1024]]
    ps_free = [xn_ready] * 6
    vt_free = [None]
    xg_used = None
    for RH in range(2 * GRP):
        half, R = RH // GRP, RH % GRP
        @p.block.sync
        def _(sy):
            sy.wait_ge(io.sem, ccx_half[half])
            if RH > 0:
                sy.wait_ge(pe.sem, xg_used)
            if xg_s.v > 0:
                sy.wait_ge(xg_s.sem, xg_s.v)
            sy.dma_start(out=XG[:, :, :], in_=gXn_v[half, R]).then_inc(xg_s.sem, 16)
        xg_s.inc(16)
        xg_ev = (xg_s.sem, xg_s.v)

        for s4 in range(2):
            tsl = slice(s4 * 512, (s4 + 1) * 512)
            gsl = slice(R * 2048 + half * 1024 + s4 * 512, R * 2048 + half * 1024 + (s4 + 1) * 512)
            for j in range(4):
                @p.block.tensor
                def _(t):
                    if RH == 0 and s4 == 0 and j == 0:
                        t.wait_ge(w_ready[0], w_ready[1])
                    t.wait_ge(xg_ev[0], xg_ev[1])
                    t.wait_ge(dv.sem, ps_free[j])
                    for k in range(NCH):
                        mm = t.matmul(slots6[j], wq[:, k, j * 128:(j + 1) * 128], XG[:, k, tsl],
                                      start=(k == 0), stop=(k == NCH - 1))
                    mm.then_inc(pe.sem, 1)
                mmv = pe.inc()

                @p.block.vector
                def _(v):
                    v.wait_ge(pe.sem, mmv)
                    pj = j % 2
                    if j < 2:
                        dst = Q2[:, pj, gsl]
                        bias = bq_sb[:, pj:pj + 1] if rope_bias else None
                    else:
                        dst = K2[:, pj, gsl]
                        bias = bk_sb[:, pj:pj + 1] if rope_bias else None
                    if rope_bias:
                        v.scalar_tensor_tensor(dst, slots6[j], bias, f_bf[:, gsl],
                                               ALU.add, ALU.mult).then_inc(dv.sem, 1)
                    else:
                        v.tensor_copy(dst, slots6[j]).then_inc(dv.sem, 1)
                ps_free[j] = dv.inc()

        for tt in range(8):
            kt = R * 16 + half * 8 + tt
            ttsl = slice(tt * 128, (tt + 1) * 128)
            for pj in range(2):
                @p.block.tensor
                def _(t):
                    if vt_free[0] is not None:
                        t.wait_ge(dv.sem, vt_free[0])
                    if RH == 0 and tt == 0 and pj == 0:
                        t.wait_ge(p.sCN.sem, vones_done)
                    for k in range(NCH):
                        last_k = (k == NCH - 1) and not rope_bias
                        mm = t.matmul(psV[:, 0:128], XG[:, k, ttsl],
                                      wq[:, k, 512 + pj * 128:512 + (pj + 1) * 128],
                                      start=(k == 0), stop=last_k)
                    if rope_bias:
                        mm = t.matmul(psV[:, 0:128], onesb[0:1, 0:128],
                                      bv_sb[0:1, pj * 128:(pj + 1) * 128],
                                      start=False, stop=True)
                    mm.then_inc(pe.sem, 1)
                mmv2 = pe.inc()

                @p.block.vector
                def _(v):
                    v.wait_ge(pe.sem, mmv2)
                    v.tensor_copy(V2[:, pj, kt, 0:64], psV[:, 0:64])
                    v.tensor_copy(V2[:, pj, kt, 65:129], psV[:, 64:128]).then_inc(dv.sem, 1)
                vt_free[0] = dv.inc()
        xg_used = pe.v
    proj_done = dv.v

    nrm_prev = [None]

    def headmax02(src2, col0):
        for pj in range(2):
            for seg in range(4):
                ssl = slice(seg * 2048, (seg + 1) * 2048)

                @p.block.vector
                def _(v):
                    v.wait_ge(dv.sem, dv.v)
                    if nrm_prev[0] is not None:
                        v.wait_ge(pe.sem, nrm_prev[0])
                    if pj == 0 and seg == 0 and col0 == 0:
                        v.wait_ge(dv.sem, proj_done)
                    v.tensor_mul(xg_flat[:, 0:2048],
                                 src2[:, pj, ssl], src2[:, pj, ssl]).then_inc(dv.sem, 1)
                sqv = dv.inc()

                for q in range(4):
                    first = (seg == 0 and q == 0)

                    @p.block.tensor
                    def _(t):
                        t.wait_ge(dv.sem, sqv if q == 0 else dv.v)
                        t.matmul(psO[0:2, 0:512], onespat[:, :],
                                 xg_flat[:, q * 512:(q + 1) * 512],
                                 start=True, stop=True).then_inc(pe.sem, 1)
                    nmv = pe.inc()
                    nrm_prev[0] = nmv

                    @p.block.vector
                    def _(v):
                        v.wait_ge(pe.sem, nmv)
                        if first:
                            v.tensor_reduce(mx[0:2, col0 + pj:col0 + pj + 1], psO[0:2, 0:512],
                                            AX.X, ALU.max).then_inc(dv.sem, 1)
                            dv.inc()
                        else:
                            v.tensor_reduce(mx[0:2, 4 + col0 + pj:4 + col0 + pj + 1], psO[0:2, 0:512],
                                            AX.X, ALU.max).then_inc(dv.sem, 1)
                            v.wait_ge(dv.sem, dv.inc())
                            v.tensor_max(mx[0:2, col0 + pj:col0 + pj + 1], mx[0:2, col0 + pj:col0 + pj + 1],
                                         mx[0:2, 4 + col0 + pj:4 + col0 + pj + 1]).then_inc(dv.sem, 1)
                            dv.inc()

    headmax02(K2, 0)
    mx_done = dv.v

    tup_s = p.dma_sem("sTup")

    @p.block.gpsimd
    def _(g):
        g.wait_ge(dv.sem, proj_done)
        g.dma_start(out=tupp, in_=tupp_d).then_inc(tup_s.sem, 16)
    tup_s.inc(16)

    @p.block.gpsimd
    def _(g):
        if rope_bias:
            g.wait_ge(dv.sem, proj_done)
        g.dma_start(out=WP, in_=Wp_d.rearrange("(k p) j -> p k j", p=128)).then_inc(swt.sem, 16)
    wp_ready = (swt.sem, swt.inc(16))

    xg_flat2 = XG.ap().rearrange("p c t -> p (c t)")
    Qp = xg_flat2[:, 0:2048].rearrange("p (s h t) -> p s h t", s=2, h=2)
    Kp = xg_flat2[:, 2048:3072].rearrange("p (s k) -> p s k", s=8)
    P4a = [P4.ap()[:, 0, :], P4.ap()[:, 1, :], xg_flat2[:, 3072:3584]]
    qS = [[DmaSlot(p, f"sQp{sl}{j}") for j in range(4)] for sl in range(2)]
    kS = [DmaSlot(p, f"sKp{i}") for i in range(8)]
    kb_ev = [None] * 8
    kp_lastuse = [None] * 8
    qp_lastuse = [None, None]
    qp_ready = [None, None]
    Sslot = [psBIG[:, 0:512], psBIG[:, 512:1024], psBIG[:, 1536:2048]]
    psBB = psBIG[:, 1024:1536]

    def emit_og_chunk(c):
        @p.block.gpsimd
        def _(g, c=c):
            g.wait_ge(cow.ctr.sem, cow.ctr.v)
            g.collective_compute("AllGather", ALU.bypass,
                                 replica_groups=[[0, 1, 2, 3], [4, 5, 6, 7]],
                                 ins=[cO.ap()[c].opt()], outs=[gO.ap()[c].opt()]).then_inc(io.sem, 1)
        io.inc()
    s_exp = [None] * 3
    p_cons = [None] * 3
    Oslot_par = [[psS[:, 0:512], psS[:, 512:1024]], [psO[:, 0:512], psV[:, 0:512]]]
    o_free_banks = [[mx_done, mx_done], [mx_done, mx_done]]
    pending_epi = [None]
    cow = DmaSlot(p, "sCO")

    def emit_qp_build(pj, st):
        sq = st % 2
        qsl2 = slice(st * 512, (st + 1) * 512)

        @p.block.gpsimd
        def _(g):
            if st < 2:
                g.wait_ge(shs.ctr.sem, shs.ctr.v)
                g.wait_ge(dv.sem, mx_done)
            if pj == 0 and st == 0:
                g.memset(Kp[64:96, :, :], 0.0)
                g.memset(Qp[64:96, :, :, :], 0.0).then_inc(p.sCN.sem, 1)
                g.wait_ge(p.sCN.sem, p.sCN.inc())
                g.memset(Kp[64:65, :, :], 1.0)
            if qp_lastuse[sq] is not None:
                g.wait_ge(pe.sem, qp_lastuse[sq])
            evs = [
                qS[sq][0].issue(g, Qp[0:64, sq, 0, :], Q2[0:64, pj, qsl2]),
                qS[sq][1].issue(g, Qp[0:64, sq, 1, :], Q2[64:128, pj, qsl2]),
                qS[sq][2].issue(g, Qp[64:65, sq, 0, :], shb.ap()[0:1, qsl2]),
                qS[sq][3].issue(g, Qp[64:65, sq, 1, :], shb.ap()[1:2, qsl2]),
            ]
        qp_ready[sq] = [(s.ctr.sem, s.ctr.v) for s in qS[sq]]
    shs = DmaSlot(p, "sShs")
    sh_prev = [None]

    for pj in range(2):
        dv0_sh = dv.v
        ac0_sh = ac.v
        for ch in range(16):
            csl = slice(ch * 512, (ch + 1) * 512)
            sl2 = ch % 2

            @p.block.vector
            def _(v):
                v.wait_ge(dv.sem, dv.v)
                if sh_prev[0] is not None:
                    v.wait_ge(pe.sem, sh_prev[0])
                if pj == 0 and ch == 0:
                    v.wait_ge(dv.sem, mx_done)
                v.tensor_mul(scr512[:, :], Q2[:, pj, csl], K2[:, pj, csl]).then_inc(dv.sem, 1)
            d1 = dv.inc()

            @p.block.tensor
            def _(t):
                t.wait_ge(dv.sem, d1)
                t.matmul(psO[0:2, 0:512], onespat[:, :], scr512[:, :],
                         start=True, stop=True).then_inc(pe.sem, 1)
            m1 = pe.inc()

            @p.block.vector
            def _(v):
                v.wait_ge(pe.sem, m1)
                v.tensor_mul(scr512[:, :], Q2[:, pj, csl], Q2[:, pj, csl]).then_inc(dv.sem, 1)
            d2 = dv.inc()

            @p.block.tensor
            def _(t):
                t.wait_ge(dv.sem, d2)
                t.matmul(psV[0:2, 0:512], onespat[:, :], scr512[:, :],
                         start=True, stop=True).then_inc(pe.sem, 1)
            m2 = pe.inc()
            sh_prev[0] = m2

            @p.block.vector
            def _(v):
                v.wait_ge(pe.sem, m2)
                if shs.ctr.v > 0:
                    v.wait_ge(shs.ctr.sem, shs.ctr.v)
                v.tensor_scalar(shst[0:2, :], psV[0:2, 0:512],
                                mx[0:2, pj:pj + 1], None, ALU.mult).then_inc(dv.sem, 1)
            d3 = dv.inc()

            assert d3 == dv0_sh + 5 * ch + 3

            @p.block.scalar
            def _(s):
                s.wait_ge(dv.sem, d3)
                s.activation(shst[0:2, :], shst[0:2, :], AF.Sqrt).then_inc(ac.sem, 1)
            a1 = ac.inc()
            assert a1 == ac0_sh + ch + 1

            @p.block.vector
            def _(v):
                v.wait_ge(ac.sem, a1)
                v.tensor_add(shst[0:2, :], shst[0:2, :], psO[0:2, 0:512]).then_inc(dv.sem, 1)
                v.wait_ge(dv.sem, dv.inc())
                v.tensor_scalar_mul(shst[0:2, :], shst[0:2, :], -0.5).then_inc(dv.sem, 1)
            d4 = dv.inc()

            @p.block.sync
            def _(sy):
                sy.wait_ge(dv.sem, d4)
                if pj == 1 and ch == 0:
                    for qsl_ in qS:
                        for s_ in qsl_:
                            if s_.ctr.v > 0:
                                sy.wait_ge(s_.ctr.sem, s_.ctr.v)
                shs.issue(sy, shb.ap()[:, csl], shst[0:2, :])

        o_free_banks[1] = [dv.v, dv.v]

        emit_qp_build(pj, 0)
        emit_qp_build(pj, 1)

        for st in range(16):
            if pj == 1 and st >= 5 and (st - 5) % 4 == 0:
                emit_og_chunk((st - 5) // 4)
            qsl = slice(st * 512, (st + 1) * 512)
            ktmax = 4 * (st + 1)
            iters = []
            for kt in range(ktmax):
                diag_sub = kt - 4 * st
                q0 = diag_sub * 128 if diag_sub >= 0 else 0
                for hh in range(2):
                    iters.append((kt, hh, len(iters) % 3, q0, 512 - q0, diag_sub))

            L = len(iters)
            par = st % 2
            Osl = Oslot_par[par]

            pe0 = pe.v
            ac0 = ac.v
            carry_pcons = list(p_cons)
            i_epi = min(8, L - 1) if pending_epi[0] is not None else None
            qk_val = [None] * L
            pv_val = [None] * L
            cnt = pe0
            for i in range(L):
                cnt += 1
                qk_val[i] = cnt
                if i >= 2:
                    cnt += 1
                    pv_val[i - 2] = cnt
                if i == i_epi:
                    cnt += 2
            pv_val[L - 2] = cnt + 1
            pv_val[L - 1] = cnt + 2
            qp_ev = qp_ready[st % 2]
            qp_lastuse[st % 2] = qk_val[-1]
            if st + 2 < 16:
                emit_qp_build(pj, st + 2)

            @p.block.scalar
            def _(s, iters=iters, qk_val=qk_val, pv_val=pv_val, carry=carry_pcons):
                for i, (kt, hh, si, q0, Nq, ds) in enumerate(iters):
                    s.wait_ge(pe.sem, qk_val[i])
                    if i >= 3:
                        s.wait_ge(pe.sem, pv_val[i - 3])
                    elif carry[si] is not None:
                        s.wait_ge(pe.sem, carry[si])
                    s.activation(P4a[si][:, 0:Nq], Sslot[si][:, 0:Nq], AF.Exp,
                                 scale=co8[:, 0:1]).then_inc(ac.sem, 1)
            ac.inc(L)

            mskv_arr = [None] * L

            def emit_pv(j, iters=iters, par=par, Osl=Osl, ac0=ac0,
                        pv_val=pv_val, ktmax=ktmax, pj=pj):
                kt, hh, si, q0, Nq, diag_sub = iters[j]
                expv = ac0 + j + 1

                @p.block.tensor
                def _(t):
                    if mskv_arr[j] is not None:
                        t.wait_ge(dv.sem, mskv_arr[j])
                    else:
                        t.wait_ge(ac.sem, expv)
                    if kt == 0:
                        t.wait_ge(dv.sem, o_free_banks[par][hh])
                    t.matmul(Osl[hh][0:65, q0:512], V2[:, pj, kt, hh * 65:(hh + 1) * 65],
                             P4a[si][:, 0:Nq], start=(kt == 0), stop=(kt == ktmax - 1),
                             skip_group_check=True).then_inc(pe.sem, 1)
                p_cons[si] = pe.inc()
                assert p_cons[si] == pv_val[j]

            for i, (kt, hh, si, q0, Nq, diag_sub) in enumerate(iters):
                    hsl = slice(hh * 64, hh * 64 + 64)
                    ksl = i % 8

                    @p.block.sync
                    def _(sy):
                        if kp_lastuse[ksl] is not None:
                            sy.wait_ge(pe.sem, kp_lastuse[ksl])
                        if pj == 0 and st == 0 and i < 4:
                            sy.wait_ge(shs.ctr.sem, shs.ctr.v)
                        kb_ev[ksl] = kS[ksl].issue(sy, Kp[0:64, ksl, :],
                                                   K2[hsl, pj, kt * 128:(kt + 1) * 128])

                    @p.block.tensor
                    def _(t):
                        t.wait_ge(kb_ev[ksl][0], kb_ev[ksl][1])
                        if i < 2:
                            for ev in qp_ev:
                                t.wait_ge(ev[0], ev[1])
                        if s_exp[si] is not None:
                            t.wait_ge(ac.sem, s_exp[si])
                        t.matmul(Sslot[si][:, 0:Nq], Kp[0:96, ksl, :],
                                 Qp[0:96, st % 2, hh, q0:512],
                                 start=True, stop=True).then_inc(pe.sem, 1)
                    qkv_ = pe.inc()
                    assert qkv_ == qk_val[i]
                    kp_lastuse[ksl] = qkv_

                    expv = ac0 + i + 1
                    s_exp[si] = expv

                    if diag_sub >= 0:
                        @p.block.vector
                        def _(v):
                            v.wait_ge(ac.sem, expv)
                            v.wait_ge(tup_s.sem, tup_s.v)
                            v.tensor_mul(P4a[si][:, 0:128], P4a[si][:, 0:128], tupp).then_inc(dv.sem, 1)
                        mskv_arr[i] = dv.inc()

                    if i >= 2:
                        emit_pv(i - 2)
                    if i == i_epi:
                        pending_epi[0]()
                        pending_epi[0] = None

            emit_pv(L - 2)
            emit_pv(L - 1)
            if pending_epi[0] is not None:
                pending_epi[0]()
                pending_epi[0] = None

            def make_epilogue(pj=pj, st=st, par=par, qsl=qsl, pv_done=pe.v):
                def emit():
                    rbv_prev = None
                    for hh in range(2):
                        @p.block.vector
                        def _(v, hh=hh, rbv_prev=rbv_prev):
                            v.wait_ge(pe.sem, pv_done)
                            if rbv_prev is not None:
                                v.wait_ge(pe.sem, rbv_prev)
                            v.reciprocal(rl[:, :], Oslot_par[par][hh][64:65, 0:512]).then_inc(dv.sem, 1)
                            v.wait_ge(dv.sem, dv.inc())
                            v.tensor_copy(Ost[:, :], Oslot_par[par][hh][0:64, 0:512]).then_inc(dv.sem, 1)
                        oe = dv.inc()

                        @p.block.tensor
                        def _(t):
                            t.wait_ge(dv.sem, oe)
                            t.matmul(psBB[0:64, 0:512], p.ones1x128_f[0:1, 0:64], rl[0:1, :],
                                     start=True, stop=True).then_inc(pe.sem, 1)
                        rbv = pe.inc()
                        rbv_prev = rbv

                        @p.block.vector
                        def _(v):
                            v.wait_ge(pe.sem, rbv)
                            if cow.ctr.v > 0:
                                v.wait_ge(cow.ctr.sem, cow.ctr.v)
                            v.tensor_mul(Ostb[:, :], Ost[:, :], psBB[0:64, 0:512]).then_inc(dv.sem, 1)
                        o_free_banks[par][hh] = dv.inc()

                        @p.block.sync
                        def _(sy, mfree=o_free_banks[par][hh], hh=hh):
                            sy.wait_ge(dv.sem, mfree)
                            r0_ = (2 * pj + hh) * 64
                            cow.issue(sy, cO.ap()[st // 4, r0_:r0_ + 64,
                                                  (st % 4) * 512:(st % 4) * 512 + 512],
                                      Ostb[:, :])
                return emit

            pending_epi[0] = make_epilogue()

        if pending_epi[0] is not None:
            pending_epi[0]()
            pending_epi[0] = None
    attn_done = dv.v

    emit_og_chunk(3)
    cco_done = io.v

    oq_s = p.dma_sem("sOq")

    @p.block.sync
    def _(sy):
        sy.wait_ge(io.sem, cco_done)
        rq = nc.partition_id() % 4
        sy.dma_start(out=OQ, in_=gO_v[rq]).then_inc(oq_s.sem, 16)
    oq_s.inc(16)
    oq_ev = (oq_s.sem, oq_s.v)

    slots = [psBIG[:, i * 512:(i + 1) * 512] for i in range(4)]

    def cb_out(m, q, ps, mmv):
        @p.block.vector
        def _(v):
            v.wait_ge(pe.sem, mmv)
            v.wait_ge(dv.sem, dv.v)
            bias = bp_sb[:, m:m + 1] if rope_bias else 0.0
            v.scalar_tensor_tensor(xres[:, m, q * 512:(q + 1) * 512], ps, bias,
                                   xres[:, m, q * 512:(q + 1) * 512],
                                   ALU.add, ALU.add).then_inc(dv.sem, 1)
        return dv.inc()

    @p.block.tensor
    def _(t):
        t.wait_ge(oq_ev[0], oq_ev[1])
        t.wait_ge(wp_ready[0], wp_ready[1])
    emit_mm_dense(p, WP, OQ, NCH, cb_out, wait_dv=attn_done, psum_rot=slots)

    return emit_store_x(p, xres, xo_d, wait_dv=dv.v)


def build_ffn0():
    return _standalone(sub_ffn0, "f0_")


def build_ffn1():
    return _standalone(sub_ffn1, "f1_")


def build_att5():
    return _standalone(sub_att5, "a5_")


def build_att7():
    return _standalone(sub_att7, "a7_")


def build_att0():
    return _standalone(lambda p, x, xo, after=None: sub_att02(p, x, xo, True, after), "a0_")


def build_att2():
    return _standalone(lambda p, x, xo, after=None: sub_att02(p, x, xo, False, after), "a2_")





def emit_barrier(p, extra=()):
    vals = [(p.sPE.sem, p.sPE.v), (p.sDV.sem, p.sDV.v), (p.sAC.sem, p.sAC.v),
            (p.sIO.sem, p.sIO.v), (p.sCN.sem, p.sCN.v), (p.sST.sem, p.sST.v)]
    vals += list(extra)
    for secf in [p.block.tensor, p.block.vector, p.block.scalar, p.block.gpsimd, p.block.sync]:
        @secf
        def _(e):
            for s, v in vals:
                if v > 0:
                    e.wait_ge(s, v)


def build_fused():
    p = Prog("fx")
    x_d = p.nc.dram_tensor("x", [D, TLOC], F32, kind="ExternalInput").ap()
    xo_d = p.nc.dram_tensor("xo", [D, TLOC], F32, kind="ExternalOutput").ap()
    xs = [x_d] + [p.nc.dram_tensor(f"xmid{i}", [D, TLOC], F32).ap() for i in range(7)] + [xo_d]
    common_sems(p)
    p.xres = p.sbuf("xres_root", [128, NCH, TLOC], F32)
    p.start()
    subs = [
        (lambda pp, a, b, after=None: sub_att02(pp, a, b, True, after), "s0_"),
        (sub_ffn0, "s1_"),
        (lambda pp, a, b, after=None: sub_att02(pp, a, b, False, after), "s2_"),
        (sub_ffn0, "s3_"),
        (sub_att5, "s4_"),
        (sub_ffn1, "s5_"),
        (sub_att7, "s6_"),
        (sub_ffn1, "s7_"),
    ]
    io_map = [(xs[0], None), (None, None), (None, None), (None, None),
              (None, None), (None, xs[6]), (xs[6], xs[7]), (xs[7], xs[8])]
    for i, (fn, pref) in enumerate(subs):
        p.push_scope(pref)
        r = fn(p, io_map[i][0], io_map[i][1])
        p.pop_scope()
        evs = r if isinstance(r, list) else [r]
        emit_barrier(p, extra=[e for e in evs if e is not None])
    return p.finish()


_fused_cache = {}


def _get_fused():
    if "nc" not in _fused_cache:
        _fused_cache["nc"] = build_fused()
    return _fused_cache["nc"]


FUSED = True


_BUILDERS = {
    SUB_FFN0A: build_ffn0,
    SUB_FFN0B: build_ffn0,
    SUB_FFN1A: build_ffn1,
    SUB_FFN1B: build_ffn1,
    SUB_ATT5: build_att5,
    SUB_ATT7: build_att7,
    SUB_ATTA0: build_att0,
    SUB_ATTB0: build_att2,
}

_prog_cache = {}


def _get_prog(idx):
    key = _BUILDERS[idx].__name__
    if key not in _prog_cache:
        _prog_cache[key] = _BUILDERS[idx]()
    return _prog_cache[key]


def run_sublayer(idx, in_maps, x_cur):
    nc = _get_prog(idx)
    names = SUB_INPUTS[idx]
    maps = []
    for c in range(NCORE):
        m = {k: in_maps[c][k] for k in names if k != "x"}
        m["x"] = x_cur[c]
        maps.append(m)
    res = run_bass_kernel_spmd(nc, maps, core_ids=list(range(NCORE)))
    return [np.asarray(res.results[c]["xo"]) for c in range(NCORE)]


def kernel(**inputs):
    in_maps = host_prepare(inputs)
    out = np.empty((B, S, D), dtype=np.float32)
    if FUSED:
        nc = _get_fused()
        res = run_bass_kernel_spmd(nc, in_maps, core_ids=list(range(NCORE)))
        for c in range(NCORE):
            b, r = c // GRP, c % GRP
            out[b, r * TLOC:(r + 1) * TLOC, :] = np.asarray(res.results[c]["xo"]).T
        return out
    x_cur = [im["x"] for im in in_maps]
    for idx in range(8):
        x_cur = run_sublayer(idx, in_maps, x_cur)
    for c in range(NCORE):
        b, r = c // GRP, c % GRP
        out[b, r * TLOC:(r + 1) * TLOC, :] = x_cur[c].T
    return out


# version 20 (final)
# speedup vs baseline: 1.0120x; 1.0004x over previous
import numpy as np
import ml_dtypes
from contextlib import ExitStack

import concourse.bass as bass
import concourse.mybir as mybir
from concourse.bass_utils import run_bass_kernel_spmd

F32 = mybir.dt.float32
BF16 = mybir.dt.bfloat16
AF = mybir.ActivationFunctionType
ALU = mybir.AluOpType
AX = mybir.AxisListType

NCORE = 8
GRP = 4
B, S, D = 2, 8192, 1024
TLOC = 2048
NCH = D // 128
NH, HD = 16, 64
HPC = NH // GRP
EPS = 1e-5
BF = ml_dtypes.bfloat16

SUB_ATTA0, SUB_FFN0A, SUB_ATTB0, SUB_FFN0B, SUB_ATT5, SUB_FFN1A, SUB_ATT7, SUB_FFN1B = range(8)



def _bias_cols(v):
    return np.ascontiguousarray(v.reshape(NCH, 128).T)


def _rope_table():
    theta = (1.0 / 10000.0 ** (2.0 * np.arange(0, HD // 2, dtype=np.float32) / HD)).astype(np.float32)
    seq = np.arange(1, S + 1, dtype=np.float32)
    ang = np.einsum("n,d->nd", seq, theta).astype(np.float32)
    ang = np.concatenate([ang, ang], axis=-1)
    f = (np.cos(ang) + np.sin(ang)).astype(np.float32)
    return np.ascontiguousarray(np.tile(f.T, (2, 1)))


def _masks():
    p = np.arange(128)[:, None]
    j = np.arange(128)[None, :]
    t_upp = (j >= p).astype(BF)
    t_low = (j <= p).astype(BF)
    return t_upp, np.ascontiguousarray((j <= p).astype(BF)), t_low


def host_prepare(inp):
    x = np.asarray(inp["L0_attn_a_x"])

    WattnT = np.ascontiguousarray(np.asarray(inp["L0_attn_a_W_attn"]).T)
    b_attn = np.asarray(inp["L0_attn_a_b_attn"])
    WprojT = np.ascontiguousarray(np.asarray(inp["L0_attn_a_W_proj"]).T)
    b_proj = np.asarray(inp["L0_attn_a_b_proj"])
    WQT = np.ascontiguousarray(np.asarray(inp["L0_attn_b_W_Q"]).T)
    WKT = np.ascontiguousarray(np.asarray(inp["L0_attn_b_W_K"]).T)
    WVT = np.ascontiguousarray(np.asarray(inp["L0_attn_b_W_V"]).T)
    WOT = np.ascontiguousarray(np.asarray(inp["L0_attn_b_W_O"]).T)
    W0 = np.asarray(inp["L0_ffn_weight"])
    A1 = np.asarray(inp["L0_ffn_aux_weight1"])
    A2 = np.asarray(inp["L0_ffn_aux_weight2"])
    b0 = np.asarray(inp["L0_ffn_bias"])
    W5 = np.asarray(inp["L1_attn_a_W"])
    g7 = np.asarray(inp["L1_attn_b_ln_gamma"])
    be7 = np.asarray(inp["L1_attn_b_ln_beta"])
    Wqkv7 = np.asarray(inp["L1_attn_b_w_qkv"])
    Wout7 = np.asarray(inp["L1_attn_b_w_out"])
    W6T = np.ascontiguousarray(np.asarray(inp["L1_ffn_weights"]).T)
    sc6 = np.asarray(inp["L1_ffn_scale"])
    b6 = np.asarray(inp["L1_ffn_bias"])

    f_rope = _rope_table()
    t_upp, t_low, _ = _masks()
    ident = np.eye(128, dtype=BF)

    pj = (np.arange(128)[:, None] - np.arange(512)[None, :]).astype(np.float32)

    in_maps = []
    for c in range(NCORE):
        b = c // GRP
        r = c % GRP
        g = r
        qbase = r * TLOC
        xT = np.ascontiguousarray(x[b, qbase:qbase + TLOC, :].T)

        hsl = slice(256 * g, 256 * (g + 1))
        Wqkv0_my = np.ascontiguousarray(
            np.concatenate([WattnT[:, 0:1024][:, hsl], WattnT[:, 1024:2048][:, hsl], WattnT[:, 2048:3072][:, hsl]], axis=1))
        bq_my = np.ascontiguousarray(b_attn[0:1024][hsl].reshape(2, 128).T)
        bk_my = np.ascontiguousarray(b_attn[1024:2048][hsl].reshape(2, 128).T)
        bv_my = np.ascontiguousarray(b_attn[2048:3072][hsl].reshape(1, 256))
        Wqkv3_my = np.ascontiguousarray(
            np.concatenate([WQT[:, hsl], WKT[:, hsl], WVT[:, hsl]], axis=1))

        cbc5 = np.zeros((128, 256), dtype=np.float32)
        for st_ in range(4):
            for kt in range(64):
                cbc5[:, st_ * 64 + kt] = qbase + st_ * 512 - kt * 128
        mask7 = np.zeros((128, 128), dtype=BF) if r == 0 else t_low

        m = {
            "x": xT,
            "Wqkv0_my": Wqkv0_my, "bq_my": bq_my, "bk_my": bk_my, "bv_my": bv_my,
            "WprojT": WprojT, "b_proj": _bias_cols(b_proj),
            "Wqkv3_my": Wqkv3_my, "WOT": WOT,
            "W0": W0, "A1": A1, "A2": A2, "b0": _bias_cols(b0),
            "W5": W5,
            "g7": _bias_cols(g7), "be7": _bias_cols(be7),
            "Wqkv7": Wqkv7, "Wout7": Wout7,
            "W6T": W6T, "sc6": _bias_cols(sc6), "b6": _bias_cols(b6),
            "f_rope": f_rope, "t_upp": t_upp, "t_low": t_low,
            "ident": ident, "pj": pj, "cbc5": cbc5, "mask7": mask7,
        }
        in_maps.append(m)
    return in_maps


SUB_INPUTS = {
    SUB_ATTA0: ["x", "Wqkv0_my", "bq_my", "bk_my", "bv_my", "WprojT", "b_proj", "f_rope", "t_upp"],
    SUB_FFN0A: ["x", "W0", "A1", "A2", "b0"],
    SUB_ATTB0: ["x", "Wqkv3_my", "WOT", "t_upp"],
    SUB_FFN0B: ["x", "W0", "A1", "A2", "b0"],
    SUB_ATT5:  ["x", "W5", "pj", "cbc5", "ident"],
    SUB_FFN1A: ["x", "W6T", "sc6", "b6"],
    SUB_ATT7:  ["x", "g7", "be7", "Wqkv7", "Wout7", "t_upp", "t_low", "mask7", "ident"],
    SUB_FFN1B: ["x", "W6T", "sc6", "b6"],
}




def _ap(t):
    return t if isinstance(t, bass.AP) else t.ap()

class Ctr:
    def __init__(self, sem):
        self.sem = sem
        self.v = 0

    def inc(self, n=1):
        self.v += n
        return self.v


class Prog:
    def __init__(self, name="prog"):
        self.nc = bass.Bass("TRN2", target_bir_lowering=False)
        self.ctx = ExitStack()
        self.name = name
        self.prefix = ""
        self._scope = None
        self._inputs = {}

    def _stack(self):
        return self._scope if self._scope is not None else self.ctx

    def push_scope(self, prefix):
        assert self._scope is None
        self._scope = ExitStack()
        self.prefix = prefix

    def pop_scope(self):
        self._scope.close()
        self._scope = None
        self.prefix = ""

    def sbuf(self, name, shape, dtype):
        return self._stack().enter_context(self.nc.sbuf_tensor(self.prefix + name, shape, dtype))

    def psum(self, name, shape, dtype=F32):
        return self._stack().enter_context(self.nc.psum_tensor(self.prefix + name, shape, dtype))

    def dram(self, name, shape, dtype):
        return self.nc.dram_tensor(self.prefix + name, shape, dtype)

    def in_ap(self, name, shape, dtype):
        if name not in self._inputs:
            self._inputs[name] = self.nc.dram_tensor(name, shape, dtype, kind="ExternalInput").ap()
        return self._inputs[name]

    def sem(self, name):
        return Ctr(self.ctx.enter_context(self.nc.semaphore(self.prefix + name)))

    def dma_sem(self, name=None):
        self._dma_sem_i = getattr(self, "_dma_sem_i", 0) + 1
        return Ctr(self.ctx.enter_context(
            self.nc.semaphore(self.prefix + (name or "sD") + str(self._dma_sem_i))))

    def start(self):
        self.block = self.ctx.enter_context(self.nc.Block())

    def finish(self):
        self.ctx.close()
        return self.nc


def common_sems(p):
    p.sPE = p.sem("sPE")
    p.sDV = p.sem("sDV")
    p.sAC = p.sem("sAC")
    p.sIO = p.sem("sIO")
    p.sST = p.dma_sem("sST")
    p.sCN = p.sem("sCN")



def emit_load_x(p, x_d, xres, after=None):
    if x_d is None:
        return (p.sDV.sem, 0)
    w = p.dma_sem("sX")

    @p.block.sync
    def _(sy):
        if after is not None:
            sy.wait_ge(after[0], after[1])
        sy.dma_start(out=xres[:, :, :], in_=x_d.rearrange("(c p) t -> p c t", p=128)).then_inc(w.sem, 16)
    w.inc(16)
    return (w.sem, w.v)


def emit_store_x(p, xres, xo_d, wait_dv=None):
    if xo_d is None:
        return None
    st = p.sST

    @p.block.sync
    def _(sy):
        if wait_dv is not None:
            sy.wait_ge(p.sDV.sem, wait_dv)
        sy.dma_start(out=xo_d.rearrange("(c p) t -> p c t", p=128), in_=xres[:, :, :]).then_inc(st.sem, 16)
        st.inc(16)
        sy.wait_ge(st.sem, st.v)
    return (st.sem, st.v)


def emit_consts(p):
    cn = p.sCN

    @p.block.gpsimd
    def _(g):
        g.memset(p.ones128_bf[:, :], 1.0)
        g.memset(p.ones1x128_f[:, :], 1.0).then_inc(cn.sem, 1)
    cn.inc(1)
    return cn.v


def emit_ln(p, xres, xn, xsq_scratch, tmp_f32, x_ready, extra=None):
    pe, dv, ac = p.sPE, p.sDV, p.sAC
    psBIG = p.psBIG
    xall = _ap(xres).rearrange("p c t -> p (c t)")
    xnall = _ap(xn).rearrange("p c t -> p (c t)")
    sqall = _ap(xsq_scratch).rearrange("p c t -> p (c t)")

    @p.block.vector
    def _(v):
        v.wait_ge(x_ready[0], x_ready[1])
        v.tensor_copy(xnall, xall).then_inc(dv.sem, 1)
        cast_done_ = dv.inc()
        v.wait_ge(dv.sem, cast_done_)
        v.tensor_mul(sqall, xnall, xnall).then_inc(dv.sem, 1)
        dv.inc()
    cast_done = dv.v - 1
    sq_done = dv.v

    @p.block.tensor
    def _(t):
        t.wait_ge(dv.sem, cast_done)
        t.wait_ge(p.sCN.sem, p.sCN.v)
        for c in range(NCH):
            for q in range(4):
                mm = t.matmul(psBIG[0:1, q * 512:(q + 1) * 512],
                              p.ones128_bf[:, :], xn[:, c, q * 512:(q + 1) * 512],
                              start=(c == 0), stop=(c == NCH - 1))
        mm.then_inc(pe.sem, 1)
    sx_done = pe.inc()

    @p.block.vector
    def _(v):
        v.wait_ge(pe.sem, sx_done)
        v.tensor_scalar_mul(p.r_m[:, :], psBIG[0:1, 0:2048], 1.0 / D).then_inc(dv.sem, 1)
    m_done = dv.inc()

    @p.block.tensor
    def _(t):
        t.wait_ge(dv.sem, max(sq_done, m_done))
        for c in range(NCH):
            for q in range(4):
                mm = t.matmul(psBIG[0:1, q * 512:(q + 1) * 512],
                              p.ones128_bf[:, :], xsq_scratch[:, c, q * 512:(q + 1) * 512],
                              start=(c == 0), stop=(c == NCH - 1))
        mm.then_inc(pe.sem, 1)
    sxx_done = pe.inc()

    @p.block.vector
    def _(v):
        v.wait_ge(pe.sem, sxx_done)
        v.tensor_scalar_mul(p.r_a[:, :], psBIG[0:1, 0:2048], 1.0 / D).then_inc(dv.sem, 1)
        dv.inc()
        v.tensor_mul(p.r_b[:, :], p.r_m[:, :], p.r_m[:, :]).then_inc(dv.sem, 1)
        v.wait_ge(dv.sem, dv.inc())
        v.tensor_sub(p.r_a[:, :], p.r_a[:, :], p.r_b[:, :]).then_inc(dv.sem, 1)
        v.wait_ge(dv.sem, dv.inc())
        v.tensor_scalar_add(p.r_a[:, :], p.r_a[:, :], EPS).then_inc(dv.sem, 1)
    vpe_done = dv.inc()

    @p.block.scalar
    def _(s):
        s.wait_ge(dv.sem, vpe_done)
        s.activation(p.r_b[:, :], p.r_a[:, :], AF.Sqrt).then_inc(ac.sem, 1)
    sqrt_done = ac.inc()

    @p.block.vector
    def _(v):
        v.wait_ge(ac.sem, sqrt_done)
        v.reciprocal(p.r_a[:, :], p.r_b[:, :]).then_inc(dv.sem, 1)
    rows_done = dv.inc()

    psum_free = rows_done
    for h in range(2):
        tok = slice(h * 1024, (h + 1) * 1024)

        @p.block.tensor
        def _(t):
            t.wait_ge(dv.sem, psum_free)
            for q in range(2):
                sl = slice(h * 1024 + q * 512, h * 1024 + (q + 1) * 512)
                t.matmul(psBIG[:, q * 512:(q + 1) * 512],
                         p.ones1x128_f[:, :], p.r_m[0:1, sl], start=True, stop=True)
                mm = t.matmul(psBIG[:, 1024 + q * 512:1024 + (q + 1) * 512],
                              p.ones1x128_f[:, :], p.r_a[0:1, sl], start=True, stop=True)
            mm.then_inc(pe.sem, 1)
        bc_done = pe.inc()

        @p.block.vector
        def _(v):
            v.wait_ge(pe.sem, bc_done)
            for c in range(NCH):
                v.tensor_sub(tmp_f32[:, 0:1024], xres[:, c, tok], psBIG[:, 0:1024]).then_inc(dv.sem, 1)
                v.wait_ge(dv.sem, dv.inc())
                v.tensor_mul(xn[:, c, tok], tmp_f32[:, 0:1024], psBIG[:, 1024:2048]).then_inc(dv.sem, 1)
                v.wait_ge(dv.sem, dv.inc())
        psum_free = dv.v

    if extra is not None:
        extra["m_row"] = p.r_m
        extra["rstd_row"] = p.r_a[:, :]
    return dv.v


def alloc_ln_rows(p):
    p.r_m = p.sbuf("r_m_t", [1, TLOC], F32).ap()
    p.r_a = p.sbuf("r_a", [1, TLOC], F32).ap()
    p.r_b = p.sbuf("r_b", [1, TLOC], F32).ap()


def emit_w_load(p, dram_ap, sbuf_t, n_din_ch, sem_ctr=None):
    w = sem_ctr if sem_ctr is not None else p.dma_sem()

    @p.block.gpsimd
    def _(g):
        g.dma_start(out=sbuf_t[:, :, :],
                    in_=dram_ap.rearrange("(k p) j -> p k j", p=128)).then_inc(w.sem, 16)
    w.inc(16)
    return (w.sem, w.v)


def emit_mm_dense(p, w_sb, x_sb, n_k, out_cb, wait_w=None, wait_dv=None,
                  m_chunks=None, psum_rot=None):
    pe, dv = p.sPE, p.sDV
    if m_chunks is None:
        m_chunks = range(NCH)
    slots = psum_rot
    n_slots = len(slots)
    evac_vals = [None] * n_slots

    it = 0
    for m in m_chunks:
        for q in range(4):
            slot = it % n_slots
            qsl = slice(q * 512, (q + 1) * 512)

            @p.block.tensor
            def _(t):
                if it == 0:
                    if wait_dv is not None:
                        t.wait_ge(dv.sem, wait_dv)
                    if wait_w is not None:
                        t.wait_ge(wait_w[0], wait_w[1])
                if evac_vals[slot] is not None:
                    t.wait_ge(dv.sem, evac_vals[slot])
                for k in range(n_k):
                    last = t.matmul(slots[slot], w_sb[:, k, m * 128:(m + 1) * 128], x_sb[:, k, qsl],
                                    start=(k == 0), stop=(k == n_k - 1))
                last.then_inc(pe.sem, 1)
            mm_done = pe.inc()
            evac_vals[slot] = out_cb(m, q, slots[slot], mm_done)
            it += 1




def _standalone(sub, prefix):
    p = Prog(prefix)
    x_d = p.nc.dram_tensor("x", [D, TLOC], F32, kind="ExternalInput").ap()
    xo_d = p.nc.dram_tensor("xo", [D, TLOC], F32, kind="ExternalOutput").ap()
    common_sems(p)
    p.xres = p.sbuf("xres_root", [128, NCH, TLOC], F32)
    p.start()
    p.push_scope(prefix)
    sub(p, x_d, xo_d)
    p.pop_scope()
    return p.finish()


def sub_ffn0(p, x_d, xo_d, after=None):
    nc = p.nc
    W0_d = p.in_ap("W0", [D, D], F32)
    A1_d = p.in_ap("A1", [D, D], F32)
    A2_d = p.in_ap("A2", [D, D], F32)
    b0_d = p.in_ap("b0", [128, NCH], F32)

    xres = p.xres
    xn = p.sbuf("xn", [128, NCH, TLOC], BF16)
    hA = p.sbuf("hA", [128, NCH, TLOC], BF16)
    w0 = p.sbuf("w0", [128, NCH, D], BF16)
    w1 = p.sbuf("w1", [128, NCH, D], BF16)
    w2 = p.sbuf("w2", [128, NCH, D], BF16)
    b0_sb = p.sbuf("b0_sb", [128, NCH], F32)
    tres = p.sbuf("tres", [128, 2, 512], F32)
    tmp_f32 = tres.ap().rearrange("p a b -> p (a b)")
    p.ones128_bf = p.sbuf("ones128_bf", [128, 1], BF16)
    p.ones1x128_f = p.sbuf("ones1x128_f", [1, 128], F32)
    alloc_ln_rows(p)
    p.psBIG = p.psum("psBIG", [128, 2048], F32)

    emit_consts(p)
    x_ready = emit_load_x(p, x_d, xres, after)
    swt = p.dma_sem("sWts")
    swt_w = [p.dma_sem(f"sW{j}") for j in range(3)]
    emit_w_load(p, W0_d, w0, NCH, swt_w[0])
    emit_w_load(p, A1_d, w1, NCH, swt_w[1])
    emit_w_load(p, A2_d, w2, NCH, swt_w[2])

    @p.block.gpsimd
    def _(g):
        g.dma_start(out=b0_sb[:, :], in_=b0_d).then_inc(swt.sem, 16)
    swt.inc(16)
    b_ready = (swt.sem, swt.v)
    w_ready = [(s.sem, s.v) for s in swt_w]

    xn_ready = emit_ln(p, xres, xn, hA, tmp_f32, x_ready)

    done = emit_ffn0_body(p, xn, hA, w0, w1, w2, b0_sb, xres, tres,
                          xn_ready, w_ready, b_ready)

    return emit_store_x(p, xres, xo_d, wait_dv=done)


def emit_ffn0_body(p, xn, hA, w0, w1, w2, b0_sb, xres, tres, xn_ready, w_ready, b_ready):
    pe, dv, ac = p.sPE, p.sDV, p.sAC
    psBIG = p.psBIG
    slots = [psBIG[:, i * 512:(i + 1) * 512] for i in range(4)]

    def cb1(m, q, ps, mmv):
        @p.block.vector
        def _(v):
            v.wait_ge(pe.sem, mmv)
            v.tensor_copy(hA[:, m, q * 512:(q + 1) * 512], ps).then_inc(dv.sem, 1)
        return dv.inc()

    emit_mm_dense(p, w0, xn, NCH, cb1, wait_w=w_ready[0], wait_dv=xn_ready, psum_rot=slots)
    l1_done = dv.v

    def cb2(m, q, ps, mmv):
        @p.block.vector
        def _(v):
            v.wait_ge(pe.sem, mmv)
            v.tensor_copy(xn[:, m, q * 512:(q + 1) * 512], ps).then_inc(dv.sem, 1)
        return dv.inc()

    emit_mm_dense(p, w1, hA, NCH, cb2, wait_w=w_ready[1], wait_dv=l1_done, psum_rot=slots)
    l2_done = dv.v

    pe0_3 = pe.v
    ac0_3 = ac.v
    dv0_3 = dv.v

    @p.block.scalar
    def _(s):
        psl = [psBIG[:, i * 512:(i + 1) * 512] for i in range(4)]
        for i in range(32):
            m = i // 4
            s.wait_ge(pe.sem, pe0_3 + i + 1)
            if i == 0:
                s.wait_ge(b_ready[0], b_ready[1])
            if i >= 2:
                s.wait_ge(dv.sem, dv0_3 + i - 1)
            s.activation(tres[:, i % 2, :], psl[i % 4], AF.Tanh,
                         bias=b0_sb[:, m:m + 1]).then_inc(ac.sem, 1)
    ac.inc(32)

    tcnt = [0]

    def cb3(m, q, ps, mmv):
        i = tcnt[0]
        slot3 = i % 2
        acv = ac0_3 + i + 1

        @p.block.vector
        def _(v):
            v.wait_ge(ac.sem, acv)
            v.tensor_add(xres[:, m, q * 512:(q + 1) * 512],
                         xres[:, m, q * 512:(q + 1) * 512], tres[:, slot3, :]).then_inc(dv.sem, 1)
        tcnt[0] += 1
        val = dv.inc()
        assert val == dv0_3 + i + 1
        return val

    emit_mm_dense(p, w2, xn, NCH, cb3, wait_w=w_ready[2], wait_dv=l2_done, psum_rot=slots)
    return dv.v



def sub_ffn1(p, x_d, xo_d, after=None):
    nc = p.nc
    W6T_d = p.in_ap("W6T", [D, D], F32)
    sc_d = p.in_ap("sc6", [128, NCH], F32)
    b_d = p.in_ap("b6", [128, NCH], F32)

    xres = p.xres
    xn = p.sbuf("xn", [128, NCH, TLOC], BF16)
    scr = p.sbuf("scr", [128, NCH, TLOC], BF16)
    wf = p.sbuf("wf", [128, NCH, 1024], F32)
    ws = p.sbuf("ws", [128, NCH, D], BF16)
    sc_sb = p.sbuf("sc_sb", [128, NCH], F32)
    b_sb = p.sbuf("b_sb", [128, NCH], F32)
    tres = p.sbuf("tres", [128, 2, 512], F32)
    tmp_f32 = tres.ap().rearrange("p a b -> p (a b)")
    p.ones128_bf = p.sbuf("ones128_bf", [128, 1], BF16)
    p.ones1x128_f = p.sbuf("ones1x128_f", [1, 128], F32)
    alloc_ln_rows(p)
    p.psBIG = p.psum("psBIG", [128, 2048], F32)

    emit_consts(p)
    x_ready = emit_load_x(p, x_d, xres, after)

    swt = p.dma_sem("sWts")

    @p.block.sync
    def _(sy):
        sy.dma_start(out=wf[:, :, :], in_=W6T_d.rearrange("(k p) j -> p k j", p=128)).then_inc(swt.sem, 16)
        sy.dma_start(out=sc_sb[:, :], in_=sc_d).then_inc(swt.sem, 16)
        sy.dma_start(out=b_sb[:, :], in_=b_d).then_inc(swt.sem, 16)
    w_ready = (swt.sem, swt.inc(48))

    @p.block.vector
    def _(v):
        v.wait_ge(w_ready[0], w_ready[1])
        wfa = wf.ap().rearrange("p k j -> p (k j)")
        wsa = ws.ap().rearrange("p k j -> p (k j)")
        v.tensor_scalar(wsa, wfa, 0.0, 2.0, ALU.is_ge, ALU.mult).then_inc(p.sDV.sem, 1)
        v.wait_ge(p.sDV.sem, p.sDV.inc())
        v.tensor_scalar(wsa, wsa, 1.0, None, ALU.subtract).then_inc(p.sDV.sem, 1)
    sign_done = p.sDV.inc()

    xn_ready = emit_ln(p, xres, xn, scr, tmp_f32, x_ready)

    pe, dv = p.sPE, p.sDV
    slots = [p.psBIG[:, i * 512:(i + 1) * 512] for i in range(4)]

    def cb(m, q, ps, mmv):
        @p.block.vector
        def _(v):
            v.wait_ge(pe.sem, mmv)
            v.wait_ge(dv.sem, dv.v)
            tslice = tres[:, 0, :]
            v.tensor_scalar(tslice, ps, sc_sb[:, m:m + 1], b_sb[:, m:m + 1], ALU.mult, ALU.add).then_inc(dv.sem, 1)
            v.wait_ge(dv.sem, dv.inc())
            v.tensor_scalar(tslice, tslice, 100.0, -100.0, ALU.min, ALU.max).then_inc(dv.sem, 1)
            v.wait_ge(dv.sem, dv.inc())
            v.tensor_add(xres[:, m, q * 512:(q + 1) * 512],
                         xres[:, m, q * 512:(q + 1) * 512], tslice).then_inc(dv.sem, 1)
        return dv.inc()

    emit_mm_dense(p, ws, xn, NCH, cb, wait_dv=max(xn_ready, sign_done), psum_rot=slots)

    return emit_store_x(p, xres, xo_d, wait_dv=dv.v)



class DmaSlot:
    def __init__(self, p, name):
        self.ctr = p.dma_sem(name)

    def issue(self, eng, out, in_):
        if self.ctr.v > 0:
            eng.wait_ge(self.ctr.sem, self.ctr.v)
        eng.dma_start(out=out, in_=in_).then_inc(self.ctr.sem, 16)
        self.ctr.inc(16)
        return (self.ctr.sem, self.ctr.v)


def sub_att5(p, x_d, xo_d, after=None):
    nc = p.nc
    W5_d = p.in_ap("W5", [D, 3 * D], F32)
    pj_d = p.in_ap("pj", [128, 512], F32)
    cbc5_d = p.in_ap("cbc5", [128, 256], F32)
    ident_d = p.in_ap("ident", [128, 128], BF16)

    NROW = TLOC + 1
    cin = p.dram("cin5", [NROW, 2056], BF16)
    coutA = p.dram("cout5A", [GRP * 1024, 2056], BF16)
    coutB = p.dram("cout5B", [GRP * 1024, 2056], BF16)
    coutS = p.dram("cout5S", [GRP, 2056], BF16)
    cA_v = coutA.ap().rearrange("(r n) c -> r n c", r=GRP)
    cB_v = coutB.ap().rearrange("(r n) c -> r n c", r=GRP)

    def cout_v(rank, r0, csl):
        if r0 < 1024:
            return cA_v[rank, r0:r0 + 128, csl]
        return cB_v[rank, r0 - 1024:r0 - 1024 + 128, csl]
    KB = p.dram("KB5", [S, 1024], BF16)
    KB_v = KB.ap().rearrange("(t p) c -> t p c", p=128)

    xres = p.xres
    qfm = p.sbuf("qfm", [128, NCH, TLOC], BF16)
    arena2 = p.sbuf("arena2", [128, 4 * TLOC], F32)
    xn = arena2.ap().bitcast(BF16).rearrange("p (c t) -> p c t", c=NCH)
    O_sb = arena2.ap()[:, 0:NCH * 512].rearrange("p (c t) -> p c t", c=NCH)
    arena1 = p.sbuf("arena1", [128, 24576], BF16)
    w5 = arena1.ap().rearrange("p (k j) -> p k j", k=NCH)
    Vg = arena1.ap()[:, 0:16384].rearrange("p (g k t) -> p g k t", g=2, k=8)
    Pg = arena1.ap()[:, 16384:20480].rearrange("p (k t) -> p k t", k=8)
    Kt = arena1.ap()[:, 20480:22528].rearrange("p (s t) -> p s t", s=2)
    Praw = arena1.ap()[:, 22528:23552].rearrange("p (s t) -> p s t", s=2)
    qsq = arena1.ap()[:, 16384:20480].rearrange("p (a t) -> p a t", a=2)
    arena3 = p.sbuf("arena3", [128, 6144], F32)
    a3b = arena3.ap().bitcast(BF16)
    CT = a3b[:, 0:4112].rearrange("p (s c) -> p s c", s=2)
    sq_scr = a3b[:, 4112:5136]
    normc = a3b[:, 5136:5152]
    a2b = arena2.ap().bitcast(BF16)
    KBs = a2b[:, 10240:12288].rearrange("p (s c) -> p s c", s=2)
    Kt2 = a2b[:, 8192:10240].rearrange("p (s c) -> p s c", s=2)
    nm16f = p.sbuf("nm16f", [16, 1], F32)
    nm16b = p.sbuf("nm16b", [16, 1], BF16)
    srow = p.sbuf("srow", [1, 8], F32)
    srow_b = p.sbuf("srow_b", [1, 8], BF16)
    kms = p.sbuf("kms", [1, 4], BF16)
    negMb = p.sbuf("negMb", [128, 1], F32)
    rl = p.sbuf("rl", [1, 512], F32)
    pj_sb = p.sbuf("pj_sb", [128, 512], F32)
    cbc_sb = p.sbuf("cbc_sb", [128, 256], F32)
    ident = p.sbuf("ident_sb", [128, 128], BF16)
    tmp_f32 = arena3.ap()[:, 4096:5120]
    p.ones128_bf = p.sbuf("ones128_bf", [128, 1], BF16)
    p.ones1x128_f = p.sbuf("ones1x128_f", [1, 128], F32)
    p.r_m = arena3.ap()[0:1, 0:2048]
    p.r_a = arena3.ap()[0:1, 2048:4096]
    p.r_b = arena3.ap()[0:1, 4096:6144]

    p.psBIG = p.psum("psBIG", [128, 2048], F32)
    psE = p.psum("psE", [128, 1024], F32)
    psT_a = p.psum("psT_a", [128, 128], BF16)
    psT_b = p.psum("psT_b", [128, 128], BF16)
    psBIG = p.psBIG
    S_bank = [psBIG[:, 0:512], psBIG[:, 512:1024]]
    l_bank = psBIG[0:1, 1024:1536]
    O_bank = [psBIG[:, 1536:2048], psE[:, 0:512]]
    rlb_bank = psE[:, 512:1024]

    pe, dv, ac, io = p.sPE, p.sDV, p.sAC, p.sIO

    emit_consts(p)
    x_ready = emit_load_x(p, x_d, xres, after)
    swt = p.dma_sem("sWts")

    @p.block.gpsimd
    def _(g):
        g.dma_start(out=w5[:, :, :], in_=W5_d.rearrange("(k p) j -> p k j", p=128)).then_inc(swt.sem, 16)
        g.dma_start(out=pj_sb[:, :], in_=pj_d).then_inc(swt.sem, 16)
        g.dma_start(out=cbc_sb[:, :], in_=cbc5_d).then_inc(swt.sem, 16)
        g.dma_start(out=ident[:, :], in_=ident_d).then_inc(swt.sem, 16)
    w_ready = (swt.sem, swt.inc(64))

    xn_ready = emit_ln(p, xres, xn, qfm.ap().rearrange("p c t -> p c t"), tmp_f32, x_ready)

    slots = [psBIG[:, i * 512:(i + 1) * 512] for i in range(4)]

    def cb_q(m, q, ps, mmv):
        @p.block.vector
        def _(v):
            v.wait_ge(pe.sem, mmv)
            v.tensor_copy(qfm[:, m, q * 512:(q + 1) * 512], ps).then_inc(dv.sem, 1)
        return dv.inc()

    emit_mm_dense(p, w5, xn, NCH, cb_q, wait_w=w_ready, wait_dv=xn_ready, psum_rot=slots)
    q_done = dv.v

    co32 = p.sbuf("co32", [128, 1], F32)

    @p.block.gpsimd
    def _(g):
        g.memset(co32[:, :], 1.0 / 32.0)
        g.memset(srow[:, :], 0.0)
        g.wait_ge(dv.sem, xn_ready)
        g.memset(CT[:, 0, 2049:2056], 0.0)
        g.memset(CT[:, 1, 2049:2056], 0.0).then_inc(p.sCN.sem, 1)
    ct_pad_done = p.sCN.inc()

    ct_slot = [DmaSlot(p, f"sCT{i}") for i in range(2)]
    ct_out_ev = [None, None]
    ps_evac = [q_done] * 4
    for tt in range(16):
        sl = tt % 2
        tsl = slice(tt * 128, (tt + 1) * 128)
        copy_vals = []
        for j in range(4):
            @p.block.tensor
            def _(t):
                t.wait_ge(dv.sem, ps_evac[j])
                for k in range(NCH):
                    mm = t.matmul(slots[j], xn[:, k, tsl], w5[:, k, 1024 + j * 512:1024 + (j + 1) * 512],
                                  start=(k == 0), stop=(k == NCH - 1))
                mm.then_inc(pe.sem, 1)
            mmv = pe.inc()

            @p.block.vector
            def _(v):
                v.wait_ge(pe.sem, mmv)
                if tt < 2 and j == 0:
                    v.wait_ge(p.sCN.sem, ct_pad_done)
                if j == 0 and ct_out_ev[sl] is not None:
                    v.wait_ge(ct_out_ev[sl][0], ct_out_ev[sl][1])
                v.tensor_copy(CT[:, sl, j * 512:(j + 1) * 512], slots[j]).then_inc(dv.sem, 1)
            ps_evac[j] = dv.inc()
            copy_vals.append(ps_evac[j])

        @p.block.vector
        def _(v):
            v.wait_ge(dv.sem, copy_vals[-1])
            v.tensor_mul(sq_scr[:, :], CT[:, sl, 0:1024], CT[:, sl, 0:1024]).then_inc(dv.sem, 1)
            v.wait_ge(dv.sem, dv.inc())
            with nc.allow_low_precision(reason="k-norm row for softmax bound only"):
                v.tensor_reduce(CT[:, sl, 2048:2049], sq_scr[:, :], AX.X, ALU.add).then_inc(dv.sem, 1)
            v.wait_ge(dv.sem, dv.inc())
            v.tensor_copy(normc[:, tt:tt + 1], CT[:, sl, 2048:2049]).then_inc(dv.sem, 1)
        ct_stage_done = dv.inc()

        @p.block.sync
        def _(sy):
            sy.wait_ge(dv.sem, ct_stage_done)
            ct_out_ev[sl] = ct_slot[sl].issue(sy, cin.ap()[tsl, :], CT[:, sl, :])
        if tt == 7:
            @p.block.gpsimd
            def _(g, w0=ct_slot[0].ctr.v, w1=ct_slot[1].ctr.v):
                g.wait_ge(ct_slot[0].ctr.sem, w0)
                g.wait_ge(ct_slot[1].ctr.sem, w1)
                g.collective_compute(
                    "AllGather", ALU.bypass,
                    replica_groups=[[0, 1, 2, 3], [4, 5, 6, 7]],
                    ins=[cin.ap()[0:1024, :].opt()], outs=[coutA.ap().opt()],
                ).then_inc(io.sem, 1)
            cc_h = [io.inc(), None]
    kv_proj_done = dv.v

    @p.block.tensor
    def _(t):
        t.wait_ge(dv.sem, kv_proj_done)
        t.transpose(psT_a[0:16, 0:128], normc[:, 0:16], ident[:, :]).then_inc(pe.sem, 1)
    trn1 = pe.inc()

    @p.block.vector
    def _(v):
        v.wait_ge(pe.sem, trn1)
        v.tensor_reduce(nm16f[:, :], psT_a[0:16, 0:128], AX.X, ALU.max).then_inc(dv.sem, 1)
        v.wait_ge(dv.sem, dv.inc())
        v.tensor_copy(nm16b[:, :], nm16f[:, :]).then_inc(dv.sem, 1)
    nmb_done = dv.inc()

    @p.block.tensor
    def _(t):
        t.wait_ge(dv.sem, nmb_done)
        t.transpose(psT_b[0:1, 0:16], nm16b[0:16, 0:1], ident[0:16, 0:16]).then_inc(pe.sem, 1)
    trn2 = pe.inc()

    @p.block.vector
    def _(v):
        v.wait_ge(pe.sem, trn2)
        v.wait_ge(p.sCN.sem, ct_pad_done)
        v.tensor_reduce(srow[:, 1:2], psT_b[0:1, 0:16], AX.X, ALU.max).then_inc(dv.sem, 1)
        v.wait_ge(dv.sem, dv.inc())
        v.tensor_copy(srow_b[:, 0:8], srow[:, 0:8]).then_inc(dv.sem, 1)
    maxk_done = dv.inc()

    sc_slot = DmaSlot(p, "sCsc")

    @p.block.vector
    def _(v):
        v.wait_ge(dv.sem, maxk_done)
        if ct_out_ev[0] is not None:
            v.wait_ge(ct_out_ev[0][0], ct_out_ev[0][1])
        v.memset(CT[0:1, 0, 0:2056], 0.0).then_inc(dv.sem, 1)
        v.wait_ge(dv.sem, dv.inc())
        v.tensor_copy(CT[0:1, 0, 0:8], srow_b[:, 0:8]).then_inc(dv.sem, 1)
    scrow_done = dv.inc()

    @p.block.sync
    def _(sy):
        sy.wait_ge(dv.sem, scrow_done)
        sc_slot.issue(sy, cin.ap()[2048:2049, 0:2056], CT[0:1, 0, 0:2056])

    @p.block.gpsimd
    def _(g):
        for s in ct_slot:
            g.wait_ge(s.ctr.sem, s.ctr.v)
        g.collective_compute(
            "AllGather", ALU.bypass,
            replica_groups=[[0, 1, 2, 3], [4, 5, 6, 7]],
            ins=[cin.ap()[1024:2048, :].opt()], outs=[coutB.ap().opt()],
        ).then_inc(io.sem, 1)
    cc_h[1] = io.inc()

    @p.block.gpsimd
    def _(g):
        g.wait_ge(sc_slot.ctr.sem, sc_slot.ctr.v)
        g.collective_compute(
            "AllGather", ALU.bypass,
            replica_groups=[[0, 1, 2, 3], [4, 5, 6, 7]],
            ins=[cin.ap()[2048:2049, :].opt()], outs=[coutS.ap().opt()],
        ).then_inc(io.sem, 1)
    cc_st = io.inc()
    cc_done = cc_st

    qn_prev = None
    for c in range(NCH):
        @p.block.vector
        def _(v):
            v.wait_ge(dv.sem, q_done)
            if qn_prev is not None:
                v.wait_ge(pe.sem, qn_prev)
            v.tensor_mul(qsq[:, 0, :], qfm[:, c, :], qfm[:, c, :]).then_inc(dv.sem, 1)
        sqv = dv.inc()

        @p.block.tensor
        def _(t):
            t.wait_ge(dv.sem, sqv)
            if c == 0:
                t.wait_ge(dv.sem, kv_proj_done)
            for q in range(4):
                mm = t.matmul(psBIG[0:1, q * 512:(q + 1) * 512],
                              p.ones128_bf[:, :], qsq[:, 0, q * 512:(q + 1) * 512],
                              start=(c == 0), stop=(c == NCH - 1))
            mm.then_inc(pe.sem, 1)
        qn_prev = pe.inc()

    @p.block.vector
    def _(v):
        v.wait_ge(pe.sem, qn_prev)
        v.tensor_reduce(srow[:, 0:1], psBIG[0:1, 0:2048], AX.X, ALU.max).then_inc(dv.sem, 1)
    qn_done = dv.inc()

    km_slot = DmaSlot(p, "sKms")

    @p.block.sync
    def _(sy):
        sy.wait_ge(io.sem, cc_done)
        with nc.allow_non_contiguous_dma(reason="4 scalar max-norm values"):
            km_slot.issue(sy, kms[:, :], coutS.ap()[:, 1:2].rearrange("r b -> b r"))

    @p.block.vector
    def _(v):
        v.wait_ge(km_slot.ctr.sem, km_slot.ctr.v)
        v.wait_ge(dv.sem, qn_done)
        v.tensor_reduce(srow[:, 1:2], kms[:, :], AX.X, ALU.max).then_inc(dv.sem, 1)
        v.wait_ge(dv.sem, dv.inc())
        v.tensor_mul(srow[:, 2:3], srow[:, 0:1], srow[:, 1:2]).then_inc(dv.sem, 1)
    m_mul_done = dv.inc()

    @p.block.scalar
    def _(s):
        s.wait_ge(dv.sem, m_mul_done)
        s.activation(srow[:, 3:4], srow[:, 2:3], AF.Sqrt).then_inc(ac.sem, 1)
    m_sqrt_done = ac.inc()

    @p.block.vector
    def _(v):
        v.wait_ge(ac.sem, m_sqrt_done)
        v.tensor_scalar_mul(srow[:, 4:5], srow[:, 3:4], -1.0 / 32.0).then_inc(dv.sem, 1)
    negm_done = dv.inc()

    @p.block.tensor
    def _(t):
        t.wait_ge(dv.sem, negm_done)
        t.matmul(psE[:, 0:1], p.ones1x128_f[:, :], srow[0:1, 4:5], start=True, stop=True).then_inc(pe.sem, 1)
    negmb_mm = pe.inc()

    @p.block.vector
    def _(v):
        v.wait_ge(pe.sem, negmb_mm)
        v.tensor_copy(negMb[:, :], psE[:, 0:1]).then_inc(dv.sem, 1)
    negmb_done = dv.inc()

    kin_slot = [DmaSlot(p, f"sKin{i}") for i in range(2)]
    kb_out_slot = [DmaSlot(p, f"sKBo{i}") for i in range(2)]
    kt_consumed = [None, None]
    ct2_free = [None, None]
    psT = [psT_a, psT_b]
    psT_free = [None, None]
    kin_ev = [None, None]

    kbo_ev = [None] * 64

    def emit_preproc_group(kt_lo, kt_hi):
        for kt in range(kt_lo, kt_hi):
            sl = kt % 2
            rank = kt // 16
            r0 = (kt % 16) * 128

            @p.block.sync
            def _(sy, kt=kt, sl=sl, rank=rank, r0=r0):
                sy.wait_ge(io.sem, cc_h[(kt % 16) // 8])
                if kt_consumed[sl] is not None:
                    sy.wait_ge(pe.sem, kt_consumed[sl])
                kin_ev[sl] = kin_slot[sl].issue(sy, Kt[:, sl, :], cout_v(rank, r0, slice(0, 1024)))

            tr_vals = []
            for c2 in range(NCH):
                tb = c2 % 2

                @p.block.tensor
                def _(t, sl=sl, c2=c2, tb=tb):
                    if c2 == 0:
                        t.wait_ge(kin_ev[sl][0], kin_ev[sl][1])
                    if psT_free[tb] is not None:
                        t.wait_ge(dv.sem, psT_free[tb])
                    t.transpose(psT[tb][:, :], Kt[:, sl, c2 * 128:(c2 + 1) * 128], ident[:, :]).then_inc(pe.sem, 1)
                trv = pe.inc()
                if c2 == NCH - 1:
                    kt_consumed[sl] = trv

                @p.block.vector
                def _(v, sl=sl, c2=c2, tb=tb, trv=trv):
                    v.wait_ge(pe.sem, trv)
                    if c2 == 0 and kb_out_slot[sl].ctr.v > 0:
                        v.wait_ge(kb_out_slot[sl].ctr.sem, kb_out_slot[sl].ctr.v)
                    v.tensor_copy(KBs[:, sl, c2 * 128:(c2 + 1) * 128], psT[tb][:, :]).then_inc(dv.sem, 1)
                psT_free[tb] = dv.inc()
                tr_vals.append(psT_free[tb])

            @p.block.sync
            def _(sy, kt=kt, sl=sl, last=tr_vals[-1]):
                sy.wait_ge(dv.sem, last)
                kbo_ev[kt] = kb_out_slot[sl].issue(sy, KB_v[kt, :, 0:1024], KBs[:, sl, :])

    emit_preproc_group(0, 8)

    S_free = [negmb_done, negmb_done]
    S_exp = [None, None]
    praw_free = [None, None]
    O_free = [None, None]
    l_free = negmb_done
    rlb_free = None
    kt2_consumed = [None, None]
    vg_slot = [DmaSlot(p, f"sVg{i}") for i in range(2)]
    kin2_slot = [DmaSlot(p, f"sKin2{i}") for i in range(2)]
    vg_consumed = [None, None]
    kin2_ev = [None, None]
    vg_ev = [None, None]
    B_last = None

    for st in range(4):
        qsl = slice(st * 512, (st + 1) * 512)
        ktmin = 4 * st
        strip_first_grp = True
        for g in range(8):
            gsl = g % 2
            keep = [i for i in range(8) if g * 8 + i >= ktmin]
            if not keep:
                continue
            nk = len(keep)
            if st == 0 and g < 7:
                emit_preproc_group(8 * (g + 1), 8 * (g + 2))

            @p.block.gpsimd
            def _(gq, keep=keep):
                if st == 0 and g < 2:
                    gq.wait_ge(io.sem, cc_done)
                vs = vg_slot[gsl].ctr
                if vg_consumed[gsl] is not None:
                    gq.wait_ge(pe.sem, vg_consumed[gsl])
                    gq.wait_ge(vs.sem, vs.v)
                for i in keep:
                    kt = g * 8 + i
                    rank = kt // 16
                    r0 = (kt % 16) * 128
                    gq.wait_ge(io.sem, cc_h[(kt % 16) // 8])
                    gq.dma_start(out=Vg[:, gsl, i, :],
                                 in_=cout_v(rank, r0, slice(1024, 2048))).then_inc(vs.sem, 16)
                    vs.inc(16)
                vg_ev[gsl] = (vs.sem, vs.v)

            stt_vals = {}
            pe0 = pe.v
            ac0 = ac.v
            dv0 = dv.v
            carry_praw = list(praw_free)
            first_grp = (st == 0 and g == 0)

            qk_val5 = [pe0 + (1 if pos == 0 else 2 * pos) for pos in range(nk)]

            @p.block.scalar
            def _(s, keep=keep, qk_val5=qk_val5, dv0=dv0, carry=carry_praw, first=first_grp, g=g):
                for pos, i in enumerate(keep):
                    sb = (g * 8 + i) % 2
                    s.wait_ge(pe.sem, qk_val5[pos])
                    if first and pos == 0:
                        s.wait_ge(dv.sem, negmb_done)
                    if pos >= 2:
                        s.wait_ge(dv.sem, dv0 + pos - 1)
                    elif carry[sb] is not None:
                        s.wait_ge(dv.sem, carry[sb])
                    s.activation(Praw[:, sb, :], S_bank[sb], AF.Exp, bias=negMb[:, 0:1],
                                 scale=co32[:, 0:1]).then_inc(ac.sem, 1)
            ac.inc(nk)

            def emit_lmm5(j, g=g, ktmin=ktmin):
                kt_j = g * 8 + j

                @p.block.tensor
                def _(t):
                    t.wait_ge(dv.sem, stt_vals[j])
                    if kt_j == ktmin:
                        t.wait_ge(dv.sem, l_free)
                    t.matmul(l_bank, p.ones128_bf[:, :], Pg[:, j, :],
                             start=(kt_j == ktmin), stop=(kt_j == 63), skip_group_check=True).then_inc(pe.sem, 1)
                pe.inc()

            for pos, i in enumerate(keep):
                kt = g * 8 + i
                sl = kt % 2
                rank = kt // 16
                r0 = (kt % 16) * 128

                @p.block.sync
                def _(sy):
                    if kt2_consumed[sl] is not None:
                        sy.wait_ge(pe.sem, kt2_consumed[sl])
                    if st == 0:
                        sy.wait_ge(kbo_ev[kt][0], kbo_ev[kt][1])
                    kin2_ev[sl] = kin2_slot[sl].issue(sy, Kt2[:, sl, :], KB_v[kt, :, 0:1024])

                sb = kt % 2

                @p.block.tensor
                def _(t):
                    t.wait_ge(kin2_ev[sl][0], kin2_ev[sl][1])
                    if st == 0 and g == 0 and i == 0:
                        t.wait_ge(dv.sem, negmb_done)
                    if S_exp[sb] is not None:
                        t.wait_ge(ac.sem, S_exp[sb])
                    for c in range(NCH):
                        mm = t.matmul(S_bank[sb], Kt2[:, sl, c * 128:(c + 1) * 128], qfm[:, c, qsl],
                                      start=(c == 0), stop=(c == NCH - 1))
                    mm.then_inc(pe.sem, 1)
                qk_v = pe.inc()
                kt2_consumed[sl] = qk_v
                assert qk_v == qk_val5[pos]

                exp_v = ac0 + pos + 1
                S_exp[sb] = exp_v

                @p.block.vector
                def _(v):
                    v.wait_ge(ac.sem, exp_v)
                    if B_last is not None and pos == 0:
                        v.wait_ge(pe.sem, B_last)
                    v.scalar_tensor_tensor(Pg[:, i, :], pj_sb[:, :], cbc_sb[:, st * 64 + kt:st * 64 + kt + 1],
                                           Praw[:, sb, :], ALU.is_ge, ALU.mult).then_inc(dv.sem, 1)
                praw_free[sb] = dv.inc()
                assert praw_free[sb] == dv0 + pos + 1
                stt_vals[i] = praw_free[sb]

                if pos >= 1:
                    emit_lmm5(keep[pos - 1])
            emit_lmm5(keep[-1])

            for c in range(NCH):
                ob = c % 2

                @p.block.tensor
                def _(t, keep=keep):
                    if c == 0:
                        t.wait_ge(vg_ev[gsl][0], vg_ev[gsl][1])
                    if O_free[ob] is not None:
                        t.wait_ge(dv.sem, O_free[ob])
                    for pos, i in enumerate(keep):
                        mm = t.matmul(O_bank[ob], Vg[:, gsl, i, c * 128:(c + 1) * 128], Pg[:, i, :],
                                      start=(pos == 0), stop=(pos == len(keep) - 1), skip_group_check=True)
                    mm.then_inc(pe.sem, 1)
                pv_v = pe.inc()
                if c == NCH - 1:
                    vg_consumed[gsl] = pv_v
                    B_last = pv_v

                @p.block.vector
                def _(v, strip_first_grp=strip_first_grp):
                    v.wait_ge(pe.sem, pv_v)
                    if strip_first_grp:
                        v.tensor_copy(O_sb[:, c, :], O_bank[ob]).then_inc(dv.sem, 1)
                    else:
                        v.tensor_add(O_sb[:, c, :], O_sb[:, c, :], O_bank[ob]).then_inc(dv.sem, 1)
                O_free[ob] = dv.inc()
            strip_first_grp = False

        @p.block.vector
        def _(v):
            v.wait_ge(pe.sem, B_last)
            v.reciprocal(rl[:, :], l_bank).then_inc(dv.sem, 1)
        l_free = dv.inc()

        @p.block.tensor
        def _(t):
            t.wait_ge(dv.sem, l_free)
            if rlb_free is not None:
                t.wait_ge(dv.sem, rlb_free)
            t.matmul(rlb_bank, p.ones1x128_f[:, :], rl[0:1, :], start=True, stop=True).then_inc(pe.sem, 1)
        rlb_mm = pe.inc()

        fin_vals = []
        for c in range(NCH):
            @p.block.vector
            def _(v):
                v.wait_ge(pe.sem, rlb_mm)
                v.wait_ge(dv.sem, dv.v)
                v.tensor_mul(tmp_f32[:, 0:512], O_sb[:, c, :], rlb_bank).then_inc(dv.sem, 1)
                v.wait_ge(dv.sem, dv.inc())
                v.tensor_add(xres[:, c, qsl], xres[:, c, qsl], tmp_f32[:, 0:512]).then_inc(dv.sem, 1)
            fin_vals.append(dv.inc())
        rlb_free = fin_vals[-1]

    return emit_store_x(p, xres, xo_d, wait_dv=dv.v)




def sub_att7(p, x_d, xo_d, after=None):
    nc = p.nc
    g7_d = p.in_ap("g7", [128, NCH], F32)
    be7_d = p.in_ap("be7", [128, NCH], F32)
    Wqkv_d = p.in_ap("Wqkv7", [D, 3 * D], F32)
    Wout_d = p.in_ap("Wout7", [D, D], F32)
    tupp_d = p.in_ap("t_upp", [128, 128], BF16)
    tlow_d = p.in_ap("t_low", [128, 128], BF16)
    mask7_d = p.in_ap("mask7", [128, 128], BF16)
    ident_d = p.in_ap("ident", [128, 128], BF16)

    cKb = p.dram("cKb7", [D, 128], BF16)
    gKb = p.dram("gKb7", [GRP * D, 128], BF16)
    gKb_v = gKb.ap().rearrange("(r k p) j -> r p k j", r=GRP, p=128)
    cVb = p.dram("cVb7", [128, 1040], BF16)
    gVb = p.dram("gVb7", [GRP * 128, 1040], BF16)
    gVb_v = gVb.ap().rearrange("(r p) j -> r p j", r=GRP)
    cSb = p.dram("cSb7", [2, 8], BF16)
    gSb = p.dram("gSb7", [GRP * 2, 8], BF16)
    stb = p.dram("stb7", [2, 24], F32)

    xch = p.sbuf("xch", [128, 2, TLOC], F32)
    xnscr = p.sbuf("xnscr", [128, 2, TLOC], BF16)
    xr_b = p.xres.ap().rearrange("p c t -> p (c t)").bitcast(BF16)
    qfm = xr_b[:, 0:16384].rearrange("p (c t) -> p c t", c=NCH)
    kfm = xr_b[:, 16384:32768].rearrange("p (c t) -> p c t", c=NCH)
    V7t = p.sbuf("V7t", [128, 17, 1040], BF16)
    V7 = V7t.ap().rearrange("p w (pr two c) -> p w pr two c", pr=NCH, two=2)
    arenaO = p.sbuf("arenaO", [128, NCH * TLOC // 1], BF16)
    xn = arenaO.ap().rearrange("p (c t) -> p c t", c=NCH)
    O7 = arenaO.ap().rearrange("p (c t) -> p c t", c=NCH)
    wbuf = p.sbuf("wbuf", [128, NCH, 2048], BF16)
    g7_sb = p.sbuf("g7_sb", [128, NCH], F32)
    be7_sb = p.sbuf("be7_sb", [128, NCH], F32)
    kb7 = p.sbuf("kb7", [128, NCH, 128], BF16)
    P7 = p.sbuf("P7", [128, 4, 128], BF16)
    Praw7 = p.sbuf("Praw7", [128, 4, 128], BF16)
    tupp = p.sbuf("tupp_sb", [128, 128], BF16)
    tlow = p.sbuf("tlow_sb", [128, 128], BF16)
    mask7 = p.sbuf("mask7_sb", [128, 128], BF16)
    ident = p.sbuf("ident_sb", [128, 128], BF16)
    stats = p.sbuf("stats", [2, 24], F32)
    statrow = p.sbuf("statrow", [1, 48], F32)
    stb_b = p.sbuf("stb_b", [2, 24], BF16)
    srow7 = p.sbuf("srow7", [1, 32], F32)
    negM7b = p.sbuf("negM7b", [128, 16], F32)
    co8 = p.sbuf("co8", [128, 1], F32)
    lrow65 = p.sbuf("lrow65", [65, TLOC], F32)
    lrow = lrow65.ap()[0:1, :]
    ones_b64 = p.sbuf("ones_b64", [65, 64], F32)
    co_n1 = p.sbuf("co_n1", [65, 1], F32)
    onespat = p.sbuf("onespat", [128, 2], BF16)
    Ofp = None
    sqs = p.sbuf("sqs", [128, TLOC], BF16)
    p.ones128_bf = p.sbuf("ones128_bf", [128, 1], BF16)
    p.ones1x128_f = p.sbuf("ones1x128_f", [1, 128], F32)
    tmp_f32 = sqs.ap().bitcast(F32)
    v7f = V7t.ap().rearrange("p w c -> p (w c)").bitcast(F32)
    p.r_m = v7f[0:1, 0:2048]
    p.r_a = v7f[0:1, 2048:4096]
    p.r_b = v7f[0:1, 4096:6144]
    Ofp = xch.ap()[0:64, 0, :]

    p.psBIG = p.psum("psBIG", [128, 2048], F32)
    psF = p.psum("psF", [128, 2048], F32)
    psBIG = p.psBIG

    pe, dv, ac, io = p.sPE, p.sDV, p.sAC, p.sIO

    @p.block.gpsimd
    def _(g):
        g.memset(p.ones128_bf[:, :], 1.0)
        g.memset(p.ones1x128_f[:, :], 1.0)
        g.memset(co8[:, :], 0.125)
        g.memset(ones_b64[:, :], 1.0)
        g.memset(co_n1[:, :], -1.0)
        g.memset(onespat[:, :], 0.0).then_inc(p.sCN.sem, 1)
        g.wait_ge(p.sCN.sem, p.sCN.inc())
        g.memset(onespat[0:64, 0:1], 1.0)
        g.memset(onespat[64:128, 1:2], 1.0).then_inc(p.sCN.sem, 1)
    cn_done = p.sCN.inc()

    swt = p.dma_sem("sWts")

    @p.block.gpsimd
    def _(g):
        g.dma_start(out=wbuf[:, :, :], in_=Wqkv_d[:, 0:2048].rearrange("(k p) j -> p k j", p=128)).then_inc(swt.sem, 16)
        g.dma_start(out=g7_sb[:, :], in_=g7_d).then_inc(swt.sem, 16)
        g.dma_start(out=be7_sb[:, :], in_=be7_d).then_inc(swt.sem, 16)
        g.dma_start(out=tupp[:, :], in_=tupp_d).then_inc(swt.sem, 16)
        g.dma_start(out=tlow[:, :], in_=tlow_d).then_inc(swt.sem, 16)
        g.dma_start(out=mask7[:, :], in_=mask7_d).then_inc(swt.sem, 16)
        g.dma_start(out=ident[:, :], in_=ident_d).then_inc(swt.sem, 16)
    wqk_ready = (swt.sem, swt.inc(112))

    xsl = [DmaSlot(p, f"sXc{i}") for i in range(2)]
    x_ev = [None, None]
    cast_done = [None, None]
    stat_prev = [None, None]
    stat_mm = None
    for c in range(NCH):
        sl = c % 2

        @p.block.sync
        def _(sy):
            if c == 0 and after is not None:
                sy.wait_ge(after[0], after[1])
            if cast_done[sl] is not None:
                sy.wait_ge(dv.sem, cast_done[sl])
            x_ev[sl] = xsl[sl].issue(sy, xch[:, sl, :], x_d[bass.ds(c * 128, 128), :])

        @p.block.vector
        def _(v):
            v.wait_ge(x_ev[sl][0], x_ev[sl][1])
            v.tensor_copy(xn[:, c, :], xch[:, sl, :]).then_inc(dv.sem, 1)
            cast_done[sl] = dv.inc()
            v.wait_ge(dv.sem, cast_done[sl])
            if stat_prev[sl] is not None:
                v.wait_ge(pe.sem, stat_prev[sl])
            v.tensor_mul(xnscr[:, sl, :], xn[:, c, :], xn[:, c, :]).then_inc(dv.sem, 1)
        sq_done = dv.inc()

        @p.block.tensor
        def _(t):
            t.wait_ge(dv.sem, sq_done)
            if c == 0:
                t.wait_ge(p.sCN.sem, cn_done)
            for q in range(4):
                t.matmul(psBIG[0:1, q * 512:(q + 1) * 512], p.ones128_bf[:, :],
                         xn[:, c, q * 512:(q + 1) * 512], start=(c == 0), stop=(c == NCH - 1),
                         skip_group_check=True)
            for q in range(4):
                mm = t.matmul(psF[0:1, q * 512:(q + 1) * 512], p.ones128_bf[:, :],
                              xnscr[:, sl, q * 512:(q + 1) * 512], start=(c == 0), stop=(c == NCH - 1),
                              skip_group_check=True)
            mm.then_inc(pe.sem, 1)
        stat_mm = pe.inc()
        stat_prev[sl] = stat_mm

    @p.block.vector
    def _(v):
        v.wait_ge(pe.sem, stat_mm)
        v.tensor_scalar_mul(p.r_m[:, :], psBIG[0:1, 0:2048], 1.0 / D).then_inc(dv.sem, 1)
        v.tensor_scalar_mul(p.r_a[:, :], psF[0:1, 0:2048], 1.0 / D).then_inc(dv.sem, 1)
        v.wait_ge(dv.sem, dv.inc(2))
        v.tensor_mul(p.r_b[:, :], p.r_m[:, :], p.r_m[:, :]).then_inc(dv.sem, 1)
        v.wait_ge(dv.sem, dv.inc())
        v.tensor_sub(p.r_a[:, :], p.r_a[:, :], p.r_b[:, :]).then_inc(dv.sem, 1)
        v.wait_ge(dv.sem, dv.inc())
        v.tensor_scalar_add(p.r_a[:, :], p.r_a[:, :], EPS).then_inc(dv.sem, 1)
    vpe_done = dv.inc()

    @p.block.scalar
    def _(s):
        s.wait_ge(dv.sem, vpe_done)
        s.activation(p.r_b[:, :], p.r_a[:, :], AF.Sqrt).then_inc(ac.sem, 1)
    sq1 = ac.inc()

    @p.block.vector
    def _(v):
        v.wait_ge(ac.sem, sq1)
        v.reciprocal(p.r_a[:, :], p.r_b[:, :]).then_inc(dv.sem, 1)
        v.wait_ge(dv.sem, dv.inc())
        v.tensor_mul(p.r_b[:, :], p.r_a[:, :], p.r_a[:, :]).then_inc(dv.sem, 1)
        v.wait_ge(dv.sem, dv.inc())
        v.tensor_scalar(p.r_b[:, :], p.r_b[:, :], -EPS, 1.0 + EPS, ALU.mult, ALU.add).then_inc(dv.sem, 1)
    vt_done = dv.inc()

    @p.block.scalar
    def _(s):
        s.wait_ge(dv.sem, vt_done)
        s.activation(p.r_b[:, :], p.r_b[:, :], AF.Sqrt).then_inc(ac.sem, 1)
    sq2 = ac.inc()

    @p.block.vector
    def _(v):
        v.wait_ge(ac.sem, sq2)
        v.reciprocal(lrow[:, :], p.r_b[:, :]).then_inc(dv.sem, 1)
        v.wait_ge(dv.sem, dv.inc())
        v.tensor_mul(p.r_a[:, :], p.r_a[:, :], lrow[:, :]).then_inc(dv.sem, 1)
    shat_done = dv.inc()

    psum_free = shat_done
    for h in range(2):
        tok = slice(h * 1024, (h + 1) * 1024)

        @p.block.tensor
        def _(t):
            t.wait_ge(dv.sem, psum_free)
            for q in range(2):
                sl2 = slice(h * 1024 + q * 512, h * 1024 + (q + 1) * 512)
                t.matmul(psBIG[:, q * 512:(q + 1) * 512], p.ones1x128_f[:, :], p.r_m[0:1, sl2],
                         start=True, stop=True)
                mm = t.matmul(psBIG[:, 1024 + q * 512:1024 + (q + 1) * 512],
                              p.ones1x128_f[:, :], p.r_a[0:1, sl2], start=True, stop=True)
            mm.then_inc(pe.sem, 1)
        bc_done = pe.inc()

        @p.block.vector
        def _(v):
            v.wait_ge(pe.sem, bc_done)
            if h == 0:
                v.wait_ge(wqk_ready[0], wqk_ready[1])
            for c in range(NCH):
                v.tensor_sub(tmp_f32[:, 0:1024], xn[:, c, tok], psBIG[:, 0:1024]).then_inc(dv.sem, 1)
                v.wait_ge(dv.sem, dv.inc())
                v.tensor_mul(tmp_f32[:, 0:1024], tmp_f32[:, 0:1024], psBIG[:, 1024:2048]).then_inc(dv.sem, 1)
                v.wait_ge(dv.sem, dv.inc())
                v.tensor_scalar(xn[:, c, tok], tmp_f32[:, 0:1024], g7_sb[:, c:c + 1], be7_sb[:, c:c + 1],
                                ALU.mult, ALU.add).then_inc(dv.sem, 1)
                v.wait_ge(dv.sem, dv.inc())
        psum_free = dv.v
    u_ready = dv.v

    @p.block.gpsimd
    def _(g):
        g.wait_ge(dv.sem, u_ready)
        g.memset(V7t[:, :, :], 0.0).then_inc(p.sCN.sem, 1)
        g.wait_ge(p.sCN.sem, p.sCN.inc())
        g.memset(V7[:, :, :, :, 64:65], 1.0).then_inc(p.sCN.sem, 1)
    cn2_done = p.sCN.inc()

    slots = [psBIG[:, i * 512:(i + 1) * 512] for i in range(4)]

    def cb_q(m, q, ps, mmv):
        @p.block.vector
        def _(v):
            v.wait_ge(pe.sem, mmv)
            v.tensor_copy(qfm[:, m, q * 512:(q + 1) * 512], ps).then_inc(dv.sem, 1)
        return dv.inc()

    emit_mm_dense(p, wbuf.ap()[:, :, 0:1024].rearrange("p k j -> p k j"), xn, NCH, cb_q,
                  wait_w=wqk_ready, wait_dv=u_ready, psum_rot=slots)

    def cb_k(m, q, ps, mmv):
        @p.block.vector
        def _(v):
            v.wait_ge(pe.sem, mmv)
            v.tensor_copy(kfm[:, m, q * 512:(q + 1) * 512], ps).then_inc(dv.sem, 1)
        return dv.inc()

    emit_mm_dense(p, wbuf.ap()[:, :, 1024:2048].rearrange("p k j -> p k j"), xn, NCH, cb_k,
                  wait_dv=dv.v, psum_rot=slots)
    qk_done = dv.v

    @p.block.gpsimd
    def _(g):
        g.wait_ge(pe.sem, pe.v)
        g.dma_start(out=wbuf[:, :, 0:1024],
                    in_=Wqkv_d[:, 2048:3072].rearrange("(k p) j -> p k j", p=128)).then_inc(swt.sem, 16)
    wv_ready = (swt.sem, swt.inc(16))

    vslot_ev = [None] * 4
    for w in range(16):
        for pp in range(NCH):
            j = (w * NCH + pp) % 4

            @p.block.tensor
            def _(t):
                if w == 0 and pp == 0:
                    t.wait_ge(wv_ready[0], wv_ready[1])
                    t.wait_ge(dv.sem, qk_done)
                if vslot_ev[j] is not None:
                    t.wait_ge(dv.sem, vslot_ev[j])
                for k in range(NCH):
                    mm = t.matmul(slots[j][:, 0:128], xn[:, k, w * 128:(w + 1) * 128],
                                  wbuf[:, k, pp * 128:(pp + 1) * 128],
                                  start=(k == 0), stop=(k == NCH - 1))
                mm.then_inc(pe.sem, 1)
            mmv = pe.inc()

            @p.block.vector
            def _(v):
                v.wait_ge(pe.sem, mmv)
                if w == 0 and pp == 0:
                    v.wait_ge(p.sCN.sem, cn2_done)
                v.tensor_copy(V7[:, w + 1, pp, :, 0:64],
                              slots[j][:, 0:128].rearrange("p (two c) -> p two c", two=2)).then_inc(dv.sem, 1)
            vslot_ev[j] = dv.inc()
    vprime_done = dv.v

    hm_prev = [None]

    def emit_headmax(src_fm, col0, tokens_sl, extra_cols):
        for c in range(NCH):
            @p.block.vector
            def _(v):
                v.wait_ge(dv.sem, dv.v)
                if c == 0:
                    v.wait_ge(dv.sem, vprime_done)
                if hm_prev[0] is not None:
                    v.wait_ge(pe.sem, hm_prev[0])
                v.tensor_mul(sqs[:, 0:2048], src_fm[:, c, :], src_fm[:, c, :]).then_inc(dv.sem, 1)
            sqv = dv.inc()

            @p.block.tensor
            def _(t):
                t.wait_ge(dv.sem, sqv)
                for q in range(4):
                    mm = t.matmul(psF[0:2, q * 512:(q + 1) * 512], onespat[:, :],
                                  sqs[:, q * 512:(q + 1) * 512], start=True, stop=True)
                mm.then_inc(pe.sem, 1)
            nmv = pe.inc()
            hm_prev[0] = nmv

            @p.block.vector
            def _(v):
                v.wait_ge(pe.sem, nmv)
                v.tensor_reduce(stats[0:2, col0 + c:col0 + c + 1], psF[0:2, tokens_sl], AX.X, ALU.max).then_inc(dv.sem, 1)
                if extra_cols is not None:
                    v.wait_ge(dv.sem, dv.inc())
                    v.tensor_reduce(stats[0:2, extra_cols + c:extra_cols + c + 1],
                                    psF[0:2, 1920:2048], AX.X, ALU.max).then_inc(dv.sem, 1)
                    dv.inc()
                else:
                    dv.inc()

    emit_headmax(kfm, 0, slice(0, 2048), 16)
    emit_headmax(qfm, 8, slice(0, 2048), None)
    stats_done = dv.v

    @p.block.vector
    def _(v):
        v.wait_ge(dv.sem, stats_done)
        v.tensor_copy(stb_b[:, :], stats[:, :]).then_inc(dv.sem, 1)
    stb_done = dv.inc()

    bnd = DmaSlot(p, "sBnd")

    @p.block.sync
    def _(sy):
        sy.wait_ge(dv.sem, stb_done)
        for c in range(NCH):
            sy.dma_start(out=cKb.ap().rearrange("(k p) j -> p k j", p=128)[:, c, :],
                         in_=kfm[:, c, 1920:2048]).then_inc(bnd.ctr.sem, 16)
            bnd.ctr.inc(16)
        sy.dma_start(out=cVb.ap(), in_=V7t[:, 16, :]).then_inc(bnd.ctr.sem, 16)
        sy.dma_start(out=cSb.ap(), in_=stb_b[:, 0:8]).then_inc(bnd.ctr.sem, 16)
        bnd.ctr.inc(32)

    @p.block.gpsimd
    def _(g):
        g.wait_ge(bnd.ctr.sem, bnd.ctr.v)
        g.collective_compute("AllGather", ALU.bypass, replica_groups=[[0, 1, 2, 3], [4, 5, 6, 7]],
                             ins=[cKb.ap().opt()], outs=[gKb.ap().opt()]).then_inc(io.sem, 1)
        g.collective_compute("AllGather", ALU.bypass, replica_groups=[[0, 1, 2, 3], [4, 5, 6, 7]],
                             ins=[cVb.ap().opt()], outs=[gVb.ap().opt()]).then_inc(io.sem, 1)
        g.collective_compute("AllGather", ALU.bypass, replica_groups=[[0, 1, 2, 3], [4, 5, 6, 7]],
                             ins=[cSb.ap().opt()], outs=[gSb.ap().opt()]).then_inc(io.sem, 1)
    cc_done = io.inc(3)

    bin_ = DmaSlot(p, "sBin")

    @p.block.sync
    def _(sy):
        sy.wait_ge(io.sem, cc_done)
        pid = nc.partition_id()
        prev = (pid + 3) % 4
        prev4 = (pid % 4 + 3) % 4
        sy.dma_start(out=kb7[:, :, :], in_=gKb_v[prev4]).then_inc(bin_.ctr.sem, 16)
        sy.dma_start(out=V7t[:, 0, :], in_=gVb_v[prev4]).then_inc(bin_.ctr.sem, 16)
        bin_.ctr.inc(32)

    stb_slot = DmaSlot(p, "sStb")

    @p.block.sync
    def _(sy):
        sy.wait_ge(dv.sem, stb_done)
        sy.dma_start(out=stb.ap(), in_=stats[:, :]).then_inc(stb_slot.ctr.sem, 16)
        stb_slot.ctr.inc(16)
        sy.wait_ge(stb_slot.ctr.sem, stb_slot.ctr.v)
        sy.dma_start(out=statrow[0:1, 0:48], in_=stb.ap().rearrange("a b -> (a b)")).then_inc(stb_slot.ctr.sem, 16)
        stb_slot.ctr.inc(16)

    prevst = DmaSlot(p, "sPrevSt")

    @p.block.gpsimd
    def _(gq):
        gq.wait_ge(io.sem, cc_done)
        pid = nc.partition_id()
        prev4 = (pid % 4 + 3) % 4
        gq.dma_start(out=lrow[0:1, 0:16],
                     in_=gSb.ap().rearrange("(r a) b -> r (a b)", r=GRP)[bass.ds(prev4, 1)]).then_inc(prevst.ctr.sem, 16)
        prevst.ctr.inc(16)

    @p.block.vector
    def _(v):
        v.wait_ge(stb_slot.ctr.sem, stb_slot.ctr.v)
        v.wait_ge(prevst.ctr.sem, prevst.ctr.v)
        stv = statrow.ap()
        lv = lrow
        v.tensor_max(srow7[0:1, 0:8], stv[0:1, 0:8], lv[0:1, 0:8])
        v.tensor_max(srow7[0:1, 8:16], stv[0:1, 24:32], lv[0:1, 8:16]).then_inc(dv.sem, 1)
        v.wait_ge(dv.sem, dv.inc())
        v.tensor_mul(srow7[0:1, 0:8], srow7[0:1, 0:8], stv[0:1, 8:16])
        v.tensor_mul(srow7[0:1, 8:16], srow7[0:1, 8:16], stv[0:1, 32:40]).then_inc(dv.sem, 1)
    prod_done = dv.inc()

    @p.block.scalar
    def _(s):
        s.wait_ge(dv.sem, prod_done)
        s.activation(srow7[0:1, 16:32], srow7[0:1, 0:16], AF.Sqrt).then_inc(ac.sem, 1)
    msq = ac.inc()

    @p.block.vector
    def _(v):
        v.wait_ge(ac.sem, msq)
        v.tensor_scalar_mul(srow7[0:1, 0:16], srow7[0:1, 16:32], -0.125).then_inc(dv.sem, 1)
    negm_done = dv.inc()

    @p.block.tensor
    def _(t):
        t.wait_ge(dv.sem, negm_done)
        t.matmul(psF[:, 0:16], p.ones1x128_f[:, :], srow7[0:1, 0:16], start=True, stop=True).then_inc(pe.sem, 1)
    nbm = pe.inc()

    @p.block.vector
    def _(v):
        v.wait_ge(pe.sem, nbm)
        v.tensor_copy(negM7b[:, :], psF[:, 0:16]).then_inc(dv.sem, 1)
    negmb_done = dv.inc()

    Sslot = [psF[:, 0:128], psF[:, 512:640], psF[:, 1024:1152], psF[:, 1536:1664]]
    s_exp = [None] * 4
    p_used = [None] * 4
    o_evac = negmb_done
    prev_head = [None]
    for pp in range(NCH):
        for hh in range(2):
            head_col = hh * 8 + pp
            hsl = slice(hh * 64, hh * 64 + 64)
            pending = prev_head[0]

            pe0 = pe.v
            ac0 = ac.v
            dv0 = dv.v
            carry_pused = list(p_used)
            qk_val = [None] * 32
            pv_val = [None] * 32
            cnt = pe0
            for i in range(32):
                cnt += 1
                qk_val[i] = cnt
                if i == 0 and pending is not None:
                    cnt += 1
                    rbv_val = cnt
                if i >= 2:
                    cnt += 1
                    pv_val[i - 2] = cnt
            pv_val[30] = cnt + 1
            pv_val[31] = cnt + 2
            pv_flush = pv_val[31]
            if pending is not None:
                mul_val = dv0 + 1
                msk_val = [dv0 + i + 2 for i in range(32)]
                o_evac = mul_val
            else:
                msk_val = [dv0 + i + 1 for i in range(32)]
            acr = 2 if pending is not None else 0
            exp_val = [ac0 + acr + i + 1 for i in range(32)]
            recip_val = ac0 + 2 if pending is not None else None

            @p.block.scalar
            def _(s, pending=pending, qk_val=qk_val, pv_val=pv_val,
                  carry=carry_pused, head_col=head_col):
                if pending is not None:
                    s.wait_ge(pe.sem, pending["pv_flush"])
                    s.activation(lrow65[64:65, 0:TLOC], psBIG[64:65, 0:TLOC],
                                 AF.Ln).then_inc(ac.sem, 1)
                    s.wait_ge(ac.sem, ac.v + 1)
                    s.activation(lrow65[64:65, 0:TLOC], lrow65[64:65, 0:TLOC],
                                 AF.Exp, scale=co_n1[64:65, 0:1]).then_inc(ac.sem, 1)
                for i in range(32):
                    si = i % 4
                    s.wait_ge(pe.sem, qk_val[i])
                    if i >= 4:
                        s.wait_ge(pe.sem, pv_val[i - 4])
                    elif carry[si] is not None:
                        s.wait_ge(pe.sem, carry[si])
                    s.activation(Praw7[:, si, :], Sslot[si], AF.Exp,
                                 bias=negM7b[:, head_col:head_col + 1],
                                 scale=co8[:, 0:1]).then_inc(ac.sem, 1)
            ac.inc(32 + acr)

            def emit_pv7(j, pp=pp, hh=hh, msk_val=msk_val, pv_val=pv_val,
                         oe0=o_evac):
                w_, pv_ = j // 2, j % 2
                si = j % 4

                @p.block.tensor
                def _(t):
                    t.wait_ge(dv.sem, msk_val[j])
                    if j == 0:
                        t.wait_ge(dv.sem, oe0)
                    t.matmul(psBIG[0:65, w_ * 128:(w_ + 1) * 128],
                             V7[:, w_ + pv_, pp, hh, :], P7[:, si, :],
                             start=(pv_ == 0), stop=(pv_ == 1), skip_group_check=True).then_inc(pe.sem, 1)
                p_used[si] = pe.inc()
                assert p_used[si] == pv_val[j]

            for w in range(16):
                for pv in range(2):
                    i = w * 2 + pv
                    si = i % 4

                    @p.block.tensor
                    def _(t):
                        if pp == 0 and hh == 0 and w == 0 and pv == 0:
                            t.wait_ge(bin_.ctr.sem, bin_.ctr.v)
                            t.wait_ge(dv.sem, negmb_done)
                        if s_exp[si] is not None:
                            t.wait_ge(ac.sem, s_exp[si])
                        if pv == 0 and w == 0:
                            lhs = kb7[hsl, pp, :]
                        elif pv == 0:
                            lhs = kfm[hsl, pp, (w - 1) * 128:w * 128]
                        else:
                            lhs = kfm[hsl, pp, w * 128:(w + 1) * 128]
                        t.matmul(Sslot[si], lhs, qfm[hsl, pp, w * 128:(w + 1) * 128],
                                 start=True, stop=True).then_inc(pe.sem, 1)
                    qkv_ = pe.inc()
                    assert qkv_ == qk_val[i]

                    if i == 0 and pending is not None:
                        @p.block.tensor
                        def _(t, pending=pending, recip_val=recip_val):
                            t.wait_ge(ac.sem, recip_val)
                            t.wait_ge(dv.sem, pending["copy_v"])
                            for q in range(4):
                                mm = t.matmul(psBIG[0:64, q * 512:(q + 1) * 512],
                                              ones_b64[64:65, 0:64],
                                              lrow65[64:65, q * 512:(q + 1) * 512],
                                              start=True, stop=True)
                            mm.then_inc(pe.sem, 1)
                        rv = pe.inc()
                        assert rv == rbv_val

                        @p.block.vector
                        def _(v, pending=pending, rv=rv):
                            v.wait_ge(pe.sem, rv)
                            v.tensor_mul(O7[pending["hsl"], pending["pp"], :], Ofp[:, :],
                                         psBIG[0:64, 0:2048]).then_inc(dv.sem, 1)
                        mv = dv.inc()
                        assert mv == mul_val

                    expv = exp_val[i]
                    s_exp[si] = expv

                    @p.block.vector
                    def _(v):
                        v.wait_ge(ac.sem, expv)
                        msk = tupp if pv == 1 else (mask7 if w == 0 else tlow)
                        v.tensor_mul(P7[:, si, :], Praw7[:, si, :], msk[:, :]).then_inc(dv.sem, 1)
                    mskv = dv.inc()
                    assert mskv == msk_val[i]

                    if i >= 2:
                        emit_pv7(i - 2)
            emit_pv7(30)
            emit_pv7(31)

            @p.block.vector
            def _(v, pv_flush=pv_flush):
                v.wait_ge(pe.sem, pv_flush)
                v.tensor_copy(Ofp[:, :], psBIG[0:64, 0:2048]).then_inc(dv.sem, 1)
            copy_v = dv.inc()
            prev_head[0] = {"hsl": hsl, "pp": pp, "pv_flush": pv_flush,
                            "copy_v": copy_v}

    pending = prev_head[0]

    @p.block.scalar
    def _(s, pending=pending):
        s.wait_ge(pe.sem, pending["pv_flush"])
        s.activation(lrow65[64:65, 0:TLOC], psBIG[64:65, 0:TLOC],
                     AF.Ln).then_inc(ac.sem, 1)
        s.wait_ge(ac.sem, ac.v + 1)
        s.activation(lrow65[64:65, 0:TLOC], lrow65[64:65, 0:TLOC],
                     AF.Exp, scale=co_n1[64:65, 0:1]).then_inc(ac.sem, 1)
    ac.inc()
    recip_val = ac.inc()

    @p.block.tensor
    def _(t):
        t.wait_ge(ac.sem, recip_val)
        t.wait_ge(dv.sem, pending["copy_v"])
        for q in range(4):
            mm = t.matmul(psBIG[0:64, q * 512:(q + 1) * 512], ones_b64[64:65, 0:64],
                          lrow65[64:65, q * 512:(q + 1) * 512], start=True, stop=True)
        mm.then_inc(pe.sem, 1)
    rv = pe.inc()

    @p.block.vector
    def _(v):
        v.wait_ge(pe.sem, rv)
        v.tensor_mul(O7[pending["hsl"], pending["pp"], :], Ofp[:, :],
                     psBIG[0:64, 0:2048]).then_inc(dv.sem, 1)
    dv.inc()
    attn_done = dv.v


    @p.block.gpsimd
    def _(g):
        g.wait_ge(pe.sem, pe.v)
        g.dma_start(out=wbuf[:, :, 1024:2048],
                    in_=Wout_d.rearrange("(k p) j -> p k j", p=128)).then_inc(swt.sem, 16)
    wo_ready = (swt.sem, swt.inc(16))

    xr_ev = [None, None]
    xuse = [None, None]
    it2 = [0]

    def cb_out(m, q, ps, mmv):
        sl = it2[0] % 2

        @p.block.sync
        def _(sy):
            sy.wait_ge(dv.sem, xuse[sl] if xuse[sl] is not None else attn_done)
            xr_ev[sl] = xsl[sl].issue(sy, xch[:, sl, 0:512], x_d[bass.ds(m * 128, 128), q * 512:(q + 1) * 512])

        @p.block.vector
        def _(v):
            v.wait_ge(pe.sem, mmv)
            v.wait_ge(xr_ev[sl][0], xr_ev[sl][1])
            v.scalar_tensor_tensor(xch[:, sl, 512:1024][:, 0:512], ps, 0.0, xch[:, sl, 0:512],
                                   ALU.add, ALU.add).then_inc(dv.sem, 1)
        outv = dv.inc()
        xuse[sl] = outv

        @p.block.sync
        def _(sy):
            sy.wait_ge(dv.sem, outv)
            xsl[sl].issue(sy, xo_d[bass.ds(m * 128, 128), q * 512:(q + 1) * 512], xch[:, sl, 512:1024][:, 0:512])
        it2[0] += 1
        return outv

    emit_mm_dense(p, wbuf.ap()[:, :, 1024:2048].rearrange("p k j -> p k j"), O7, NCH, cb_out,
                  wait_w=wo_ready, wait_dv=attn_done, psum_rot=slots)

    @p.block.sync
    def _(sy):
        for s in xsl:
            sy.wait_ge(s.ctr.sem, s.ctr.v)
    ev = (xsl[0].ctr.sem, xsl[0].ctr.v)
    return [(s.ctr.sem, s.ctr.v) for s in xsl]



def sub_att02(p, x_d, xo_d, rope_bias, after=None):
    nc = p.nc
    wname = "Wqkv0_my" if rope_bias else "Wqkv3_my"
    Wqkv_d = p.in_ap(wname, [D, 768], F32)
    if rope_bias:
        bq_d = p.in_ap("bq_my", [128, 2], F32)
        bk_d = p.in_ap("bk_my", [128, 2], F32)
        bv_d = p.in_ap("bv_my", [1, 256], F32)
        Wp_d = p.in_ap("WprojT", [D, D], F32)
        bp_d = p.in_ap("b_proj", [128, NCH], F32)
        f_d = p.in_ap("f_rope", [128, S], F32)
    else:
        Wp_d = p.in_ap("WOT", [D, D], F32)
    tupp_d = p.in_ap("t_upp", [128, 128], BF16)

    cXn = p.dram("cXn", [2, D, TLOC // 2], BF16)
    gXn = p.dram("gXn", [2, GRP * D, TLOC // 2], BF16)
    gXn_v = gXn.ap().rearrange("h (r c p) t -> h r p c t", r=GRP, p=128)
    cO = p.dram("cO", [4, 4 * HD, TLOC], BF16)
    gO = p.dram("gO", [4, GRP * 4 * HD, TLOC], BF16)
    gO_v = gO.ap().rearrange("ch (k p) t -> ch p k t", p=128)
    shb = None

    xres = p.xres
    Q2t = p.sbuf("Q2t", [128, 2, S], BF16)
    xn = Q2t.ap().rearrange("p a s -> p (a s)").rearrange("p (c t) -> p c t", c=NCH)
    Q2 = Q2t.ap()
    K2t = p.sbuf("K2t", [128, 2, S], BF16)
    K2 = K2t.ap()
    V2t = p.sbuf("V2t", [128, 2, 64, 130], BF16)
    V2 = V2t.ap()
    v2f = V2t.ap().rearrange("p a k c -> p (a k c)").bitcast(F32)
    p.r_m = v2f[0:1, 0:2048]
    p.r_a = v2f[0:1, 2048:4096]
    p.r_b = v2f[0:1, 4096:6144]
    XG = p.sbuf("XG", [128, NCH, 1024], BF16)
    wq = p.sbuf("wq", [128, NCH, 768], BF16)
    P4 = p.sbuf("P4", [128, 2, 512], BF16)
    tmpf = P4.ap().rearrange("p a t -> p (a t)").bitcast(F32)[:, 0:512]
    if rope_bias:
        f_bf = p.sbuf("f_bf", [128, S], BF16)
        WP = f_bf.ap().rearrange("p (k j) -> p k j", k=NCH)
        bq_sb = p.sbuf("bq_sb", [128, 2], F32)
        bk_sb = p.sbuf("bk_sb", [128, 2], F32)
        bv_sb = P4.ap().rearrange("p a t -> p (a t)")[0:1, 512:768]
        bp_sb = p.sbuf("bp_sb", [128, NCH], F32)

    else:
        WP = p.sbuf("WP", [128, NCH, 1024], BF16).ap()
    mx = p.sbuf("mx", [2, 8], F32)
    shb = p.dram("shb02", [2, S], F32)
    scr512 = P4.ap().rearrange("p a t -> p (a t)")[:, 0:512]
    shst = None
    onesb = p.sbuf("onesb", [128, 128], BF16)
    co8 = p.sbuf("co8_02", [128, 1], F32)
    rl = None
    tupp = wq.ap().rearrange("p k j -> p (k j)")[:, 0:128]
    p.ones128_bf = p.sbuf("ones128_bf", [128, 1], BF16)
    p.ones1x128_f = p.sbuf("ones1x128_f", [1, 128], F32)
    onespat = p.sbuf("onespat02", [128, 2], BF16)

    tmp_f32 = K2t.ap().rearrange("p a s -> p (a s)").bitcast(F32)[:, 0:1024]
    xg_flat = XG.ap().rearrange("p c t -> p (c t)")
    wq_flat_b = wq.ap().rearrange("p k j -> p (k j)")
    wq_flat_f = wq_flat_b.bitcast(F32)
    rl = wq_flat_f[0:1, 1792:2304]
    Ost = wq_flat_f[0:64, 256:768]
    Ostb = wq_flat_b[0:64, 4608:5120]
    shst = wq_flat_f[0:2, 2560:3072]
    OQ = Q2t.ap().rearrange("p a s -> p (a s)").rearrange("p (c t) -> p c t", c=NCH)

    p.psBIG = p.psum("psBIG", [128, 2048], F32)
    psS = p.psum("psS", [128, 1024], F32)
    psO = p.psum("psO", [128, 512], F32)
    psV = p.psum("psV", [128, 512], F32)
    psBIG = p.psBIG

    pe, dv, ac, io = p.sPE, p.sDV, p.sAC, p.sIO

    @p.block.gpsimd
    def _(g):
        g.memset(p.ones128_bf[:, :], 1.0)
        g.memset(p.ones1x128_f[:, :], 1.0)
        g.memset(co8[:, :], 0.125)
        g.memset(onesb[:, :], 1.0)
        g.memset(onespat[:, :], 0.0).then_inc(p.sCN.sem, 1)
        g.wait_ge(p.sCN.sem, p.sCN.inc())
        g.memset(onespat[0:64, 0:1], 1.0)
        g.memset(onespat[64:128, 1:2], 1.0).then_inc(p.sCN.sem, 1)
    cn_done = p.sCN.inc()

    x_ready = emit_load_x(p, x_d, xres, after)
    swt = p.dma_sem("sWts")

    @p.block.gpsimd
    def _(g):
        g.dma_start(out=wq[:, :, :], in_=Wqkv_d.rearrange("(k p) j -> p k j", p=128)).then_inc(swt.sem, 16)
        n = 16
        if rope_bias:
            g.dma_start(out=bq_sb[:, :], in_=bq_d).then_inc(swt.sem, 16)
            g.dma_start(out=bk_sb[:, :], in_=bk_d).then_inc(swt.sem, 16)
            g.dma_start(out=bv_sb[:, :], in_=bv_d).then_inc(swt.sem, 16)
            g.dma_start(out=bp_sb[:, :], in_=bp_d).then_inc(swt.sem, 16)
            g.dma_start(out=f_bf[:, :], in_=f_d).then_inc(swt.sem, 16)
            n += 80
        swt.inc(n)
    w_ready = (swt.sem, swt.v)

    xn_ready = emit_ln(p, xres, xn, K2t.ap().rearrange("p a s -> p (a s)").rearrange("p (c t) -> p c t", c=NCH), tmp_f32, x_ready)

    cxn_sl = [DmaSlot(p, f"sCxn{h}") for h in range(2)]
    ccx_half = [None, None]
    for h in range(2):
        @p.block.sync
        def _(sy, h=h):
            sy.wait_ge(dv.sem, xn_ready)
            cxn_sl[h].issue(sy, cXn.ap()[h].rearrange("(c p) t -> p c t", p=128),
                            xn[:, :, h * 1024:(h + 1) * 1024])

        @p.block.gpsimd
        def _(g, h=h):
            g.wait_ge(cxn_sl[h].ctr.sem, cxn_sl[h].ctr.v)
            g.collective_compute("AllGather", ALU.bypass, replica_groups=[[0, 1, 2, 3], [4, 5, 6, 7]],
                                 ins=[cXn.ap()[h].opt()], outs=[gXn.ap()[h].opt()]).then_inc(io.sem, 1)
        ccx_half[h] = io.inc()
    ccx_done = ccx_half[1]

    @p.block.gpsimd
    def _(g):
        g.wait_ge(dv.sem, xn_ready)
        g.memset(V2t[:, :, :, :], 0.0).then_inc(p.sCN.sem, 1)
        g.wait_ge(p.sCN.sem, p.sCN.inc())
        g.memset(V2[:, :, :, 64:65], 1.0)
        g.memset(V2[:, :, :, 129:130], 1.0).then_inc(p.sCN.sem, 1)
    vones_done = p.sCN.inc()

    xg_s = p.dma_sem("sXg")
    slots6 = [psBIG[:, i * 512:(i + 1) * 512] for i in range(4)] + [psS[:, 0:512], psS[:, 512:1024]]
    ps_free = [xn_ready] * 6
    vt_free = [None]
    xg_used = None
    for RH in range(2 * GRP):
        half, R = RH // GRP, RH % GRP
        @p.block.sync
        def _(sy):
            sy.wait_ge(io.sem, ccx_half[half])
            if RH > 0:
                sy.wait_ge(pe.sem, xg_used)
            if xg_s.v > 0:
                sy.wait_ge(xg_s.sem, xg_s.v)
            sy.dma_start(out=XG[:, :, :], in_=gXn_v[half, R]).then_inc(xg_s.sem, 16)
        xg_s.inc(16)
        xg_ev = (xg_s.sem, xg_s.v)

        for s4 in range(2):
            tsl = slice(s4 * 512, (s4 + 1) * 512)
            gsl = slice(R * 2048 + half * 1024 + s4 * 512, R * 2048 + half * 1024 + (s4 + 1) * 512)
            for j in range(4):
                @p.block.tensor
                def _(t):
                    if RH == 0 and s4 == 0 and j == 0:
                        t.wait_ge(w_ready[0], w_ready[1])
                    t.wait_ge(xg_ev[0], xg_ev[1])
                    t.wait_ge(dv.sem, ps_free[j])
                    for k in range(NCH):
                        mm = t.matmul(slots6[j], wq[:, k, j * 128:(j + 1) * 128], XG[:, k, tsl],
                                      start=(k == 0), stop=(k == NCH - 1))
                    mm.then_inc(pe.sem, 1)
                mmv = pe.inc()

                @p.block.vector
                def _(v):
                    v.wait_ge(pe.sem, mmv)
                    pj = j % 2
                    if j < 2:
                        dst = Q2[:, pj, gsl]
                        bias = bq_sb[:, pj:pj + 1] if rope_bias else None
                    else:
                        dst = K2[:, pj, gsl]
                        bias = bk_sb[:, pj:pj + 1] if rope_bias else None
                    if rope_bias:
                        v.scalar_tensor_tensor(dst, slots6[j], bias, f_bf[:, gsl],
                                               ALU.add, ALU.mult).then_inc(dv.sem, 1)
                    else:
                        v.tensor_copy(dst, slots6[j]).then_inc(dv.sem, 1)
                ps_free[j] = dv.inc()

        for tt in range(8):
            kt = R * 16 + half * 8 + tt
            ttsl = slice(tt * 128, (tt + 1) * 128)
            for pj in range(2):
                @p.block.tensor
                def _(t):
                    if vt_free[0] is not None:
                        t.wait_ge(dv.sem, vt_free[0])
                    if RH == 0 and tt == 0 and pj == 0:
                        t.wait_ge(p.sCN.sem, vones_done)
                    for k in range(NCH):
                        last_k = (k == NCH - 1) and not rope_bias
                        mm = t.matmul(psV[:, 0:128], XG[:, k, ttsl],
                                      wq[:, k, 512 + pj * 128:512 + (pj + 1) * 128],
                                      start=(k == 0), stop=last_k)
                    if rope_bias:
                        mm = t.matmul(psV[:, 0:128], onesb[0:1, 0:128],
                                      bv_sb[0:1, pj * 128:(pj + 1) * 128],
                                      start=False, stop=True)
                    mm.then_inc(pe.sem, 1)
                mmv2 = pe.inc()

                @p.block.vector
                def _(v):
                    v.wait_ge(pe.sem, mmv2)
                    v.tensor_copy(V2[:, pj, kt, 0:64], psV[:, 0:64])
                    v.tensor_copy(V2[:, pj, kt, 65:129], psV[:, 64:128]).then_inc(dv.sem, 1)
                vt_free[0] = dv.inc()
        xg_used = pe.v
    proj_done = dv.v

    nrm_prev = [None]

    def headmax02(src2, col0):
        for pj in range(2):
            for seg in range(4):
                ssl = slice(seg * 2048, (seg + 1) * 2048)

                @p.block.vector
                def _(v):
                    v.wait_ge(dv.sem, dv.v)
                    if nrm_prev[0] is not None:
                        v.wait_ge(pe.sem, nrm_prev[0])
                    if pj == 0 and seg == 0 and col0 == 0:
                        v.wait_ge(dv.sem, proj_done)
                    v.tensor_mul(xg_flat[:, 0:2048],
                                 src2[:, pj, ssl], src2[:, pj, ssl]).then_inc(dv.sem, 1)
                sqv = dv.inc()

                for q in range(4):
                    first = (seg == 0 and q == 0)

                    @p.block.tensor
                    def _(t):
                        t.wait_ge(dv.sem, sqv if q == 0 else dv.v)
                        t.matmul(psO[0:2, 0:512], onespat[:, :],
                                 xg_flat[:, q * 512:(q + 1) * 512],
                                 start=True, stop=True).then_inc(pe.sem, 1)
                    nmv = pe.inc()
                    nrm_prev[0] = nmv

                    @p.block.vector
                    def _(v):
                        v.wait_ge(pe.sem, nmv)
                        if first:
                            v.tensor_reduce(mx[0:2, col0 + pj:col0 + pj + 1], psO[0:2, 0:512],
                                            AX.X, ALU.max).then_inc(dv.sem, 1)
                            dv.inc()
                        else:
                            v.tensor_reduce(mx[0:2, 4 + col0 + pj:4 + col0 + pj + 1], psO[0:2, 0:512],
                                            AX.X, ALU.max).then_inc(dv.sem, 1)
                            v.wait_ge(dv.sem, dv.inc())
                            v.tensor_max(mx[0:2, col0 + pj:col0 + pj + 1], mx[0:2, col0 + pj:col0 + pj + 1],
                                         mx[0:2, 4 + col0 + pj:4 + col0 + pj + 1]).then_inc(dv.sem, 1)
                            dv.inc()

    headmax02(K2, 0)
    mx_done = dv.v

    tup_s = p.dma_sem("sTup")

    @p.block.gpsimd
    def _(g):
        g.wait_ge(dv.sem, proj_done)
        g.dma_start(out=tupp, in_=tupp_d).then_inc(tup_s.sem, 16)
    tup_s.inc(16)

    @p.block.gpsimd
    def _(g):
        if rope_bias:
            g.wait_ge(dv.sem, proj_done)
        g.dma_start(out=WP, in_=Wp_d.rearrange("(k p) j -> p k j", p=128)).then_inc(swt.sem, 16)
    wp_ready = (swt.sem, swt.inc(16))

    xg_flat2 = XG.ap().rearrange("p c t -> p (c t)")
    Qp = xg_flat2[:, 0:2048].rearrange("p (s h t) -> p s h t", s=2, h=2)
    Kp = xg_flat2[:, 2048:3072].rearrange("p (s k) -> p s k", s=8)
    P4a = [P4.ap()[:, 0, :], P4.ap()[:, 1, :], xg_flat2[:, 3072:3584]]
    qS = [[DmaSlot(p, f"sQp{sl}{j}") for j in range(4)] for sl in range(2)]
    kS = [DmaSlot(p, f"sKp{i}") for i in range(8)]
    kb_ev = [None] * 8
    kp_lastuse = [None] * 8
    qp_lastuse = [None, None]
    qp_ready = [None, None]
    Sslot = [psBIG[:, 0:512], psBIG[:, 512:1024], psBIG[:, 1536:2048]]
    psBB = psBIG[:, 1024:1536]

    def emit_og_chunk(c):
        @p.block.gpsimd
        def _(g, c=c):
            g.wait_ge(cow.ctr.sem, cow.ctr.v)
            g.collective_compute("AllGather", ALU.bypass,
                                 replica_groups=[[0, 1, 2, 3], [4, 5, 6, 7]],
                                 ins=[cO.ap()[c].opt()], outs=[gO.ap()[c].opt()]).then_inc(io.sem, 1)
        io.inc()
    s_exp = [None] * 3
    p_cons = [None] * 3
    Oslot_par = [[psS[:, 0:512], psS[:, 512:1024]], [psO[:, 0:512], psV[:, 0:512]]]
    o_free_banks = [[mx_done, mx_done], [mx_done, mx_done]]
    pending_epi = [None]
    cow = DmaSlot(p, "sCO")

    def emit_qp_build(pj, st):
        sq = st % 2
        qsl2 = slice(st * 512, (st + 1) * 512)

        @p.block.gpsimd
        def _(g):
            if st < 2:
                g.wait_ge(shs.ctr.sem, shs.ctr.v)
                g.wait_ge(dv.sem, mx_done)
            if pj == 0 and st == 0:
                g.memset(Kp[64:96, :, :], 0.0)
                g.memset(Qp[64:96, :, :, :], 0.0).then_inc(p.sCN.sem, 1)
                g.wait_ge(p.sCN.sem, p.sCN.inc())
                g.memset(Kp[64:65, :, :], 1.0)
            if qp_lastuse[sq] is not None:
                g.wait_ge(pe.sem, qp_lastuse[sq])
            evs = [
                qS[sq][0].issue(g, Qp[0:64, sq, 0, :], Q2[0:64, pj, qsl2]),
                qS[sq][1].issue(g, Qp[0:64, sq, 1, :], Q2[64:128, pj, qsl2]),
                qS[sq][2].issue(g, Qp[64:65, sq, 0, :], shb.ap()[0:1, qsl2]),
                qS[sq][3].issue(g, Qp[64:65, sq, 1, :], shb.ap()[1:2, qsl2]),
            ]
        qp_ready[sq] = [(s.ctr.sem, s.ctr.v) for s in qS[sq]]
    shs = DmaSlot(p, "sShs")
    sh_prev = [None]

    for pj in range(2):
        dv0_sh = dv.v
        ac0_sh = ac.v
        for ch in range(16):
            csl = slice(ch * 512, (ch + 1) * 512)
            sl2 = ch % 2

            @p.block.vector
            def _(v):
                v.wait_ge(dv.sem, dv.v)
                if sh_prev[0] is not None:
                    v.wait_ge(pe.sem, sh_prev[0])
                if pj == 0 and ch == 0:
                    v.wait_ge(dv.sem, mx_done)
                v.tensor_mul(scr512[:, :], Q2[:, pj, csl], K2[:, pj, csl]).then_inc(dv.sem, 1)
            d1 = dv.inc()

            @p.block.tensor
            def _(t):
                t.wait_ge(dv.sem, d1)
                t.matmul(psO[0:2, 0:512], onespat[:, :], scr512[:, :],
                         start=True, stop=True).then_inc(pe.sem, 1)
            m1 = pe.inc()

            @p.block.vector
            def _(v):
                v.wait_ge(pe.sem, m1)
                v.tensor_mul(scr512[:, :], Q2[:, pj, csl], Q2[:, pj, csl]).then_inc(dv.sem, 1)
            d2 = dv.inc()

            @p.block.tensor
            def _(t):
                t.wait_ge(dv.sem, d2)
                t.matmul(psV[0:2, 0:512], onespat[:, :], scr512[:, :],
                         start=True, stop=True).then_inc(pe.sem, 1)
            m2 = pe.inc()
            sh_prev[0] = m2

            @p.block.vector
            def _(v):
                v.wait_ge(pe.sem, m2)
                if shs.ctr.v > 0:
                    v.wait_ge(shs.ctr.sem, shs.ctr.v)
                v.tensor_scalar(shst[0:2, :], psV[0:2, 0:512],
                                mx[0:2, pj:pj + 1], None, ALU.mult).then_inc(dv.sem, 1)
            d3 = dv.inc()

            assert d3 == dv0_sh + 5 * ch + 3

            @p.block.scalar
            def _(s):
                s.wait_ge(dv.sem, d3)
                s.activation(shst[0:2, :], shst[0:2, :], AF.Sqrt).then_inc(ac.sem, 1)
            a1 = ac.inc()
            assert a1 == ac0_sh + ch + 1

            @p.block.vector
            def _(v):
                v.wait_ge(ac.sem, a1)
                v.tensor_add(shst[0:2, :], shst[0:2, :], psO[0:2, 0:512]).then_inc(dv.sem, 1)
                v.wait_ge(dv.sem, dv.inc())
                v.tensor_scalar_mul(shst[0:2, :], shst[0:2, :], -0.5).then_inc(dv.sem, 1)
            d4 = dv.inc()

            @p.block.sync
            def _(sy):
                sy.wait_ge(dv.sem, d4)
                if pj == 1 and ch == 0:
                    for qsl_ in qS:
                        for s_ in qsl_:
                            if s_.ctr.v > 0:
                                sy.wait_ge(s_.ctr.sem, s_.ctr.v)
                shs.issue(sy, shb.ap()[:, csl], shst[0:2, :])

        o_free_banks[1] = [dv.v, dv.v]

        emit_qp_build(pj, 0)
        emit_qp_build(pj, 1)

        for st in range(16):
            if pj == 1 and st >= 5 and (st - 5) % 4 == 0:
                emit_og_chunk((st - 5) // 4)
            qsl = slice(st * 512, (st + 1) * 512)
            ktmax = 4 * (st + 1)
            iters = []
            for kt in range(ktmax):
                diag_sub = kt - 4 * st
                q0 = diag_sub * 128 if diag_sub >= 0 else 0
                for hh in range(2):
                    iters.append((kt, hh, len(iters) % 3, q0, 512 - q0, diag_sub))

            L = len(iters)
            par = st % 2
            Osl = Oslot_par[par]

            pe0 = pe.v
            ac0 = ac.v
            carry_pcons = list(p_cons)
            i_epi = min(8, L - 1) if pending_epi[0] is not None else None
            qk_val = [None] * L
            pv_val = [None] * L
            cnt = pe0
            for i in range(L):
                cnt += 1
                qk_val[i] = cnt
                if i >= 2:
                    cnt += 1
                    pv_val[i - 2] = cnt
                if i == i_epi:
                    cnt += 2
            pv_val[L - 2] = cnt + 1
            pv_val[L - 1] = cnt + 2
            qp_ev = qp_ready[st % 2]
            qp_lastuse[st % 2] = qk_val[-1]
            if st + 2 < 16:
                emit_qp_build(pj, st + 2)

            @p.block.scalar
            def _(s, iters=iters, qk_val=qk_val, pv_val=pv_val, carry=carry_pcons):
                for i, (kt, hh, si, q0, Nq, ds) in enumerate(iters):
                    s.wait_ge(pe.sem, qk_val[i])
                    if i >= 3:
                        s.wait_ge(pe.sem, pv_val[i - 3])
                    elif carry[si] is not None:
                        s.wait_ge(pe.sem, carry[si])
                    s.activation(P4a[si][:, 0:Nq], Sslot[si][:, 0:Nq], AF.Exp,
                                 scale=co8[:, 0:1]).then_inc(ac.sem, 1)
            ac.inc(L)

            mskv_arr = [None] * L

            def emit_pv(j, iters=iters, par=par, Osl=Osl, ac0=ac0,
                        pv_val=pv_val, ktmax=ktmax, pj=pj):
                kt, hh, si, q0, Nq, diag_sub = iters[j]
                expv = ac0 + j + 1

                @p.block.tensor
                def _(t):
                    if mskv_arr[j] is not None:
                        t.wait_ge(dv.sem, mskv_arr[j])
                    else:
                        t.wait_ge(ac.sem, expv)
                    if kt == 0:
                        t.wait_ge(dv.sem, o_free_banks[par][hh])
                    t.matmul(Osl[hh][0:65, q0:512], V2[:, pj, kt, hh * 65:(hh + 1) * 65],
                             P4a[si][:, 0:Nq], start=(kt == 0), stop=(kt == ktmax - 1),
                             skip_group_check=True).then_inc(pe.sem, 1)
                p_cons[si] = pe.inc()
                assert p_cons[si] == pv_val[j]

            for i, (kt, hh, si, q0, Nq, diag_sub) in enumerate(iters):
                    hsl = slice(hh * 64, hh * 64 + 64)
                    ksl = i % 8

                    @p.block.sync
                    def _(sy):
                        if kp_lastuse[ksl] is not None:
                            sy.wait_ge(pe.sem, kp_lastuse[ksl])
                        if pj == 0 and st == 0 and i < 4:
                            sy.wait_ge(shs.ctr.sem, shs.ctr.v)
                        kb_ev[ksl] = kS[ksl].issue(sy, Kp[0:64, ksl, :],
                                                   K2[hsl, pj, kt * 128:(kt + 1) * 128])

                    @p.block.tensor
                    def _(t):
                        t.wait_ge(kb_ev[ksl][0], kb_ev[ksl][1])
                        if i < 2:
                            for ev in qp_ev:
                                t.wait_ge(ev[0], ev[1])
                        if s_exp[si] is not None:
                            t.wait_ge(ac.sem, s_exp[si])
                        t.matmul(Sslot[si][:, 0:Nq], Kp[0:96, ksl, :],
                                 Qp[0:96, st % 2, hh, q0:512],
                                 start=True, stop=True).then_inc(pe.sem, 1)
                    qkv_ = pe.inc()
                    assert qkv_ == qk_val[i]
                    kp_lastuse[ksl] = qkv_

                    expv = ac0 + i + 1
                    s_exp[si] = expv

                    if diag_sub >= 0:
                        @p.block.vector
                        def _(v):
                            v.wait_ge(ac.sem, expv)
                            v.wait_ge(tup_s.sem, tup_s.v)
                            v.tensor_mul(P4a[si][:, 0:128], P4a[si][:, 0:128], tupp).then_inc(dv.sem, 1)
                        mskv_arr[i] = dv.inc()

                    if i >= 2:
                        emit_pv(i - 2)
                    if i == i_epi:
                        pending_epi[0]()
                        pending_epi[0] = None

            emit_pv(L - 2)
            emit_pv(L - 1)
            if pending_epi[0] is not None:
                pending_epi[0]()
                pending_epi[0] = None

            def make_epilogue(pj=pj, st=st, par=par, qsl=qsl, pv_done=pe.v):
                def emit():
                    rbv_prev = None
                    for hh in range(2):
                        @p.block.vector
                        def _(v, hh=hh, rbv_prev=rbv_prev):
                            v.wait_ge(pe.sem, pv_done)
                            if rbv_prev is not None:
                                v.wait_ge(pe.sem, rbv_prev)
                            v.reciprocal(rl[:, :], Oslot_par[par][hh][64:65, 0:512]).then_inc(dv.sem, 1)
                            v.wait_ge(dv.sem, dv.inc())
                            v.tensor_copy(Ost[:, :], Oslot_par[par][hh][0:64, 0:512]).then_inc(dv.sem, 1)
                        oe = dv.inc()

                        @p.block.tensor
                        def _(t):
                            t.wait_ge(dv.sem, oe)
                            t.matmul(psBB[0:64, 0:512], p.ones1x128_f[0:1, 0:64], rl[0:1, :],
                                     start=True, stop=True).then_inc(pe.sem, 1)
                        rbv = pe.inc()
                        rbv_prev = rbv

                        @p.block.vector
                        def _(v):
                            v.wait_ge(pe.sem, rbv)
                            if cow.ctr.v > 0:
                                v.wait_ge(cow.ctr.sem, cow.ctr.v)
                            v.tensor_mul(Ostb[:, :], Ost[:, :], psBB[0:64, 0:512]).then_inc(dv.sem, 1)
                        o_free_banks[par][hh] = dv.inc()

                        @p.block.sync
                        def _(sy, mfree=o_free_banks[par][hh], hh=hh):
                            sy.wait_ge(dv.sem, mfree)
                            r0_ = (2 * pj + hh) * 64
                            cow.issue(sy, cO.ap()[st // 4, r0_:r0_ + 64,
                                                  (st % 4) * 512:(st % 4) * 512 + 512],
                                      Ostb[:, :])
                return emit

            pending_epi[0] = make_epilogue()

        if pending_epi[0] is not None:
            pending_epi[0]()
            pending_epi[0] = None
    attn_done = dv.v

    emit_og_chunk(3)
    cco_done = io.v

    oq_s = p.dma_sem("sOq")

    @p.block.sync
    def _(sy):
        sy.wait_ge(io.sem, cco_done)
        rq = nc.partition_id() % 4
        sy.dma_start(out=OQ, in_=gO_v[rq]).then_inc(oq_s.sem, 16)
    oq_s.inc(16)
    oq_ev = (oq_s.sem, oq_s.v)

    slots = [psBIG[:, i * 512:(i + 1) * 512] for i in range(4)]

    def cb_out(m, q, ps, mmv):
        @p.block.vector
        def _(v):
            v.wait_ge(pe.sem, mmv)
            v.wait_ge(dv.sem, dv.v)
            bias = bp_sb[:, m:m + 1] if rope_bias else 0.0
            v.scalar_tensor_tensor(tmpf, ps, bias, xres[:, m, q * 512:(q + 1) * 512],
                                   ALU.add, ALU.add).then_inc(dv.sem, 1)
            v.wait_ge(dv.sem, dv.inc())
            v.tensor_copy(xres[:, m, q * 512:(q + 1) * 512], tmpf).then_inc(dv.sem, 1)
        return dv.inc()

    @p.block.tensor
    def _(t):
        t.wait_ge(oq_ev[0], oq_ev[1])
        t.wait_ge(wp_ready[0], wp_ready[1])
    emit_mm_dense(p, WP, OQ, NCH, cb_out, wait_dv=attn_done, psum_rot=slots)

    return emit_store_x(p, xres, xo_d, wait_dv=dv.v)


def build_ffn0():
    return _standalone(sub_ffn0, "f0_")


def build_ffn1():
    return _standalone(sub_ffn1, "f1_")


def build_att5():
    return _standalone(sub_att5, "a5_")


def build_att7():
    return _standalone(sub_att7, "a7_")


def build_att0():
    return _standalone(lambda p, x, xo, after=None: sub_att02(p, x, xo, True, after), "a0_")


def build_att2():
    return _standalone(lambda p, x, xo, after=None: sub_att02(p, x, xo, False, after), "a2_")





def emit_barrier(p, extra=()):
    vals = [(p.sPE.sem, p.sPE.v), (p.sDV.sem, p.sDV.v), (p.sAC.sem, p.sAC.v),
            (p.sIO.sem, p.sIO.v), (p.sCN.sem, p.sCN.v), (p.sST.sem, p.sST.v)]
    vals += list(extra)
    for secf in [p.block.tensor, p.block.vector, p.block.scalar, p.block.gpsimd, p.block.sync]:
        @secf
        def _(e):
            for s, v in vals:
                if v > 0:
                    e.wait_ge(s, v)


def build_fused():
    p = Prog("fx")
    x_d = p.nc.dram_tensor("x", [D, TLOC], F32, kind="ExternalInput").ap()
    xo_d = p.nc.dram_tensor("xo", [D, TLOC], F32, kind="ExternalOutput").ap()
    xs = [x_d] + [p.nc.dram_tensor(f"xmid{i}", [D, TLOC], F32).ap() for i in range(7)] + [xo_d]
    common_sems(p)
    p.xres = p.sbuf("xres_root", [128, NCH, TLOC], F32)
    p.start()
    subs = [
        (lambda pp, a, b, after=None: sub_att02(pp, a, b, True, after), "s0_"),
        (sub_ffn0, "s1_"),
        (lambda pp, a, b, after=None: sub_att02(pp, a, b, False, after), "s2_"),
        (sub_ffn0, "s3_"),
        (sub_att5, "s4_"),
        (sub_ffn1, "s5_"),
        (sub_att7, "s6_"),
        (sub_ffn1, "s7_"),
    ]
    io_map = [(xs[0], None), (None, None), (None, None), (None, None),
              (None, None), (None, xs[6]), (xs[6], xs[7]), (xs[7], xs[8])]
    for i, (fn, pref) in enumerate(subs):
        p.push_scope(pref)
        r = fn(p, io_map[i][0], io_map[i][1])
        p.pop_scope()
        evs = r if isinstance(r, list) else [r]
        emit_barrier(p, extra=[e for e in evs if e is not None])
    return p.finish()


_fused_cache = {}


def _get_fused():
    if "nc" not in _fused_cache:
        _fused_cache["nc"] = build_fused()
    return _fused_cache["nc"]


FUSED = True


_BUILDERS = {
    SUB_FFN0A: build_ffn0,
    SUB_FFN0B: build_ffn0,
    SUB_FFN1A: build_ffn1,
    SUB_FFN1B: build_ffn1,
    SUB_ATT5: build_att5,
    SUB_ATT7: build_att7,
    SUB_ATTA0: build_att0,
    SUB_ATTB0: build_att2,
}

_prog_cache = {}


def _get_prog(idx):
    key = _BUILDERS[idx].__name__
    if key not in _prog_cache:
        _prog_cache[key] = _BUILDERS[idx]()
    return _prog_cache[key]


def run_sublayer(idx, in_maps, x_cur):
    nc = _get_prog(idx)
    names = SUB_INPUTS[idx]
    maps = []
    for c in range(NCORE):
        m = {k: in_maps[c][k] for k in names if k != "x"}
        m["x"] = x_cur[c]
        maps.append(m)
    res = run_bass_kernel_spmd(nc, maps, core_ids=list(range(NCORE)))
    return [np.asarray(res.results[c]["xo"]) for c in range(NCORE)]


def kernel(**inputs):
    in_maps = host_prepare(inputs)
    out = np.empty((B, S, D), dtype=np.float32)
    if FUSED:
        nc = _get_fused()
        res = run_bass_kernel_spmd(nc, in_maps, core_ids=list(range(NCORE)))
        for c in range(NCORE):
            b, r = c // GRP, c % GRP
            out[b, r * TLOC:(r + 1) * TLOC, :] = np.asarray(res.results[c]["xo"]).T
        return out
    x_cur = [im["x"] for im in in_maps]
    for idx in range(8):
        x_cur = run_sublayer(idx, in_maps, x_cur)
    for c in range(NCORE):
        b, r = c // GRP, c % GRP
        out[b, r * TLOC:(r + 1) * TLOC, :] = x_cur[c].T
    return out
